# version 07
# baseline (speedup 1.0000x reference)
import numpy as np
import ml_dtypes
from contextlib import ExitStack

import concourse.bass as bass
import concourse.mybir as mybir
from concourse.alu_op_type import AluOpType
from concourse.bass_utils import run_bass_kernel_spmd

F32 = mybir.dt.float32
BF16 = mybir.dt.bfloat16
F16 = mybir.dt.float16
AF = mybir.ActivationFunctionType
OP = AluOpType
AX = mybir.AxisListType

NCORES = 8
T = 4096
TILE = 512
NT = T // TILE
NG = TILE // 128
E = 1024
H = 8
EPS = 1e-5
INV_E = 1.0 / 1024.0
BF = ml_dtypes.bfloat16


class Ctr:
    def __init__(self, sem):
        self.sem = sem
        self.n = 0

    def inc(self, instr, k=1):
        instr.then_inc(self.sem, k)
        self.n += k

    def wait(self, eng, n=None):
        eng.wait_ge(self.sem, self.n if n is None else n)

    def selfw(self, eng):
        eng.wait_ge(self.sem, self.n)


def build_nc(debug_taps=(), phases=("A", "AR1", "B1", "B2", "AR2", "C"), a_tiles=NT, a_stages=9):
    nc = bass.Bass("TRN2", target_bir_lowering=False)
    din = {}

    def dr(name, shape, dt, kind=None):
        t = nc.dram_tensor(name, shape, dt, kind=kind) if kind else nc.dram_tensor(name, shape, dt)
        din[name] = t
        return t

    x_d = dr("x", [T, E], F32, "ExternalInput")
    y_d = dr("y", [T, E], F32, "ExternalOutput")
    for nm in ("wvT", "wqT", "wlgT", "woutT", "wQT", "wKT", "winvT", "rw1T", "wOT"):
        dr(nm, [E, E], F32, "ExternalInput")
    dr("wg1T", [E, 128], F32, "ExternalInput")
    dr("wg2T", [128, E], F32, "ExternalInput")
    dr("lnwb", [128, E], F32, "ExternalInput")
    for nm in ("vfwdF", "wfwdF", "winvexF", "vinvF"):
        dr(nm, [E, 512], F32, "ExternalInput")
    dr("rw2T", [E, 8], F32, "ExternalInput")
    dr("rb1c", [128, 8], F32, "ExternalInput")
    dr("b2pe_in", [128, 16], F32, "ExternalInput")
    dr("alpha_in", [128, 1], F32, "ExternalInput")
    dr("UfF", [512, E], F32, "ExternalInput")
    dr("UiF", [512, E], F32, "ExternalInput")
    dr("bfw", [8, E], F32, "ExternalInput")
    dr("biv", [8, E], F32, "ExternalInput")
    dr("ident_in", [128, 128], BF16, "ExternalInput")
    dr("maskP_in", [128, 128], F32, "ExternalInput")
    dr("Econst_in", [8, 512], BF16, "ExternalInput")
    dr("rp_in", [128, 32], F32, "ExternalInput")
    dr("evenm_in", [128, 1], F32, "ExternalInput")
    dr("oddm_in", [128, 1], F32, "ExternalInput")

    dbg = {}
    for nm in debug_taps:
        if nm == "S_dbg":
            dbg[nm] = dr(nm, [128, 1024], F32, "ExternalOutput")

    o_sp = dr("o_sp", [T, E], F16)
    qh_sp = dr("qh_sp", [H, 128, T], BF16)
    x1_sp = dr("x1_sp", [T, E], F32)
    xvw_sp = dr("xvw_sp", [2, 4, 128, T], F16)
    cum_sp = dr("cum_sp", [2, 4, 128, T], F16)
    w_sp = dr("w_sp", [2, 8, T], BF16)
    wb16 = {}
    for nm, w in (("wvT", E), ("wqT", E), ("wlgT", E), ("woutT", E), ("wQT", E),
                  ("wKT", E), ("winvT", E), ("rw1T", E), ("wOT", E),
                  ("vfwdF", 512), ("wfwdF", 512), ("winvexF", 512), ("vinvF", 512),
                  ("UfF", None), ("UiF", None)):
        if nm in ("UfF", "UiF"):
            wb16[nm] = dr(nm + "_b", [128, 4 * E], BF16)
        else:
            wb16[nm] = dr(nm + "_b", [128, 8 * w], BF16)
    wb16["wg1T"] = dr("wg1T_b", [128, 8 * 128], BF16)
    wb16["wg2T"] = dr("wg2T_b", [128, E], BF16)
    wb16["rw2T"] = dr("rw2T_b", [128, 64], BF16)
    wb16["bfw"] = dr("bfw_b", [8, E], BF16)
    wb16["biv"] = dr("biv_b", [8, E], BF16)
    cc1_in = dr("cc1_in", [128 * E], F32)
    cc1_out = dr("cc1_out", [128 * E], F32)
    cc2_in = dr("cc2_in", [128 * 8], F32)
    cc2_out = dr("cc2_out", [128 * 8], F32)

    with ExitStack() as ctx:
        Eo = ctx.enter_context
        ident = Eo(nc.sbuf_tensor("ident", [128, 128], BF16))
        maskP = Eo(nc.sbuf_tensor("maskP", [128, 128], F32))
        rp = Eo(nc.sbuf_tensor("rp", [128, 32], F32))
        evenm = Eo(nc.sbuf_tensor("evenm", [128, 1], F32))
        oddm = Eo(nc.sbuf_tensor("oddm", [128, 1], F32))
        S_in_bf = Eo(nc.sbuf_tensor("S_in_bf", [128, H * 128], BF16))
        totF = Eo(nc.sbuf_tensor("totF", [128, 4], F32))
        totI = Eo(nc.sbuf_tensor("totI", [128, 4], F32))
        alphac = Eo(nc.sbuf_tensor("alphac", [128, 1], F32))
        epscol = Eo(nc.sbuf_tensor("epscol", [128, 1], F32))
        b2pe_b = Eo(nc.sbuf_tensor("b2pe_b", [128, 8], F32))

        sems = {nm: Ctr(Eo(nc.semaphore(nm)))
                for nm in ("ld", "gld", "t", "v", "s", "g", "so", "cc")}
        ld, gld, tS, vS, sS, gS, so, ccS = (sems[n] for n in
                                            ("ld", "gld", "t", "v", "s", "g", "so", "cc"))

        def emit_ln_bn(blk, src_of, sm, extra_wait=None):
            def l1(vec):
                if extra_wait is not None:
                    extra_wait(vec)
                vS.selfw(vec)
                for g in range(NG):
                    for h in range(2):
                        vS.inc(vec.bn_stats(sm["st"][:, (2 * g + h) * 6:(2 * g + h) * 6 + 6],
                                            src_of(g, h)))
                vS.selfw(vec)
                for g in range(NG):
                    vS.inc(vec.bn_aggr(sm["mv"][:, 2 * g:2 * g + 2],
                                       sm["st"][:, g * 12:g * 12 + 12]))

            blk.vector(l1)

            def l2(sc):
                vS.wait(sc)
                sS.selfw(sc)
                sS.inc(sc.activation(sm["rcol"][:, NG:2 * NG],
                                     bass.AP(sm["mv"], 1, [[2 * NG, 128], [2, NG]]),
                                     AF.Ln, bias=epscol[:, 0:1]))
                sS.selfw(sc)
                sS.inc(sc.activation(sm["rcol"][:, 0:NG], sm["rcol"][:, NG:2 * NG],
                                     AF.Exp, scale=-0.5))

            blk.scalar(l2)

        def emit_norm_transpose(blk, x_src, xh_tm, xh_fm, psTP, sm, g):
            def n1(vec):
                sS.wait(vec)
                tS.wait(vec)
                vS.selfw(vec)
                vS.inc(vec.tensor_scalar(xh_tm[:], x_src[:, g * 1024:(g + 1) * 1024],
                                         sm["mv"][:, 2 * g:2 * g + 1],
                                         sm["rcol"][:, g:g + 1],
                                         OP.subtract, OP.mult))

            blk.vector(n1)

            def n2(te):
                vS.wait(te)
                for ci in range(8):
                    ins = te.transpose(psTP[:, ci * 128:(ci + 1) * 128],
                                       xh_tm[:, ci * 128:(ci + 1) * 128], ident[:])
                tS.inc(ins)

            blk.tensor(n2)

            def n3(vec):
                tS.wait(vec)
                vS.selfw(vec)
                vS.inc(vec.tensor_copy(
                    bass.AP(xh_fm, g * 128, [[4096, 128], [512, 8], [1, 128]]),
                    bass.AP(psTP, 0, [[1024, 128], [128, 8], [1, 128]])))

            blk.vector(n3)

        b2tmp = Eo(nc.sbuf_tensor("b2tmp", [128, 16], F32))
        blk0_cm = nc.Block()
        blk0 = blk0_cm.__enter__()

        def p0_sync(sync):
            for dst, nm in ((ident, "ident_in"), (maskP, "maskP_in"), (rp, "rp_in"),
                            (evenm, "evenm_in"), (oddm, "oddm_in")):
                ld.selfw(sync)
                ld.inc(sync.dma_start(out=dst[:], in_=din[nm][:, :]), 16)
            ld.selfw(sync)
            ld.inc(sync.dma_start(out=b2tmp[:], in_=din["b2pe_in"][:, :]), 16)
            ld.selfw(sync)
            ld.inc(sync.dma_start(out=alphac[:], in_=din["alpha_in"][:, :]), 16)

        blk0.sync(p0_sync)

        def p0_g(gp):
            ld.wait(gp)
            gS.selfw(gp)
            gS.inc(gp.memset(epscol[:], EPS))
            gS.selfw(gp)
            gS.inc(gp.tensor_add(b2pe_b[:], b2tmp[:, 0:8], b2tmp[:, 8:16]))

        blk0.gpsimd(p0_g)

        with nc.sbuf_tensor("wstg32", [128, 2 * 8192], F32) as wstg32, \
             nc.sbuf_tensor("wstg16", [128, 2 * 8192], BF16) as wstg16:
            plist = []
            for nm, w in (("wvT", E), ("wqT", E), ("wlgT", E), ("wg1T", 128),
                          ("woutT", E), ("wQT", E), ("wKT", E), ("winvT", E),
                          ("rw1T", E), ("wOT", E), ("vfwdF", 512), ("wfwdF", 512),
                          ("winvexF", 512), ("vinvF", 512)):
                plist.append((nm, [[w, 128], [128 * w, 8], [1, w]], 8 * w))
            plist.append(("wg2T", [[E, 128], [1, E]], E))
            plist.append(("rw2T", [[8, 128], [128 * 8, 8], [1, 8]], 64))
            plist.append(("UfF", [[E, 128], [128 * E, 4], [1, E]], 4 * E))
            plist.append(("UiF", [[E, 128], [128 * E, 4], [1, E]], 4 * E))

            wls = [Ctr(Eo(nc.semaphore("wl0"))), Ctr(Eo(nc.semaphore("wl1")))]
            wss = [Ctr(Eo(nc.semaphore("ws0"))), Ctr(Eo(nc.semaphore("ws1")))]
            cast_done = []
            for i, (nm, ap, width) in enumerate(plist):
                s = i % 2
                buf = s * 8192

                def w_ld(sync, nm=nm, ap=ap, width=width, buf=buf, i=i, s=s):
                    if i >= 2:
                        vS.wait(sync, cast_done[i - 2])
                    wls[s].inc(sync.dma_start(
                        out=bass.AP(wstg32, buf, [[2 * 8192, 128], [1, width]]),
                        in_=bass.AP(din[nm], 0, ap)), 16)

                blk0.sync(w_ld)
                wl_n = wls[s].n
                ws_n = wss[s].n

                def w_cast(vec, nm=nm, width=width, buf=buf, i=i, s=s,
                           wl_n=wl_n, ws_n=ws_n):
                    vec.wait_ge(wls[s].sem, wl_n)
                    if i >= 2:
                        vec.wait_ge(wss[s].sem, ws_n)
                    vS.selfw(vec)
                    vS.inc(vec.tensor_copy(
                        bass.AP(wstg16, buf, [[2 * 8192, 128], [1, width]]),
                        bass.AP(wstg32, buf, [[2 * 8192, 128], [1, width]])))
                    cast_done.append(vS.n)

                blk0.vector(w_cast)

                def w_st(sc, nm=nm, width=width, buf=buf, i=i, s=s):
                    vS.wait(sc, cast_done[i])
                    wss[s].inc(sc.dma_start(
                        out=bass.AP(wb16[nm], 0, [[width, 128], [1, width]]),
                        in_=bass.AP(wstg16, buf, [[2 * 8192, 128], [1, width]])), 16)

                blk0.scalar(w_st)
            wst_final = (wss[0].n, wss[1].n)

            with nc.sbuf_tensor("bstg32", [8, 2 * E], F32) as bstg32, \
                 nc.sbuf_tensor("bstg16", [8, 2 * E], BF16) as bstg16:
                def b_ld(sync):
                    ld.selfw(sync)
                    ld.inc(sync.dma_start(
                        out=bass.AP(bstg32, 0, [[2 * E, 8], [1, E]]),
                        in_=din["bfw"][:, :]), 16)
                    ld.selfw(sync)
                    ld.inc(sync.dma_start(
                        out=bass.AP(bstg32, E, [[2 * E, 8], [1, E]]),
                        in_=din["biv"][:, :]), 16)

                blk0.sync(b_ld)

                def b_cast(vec):
                    ld.wait(vec)
                    vS.selfw(vec)
                    vS.inc(vec.tensor_copy(bstg16[:, :], bstg32[:, :]))

                blk0.vector(b_cast)

                def b_st(sc):
                    vS.wait(sc)
                    sS.selfw(sc)
                    so.selfw(sc)
                    so.inc(sc.dma_start(out=bass.AP(wb16["bfw"], 0, [[E, 8], [1, E]]),
                                        in_=bass.AP(bstg16, 0, [[2 * E, 8], [1, E]])), 16)
                    so.selfw(sc)
                    so.inc(sc.dma_start(out=bass.AP(wb16["biv"], 0, [[E, 8], [1, E]]),
                                        in_=bass.AP(bstg16, E, [[2 * E, 8], [1, E]])), 16)
                    so.wait(sc)

                blk0.scalar(b_st)
        blk0_cm.__exit__(None, None, None)

        if "A" not in phases:
            return nc
        with ExitStack() as actx:
            Ea = actx.enter_context
            wv_sb = Ea(nc.sbuf_tensor("wv_sb", [128, 8 * E], BF16))
            wq_sb = Ea(nc.sbuf_tensor("wq_sb", [128, 8 * E], BF16))
            wlg_sb = Ea(nc.sbuf_tensor("wlg_sb", [128, 8 * E], BF16))
            x_sb = Ea(nc.sbuf_tensor("xA", [128, 4096], F32))
            xh_tm = Ea(nc.sbuf_tensor("xhA_tm", [128, 1024], BF16))
            xh_fm = Ea(nc.sbuf_tensor("xhA_fm", [128, 4096], BF16))
            v_tm = Ea(nc.sbuf_tensor("v_tm", [128, 4096], BF16))
            qpre = Ea(nc.sbuf_tensor("qpre", [128, 4096], F16))
            sgq = Ea(nc.sbuf_tensor("sgq", [128, 512], F32))
            f_sb = Ea(nc.sbuf_tensor("f_sb", [128, 4096], F32))
            kf = Ea(nc.sbuf_tensor("kf", [128, 4096], F16))
            g_scr = Ea(nc.sbuf_tensor("g_scr", [128, 512], F32))
            Acum = Ea(nc.sbuf_tensor("Acum", [128, 4096], F32))
            carryA = Ea(nc.sbuf_tensor("carryA", [128, 8], F32))
            scrA = Ea(nc.sbuf_tensor("scrA", [128, 512], F32))
            scrA2 = Ea(nc.sbuf_tensor("scrA2", [128, 512], F32))
            scrB = Ea(nc.sbuf_tensor("scrB", [128, 512], F32))
            scrB2 = Ea(nc.sbuf_tensor("scrB2", [128, 512], F32))
            eACall = Ea(nc.sbuf_tensor("eACall", [128, 64], F32))
            qt_bf = Ea(nc.sbuf_tensor("qt_bf", [128, 4096], BF16))
            kt_bf = Ea(nc.sbuf_tensor("kt_bf", [128, 4096], BF16))
            khscr = Ea(nc.sbuf_tensor("khscr", [128, 512], BF16))
            khtm = Ea(nc.sbuf_tensor("khtm", [128, 4096], BF16))
            qhat_bf = Ea(nc.sbuf_tensor("qhat_bf", [128, 4096], BF16))
            psb_buf = Ea(nc.sbuf_tensor("psb_buf", [128, 1024], BF16))
            S_sb = Ea(nc.sbuf_tensor("S_sb", [128, 1024], F32))
            S_bf = Ea(nc.sbuf_tensor("S_bf", [128, 1024], BF16))
            o16 = Ea(nc.sbuf_tensor("o16", [128, 4096], F16))
            sm = {"st": Ea(nc.sbuf_tensor("smA_st", [128, 48], F32)),
                  "mv": Ea(nc.sbuf_tensor("smA_mv", [128, 8], F32)),
                  "rcol": Ea(nc.sbuf_tensor("smA_rcol", [128, 8], F32))}
            psTP = Ea(nc.psum_tensor("psTPa", [128, 1024], BF16))
            psProd = Ea(nc.psum_tensor("psProdA", [128, 2048], F32))
            psS = Ea(nc.psum_tensor("psSa", [128, 1024], F32))
            blkA = Ea(nc.Block())

            def pa_init(gp):
                gS.selfw(gp)
                gS.inc(gp.memset(S_sb[:], 0.0))
                gS.selfw(gp)
                gS.inc(gp.memset(S_bf[:], 0.0))
                gS.selfw(gp)
                gS.inc(gp.memset(carryA[:], 0.0))

            blkA.gpsimd(pa_init)

            def pa_wld(sync):
                so.wait(sync)
                sync.wait_ge(wss[0].sem, wst_final[0])
                sync.wait_ge(wss[1].sem, wst_final[1])
                for nm, dst in (("wvT", wv_sb), ("wqT", wq_sb), ("wlgT", wlg_sb)):
                    ld.selfw(sync)
                    ld.inc(sync.dma_start(
                        out=bass.AP(dst, 0, [[8 * E, 128], [1, 8 * E]]),
                        in_=wb16[nm][:, :]), 16)

            blkA.sync(pa_wld)

            for j in range(a_tiles):
                def a1(sync, j=j):
                    tS.wait(sync)
                    vS.wait(sync)
                    sS.wait(sync)
                    so.wait(sync)
                    ld.selfw(sync)
                    ld.inc(sync.dma_start(
                        out=bass.AP(x_sb, 0, [[4096, 128], [1024, 4], [1, 1024]]),
                        in_=bass.AP(x_d, j * TILE * E, [[E, 128], [128 * E, 4], [1, E]]),
                    ), 16)

                blkA.sync(a1)

                ldw = (ld.sem, ld.n)
                gw = (ld.sem, ld.n)

                emit_ln_bn(blkA, lambda g, h: x_sb[:, g * 1024 + h * 512: g * 1024 + h * 512 + 512],
                           sm, extra_wait=(lambda vec, w=ldw: vec.wait_ge(*w)))
                for g in range(NG):
                    emit_norm_transpose(blkA, x_sb, xh_tm, xh_fm, psTP, sm, g)

                if a_stages < 3:
                    continue
                vb_ve = vS.n

                def a3v(te, gw=gw):
                    te.wait_ge(*gw)
                    vS.wait(te)
                    sS.wait(te)
                    for g in range(NG):
                        if g >= 1:
                            te.wait_ge(vS.sem, vb_ve + g)
                        for half in range(2):
                            for fc in range(8):
                                ins = te.matmul(
                                    psProd[:, half * 512:(half + 1) * 512],
                                    xh_fm[:, fc * 512 + g * 128: fc * 512 + (g + 1) * 128],
                                    wv_sb[:, fc * E + half * 512: fc * E + (half + 1) * 512],
                                    start=(fc == 0), stop=(fc == 7))
                        tS.inc(ins)

                tb_v = tS.n
                blkA.tensor(a3v)

                def a3ve(vec):
                    for g in range(NG):
                        tS.wait(vec, tb_v + g + 1)
                        vS.selfw(vec)
                        vS.inc(vec.tensor_copy(v_tm[:, g * 1024:(g + 1) * 1024],
                                               psProd[:, 0:1024]))

                blkA.vector(a3ve)
                assert vS.n == vb_ve + NG

                vb_q = vS.n
                sb_q = sS.n

                def a3qlg(te):
                    vS.wait(te)
                    for idx in range(16):
                        isq = idx < 8
                        cc_ = idx % 8
                        reg = 1024 + (idx % 2) * 512
                        if idx >= 2:
                            prev = idx - 2
                            if prev < 8:
                                te.wait_ge(vS.sem, vb_q + prev + 1)
                            else:
                                te.wait_ge(sS.sem, sb_q + 8 + 2 * (prev - 8) + 2)
                        wsb = wq_sb if isq else wlg_sb
                        for fc in range(8):
                            ins = te.matmul(psProd[:, reg:reg + 512],
                                            wsb[:, fc * E + cc_ * 128: fc * E + (cc_ + 1) * 128],
                                            xh_fm[:, fc * 512:(fc + 1) * 512],
                                            start=(fc == 0), stop=(fc == 7))
                        tS.inc(ins)

                tb_q = tS.n
                blkA.tensor(a3qlg)

                def a3qs(sc):
                    for qc in range(8):
                        tS.wait(sc, tb_q + qc + 1)
                        if qc >= 1:
                            sc.wait_ge(vS.sem, vb_q + qc)
                        reg = 1024 + (qc % 2) * 512
                        sS.selfw(sc)
                        sS.inc(sc.activation(sgq[:], psProd[:, reg:reg + 512], AF.Sigmoid))

                blkA.scalar(a3qs)

                def a3qv(vec):
                    for qc in range(8):
                        sS.wait(vec, sb_q + qc + 1)
                        reg = 1024 + (qc % 2) * 512
                        vS.selfw(vec)
                        vS.inc(vec.tensor_mul(qpre[:, qc * 512:(qc + 1) * 512],
                                              psProd[:, reg:reg + 512], sgq[:]))

                blkA.vector(a3qv)

                def a3ls(sc):
                    for lc in range(8):
                        tS.wait(sc, tb_q + 8 + lc + 1)
                        reg = 1024 + (lc % 2) * 512
                        sS.selfw(sc)
                        sS.inc(sc.activation(f_sb[:, lc * 512:(lc + 1) * 512],
                                             psProd[:, reg:reg + 512], AF.Sigmoid))
                        sS.selfw(sc)
                        sS.inc(sc.activation(kf[:, lc * 512:(lc + 1) * 512],
                                             psProd[:, reg:reg + 512], AF.Sigmoid, scale=-1.0))

                blkA.scalar(a3ls)

                if a_stages < 4:
                    continue
                sb_g = sS.n
                vb_scan = vS.n

                def a4s(sc):
                    vS.wait(sc)
                    for h in range(H):
                        if h >= 1:
                            sc.wait_ge(vS.sem, vb_scan + h)
                        sS.selfw(sc)
                        sS.inc(sc.activation(g_scr[:], f_sb[:, h * 512:(h + 1) * 512], AF.Ln))

                blkA.scalar(a4s)

                def a4v(vec):
                    gS.wait(vec)
                    for h in range(H):
                        sS.wait(vec, sb_g + h + 1)
                        vS.selfw(vec)
                        vS.inc(vec.tensor_tensor_scan(
                            Acum[:, h * 512:(h + 1) * 512], g_scr[:], g_scr[:],
                            carryA[:, h:h + 1], OP.add, OP.bypass))

                blkA.vector(a4v)
                vb_scan_end = vS.n

                for h in range(H):
                    def a4rel(vec, h=h):
                        vS.wait(vec, vb_scan_end)
                        sS.wait(vec)
                        for c_ in range(8):
                            ref = (carryA[:, h:h + 1] if c_ == 0
                                   else Acum[:, h * 512 + c_ * 64 - 1: h * 512 + c_ * 64])
                            vS.selfw(vec)
                            vS.inc(vec.tensor_scalar(
                                scrA[:, c_ * 64:(c_ + 1) * 64],
                                Acum[:, h * 512 + c_ * 64: h * 512 + (c_ + 1) * 64],
                                ref, None, OP.subtract, OP.bypass))
                        for c_ in range(8):
                            endc = Acum[:, h * 512 + c_ * 64 + 63: h * 512 + c_ * 64 + 64]
                            vS.selfw(vec)
                            vS.inc(vec.tensor_scalar(
                                scrA2[:, c_ * 64:(c_ + 1) * 64],
                                Acum[:, h * 512 + c_ * 64: h * 512 + (c_ + 1) * 64],
                                endc, -1.0, OP.subtract, OP.mult))

                    blkA.vector(a4rel)

                    def a4exp(sc, h=h):
                        vS.wait(sc)
                        sS.selfw(sc)
                        sS.inc(sc.activation(scrB[:], scrA[:], AF.Exp))
                        sS.selfw(sc)
                        sS.inc(sc.activation(scrB2[:], scrA[:], AF.Exp, scale=-1.0))
                        sS.selfw(sc)
                        sS.inc(sc.activation(scrA[:], scrA2[:], AF.Exp))
                        sS.selfw(sc)
                        sS.inc(sc.activation(scrA2[:], Acum[:, h * 512:(h + 1) * 512],
                                             AF.Exp))

                    blkA.scalar(a4exp)

                    def a4mul(vec, h=h):
                        sS.wait(vec)
                        tS.wait(vec)
                        so.wait(vec)
                        vS.selfw(vec)
                        vS.inc(vec.tensor_copy(
                            bass.AP(eACall, h * 8, [[64, 128], [1, 8]]),
                            bass.AP(scrB, 63, [[512, 128], [64, 8]])))
                        vS.selfw(vec)
                        vS.inc(vec.tensor_mul(qt_bf[:, h * 512:(h + 1) * 512],
                                              qpre[:, h * 512:(h + 1) * 512], scrB[:]))
                        vS.selfw(vec)
                        vS.inc(vec.tensor_mul(kt_bf[:, h * 512:(h + 1) * 512],
                                              kf[:, h * 512:(h + 1) * 512], scrB2[:]))
                        vS.selfw(vec)
                        vS.inc(vec.tensor_mul(khscr[:], kf[:, h * 512:(h + 1) * 512],
                                              scrA[:]))
                        vS.selfw(vec)
                        vS.inc(vec.tensor_mul(qhat_bf[:, h * 512:(h + 1) * 512],
                                              qpre[:, h * 512:(h + 1) * 512], scrA2[:]))

                    blkA.vector(a4mul)

                    def a4tp(te, h=h):
                        vS.wait(te)
                        for g in range(4):
                            ins = te.transpose(psTP[:, g * 128:(g + 1) * 128],
                                               khscr[:, g * 128:(g + 1) * 128], ident[:])
                        tS.inc(ins)

                    blkA.tensor(a4tp)

                    def a4te(vec, h=h):
                        tS.wait(vec)
                        vS.selfw(vec)
                        vS.inc(vec.tensor_copy(khtm[:, h * 512:(h + 1) * 512],
                                               psTP[:, 0:512]))

                    blkA.vector(a4te)

                def a4carry(sc):
                    vS.wait(sc)
                    sS.selfw(sc)
                    sS.inc(sc.copy(carryA[:], bass.AP(Acum, 511, [[4096, 128], [512, 8]])))

                blkA.scalar(a4carry)

                if a_stages < 5:
                    continue
                for g in range(NG):
                    def a5p(te, g=g):
                        vS.wait(te)
                        sS.wait(te)
                        gS.wait(te)
                        for h in range(H):
                            ins = te.matmul(psProd[:, h * 128:(h + 1) * 128],
                                            kt_bf[:, h * 512 + g * 128: h * 512 + (g + 1) * 128],
                                            qt_bf[:, h * 512 + g * 128: h * 512 + (g + 1) * 128],
                                            start=True, stop=True)
                            tS.inc(ins)

                    tb_p = tS.n
                    blkA.tensor(a5p)

                    def a5m(vec, g=g, tb_p=tb_p):
                        for h in range(H):
                            tS.wait(vec, tb_p + H)
                            vS.selfw(vec)
                            vS.inc(vec.tensor_mul(psb_buf[:, h * 128:(h + 1) * 128],
                                                  psProd[:, h * 128:(h + 1) * 128],
                                                  maskP[:]))

                    blkA.vector(a5m)
                    vb_mask_end = vS.n

                    vb_su0 = vS.n

                    def a5s1(te, g=g, vb_mask_end=vb_mask_end, vb_su0=vb_su0):
                        for h in range(H):
                            te.wait_ge(vS.sem, vb_mask_end - H + h + 1)
                            first = (h == 0) or (h == 4)
                            te.matmul(psProd[:, 1024 + h * 128: 1024 + (h + 1) * 128],
                                      psb_buf[0:64, h * 128:(h + 1) * 128],
                                      v_tm[0:64, g * 1024 + h * 128: g * 1024 + (h + 1) * 128],
                                      start=first, stop=False, skip_group_check=True)
                            te.matmul(psProd[0:64, 1024 + h * 128: 1024 + (h + 1) * 128],
                                      qt_bf[:, h * 512 + g * 128: h * 512 + g * 128 + 64],
                                      S_bf[:, h * 128:(h + 1) * 128],
                                      start=False, stop=False, skip_group_check=True)
                            reg = (h % 2) * 512
                            if h >= 2:
                                te.wait_ge(vS.sem, vb_su0 + 3 * (h - 2) + 2)
                            ins = te.matmul(psS[:, reg:reg + 128],
                                            khtm[0:64, h * 512 + g * 128: h * 512 + (g + 1) * 128],
                                            v_tm[0:64, g * 1024 + h * 128: g * 1024 + (h + 1) * 128],
                                            start=True, stop=True)
                            tS.inc(ins)

                    tb_s1 = tS.n
                    blkA.tensor(a5s1)

                    def a5v1(vec, g=g, tb_s1=tb_s1):
                        for h in range(H):
                            tS.wait(vec, tb_s1 + h + 1)
                            reg = (h % 2) * 512
                            col = eACall[:, h * 8 + g * 2: h * 8 + g * 2 + 1]
                            vS.selfw(vec)
                            vS.inc(vec.tensor_scalar(S_sb[:, h * 128:(h + 1) * 128],
                                                     S_sb[:, h * 128:(h + 1) * 128],
                                                     col, None, OP.mult, OP.bypass))
                            vS.selfw(vec)
                            vS.inc(vec.tensor_add(S_sb[:, h * 128:(h + 1) * 128],
                                                  S_sb[:, h * 128:(h + 1) * 128],
                                                  psS[:, reg:reg + 128]))
                            vS.selfw(vec)
                            vS.inc(vec.tensor_copy(S_bf[:, h * 128:(h + 1) * 128],
                                                   S_sb[:, h * 128:(h + 1) * 128]))

                    blkA.vector(a5v1)
                    vb_cast1 = vS.n
                    vb_su1 = vS.n

                    def a5s2(te, g=g, vb_su0=vb_su0, vb_su1=vb_su1):
                        for h in range(H):
                            te.wait_ge(vS.sem, vb_su0 + 3 * (h + 1))
                            te.matmul(psProd[64:128, 1024 + h * 128: 1024 + (h + 1) * 128],
                                      qt_bf[:, h * 512 + g * 128 + 64: h * 512 + (g + 1) * 128],
                                      S_bf[:, h * 128:(h + 1) * 128],
                                      start=False, stop=False, skip_group_check=True)
                            reg = (h % 2) * 512
                            if h < 2:
                                te.wait_ge(vS.sem, vb_su0 + 3 * (6 + h) + 2)
                            else:
                                te.wait_ge(vS.sem, vb_su1 + 3 * (h - 2) + 2)
                            ins = te.matmul(psS[:, reg:reg + 128],
                                            khtm[64:128, h * 512 + g * 128: h * 512 + (g + 1) * 128],
                                            v_tm[64:128, g * 1024 + h * 128: g * 1024 + (h + 1) * 128],
                                            start=True, stop=True)
                            tS.inc(ins)

                    tb_s2 = tS.n
                    blkA.tensor(a5s2)

                    def a5v2(vec, g=g, tb_s2=tb_s2):
                        for h in range(H):
                            tS.wait(vec, tb_s2 + h + 1)
                            reg = (h % 2) * 512
                            col = eACall[:, h * 8 + g * 2 + 1: h * 8 + g * 2 + 2]
                            vS.selfw(vec)
                            vS.inc(vec.tensor_scalar(S_sb[:, h * 128:(h + 1) * 128],
                                                     S_sb[:, h * 128:(h + 1) * 128],
                                                     col, None, OP.mult, OP.bypass))
                            vS.selfw(vec)
                            vS.inc(vec.tensor_add(S_sb[:, h * 128:(h + 1) * 128],
                                                  S_sb[:, h * 128:(h + 1) * 128],
                                                  psS[:, reg:reg + 128]))
                            vS.selfw(vec)
                            vS.inc(vec.tensor_copy(S_bf[:, h * 128:(h + 1) * 128],
                                                   S_sb[:, h * 128:(h + 1) * 128]))

                    blkA.vector(a5v2)

                    def a5s3(te, g=g):
                        vS.wait(te)
                        for h in range(H):
                            ins = te.matmul(
                                psProd[:, 1024 + h * 128: 1024 + (h + 1) * 128],
                                psb_buf[64:128, h * 128:(h + 1) * 128],
                                v_tm[64:128, g * 1024 + h * 128: g * 1024 + (h + 1) * 128],
                                start=False, stop=(h == 3) or (h == 7),
                                skip_group_check=True)
                        tS.inc(ins)

                    blkA.tensor(a5s3)

                    def a5oe(vec, g=g):
                        tS.wait(vec)
                        vS.selfw(vec)
                        vS.inc(vec.tensor_copy(o16[:, g * 1024:(g + 1) * 1024],
                                               psProd[:, 1024:2048]))

                    blkA.vector(a5oe)

                if a_stages < 6:
                    continue

                def a6(sc, j=j):
                    vS.wait(sc)
                    sS.selfw(sc)
                    so.selfw(sc)
                    so.inc(sc.dma_start(
                        out=bass.AP(o_sp, j * TILE * E, [[E, 128], [128 * E, 4], [1, E]]),
                        in_=bass.AP(o16, 0, [[4096, 128], [1024, 4], [1, 1024]])), 16)
                    sS.selfw(sc)
                    so.selfw(sc)
                    so.inc(sc.dma_start(
                        out=bass.AP(qh_sp, j * TILE, [[T, 128], [128 * T, 8], [1, 512]]),
                        in_=bass.AP(qhat_bf, 0, [[4096, 128], [512, 8], [1, 512]])), 16)

                blkA.scalar(a6)

            def ae_v(vec):
                vS.selfw(vec)
                vS.inc(vec.tensor_scalar(S_sb[:], S_sb[:], evenm[:, 0:1], None,
                                         OP.mult, OP.bypass))

            blkA.vector(ae_v)

            if "S_dbg" in dbg:
                def ae_d(sc):
                    vS.wait(sc)
                    sS.selfw(sc)
                    so.selfw(sc)
                    so.inc(sc.dma_start(out=dbg["S_dbg"][:, :],
                                        in_=bass.AP(S_sb, 0, [[1024, 128], [1, 1024]])), 16)
                blkA.scalar(ae_d)

            def ae_g(gp):
                vS.wait(gp)
                so.wait(gp)
                gld.selfw(gp)
                gld.inc(gp.dma_start(out=cc1_in.ap(),
                                     in_=bass.AP(S_sb, 0, [[1024, 128], [1, 1024]])), 16)

            blkA.gpsimd(ae_g)

        if "AR1" not in phases:
            return nc
        sin_f32 = Eo(nc.sbuf_tensor("sin_f32", [128, 1024], F32))
        blkC1_cm = nc.Block()
        blkC1 = blkC1_cm.__enter__()

        def cc1f(gp):
            gld.wait(gp)
            ccS.inc(gp.collective_compute(
                "AllReduce", OP.add,
                replica_groups=[[0, 1], [2, 3], [4, 5], [6, 7]],
                ins=[cc1_in.ap().opt()], outs=[cc1_out.ap().opt()]))
            ccS.wait(gp)
            gld.selfw(gp)
            gld.inc(gp.dma_start(out=sin_f32[:],
                                 in_=bass.AP(cc1_out, 0, [[1024, 128], [1, 1024]])), 16)

        blkC1.gpsimd(cc1f)

        def cc1v(vec):
            gld.wait(vec)
            vS.selfw(vec)
            vS.inc(vec.tensor_scalar(sin_f32[:], sin_f32[:], oddm[:, 0:1], None,
                                     OP.mult, OP.bypass))
            vS.selfw(vec)
            gS.inc(vec.tensor_copy(S_in_bf[:], sin_f32[:]))

        blkC1.vector(cc1v)
        blkC1_cm.__exit__(None, None, None)

        if "B1" not in phases:
            return nc
        with ExitStack() as bctx:
            Eb = bctx.enter_context
            wg1_sb = Eb(nc.sbuf_tensor("wg1_sb", [128, 8 * 128], BF16))
            wg2_sb = Eb(nc.sbuf_tensor("wg2_sb", [128, E], BF16))
            wout_sb = Eb(nc.sbuf_tensor("wout_sb", [128, 8 * E], BF16))
            lnwb_sb = Eb(nc.sbuf_tensor("lnwb_sb", [128, E], F32))
            x_sb = Eb(nc.sbuf_tensor("xB", [128, 4096], F32))
            xh_tm = Eb(nc.sbuf_tensor("xhB_tm", [128, 1024], BF16))
            xh_fm = Eb(nc.sbuf_tensor("xhB_fm", [128, 4096], BF16))
            y1_bf = Eb(nc.sbuf_tensor("y1_bf", [128, 512], BF16))
            gate_t = Eb(nc.sbuf_tensor("gate_t", [128, 4096], F32))
            o16t = Eb(nc.sbuf_tensor("o16t", [128, 4096], F16))
            qht = Eb(nc.sbuf_tensor("qht", [128, 4096], BF16))
            ofull = Eb(nc.sbuf_tensor("ofull", [128, 1024], F32))
            og_t = Eb(nc.sbuf_tensor("og_t", [128, 4096], F32))
            araw_t = Eb(nc.sbuf_tensor("araw_t", [128, 4096], F32))
            ohat_scr = Eb(nc.sbuf_tensor("ohat_scr", [128, 1024], F32))
            ohat_bf = Eb(nc.sbuf_tensor("ohat_bf", [128, 1024], BF16))
            ohat_fm = Eb(nc.sbuf_tensor("ohat_fm", [128, 4096], BF16))
            x1g = Eb(nc.sbuf_tensor("x1g", [128, 1024], F32))
            t2scr = Eb(nc.sbuf_tensor("t2B", [128, 1024], F32))
            sm = {"st": Eb(nc.sbuf_tensor("smB_st", [128, 48], F32)),
                  "mv": Eb(nc.sbuf_tensor("smB_mv", [128, 8], F32)),
                  "rcol": Eb(nc.sbuf_tensor("smB_rcol", [128, 8], F32))}
            psTP = Eb(nc.psum_tensor("psTPb", [128, 1024], BF16))
            psM = Eb(nc.psum_tensor("psMb", [128, 2048], F32))
            psX = Eb(nc.psum_tensor("psXb", [128, 1024], F32))
            blkB = Eb(nc.Block())

            def b0(sync):
                so.wait(sync)
                sync.wait_ge(wss[0].sem, wst_final[0])
                sync.wait_ge(wss[1].sem, wst_final[1])
                ld.selfw(sync)
                ld.inc(sync.dma_start(
                    out=bass.AP(wg1_sb, 0, [[8 * 128, 128], [1, 8 * 128]]),
                    in_=wb16["wg1T"][:, :]), 16)
                ld.selfw(sync)
                ld.inc(sync.dma_start(
                    out=bass.AP(wg2_sb, 0, [[E, 128], [1, E]]),
                    in_=wb16["wg2T"][:, :]), 16)
                ld.selfw(sync)
                ld.inc(sync.dma_start(
                    out=bass.AP(wout_sb, 0, [[8 * E, 128], [1, 8 * E]]),
                    in_=wb16["woutT"][:, :]), 16)

            blkB.sync(b0)

            def b0s(sync):
                ld.selfw(sync)
                ld.inc(sync.dma_start(out=lnwb_sb[:], in_=din["lnwb"][:, :]), 16)

            blkB.sync(b0s)

            for j in range(NT):
                def b1(sync, j=j):
                    tS.wait(sync)
                    vS.wait(sync)
                    sS.wait(sync)
                    so.wait(sync)
                    ld.selfw(sync)
                    ld.inc(sync.dma_start(
                        out=bass.AP(x_sb, 0, [[4096, 128], [1024, 4], [1, 1024]]),
                        in_=bass.AP(x_d, j * TILE * E, [[E, 128], [128 * E, 4], [1, E]]),
                    ), 16)

                blkB.sync(b1)
                ldw = (ld.sem, ld.n)
                gw = (ld.sem, ld.n)

                emit_ln_bn(blkB, lambda g, h: x_sb[:, g * 1024 + h * 512: g * 1024 + h * 512 + 512],
                           sm, extra_wait=(lambda vec, w=ldw: vec.wait_ge(*w)))
                for g in range(NG):
                    emit_norm_transpose(blkB, x_sb, xh_tm, xh_fm, psTP, sm, g)

                def b3t(te, gw=gw):
                    te.wait_ge(*gw)
                    vS.wait(te)
                    sS.wait(te)
                    for fc in range(8):
                        ins = te.matmul(psM[:, 0:512],
                                        wg1_sb[:, fc * 128:(fc + 1) * 128],
                                        xh_fm[:, fc * 512:(fc + 1) * 512],
                                        start=(fc == 0), stop=(fc == 7))
                    tS.inc(ins)

                blkB.tensor(b3t)

                def b3e(vec):
                    tS.wait(vec)
                    vS.selfw(vec)
                    vS.inc(vec.tensor_copy(y1_bf[:], psM[:, 0:512]))

                blkB.vector(b3e)

                for g in range(NG):
                    def bg_t(te, g=g):
                        vS.wait(te)
                        if g >= 1:
                            sS.wait(te)
                        for half in range(2):
                            ins = te.matmul(psM[:, half * 512:(half + 1) * 512],
                                            y1_bf[:, g * 128:(g + 1) * 128],
                                            wg2_sb[:, half * 512:(half + 1) * 512],
                                            start=True, stop=True)
                        tS.inc(ins)

                    blkB.tensor(bg_t)

                    def bg_s(sc, g=g):
                        tS.wait(sc)
                        sS.selfw(sc)
                        sS.inc(sc.activation(gate_t[:, g * 1024:(g + 1) * 1024],
                                             psM[:, 0:1024], AF.Sigmoid))

                    blkB.scalar(bg_s)

                def b4l(sync, j=j):
                    ld.selfw(sync)
                    ld.inc(sync.dma_start(
                        out=bass.AP(o16t, 0, [[4096, 128], [1024, 4], [1, 1024]]),
                        in_=bass.AP(o_sp, j * TILE * E,
                                    [[E, 128], [128 * E, 4], [1, E]])), 16)
                    ld.selfw(sync)
                    ld.inc(sync.dma_start(
                        out=bass.AP(qht, 0, [[4096, 128], [512, 8], [1, 512]]),
                        in_=bass.AP(qh_sp, j * TILE, [[T, 128], [128 * T, 8], [1, 512]])), 16)

                blkB.sync(b4l)

                for g in range(NG):
                    def b4t(te, g=g):
                        ld.wait(te)
                        gS.wait(te)
                        vS.wait(te)
                        sS.wait(te)
                        for h in range(H):
                            ins = te.matmul(psM[:, 1024 + h * 128: 1024 + (h + 1) * 128],
                                            qht[:, h * 512 + g * 128: h * 512 + (g + 1) * 128],
                                            S_in_bf[:, h * 128:(h + 1) * 128],
                                            start=(h == 0) or (h == 4),
                                            stop=(h == 3) or (h == 7))
                        tS.inc(ins)

                    blkB.tensor(b4t)

                    def b5v(vec, g=g):
                        tS.wait(vec)
                        sS.wait(vec)
                        vS.selfw(vec)
                        vS.inc(vec.tensor_add(ofull[:], psM[:, 1024:2048],
                                              o16t[:, g * 1024:(g + 1) * 1024]))
                        vS.selfw(vec)
                        vS.inc(vec.tensor_mul(og_t[:, g * 1024:(g + 1) * 1024], ofull[:],
                                              gate_t[:, g * 1024:(g + 1) * 1024]))

                    blkB.vector(b5v)

                emit_ln_bn(blkB, lambda g, h: og_t[:, g * 1024 + h * 512: g * 1024 + h * 512 + 512], sm)

                for g in range(NG):
                    def b6v(vec, g=g):
                        sS.wait(vec)
                        tS.wait(vec)
                        vS.selfw(vec)
                        vS.inc(vec.tensor_scalar(ohat_scr[:],
                                                 og_t[:, g * 1024:(g + 1) * 1024],
                                                 sm["mv"][:, 2 * g:2 * g + 1],
                                                 sm["rcol"][:, g:g + 1],
                                                 OP.subtract, OP.mult))
                        vS.selfw(vec)
                        vS.inc(vec.tensor_mul(ohat_bf[:], ohat_scr[:], lnwb_sb[:]))

                    blkB.vector(b6v)

                    def b6t(te, g=g):
                        vS.wait(te)
                        for ci in range(8):
                            ins = te.transpose(psTP[:, ci * 128:(ci + 1) * 128],
                                               ohat_bf[:, ci * 128:(ci + 1) * 128], ident[:])
                        tS.inc(ins)

                    blkB.tensor(b6t)

                    def b6e(vec, g=g):
                        tS.wait(vec)
                        vS.selfw(vec)
                        vS.inc(vec.tensor_copy(
                            bass.AP(ohat_fm, g * 128, [[4096, 128], [512, 8], [1, 128]]),
                            bass.AP(psTP, 0, [[1024, 128], [128, 8], [1, 128]])))

                    blkB.vector(b6e)

                for g in range(NG):
                    def b7t(te, g=g):
                        vS.wait(te)
                        sS.wait(te)
                        for half in range(2):
                            for ec in range(8):
                                ins = te.matmul(
                                    psX[:, half * 512:(half + 1) * 512],
                                    ohat_fm[:, ec * 512 + g * 128: ec * 512 + (g + 1) * 128],
                                    wout_sb[:, ec * E + half * 512: ec * E + (half + 1) * 512],
                                    start=(ec == 0), stop=(ec == 7))
                        tS.inc(ins)

                    blkB.tensor(b7t)

                    def b7e(vec, g=g):
                        tS.wait(vec)
                        vS.selfw(vec)
                        vS.inc(vec.tensor_copy(araw_t[:, g * 1024:(g + 1) * 1024],
                                               psX[:, 0:1024]))

                    blkB.vector(b7e)

                emit_ln_bn(blkB, lambda g, h: araw_t[:, g * 1024 + h * 512: g * 1024 + h * 512 + 512], sm)

                for g in range(NG):
                    def b8v4(vec, g=g):
                        sS.wait(vec)
                        vS.selfw(vec)
                        vS.inc(vec.tensor_scalar(t2scr[:],
                                                 araw_t[:, g * 1024:(g + 1) * 1024],
                                                 sm["mv"][:, 2 * g:2 * g + 1],
                                                 sm["rcol"][:, g:g + 1],
                                                 OP.subtract, OP.mult))
                        so.wait(vec)
                        vS.selfw(vec)
                        vS.inc(vec.tensor_add(x1g[:], t2scr[:],
                                              x_sb[:, g * 1024:(g + 1) * 1024]))

                    blkB.vector(b8v4)

                    def b8d(sc, g=g, j=j):
                        vS.wait(sc)
                        sS.selfw(sc)
                        so.selfw(sc)
                        so.inc(sc.dma_start(
                            out=bass.AP(x1_sp, (j * TILE + g * 128) * E, [[E, 128], [1, E]]),
                            in_=bass.AP(x1g, 0, [[1024, 128], [1, 1024]])), 16)

                    blkB.scalar(b8d)

            def b9(sc):
                so.wait(sc)

            blkB.scalar(b9)

        if "B2" not in phases:
            return nc
        with ExitStack() as b2x:
            Eb = b2x.enter_context
            wQ_sb = Eb(nc.sbuf_tensor("wQ_sb", [128, 8 * E], BF16))
            wK_sb = Eb(nc.sbuf_tensor("wK_sb", [128, 8 * E], BF16))
            winv_sb = Eb(nc.sbuf_tensor("winv_sb", [128, 8 * E], BF16))
            rw1_sb = Eb(nc.sbuf_tensor("rw1_sb", [128, 8 * E], BF16))
            rw2_sb = Eb(nc.sbuf_tensor("rw2_sb", [128, 64], BF16))
            rb1_sb = Eb(nc.sbuf_tensor("rb1_sb", [128, 8], F32))
            vfwd_sb = Eb(nc.sbuf_tensor("vfwd_sb", [128, 8 * 512], BF16))
            wfwd_sb = Eb(nc.sbuf_tensor("wfwd_sb", [128, 8 * 512], BF16))
            winvex_sb = Eb(nc.sbuf_tensor("winvex_sb", [128, 8 * 512], BF16))
            vinv_sb = Eb(nc.sbuf_tensor("vinv_sb", [128, 8 * 512], BF16))
            Ec_sb = Eb(nc.sbuf_tensor("Ec_sb", [8, 512], BF16))
            x1_sb = Eb(nc.sbuf_tensor("x1B2", [128, 4096], F32))
            xh_tm = Eb(nc.sbuf_tensor("xh2_tm", [128, 1024], BF16))
            xh_fm = Eb(nc.sbuf_tensor("xh2_fm", [128, 4096], BF16))
            KA = Eb(nc.sbuf_tensor("KA", [128, 4096], BF16))
            KB = Eb(nc.sbuf_tensor("KB", [128, 4096], BF16))
            h_fm = Eb(nc.sbuf_tensor("h_fm", [128, 4096], BF16))
            yw32 = Eb(nc.sbuf_tensor("yw32", [128, 2048], F32))
            cum32 = Eb(nc.sbuf_tensor("cum32", [128, 2048], F32))
            cum16 = Eb(nc.sbuf_tensor("cum16", [128, 2048], F16))
            xvw16 = Eb(nc.sbuf_tensor("xvw16", [128, 2048], F16))
            wbar = Eb(nc.sbuf_tensor("wbar", [128, 512], BF16))
            wtm = Eb(nc.sbuf_tensor("wtm", [128, 32], F32))
            wtm_bf = Eb(nc.sbuf_tensor("wtm_bf", [128, 32], BF16))
            wsm_fm = Eb(nc.sbuf_tensor("wsm_fm", [8, 512], BF16))
            lgt = Eb(nc.sbuf_tensor("lgt", [128, 32], F32))
            smax = Eb(nc.sbuf_tensor("smax", [128, 8], F32))
            ssum = Eb(nc.sbuf_tensor("ssum", [128, 4], F32))
            esm = Eb(nc.sbuf_tensor("esm", [128, 32], F32))
            carF = Eb(nc.sbuf_tensor("carF", [128, 4], F32))
            carI = Eb(nc.sbuf_tensor("carI", [128, 4], F32))
            sm = {"st": Eb(nc.sbuf_tensor("smC_st", [128, 48], F32)),
                  "mv": Eb(nc.sbuf_tensor("smC_mv", [128, 8], F32)),
                  "rcol": Eb(nc.sbuf_tensor("smC_rcol", [128, 8], F32))}
            psTP = Eb(nc.psum_tensor("psTPc", [128, 1024], BF16))
            psM = Eb(nc.psum_tensor("psMc", [128, 2048], F32))
            psX = Eb(nc.psum_tensor("psXc", [128, 1024], F32))
            blk2 = Eb(nc.Block())

            def c0(sync):
                so.wait(sync)
                sync.wait_ge(wss[0].sem, wst_final[0])
                sync.wait_ge(wss[1].sem, wst_final[1])
                for nm, dst, w in (("wQT", wQ_sb, E), ("wKT", wK_sb, E),
                                   ("winvT", winv_sb, E), ("rw1T", rw1_sb, E),
                                   ("vfwdF", vfwd_sb, 512), ("wfwdF", wfwd_sb, 512),
                                   ("winvexF", winvex_sb, 512), ("vinvF", vinv_sb, 512)):
                    ld.selfw(sync)
                    ld.inc(sync.dma_start(
                        out=bass.AP(dst, 0, [[8 * w, 128], [1, 8 * w]]),
                        in_=wb16[nm][:, :]), 16)
                ld.selfw(sync)
                ld.inc(sync.dma_start(
                    out=bass.AP(rw2_sb, 0, [[64, 128], [1, 64]]),
                    in_=wb16["rw2T"][:, :]), 16)

            blk2.sync(c0)

            def c0g(gp):
                gS.selfw(gp)
                gS.inc(gp.memset(carF[:], 0.0))
                gS.selfw(gp)
                gS.inc(gp.memset(carI[:], 0.0))

            blk2.gpsimd(c0g)

            def c0s(sync):
                ld.selfw(sync)
                ld.inc(sync.dma_start(out=rb1_sb[:], in_=din["rb1c"][:, :]), 16)
                ld.selfw(sync)
                ld.inc(sync.dma_start(out=Ec_sb[:], in_=din["Econst_in"][:, :]), 16)

            blk2.sync(c0s)

            for j in range(NT):
                def d1(sync, j=j):
                    tS.wait(sync)
                    vS.wait(sync)
                    sS.wait(sync)
                    so.wait(sync)
                    ld.selfw(sync)
                    ld.inc(sync.dma_start(
                        out=bass.AP(x1_sb, 0, [[4096, 128], [1024, 4], [1, 1024]]),
                        in_=bass.AP(x1_sp, j * TILE * E, [[E, 128], [128 * E, 4], [1, E]]),
                    ), 16)

                blk2.sync(d1)
                ldw = (ld.sem, ld.n)
                gw = (ld.sem, ld.n)

                emit_ln_bn(blk2, lambda g, h: x1_sb[:, g * 1024 + h * 512: g * 1024 + h * 512 + 512],
                           sm, extra_wait=(lambda vec, w=ldw: vec.wait_ge(*w)))
                for g in range(NG):
                    emit_norm_transpose(blk2, x1_sb, xh_tm, xh_fm, psTP, sm, g)

                def mk_prod_fm(wsb, src, dst):
                    vb0 = vS.n

                    def pf(te):
                        vS.wait(te)
                        sS.wait(te)
                        gw and te.wait_ge(*gw)
                        for oc in range(8):
                            reg = 1024 + (oc % 2) * 512
                            if oc >= 2:
                                te.wait_ge(vS.sem, vb0 + oc - 1)
                            for fc in range(8):
                                ins = te.matmul(psM[:, reg:reg + 512],
                                                wsb[:, fc * E + oc * 128: fc * E + (oc + 1) * 128],
                                                src[:, fc * 512:(fc + 1) * 512],
                                                start=(fc == 0), stop=(fc == 7))
                            tS.inc(ins)

                    tb0 = tS.n
                    blk2.tensor(pf)

                    def pe(vec):
                        for oc in range(8):
                            tS.wait(vec, tb0 + oc + 1)
                            reg = 1024 + (oc % 2) * 512
                            vS.selfw(vec)
                            vS.inc(vec.tensor_copy(dst[:, oc * 512:(oc + 1) * 512],
                                                   psM[:, reg:reg + 512]))

                    blk2.vector(pe)

                mk_prod_fm(wK_sb, xh_fm, KA)
                mk_prod_fm(winv_sb, KA, KB)

                for br, (ksrc, wflat, car) in enumerate(
                        ((KA, wfwd_sb, carF), (KB, vinv_sb, carI))):
                    vb0 = vS.n

                    def dyw(te, ksrc=ksrc, wflat=wflat, vb0=vb0):
                        vS.wait(te)
                        for kc in range(4):
                            reg = 1024 + (kc % 2) * 512
                            if kc >= 2:
                                te.wait_ge(vS.sem, vb0 + kc - 1)
                            for fc in range(8):
                                ins = te.matmul(psM[:, reg:reg + 512],
                                                wflat[:, fc * 512 + kc * 128: fc * 512 + (kc + 1) * 128],
                                                ksrc[:, fc * 512:(fc + 1) * 512],
                                                start=(fc == 0), stop=(fc == 7))
                            tS.inc(ins)

                    tb0 = tS.n
                    blk2.tensor(dyw)

                    def dywe(vec, tb0=tb0):
                        for kc in range(4):
                            tS.wait(vec, tb0 + kc + 1)
                            reg = 1024 + (kc % 2) * 512
                            vS.selfw(vec)
                            vS.inc(vec.tensor_copy(yw32[:, kc * 512:(kc + 1) * 512],
                                                   psM[:, reg:reg + 512]))

                    blk2.vector(dywe)

                    def dsc(vec, car=car):
                        sS.wait(vec)
                        gS.wait(vec)
                        vS.selfw(vec)
                        for kc in range(4):
                            vS.selfw(vec)
                            vS.inc(vec.tensor_tensor_scan(
                                cum32[:, kc * 512:(kc + 1) * 512],
                                yw32[:, kc * 512:(kc + 1) * 512],
                                yw32[:, kc * 512:(kc + 1) * 512],
                                car[:, kc:kc + 1], OP.add, OP.bypass))

                    blk2.vector(dsc)

                    def dcar(sc, car=car, br=br, j=j):
                        vS.wait(sc)
                        so.wait(sc)
                        sS.selfw(sc)
                        sS.inc(sc.copy(car[:], bass.AP(cum32, 511, [[2048, 128], [512, 4]])))
                        sS.selfw(sc)
                        sS.inc(sc.copy(cum16[:], cum32[:]))
                        sS.selfw(sc)
                        so.selfw(sc)
                        so.inc(sc.dma_start(
                            out=bass.AP(cum_sp, br * 4 * 128 * T + j * TILE,
                                        [[T, 128], [128 * T, 4], [1, 512]]),
                            in_=bass.AP(cum16, 0, [[2048, 128], [512, 4], [1, 512]])), 16)

                    blk2.scalar(dcar)

                mk_prod_fm(wQ_sb, xh_fm, KA)
                mk_prod_fm(winv_sb, KA, KB)

                for br, (qsrc, vflat) in enumerate(((KA, vfwd_sb), (KB, winvex_sb))):
                    sb0 = sS.n

                    def dh(te, qsrc=qsrc, sb0=sb0):
                        vS.wait(te)
                        sS.wait(te)
                        for oc in range(8):
                            reg = 1024 + (oc % 2) * 512
                            if oc >= 2:
                                te.wait_ge(sS.sem, sb0 + oc - 1)
                            for fc in range(8):
                                ins = te.matmul(psM[:, reg:reg + 512],
                                                rw1_sb[:, fc * E + oc * 128: fc * E + (oc + 1) * 128],
                                                qsrc[:, fc * 512:(fc + 1) * 512],
                                                start=(fc == 0), stop=(fc == 7))
                            tS.inc(ins)

                    tb0 = tS.n
                    blk2.tensor(dh)

                    def dhe(sc, tb0=tb0):
                        for oc in range(8):
                            tS.wait(sc, tb0 + oc + 1)
                            reg = 1024 + (oc % 2) * 512
                            sS.selfw(sc)
                            sS.inc(sc.activation(h_fm[:, oc * 512:(oc + 1) * 512],
                                                 psM[:, reg:reg + 512], AF.Gelu,
                                                 bias=rb1_sb[:, oc:oc + 1]))

                    blk2.scalar(dhe)

                    def dlg(te):
                        sS.wait(te)
                        vS.wait(te)
                        for g in range(NG):
                            for rc in range(8):
                                ins = te.matmul(
                                    psX[:, g * 8:g * 8 + 8],
                                    h_fm[:, rc * 512 + g * 128: rc * 512 + (g + 1) * 128],
                                    rw2_sb[:, rc * 8:(rc + 1) * 8],
                                    start=(rc == 0), stop=(rc == 7))
                        tS.inc(ins)

                    blk2.tensor(dlg)

                    def dsm(vec):
                        tS.wait(vec)
                        sS.wait(vec)
                        vS.selfw(vec)
                        vS.inc(vec.tensor_tensor(
                            bass.AP(lgt, 0, [[32, 128], [8, 4], [1, 8]]),
                            bass.AP(psX, 0, [[1024, 128], [8, 4], [1, 8]]),
                            bass.AP(b2pe_b, 0, [[8, 128], [0, 4], [1, 8]]),
                            OP.add))
                        for g in range(NG):
                            vS.selfw(vec)
                            vS.inc(vec.tensor_reduce(smax[:, g:g + 1],
                                                     lgt[:, g * 8:(g + 1) * 8],
                                                     AX.X, OP.max))
                        vS.selfw(vec)
                        vS.inc(vec.tensor_scalar(smax[:, 4:8], smax[:, 0:4],
                                                 -1.0, None, OP.mult, OP.bypass))

                    blk2.vector(dsm)

                    def dex(sc):
                        vS.wait(sc)
                        for g in range(NG):
                            sS.selfw(sc)
                            sS.inc(sc.activation(esm[:, g * 8:(g + 1) * 8],
                                                 lgt[:, g * 8:(g + 1) * 8], AF.Exp,
                                                 bias=smax[:, 4 + g:5 + g]))

                    blk2.scalar(dex)

                    def dsm2(vec, br=br, j=j):
                        sS.wait(vec)
                        for g in range(NG):
                            vS.selfw(vec)
                            vS.inc(vec.tensor_reduce(ssum[:, g:g + 1],
                                                     esm[:, g * 8:(g + 1) * 8],
                                                     AX.X, OP.add))
                        vS.selfw(vec)
                        vS.inc(vec.reciprocal(ssum[:, 0:4], ssum[:, 0:4]))
                        if br == 1:
                            vS.selfw(vec)
                            vS.inc(vec.tensor_scalar(ssum[:, 0:4], ssum[:, 0:4],
                                                     alphac[:, 0:1], None,
                                                     OP.mult, OP.bypass))
                        for g in range(NG):
                            vS.selfw(vec)
                            vS.inc(vec.tensor_scalar(wtm[:, g * 8:(g + 1) * 8],
                                                     esm[:, g * 8:(g + 1) * 8],
                                                     ssum[:, g:g + 1],
                                                     rp[:, j * 4 + g: j * 4 + g + 1],
                                                     OP.mult, OP.mult))
                        vS.selfw(vec)
                        vS.inc(vec.tensor_copy(wtm_bf[:], wtm[:]))

                    blk2.vector(dsm2)

                    def dwt(te):
                        vS.wait(te)
                        for g in range(NG):
                            ins = te.transpose(
                                bass.AP(psTP, g * 128, [[1024, 8], [1, 128]]),
                                wtm_bf[:, g * 8:(g + 1) * 8], ident[:])
                        tS.inc(ins)

                    blk2.tensor(dwt)

                    def dwte(vec):
                        tS.wait(vec)
                        so.wait(vec)
                        vS.selfw(vec)
                        vS.inc(vec.tensor_copy(
                            bass.AP(wsm_fm, 0, [[512, 8], [128, 4], [1, 128]]),
                            bass.AP(psTP, 0, [[1024, 8], [128, 4], [1, 128]])))

                    blk2.vector(dwte)

                    def dwsp(sc, br=br, j=j):
                        vS.wait(sc)
                        sS.selfw(sc)
                        so.selfw(sc)
                        so.inc(sc.dma_start(
                            out=bass.AP(w_sp, br * 8 * T + j * TILE, [[T, 8], [1, 512]]),
                            in_=bass.AP(wsm_fm, 0, [[512, 8], [1, 512]])), 16)

                    blk2.scalar(dwsp)

                    vb0 = vS.n

                    def dxv(te, qsrc=qsrc, vflat=vflat, vb0=vb0):
                        vS.wait(te)
                        for kc in range(4):
                            reg = 1024 + (kc % 2) * 512
                            if kc >= 1:
                                te.wait_ge(vS.sem, vb0 + 2 * kc - 1)
                            for fc in range(8):
                                te.matmul(psM[:, reg:reg + 512],
                                          vflat[:, fc * 512 + kc * 128: fc * 512 + (kc + 1) * 128],
                                          qsrc[:, fc * 512:(fc + 1) * 512],
                                          start=(fc == 0), stop=(fc == 7))
                            ins = te.matmul(psX[:, 0:512], Ec_sb[:, kc * 128:(kc + 1) * 128],
                                            wsm_fm[:, :], start=True, stop=True)
                            tS.inc(ins)

                    tb0 = tS.n
                    blk2.tensor(dxv)

                    def dxve(vec, tb0=tb0):
                        so.wait(vec)
                        for kc in range(4):
                            tS.wait(vec, tb0 + kc + 1)
                            reg = 1024 + (kc % 2) * 512
                            vS.selfw(vec)
                            vS.inc(vec.tensor_copy(wbar[:], psX[:, 0:512]))
                            vS.selfw(vec)
                            vS.inc(vec.tensor_mul(xvw16[:, kc * 512:(kc + 1) * 512],
                                                  psM[:, reg:reg + 512], wbar[:]))

                    blk2.vector(dxve)

                    def dxsp(sc, br=br, j=j):
                        vS.wait(sc)
                        sS.selfw(sc)
                        so.selfw(sc)
                        so.inc(sc.dma_start(
                            out=bass.AP(xvw_sp, br * 4 * 128 * T + j * TILE,
                                        [[T, 128], [128 * T, 4], [1, 512]]),
                            in_=bass.AP(xvw16, 0, [[2048, 128], [512, 4], [1, 512]])), 16)

                    blk2.scalar(dxsp)

            def e2v(vec):
                sS.wait(vec)
                vS.selfw(vec)
                vS.inc(vec.tensor_scalar(carF[:], carF[:], evenm[:, 0:1], None,
                                         OP.mult, OP.bypass))
                vS.selfw(vec)
                vS.inc(vec.tensor_scalar(carI[:], carI[:], evenm[:, 0:1], None,
                                         OP.mult, OP.bypass))

            blk2.vector(e2v)

            def e2g(gp):
                vS.wait(gp)
                so.wait(gp)
                gld.selfw(gp)
                gld.inc(gp.dma_start(out=bass.AP(cc2_in, 0, [[4, 128], [1, 4]]),
                                     in_=carF[:]), 16)
                gld.selfw(gp)
                gld.inc(gp.dma_start(out=bass.AP(cc2_in, 512, [[4, 128], [1, 4]]),
                                     in_=carI[:]), 16)

            blk2.gpsimd(e2g)

        if "AR2" not in phases:
            return nc
        blkC2_cm = nc.Block()
        blkC2 = blkC2_cm.__enter__()

        def cc2f(gp):
            gld.wait(gp)
            ccS.inc(gp.collective_compute(
                "AllReduce", OP.add,
                replica_groups=[[0, 1], [2, 3], [4, 5], [6, 7]],
                ins=[cc2_in.ap().opt()], outs=[cc2_out.ap().opt()]))
            ccS.wait(gp)
            gld.selfw(gp)
            gld.inc(gp.dma_start(out=totF[:], in_=bass.AP(cc2_out, 0, [[4, 128], [1, 4]])), 16)
            gld.selfw(gp)
            gld.inc(gp.dma_start(out=totI[:], in_=bass.AP(cc2_out, 512, [[4, 128], [1, 4]])), 16)

        blkC2.gpsimd(cc2f)

        def cc2v(vec):
            gld.wait(vec)
            vS.selfw(vec)
            vS.inc(vec.tensor_scalar(totF[:], totF[:], oddm[:, 0:1], None, OP.mult, OP.bypass))
            vS.selfw(vec)
            gS.inc(vec.tensor_scalar(totI[:], totI[:], oddm[:, 0:1], None, OP.mult, OP.bypass))

        blkC2.vector(cc2v)
        blkC2_cm.__exit__(None, None, None)

        if "C" not in phases:
            return nc
        with ExitStack() as cctx:
            Ec_ = cctx.enter_context
            Uf_sb = Ec_(nc.sbuf_tensor("Uf_sb", [128, 4 * E], BF16))
            Ui_sb = Ec_(nc.sbuf_tensor("Ui_sb", [128, 4 * E], BF16))
            bf_sb = Ec_(nc.sbuf_tensor("bf_sb", [8, E], BF16))
            bi_sb = Ec_(nc.sbuf_tensor("bi_sb", [8, E], BF16))
            wO_sb = Ec_(nc.sbuf_tensor("wO_sb", [128, 8 * E], BF16))
            x1_sb = Ec_(nc.sbuf_tensor("x1C", [128, 4096], F32))
            xvwL = Ec_(nc.sbuf_tensor("xvwL", [128, 4096], F16))
            cumL = Ec_(nc.sbuf_tensor("cumL", [128, 4096], F16))
            wfmL = Ec_(nc.sbuf_tensor("wfmL", [8, 1024], BF16))
            tscr = Ec_(nc.sbuf_tensor("tscr", [128, 2048], F32))
            prodw = Ec_(nc.sbuf_tensor("prodw", [128, 2048], BF16))
            prodw2 = Ec_(nc.sbuf_tensor("prodw2", [128, 2048], BF16))
            comb = Ec_(nc.sbuf_tensor("comb", [128, 4096], BF16))
            yg = Ec_(nc.sbuf_tensor("yg", [128, 1024], F32))
            wraw_t = Ec_(nc.sbuf_tensor("wraw_t", [128, 4096], F32))
            t2scr = Ec_(nc.sbuf_tensor("t2C", [128, 1024], F32))
            sm = {"st": Ec_(nc.sbuf_tensor("smD_st", [128, 48], F32)),
                  "mv": Ec_(nc.sbuf_tensor("smD_mv", [128, 8], F32)),
                  "rcol": Ec_(nc.sbuf_tensor("smD_rcol", [128, 8], F32))}
            psU = Ec_(nc.psum_tensor("psU", [128, 2048], F32))
            psW = Ec_(nc.psum_tensor("psW", [128, 1024], F32))
            psE = Ec_(nc.psum_tensor("psE", [128, 1024], F32))
            blkC = Ec_(nc.Block())

            def f0(sync):
                so.wait(sync)
                sync.wait_ge(wss[0].sem, wst_final[0])
                sync.wait_ge(wss[1].sem, wst_final[1])
                for nm, dst in (("UfF", Uf_sb), ("UiF", Ui_sb)):
                    ld.selfw(sync)
                    ld.inc(sync.dma_start(
                        out=bass.AP(dst, 0, [[4 * E, 128], [1, 4 * E]]),
                        in_=wb16[nm][:, :]), 16)
                ld.selfw(sync)
                ld.inc(sync.dma_start(
                    out=bass.AP(wO_sb, 0, [[8 * E, 128], [1, 8 * E]]),
                    in_=wb16["wOT"][:, :]), 16)
                ld.selfw(sync)
                ld.inc(sync.dma_start(out=bass.AP(bf_sb, 0, [[E, 8], [1, E]]),
                                       in_=wb16["bfw"][:, :]), 16)
                ld.selfw(sync)
                ld.inc(sync.dma_start(out=bass.AP(bi_sb, 0, [[E, 8], [1, E]]),
                                       in_=wb16["biv"][:, :]), 16)

            blkC.sync(f0)

            for j in range(NT):
                def g1(sync, j=j):
                    tS.wait(sync)
                    vS.wait(sync)
                    sS.wait(sync)
                    so.wait(sync)
                    ld.selfw(sync)
                    ld.inc(sync.dma_start(
                        out=bass.AP(x1_sb, 0, [[4096, 128], [1024, 4], [1, 1024]]),
                        in_=bass.AP(x1_sp, j * TILE * E, [[E, 128], [128 * E, 4], [1, E]]),
                    ), 16)

                blkC.sync(g1)

                def g2l(sync, j=j):
                    for br in range(2):
                        ld.inc(sync.dma_start(
                            out=bass.AP(xvwL, br * 2048, [[2 * 2048, 128], [512, 4], [1, 512]]),
                            in_=bass.AP(xvw_sp, br * 4 * 128 * T + j * TILE,
                                        [[T, 128], [128 * T, 4], [1, 512]])), 16)
                        ld.inc(sync.dma_start(
                            out=bass.AP(cumL, br * 2048, [[2 * 2048, 128], [512, 4], [1, 512]]),
                            in_=bass.AP(cum_sp, br * 4 * 128 * T + j * TILE,
                                        [[T, 128], [128 * T, 4], [1, 512]])), 16)
                        ld.inc(sync.dma_start(
                            out=bass.AP(wfmL, br * 512, [[1024, 8], [1, 512]]),
                            in_=bass.AP(w_sp, br * 8 * T + j * TILE, [[T, 8], [1, 512]])), 16)

                blkC.sync(g2l)

                def g2v(vec):
                    ld.wait(vec)
                    gS.wait(vec)
                    tS.wait(vec)
                    for br in range(2):
                        tot = totF if br == 0 else totI
                        dst = prodw if br == 0 else prodw2
                        vS.selfw(vec)
                        vS.inc(vec.tensor_tensor(
                            bass.AP(tscr, 0, [[2048, 128], [512, 4], [1, 512]]),
                            bass.AP(cumL, br * 2048, [[2 * 2048, 128], [512, 4], [1, 512]]),
                            bass.AP(tot, 0, [[4, 128], [1, 4], [0, 512]]),
                            OP.add))
                        vS.selfw(vec)
                        vS.inc(vec.tensor_mul(dst[:, :],
                                              bass.AP(xvwL, br * 2048,
                                                      [[2 * 2048, 128], [1, 2048]]),
                                              tscr[:]))

                blkC.vector(g2v)

                for wave in range(2):
                    def g3t(te, wave=wave):
                        vS.wait(te)
                        te.wait_ge(ld.sem, ld.n)
                        for dq in range(4):
                            dc = wave * 4 + dq
                            for b2 in range(2):
                                usb = Uf_sb if b2 == 0 else Ui_sb
                                psrc = prodw if b2 == 0 else prodw2
                                for kc in range(4):
                                    te.matmul(psU[:, dq * 512:(dq + 1) * 512],
                                              usb[:, kc * E + dc * 128: kc * E + (dc + 1) * 128],
                                              psrc[:, kc * 512:(kc + 1) * 512],
                                              start=(b2 == 0 and kc == 0), stop=False)
                            for b2 in range(2):
                                bsb = bf_sb if b2 == 0 else bi_sb
                                ins = te.matmul(psU[:, dq * 512:(dq + 1) * 512],
                                                bsb[:, dc * 128:(dc + 1) * 128],
                                                wfmL[:, b2 * 512:(b2 + 1) * 512],
                                                start=False, stop=(b2 == 1))
                            tS.inc(ins)

                    tb0 = tS.n
                    blkC.tensor(g3t)

                    def g3e(vec, wave=wave, tb0=tb0):
                        for dq in range(4):
                            dc = wave * 4 + dq
                            tS.wait(vec, tb0 + dq + 1)
                            vS.selfw(vec)
                            vS.inc(vec.tensor_copy(comb[:, dc * 512:(dc + 1) * 512],
                                                   psU[:, dq * 512:(dq + 1) * 512]))

                    blkC.vector(g3e)

                for g in range(NG):
                    def g4t(te, g=g):
                        vS.wait(te)
                        sS.wait(te)
                        for half in range(2):
                            for dc in range(8):
                                ins = te.matmul(
                                    psW[:, half * 512:(half + 1) * 512],
                                    comb[:, dc * 512 + g * 128: dc * 512 + (g + 1) * 128],
                                    wO_sb[:, dc * E + half * 512: dc * E + (half + 1) * 512],
                                    start=(dc == 0), stop=(dc == 7))
                        tS.inc(ins)

                    blkC.tensor(g4t)

                    def g4e(vec, g=g):
                        tS.wait(vec)
                        vS.selfw(vec)
                        vS.inc(vec.tensor_copy(wraw_t[:, g * 1024:(g + 1) * 1024],
                                               psW[:, 0:1024]))

                    blkC.vector(g4e)

                emit_ln_bn(blkC, lambda g, h: wraw_t[:, g * 1024 + h * 512: g * 1024 + h * 512 + 512], sm)

                for g in range(NG):
                    def g5v4(vec, g=g):
                        sS.wait(vec)
                        vS.selfw(vec)
                        vS.inc(vec.tensor_scalar(t2scr[:],
                                                 wraw_t[:, g * 1024:(g + 1) * 1024],
                                                 sm["mv"][:, 2 * g:2 * g + 1],
                                                 sm["rcol"][:, g:g + 1],
                                                 OP.subtract, OP.mult))
                        so.wait(vec)
                        vS.selfw(vec)
                        vS.inc(vec.tensor_add(yg[:], t2scr[:],
                                              x1_sb[:, g * 1024:(g + 1) * 1024]))

                    blkC.vector(g5v4)

                    def g5d(sc, g=g, j=j):
                        vS.wait(sc)
                        sS.selfw(sc)
                        so.selfw(sc)
                        so.inc(sc.dma_start(
                            out=bass.AP(y_d, (j * TILE + g * 128) * E, [[E, 128], [1, E]]),
                            in_=bass.AP(yg, 0, [[1024, 128], [1, 1024]])), 16)

                    blkC.scalar(g5d)

            def gfin(sc):
                so.wait(sc)

            blkC.scalar(gfin)

    return nc


_NC_CACHE = {}


def make_host_inputs(inputs):
    ins = {k: np.asarray(v) for k, v in inputs.items()}
    W_in = ins["L0_attn_W_in"]
    host = {}
    host["wvT"] = np.ascontiguousarray(W_in[0:1024].T)
    host["wqT"] = np.ascontiguousarray(W_in[1024:2048].T)
    host["wlgT"] = np.ascontiguousarray(W_in[2048:3072].T)
    host["woutT"] = np.ascontiguousarray(ins["L0_attn_W_out"].T)
    host["wg1T"] = np.ascontiguousarray(ins["L0_attn_W_g1"].T)
    host["wg2T"] = np.ascontiguousarray(ins["L0_attn_W_g2"].T)
    host["lnwb"] = np.ascontiguousarray(np.tile(ins["L0_attn_ln_w"][None, :], (128, 1)).astype(np.float32))
    host["wQT"] = np.ascontiguousarray(ins["L0_ffn_W_Q"].T)
    host["wKT"] = np.ascontiguousarray(ins["L0_ffn_W_K"].T)
    host["winvT"] = np.ascontiguousarray(ins["L0_ffn_W_inv"].T)
    host["wOT"] = np.ascontiguousarray(ins["L0_ffn_W_O"].T)
    host["rw1T"] = np.ascontiguousarray(ins["L0_ffn_router_w1"].T)
    host["rw2T"] = np.ascontiguousarray(ins["L0_ffn_router_w2"].T)
    host["rb1c"] = np.ascontiguousarray(ins["L0_ffn_router_b1"].reshape(8, 128).T)
    host["vfwdF"] = np.ascontiguousarray(ins["L0_ffn_V_fwd"].transpose(1, 0, 2).reshape(1024, 512))
    host["wfwdF"] = np.ascontiguousarray(ins["L0_ffn_W_fwd"].transpose(1, 0, 2).reshape(1024, 512))
    host["winvexF"] = np.ascontiguousarray(ins["L0_ffn_W_inv_exp"].transpose(1, 0, 2).reshape(1024, 512))
    host["vinvF"] = np.ascontiguousarray(ins["L0_ffn_V_inv"].transpose(1, 0, 2).reshape(1024, 512))
    host["UfF"] = np.ascontiguousarray(ins["L0_ffn_U_fwd"].transpose(0, 2, 1).reshape(512, 1024))
    host["UiF"] = np.ascontiguousarray(ins["L0_ffn_U_inv"].transpose(0, 2, 1).reshape(512, 1024))
    host["bfw"] = np.ascontiguousarray(ins["L0_ffn_b_fwd"])
    host["biv"] = np.ascontiguousarray(ins["L0_ffn_b_inv"])
    host["b2pe_in"] = np.tile(np.concatenate(
        [ins["L0_ffn_router_b2"], ins["L0_ffn_expert_bias"]]).reshape(1, 16), (128, 1)).astype(np.float32)
    host["alpha_in"] = np.tile(np.asarray(ins["L0_ffn_alpha_bi"]).reshape(1, 1), (128, 1)).astype(np.float32)
    host["ident_in"] = np.eye(128, dtype=BF)
    m = np.zeros((128, 128), np.float32)
    for b in range(2):
        i0 = b * 64
        m[i0:i0 + 64, i0:i0 + 64] = np.triu(np.ones((64, 64), np.float32))
    host["maskP_in"] = m
    Ec = np.zeros((8, 512), BF)
    for kk in range(8):
        Ec[kk, kk * 64:(kk + 1) * 64] = 1.0
    host["Econst_in"] = Ec
    return host


def kernel(**inputs):
    x = np.ascontiguousarray(
        np.asarray(inputs["L0_attn_x"]).reshape(4 * 8192, 1024).astype(np.float32))
    host = make_host_inputs(inputs)

    if "nc" not in _NC_CACHE:
        _NC_CACHE["nc"] = build_nc()
    nc = _NC_CACHE["nc"]

    in_maps = []
    for cid in range(NCORES):
        half = cid % 2
        im = dict(host)
        im["x"] = x[cid * T:(cid + 1) * T]
        rpj = np.zeros((128, 32), np.float32)
        for gi in range(32):
            base = half * T + gi * 128
            rpj[:, gi] = 1.0 / (base + 1 + np.arange(128, dtype=np.float32))
        im["rp_in"] = rpj
        ev = 1.0 if half == 0 else 0.0
        im["evenm_in"] = np.full((128, 1), ev, np.float32)
        im["oddm_in"] = np.full((128, 1), 1.0 - ev, np.float32)
        in_maps.append(im)

    res = run_bass_kernel_spmd(nc, in_maps, core_ids=list(range(NCORES)))
    out = np.empty((4 * 8192, 1024), np.float32)
    for cid in range(NCORES):
        out[cid * T:(cid + 1) * T] = np.asarray(res.results[cid]["y"], np.float32)
    return out.reshape(4, 8192, 1024)


# version 10
# speedup vs baseline: 1.2276x; 1.2276x over previous
import numpy as np
import ml_dtypes
from contextlib import ExitStack

import concourse.bass as bass
import concourse.mybir as mybir
from concourse.alu_op_type import AluOpType
from concourse.bass_utils import run_bass_kernel_spmd

F32 = mybir.dt.float32
BF16 = mybir.dt.bfloat16
F16 = mybir.dt.float16
AF = mybir.ActivationFunctionType
OP = AluOpType
AX = mybir.AxisListType

NCORES = 8
T = 4096
TILE = 512
NT = T // TILE
NG = TILE // 128
E = 1024
H = 8
EPS = 1e-5
INV_E = 1.0 / 1024.0
BF = ml_dtypes.bfloat16


class Ctr:
    def __init__(self, sem):
        self.sem = sem
        self.n = 0

    def inc(self, instr, k=1):
        instr.then_inc(self.sem, k)
        self.n += k

    def wait(self, eng, n=None):
        eng.wait_ge(self.sem, self.n if n is None else n)

    def selfw(self, eng):
        eng.wait_ge(self.sem, self.n)


def build_nc(debug_taps=(), phases=("A", "AR1", "B1", "B2", "AR2", "C"), a_tiles=NT, a_stages=9):
    nc = bass.Bass("TRN2", target_bir_lowering=False)
    din = {}

    def dr(name, shape, dt, kind=None):
        t = nc.dram_tensor(name, shape, dt, kind=kind) if kind else nc.dram_tensor(name, shape, dt)
        din[name] = t
        return t

    x_d = dr("x", [T, E], F32, "ExternalInput")
    y_d = dr("y", [T, E], F32, "ExternalOutput")
    for nm in ("wvT", "wqT", "wlgT", "woutT", "wQT", "wKT", "winvT", "rw1T", "wOT"):
        dr(nm, [E, E], F32, "ExternalInput")
    dr("wg1T", [E, 128], F32, "ExternalInput")
    dr("wg2T", [128, E], F32, "ExternalInput")
    dr("lnwb", [128, E], F32, "ExternalInput")
    for nm in ("vfwdF", "wfwdF", "winvexF", "vinvF"):
        dr(nm, [E, 512], F32, "ExternalInput")
    dr("rw2T", [E, 8], F32, "ExternalInput")
    dr("rb1c", [128, 8], F32, "ExternalInput")
    dr("b2pe_in", [128, 16], F32, "ExternalInput")
    dr("alpha_in", [128, 1], F32, "ExternalInput")
    dr("UfF", [512, E], F32, "ExternalInput")
    dr("UiF", [512, E], F32, "ExternalInput")
    dr("bfw", [8, E], F32, "ExternalInput")
    dr("biv", [8, E], F32, "ExternalInput")
    dr("ident_in", [128, 128], BF16, "ExternalInput")
    dr("maskP_in", [128, 128], F32, "ExternalInput")
    dr("Econst_in", [8, 512], BF16, "ExternalInput")
    dr("rp_in", [128, 32], F32, "ExternalInput")
    dr("evenm_in", [128, 1], F32, "ExternalInput")
    dr("oddm_in", [128, 1], F32, "ExternalInput")

    dbg = {}
    for nm in debug_taps:
        if nm == "S_dbg":
            dbg[nm] = dr(nm, [128, 1024], F32, "ExternalOutput")

    o_sp = dr("o_sp", [T, E], F16)
    qh_sp = dr("qh_sp", [H, 128, T], BF16)
    x1_sp = dr("x1_sp", [T, E], F32)
    xvw_sp = dr("xvw_sp", [2, 4, 128, T], F16)
    cum_sp = dr("cum_sp", [2, 4, 128, T], F16)
    w_sp = dr("w_sp", [2, 8, T], BF16)
    wb16 = {}
    for nm, w in (("wvT", E), ("wqT", E), ("wlgT", E), ("woutT", E), ("wQT", E),
                  ("wKT", E), ("winvT", E), ("rw1T", E), ("wOT", E),
                  ("vfwdF", 512), ("wfwdF", 512), ("winvexF", 512), ("vinvF", 512),
                  ("UfF", None), ("UiF", None)):
        if nm in ("UfF", "UiF"):
            wb16[nm] = dr(nm + "_b", [128, 4 * E], BF16)
        else:
            wb16[nm] = dr(nm + "_b", [128, 8 * w], BF16)
    wb16["wg1T"] = dr("wg1T_b", [128, 8 * 128], BF16)
    wb16["wg2T"] = dr("wg2T_b", [128, E], BF16)
    wb16["rw2T"] = dr("rw2T_b", [128, 64], BF16)
    wb16["bfw"] = dr("bfw_b", [8, E], BF16)
    wb16["biv"] = dr("biv_b", [8, E], BF16)
    cc1_in = dr("cc1_in", [128 * E], F32)
    cc1_out = dr("cc1_out", [128 * E], F32)
    cc2_in = dr("cc2_in", [128 * 8], F32)
    cc2_out = dr("cc2_out", [128 * 8], F32)

    with ExitStack() as ctx:
        Eo = ctx.enter_context
        ident = Eo(nc.sbuf_tensor("ident", [128, 128], BF16))
        maskP = Eo(nc.sbuf_tensor("maskP", [128, 128], F32))
        rp = Eo(nc.sbuf_tensor("rp", [128, 32], F32))
        evenm = Eo(nc.sbuf_tensor("evenm", [128, 1], F32))
        oddm = Eo(nc.sbuf_tensor("oddm", [128, 1], F32))
        S_in_bf = Eo(nc.sbuf_tensor("S_in_bf", [128, H * 128], BF16))
        totF = Eo(nc.sbuf_tensor("totF", [128, 4], F32))
        totI = Eo(nc.sbuf_tensor("totI", [128, 4], F32))
        alphac = Eo(nc.sbuf_tensor("alphac", [128, 1], F32))
        epscol = Eo(nc.sbuf_tensor("epscol", [128, 1], F32))
        b2pe_b = Eo(nc.sbuf_tensor("b2pe_b", [128, 8], F32))

        sems = {nm: Ctr(Eo(nc.semaphore(nm)))
                for nm in ("ld", "gld", "t", "v", "s", "g", "so", "cc")}
        ld, gld, tS, vS, sS, gS, so, ccS = (sems[n] for n in
                                            ("ld", "gld", "t", "v", "s", "g", "so", "cc"))

        def emit_ln_bn(blk, src_of, sm, extra_wait=None):
            def l1(vec):
                if extra_wait is not None:
                    extra_wait(vec)
                vS.selfw(vec)
                for g in range(NG):
                    for h in range(2):
                        vS.inc(vec.bn_stats(sm["st"][:, (2 * g + h) * 6:(2 * g + h) * 6 + 6],
                                            src_of(g, h)))
                vS.selfw(vec)
                for g in range(NG):
                    vS.inc(vec.bn_aggr(sm["mv"][:, 2 * g:2 * g + 2],
                                       sm["st"][:, g * 12:g * 12 + 12]))

            blk.vector(l1)

            def l2(sc):
                vS.wait(sc)
                sS.selfw(sc)
                sS.inc(sc.activation(sm["rcol"][:, NG:2 * NG],
                                     bass.AP(sm["mv"], 1, [[2 * NG, 128], [2, NG]]),
                                     AF.Ln, bias=epscol[:, 0:1]))
                sS.selfw(sc)
                sS.inc(sc.activation(sm["rcol"][:, 0:NG], sm["rcol"][:, NG:2 * NG],
                                     AF.Exp, scale=-0.5))

            blk.scalar(l2)

        def emit_norm_transpose(blk, x_src, xh_tm, xh_fm, psTP, sm, g):
            def n1(vec):
                sS.wait(vec)
                tS.wait(vec)
                vS.selfw(vec)
                vS.inc(vec.tensor_scalar(xh_tm[:], x_src[:, g * 1024:(g + 1) * 1024],
                                         sm["mv"][:, 2 * g:2 * g + 1],
                                         sm["rcol"][:, g:g + 1],
                                         OP.subtract, OP.mult))

            blk.vector(n1)

            def n2(te):
                vS.wait(te)
                for ci in range(8):
                    ins = te.transpose(psTP[:, ci * 128:(ci + 1) * 128],
                                       xh_tm[:, ci * 128:(ci + 1) * 128], ident[:])
                tS.inc(ins)

            blk.tensor(n2)

            def n3(vec):
                tS.wait(vec)
                vS.selfw(vec)
                vS.inc(vec.tensor_copy(
                    bass.AP(xh_fm, g * 128, [[4096, 128], [512, 8], [1, 128]]),
                    bass.AP(psTP, 0, [[1024, 128], [128, 8], [1, 128]])))

            blk.vector(n3)

        b2tmp = Eo(nc.sbuf_tensor("b2tmp", [128, 16], F32))
        blk0_cm = nc.Block()
        blk0 = blk0_cm.__enter__()

        def p0_sync(sync):
            for dst, nm in ((ident, "ident_in"), (maskP, "maskP_in"), (rp, "rp_in"),
                            (evenm, "evenm_in"), (oddm, "oddm_in")):
                ld.selfw(sync)
                ld.inc(sync.dma_start(out=dst[:], in_=din[nm][:, :]), 16)
            ld.selfw(sync)
            ld.inc(sync.dma_start(out=b2tmp[:], in_=din["b2pe_in"][:, :]), 16)
            ld.selfw(sync)
            ld.inc(sync.dma_start(out=alphac[:], in_=din["alpha_in"][:, :]), 16)

        blk0.sync(p0_sync)

        def p0_g(gp):
            ld.wait(gp)
            gS.selfw(gp)
            gS.inc(gp.memset(epscol[:], EPS))
            gS.selfw(gp)
            gS.inc(gp.tensor_add(b2pe_b[:], b2tmp[:, 0:8], b2tmp[:, 8:16]))

        blk0.gpsimd(p0_g)

        with nc.sbuf_tensor("wstg32", [128, 2 * 8192], F32) as wstg32, \
             nc.sbuf_tensor("wstg16", [128, 2 * 8192], BF16) as wstg16:
            plist = []
            for nm, w in (("wvT", E), ("wqT", E), ("wlgT", E), ("wg1T", 128),
                          ("woutT", E), ("wQT", E), ("wKT", E), ("winvT", E),
                          ("rw1T", E), ("wOT", E), ("vfwdF", 512), ("wfwdF", 512),
                          ("winvexF", 512), ("vinvF", 512)):
                plist.append((nm, [[w, 128], [128 * w, 8], [1, w]], 8 * w))
            plist.append(("wg2T", [[E, 128], [1, E]], E))
            plist.append(("rw2T", [[8, 128], [128 * 8, 8], [1, 8]], 64))
            plist.append(("UfF", [[E, 128], [128 * E, 4], [1, E]], 4 * E))
            plist.append(("UiF", [[E, 128], [128 * E, 4], [1, E]], 4 * E))

            wls = [Ctr(Eo(nc.semaphore("wl0"))), Ctr(Eo(nc.semaphore("wl1")))]
            wss = [Ctr(Eo(nc.semaphore("ws0"))), Ctr(Eo(nc.semaphore("ws1")))]
            cast_done = []
            for i, (nm, ap, width) in enumerate(plist):
                s = i % 2
                buf = s * 8192

                def w_ld(sync, nm=nm, ap=ap, width=width, buf=buf, i=i, s=s):
                    if i >= 2:
                        vS.wait(sync, cast_done[i - 2])
                    wls[s].inc(sync.dma_start(
                        out=bass.AP(wstg32, buf, [[2 * 8192, 128], [1, width]]),
                        in_=bass.AP(din[nm], 0, ap)), 16)

                blk0.sync(w_ld)
                wl_n = wls[s].n
                ws_n = wss[s].n

                def w_cast(vec, nm=nm, width=width, buf=buf, i=i, s=s,
                           wl_n=wl_n, ws_n=ws_n):
                    vec.wait_ge(wls[s].sem, wl_n)
                    if i >= 2:
                        vec.wait_ge(wss[s].sem, ws_n)
                    vS.selfw(vec)
                    vS.inc(vec.tensor_copy(
                        bass.AP(wstg16, buf, [[2 * 8192, 128], [1, width]]),
                        bass.AP(wstg32, buf, [[2 * 8192, 128], [1, width]])))
                    cast_done.append(vS.n)

                blk0.vector(w_cast)

                def w_st(sc, nm=nm, width=width, buf=buf, i=i, s=s):
                    vS.wait(sc, cast_done[i])
                    wss[s].inc(sc.dma_start(
                        out=bass.AP(wb16[nm], 0, [[width, 128], [1, width]]),
                        in_=bass.AP(wstg16, buf, [[2 * 8192, 128], [1, width]])), 16)

                blk0.scalar(w_st)
            wst_final = (wss[0].n, wss[1].n)

            with nc.sbuf_tensor("bstg32", [8, 2 * E], F32) as bstg32, \
                 nc.sbuf_tensor("bstg16", [8, 2 * E], BF16) as bstg16:
                def b_ld(sync):
                    ld.selfw(sync)
                    ld.inc(sync.dma_start(
                        out=bass.AP(bstg32, 0, [[2 * E, 8], [1, E]]),
                        in_=din["bfw"][:, :]), 16)
                    ld.selfw(sync)
                    ld.inc(sync.dma_start(
                        out=bass.AP(bstg32, E, [[2 * E, 8], [1, E]]),
                        in_=din["biv"][:, :]), 16)

                blk0.sync(b_ld)

                def b_cast(vec):
                    ld.wait(vec)
                    vS.selfw(vec)
                    vS.inc(vec.tensor_copy(bstg16[:, :], bstg32[:, :]))

                blk0.vector(b_cast)

                def b_st(sc):
                    vS.wait(sc)
                    sS.selfw(sc)
                    so.selfw(sc)
                    so.inc(sc.dma_start(out=bass.AP(wb16["bfw"], 0, [[E, 8], [1, E]]),
                                        in_=bass.AP(bstg16, 0, [[2 * E, 8], [1, E]])), 16)
                    so.selfw(sc)
                    so.inc(sc.dma_start(out=bass.AP(wb16["biv"], 0, [[E, 8], [1, E]]),
                                        in_=bass.AP(bstg16, E, [[2 * E, 8], [1, E]])), 16)
                    so.wait(sc)

                blk0.scalar(b_st)
        blk0_cm.__exit__(None, None, None)

        if "A" not in phases:
            return nc
        with ExitStack() as actx:
            Ea = actx.enter_context
            wv_sb = Ea(nc.sbuf_tensor("wv_sb", [128, 8 * E], BF16))
            wq_sb = Ea(nc.sbuf_tensor("wq_sb", [128, 8 * E], BF16))
            wlg_sb = Ea(nc.sbuf_tensor("wlg_sb", [128, 8 * E], BF16))
            x_sb = Ea(nc.sbuf_tensor("xA", [128, 4096], F32))
            xh_tm = Ea(nc.sbuf_tensor("xhA_tm", [128, 1024], BF16))
            xh_fm = Ea(nc.sbuf_tensor("xhA_fm", [128, 4096], BF16))
            v_tm = Ea(nc.sbuf_tensor("v_tm", [128, 4096], BF16))
            qpre = Ea(nc.sbuf_tensor("qpre", [128, 4096], F16))
            f_sb = Ea(nc.sbuf_tensor("f_sb", [128, 4096], F32))
            kf = Ea(nc.sbuf_tensor("kf", [128, 4096], F16))
            Acum = Ea(nc.sbuf_tensor("Acum", [128, 4096], F32))
            carryA = Ea(nc.sbuf_tensor("carryA", [128, 8], F32))
            ends = Ea(nc.sbuf_tensor("ends", [128, 64], F32))
            refs = Ea(nc.sbuf_tensor("refs", [128, 64], F32))
            eACall = Ea(nc.sbuf_tensor("eACall", [128, 64], F32))
            qt_bf = Ea(nc.sbuf_tensor("qt_bf", [128, 4096], BF16))
            kt_bf = Ea(nc.sbuf_tensor("kt_bf", [128, 4096], BF16))
            kh_full = Ea(nc.sbuf_tensor("kh_full", [128, 4096], BF16))
            khtm = Ea(nc.sbuf_tensor("khtm", [128, 4096], BF16))
            qhat_bf = Ea(nc.sbuf_tensor("qhat_bf", [128, 4096], BF16))
            psb_buf = Ea(nc.sbuf_tensor("psb_buf", [128, 1024], BF16))
            S_sb = Ea(nc.sbuf_tensor("S_sb", [128, 1024], F32))
            S_bf = Ea(nc.sbuf_tensor("S_bf", [128, 1024], BF16))
            S_bf2 = Ea(nc.sbuf_tensor("S_bf2", [128, 1024], BF16))
            o16 = Ea(nc.sbuf_tensor("o16", [128, 4096], F16))
            sm = {"st": Ea(nc.sbuf_tensor("smA_st", [128, 48], F32)),
                  "mv": Ea(nc.sbuf_tensor("smA_mv", [128, 8], F32)),
                  "rcol": Ea(nc.sbuf_tensor("smA_rcol", [128, 8], F32))}
            psTP = Ea(nc.psum_tensor("psTPa", [128, 1024], BF16))
            psProd = Ea(nc.psum_tensor("psProdA", [128, 2048], F32))
            psS = Ea(nc.psum_tensor("psSa", [128, 1024], F32))
            blkA = Ea(nc.Block())

            def pa_init(gp):
                gS.selfw(gp)
                gS.inc(gp.memset(S_sb[:], 0.0))
                gS.selfw(gp)
                gS.inc(gp.memset(S_bf[:], 0.0))
                gS.selfw(gp)
                gS.inc(gp.memset(carryA[:], 0.0))

            blkA.gpsimd(pa_init)

            def pa_wld(sync):
                so.wait(sync)
                sync.wait_ge(wss[0].sem, wst_final[0])
                sync.wait_ge(wss[1].sem, wst_final[1])
                for nm, dst in (("wvT", wv_sb), ("wqT", wq_sb), ("wlgT", wlg_sb)):
                    ld.selfw(sync)
                    ld.inc(sync.dma_start(
                        out=bass.AP(dst, 0, [[8 * E, 128], [1, 8 * E]]),
                        in_=wb16[nm][:, :]), 16)

            blkA.sync(pa_wld)

            for j in range(a_tiles):
                def a1(sync, j=j):
                    tS.wait(sync)
                    vS.wait(sync)
                    sS.wait(sync)
                    so.wait(sync)
                    ld.selfw(sync)
                    ld.inc(sync.dma_start(
                        out=bass.AP(x_sb, 0, [[4096, 128], [1024, 4], [1, 1024]]),
                        in_=bass.AP(x_d, j * TILE * E, [[E, 128], [128 * E, 4], [1, E]]),
                    ), 16)

                blkA.sync(a1)

                ldw = (ld.sem, ld.n)
                gw = (ld.sem, ld.n)

                emit_ln_bn(blkA, lambda g, h: x_sb[:, g * 1024 + h * 512: g * 1024 + h * 512 + 512],
                           sm, extra_wait=(lambda vec, w=ldw: vec.wait_ge(*w)))
                for g in range(NG):
                    emit_norm_transpose(blkA, x_sb, xh_tm, xh_fm, psTP, sm, g)

                if a_stages < 3:
                    continue
                vb_ve = vS.n

                def a3v(te, gw=gw):
                    te.wait_ge(*gw)
                    vS.wait(te)
                    sS.wait(te)
                    for g in range(NG):
                        if g >= 1:
                            te.wait_ge(vS.sem, vb_ve + g)
                        for half in range(2):
                            for fc in range(8):
                                ins = te.matmul(
                                    psProd[:, half * 512:(half + 1) * 512],
                                    xh_fm[:, fc * 512 + g * 128: fc * 512 + (g + 1) * 128],
                                    wv_sb[:, fc * E + half * 512: fc * E + (half + 1) * 512],
                                    start=(fc == 0), stop=(fc == 7))
                        tS.inc(ins)

                tb_v = tS.n
                blkA.tensor(a3v)

                def a3ve(vec):
                    for g in range(NG):
                        tS.wait(vec, tb_v + g + 1)
                        vS.selfw(vec)
                        vS.inc(vec.tensor_copy(v_tm[:, g * 1024:(g + 1) * 1024],
                                               psProd[:, 0:1024]))

                blkA.vector(a3ve)
                assert vS.n == vb_ve + NG

                sb_q0 = sS.n

                def a3q(te, sb_q0=sb_q0):
                    vS.wait(te)
                    sS.wait(te)
                    for cc in range(8):
                        reg = 1024 + (cc % 2) * 512
                        if cc >= 2:
                            te.wait_ge(sS.sem, sb_q0 + cc - 1)
                        for fc in range(8):
                            ins = te.matmul(psProd[:, reg:reg + 512],
                                            wq_sb[:, fc * E + cc * 128: fc * E + (cc + 1) * 128],
                                            xh_fm[:, fc * 512:(fc + 1) * 512],
                                            start=(fc == 0), stop=(fc == 7))
                        tS.inc(ins)

                tb_q = tS.n
                blkA.tensor(a3q)

                def a3qs(sc, tb_q=tb_q):
                    for cc in range(8):
                        tS.wait(sc, tb_q + cc + 1)
                        reg = 1024 + (cc % 2) * 512
                        sS.selfw(sc)
                        sS.inc(sc.activation(qpre[:, cc * 512:(cc + 1) * 512],
                                             psProd[:, reg:reg + 512], AF.Silu))

                blkA.scalar(a3qs)
                assert sS.n == sb_q0 + 8

                sb_l0 = sS.n

                def a3l(te, sb_l0=sb_l0):
                    for lc in range(8):
                        reg = 1024 + (lc % 2) * 512
                        if lc < 2:
                            te.wait_ge(sS.sem, sb_q0 + 7 + lc)
                        else:
                            te.wait_ge(sS.sem, sb_l0 + 2 * (lc - 2) + 2)
                        for fc in range(8):
                            ins = te.matmul(psProd[:, reg:reg + 512],
                                            wlg_sb[:, fc * E + lc * 128: fc * E + (lc + 1) * 128],
                                            xh_fm[:, fc * 512:(fc + 1) * 512],
                                            start=(fc == 0), stop=(fc == 7))
                        tS.inc(ins)

                tb_l = tS.n
                blkA.tensor(a3l)

                def a3ls(sc, tb_l=tb_l):
                    for lc in range(8):
                        tS.wait(sc, tb_l + lc + 1)
                        reg = 1024 + (lc % 2) * 512
                        sS.selfw(sc)
                        sS.inc(sc.activation(f_sb[:, lc * 512:(lc + 1) * 512],
                                             psProd[:, reg:reg + 512], AF.Sigmoid))
                        sS.selfw(sc)
                        sS.inc(sc.activation(kf[:, lc * 512:(lc + 1) * 512],
                                             psProd[:, reg:reg + 512], AF.Sigmoid,
                                             scale=-1.0))

                blkA.scalar(a3ls)

                if a_stages < 4:
                    continue
                def a3g(sc):
                    sS.selfw(sc)
                    sS.inc(sc.activation(f_sb[:], f_sb[:], AF.Ln))

                blkA.scalar(a3g)

                def a4v(vec):
                    gS.wait(vec)
                    sS.wait(vec)
                    for h in range(H):
                        vS.selfw(vec)
                        vS.inc(vec.tensor_tensor_scan(
                            Acum[:, h * 512:(h + 1) * 512],
                            f_sb[:, h * 512:(h + 1) * 512],
                            f_sb[:, h * 512:(h + 1) * 512],
                            carryA[:, h:h + 1], OP.add, OP.bypass))

                blkA.vector(a4v)

                def a4r(vec):
                    vS.selfw(vec)
                    vS.inc(vec.tensor_copy(ends[:],
                                           bass.AP(Acum, 63, [[4096, 128], [64, 64]])))
                    vS.selfw(vec)
                    vS.inc(vec.tensor_copy(bass.AP(refs, 0, [[64, 128], [8, 8]]),
                                           carryA[:, 0:8]))
                    vS.selfw(vec)
                    vS.inc(vec.tensor_copy(bass.AP(refs, 1, [[64, 128], [8, 8], [1, 7]]),
                                           bass.AP(ends, 0, [[64, 128], [8, 8], [1, 7]])))
                    vS.selfw(vec)
                    gS.inc(vec.tensor_copy(carryA[:, 0:8],
                                           bass.AP(ends, 7, [[64, 128], [8, 8]])))
                    vS.selfw(vec)
                    vS.inc(vec.tensor_tensor(
                        bass.AP(x_sb, 0, [[4096, 128], [64, 64], [1, 64]]),
                        bass.AP(Acum, 0, [[4096, 128], [64, 64], [1, 64]]),
                        bass.AP(refs, 0, [[64, 128], [1, 64], [0, 64]]), OP.subtract))
                    vS.selfw(vec)
                    vS.inc(vec.tensor_tensor(
                        bass.AP(f_sb, 0, [[4096, 128], [64, 64], [1, 64]]),
                        bass.AP(Acum, 0, [[4096, 128], [64, 64], [1, 64]]),
                        bass.AP(ends, 0, [[64, 128], [1, 64], [0, 64]]), OP.subtract))

                blkA.vector(a4r)
                vb_rel = vS.n

                sb_e0 = sS.n

                def a4e(sc, vb_rel=vb_rel):
                    vS.wait(sc, vb_rel)
                    sS.selfw(sc)
                    sS.inc(sc.activation(eACall[:],
                                         bass.AP(x_sb, 63, [[4096, 128], [64, 64]]),
                                         AF.Exp))
                    sS.selfw(sc)
                    sS.inc(sc.activation(qt_bf[:],
                                         bass.AP(x_sb, 0, [[4096, 128], [1, 4096]]),
                                         AF.Exp))
                    sS.selfw(sc)
                    sS.inc(sc.activation(kt_bf[:],
                                         bass.AP(x_sb, 0, [[4096, 128], [1, 4096]]),
                                         AF.Exp, scale=-1.0))
                    sS.selfw(sc)
                    sS.inc(sc.activation(kh_full[:],
                                         bass.AP(f_sb, 0, [[4096, 128], [1, 4096]]),
                                         AF.Exp, scale=-1.0))
                    sS.selfw(sc)
                    sS.inc(sc.activation(qhat_bf[:], Acum[:], AF.Exp))

                blkA.scalar(a4e)

                def a4m(vec, sb_e0=sb_e0):
                    vec.wait_ge(sS.sem, sb_e0 + 2)
                    vS.selfw(vec)
                    vS.inc(vec.tensor_mul(qt_bf[:], qt_bf[:], qpre[:]))
                    vec.wait_ge(sS.sem, sb_e0 + 3)
                    vS.selfw(vec)
                    vS.inc(vec.tensor_mul(kt_bf[:], kt_bf[:], kf[:]))
                    vec.wait_ge(sS.sem, sb_e0 + 4)
                    vS.selfw(vec)
                    vS.inc(vec.tensor_mul(kh_full[:], kh_full[:], kf[:]))
                    vec.wait_ge(sS.sem, sb_e0 + 5)
                    vS.selfw(vec)
                    vS.inc(vec.tensor_mul(qhat_bf[:], qhat_bf[:], qpre[:]))

                blkA.vector(a4m)
                vb_kh = vS.n - 1

                for tb in range(4):
                    def a4tp(te, tb=tb, vb_kh=vb_kh):
                        te.wait_ge(vS.sem, vb_kh)
                        if tb >= 1:
                            vS.wait(te)
                        for bi in range(8):
                            h = tb * 2 + bi // 4
                            g_ = bi % 4
                            ins = te.transpose(
                                psTP[:, bi * 128:(bi + 1) * 128],
                                kh_full[:, h * 512 + g_ * 128: h * 512 + (g_ + 1) * 128],
                                ident[:])
                        tS.inc(ins)

                    blkA.tensor(a4tp)

                    def a4te(vec, tb=tb):
                        tS.wait(vec)
                        vS.selfw(vec)
                        vS.inc(vec.tensor_copy(khtm[:, tb * 1024:(tb + 1) * 1024],
                                               psTP[:, 0:1024]))

                    blkA.vector(a4te)

                if a_stages < 5:
                    continue
                for g in range(NG):
                    cA = 2 * g
                    cB = 2 * g + 1
                    vb_pre = vS.n

                    def a5suA(te, g=g, vb_pre=vb_pre):
                        te.wait_ge(vS.sem, vb_pre)
                        for h in range(H):
                            ins = te.matmul(
                                psProd[:, 1024 + h * 128: 1024 + (h + 1) * 128],
                                khtm[0:64, h * 512 + g * 128: h * 512 + (g + 1) * 128],
                                v_tm[0:64, g * 1024 + h * 128: g * 1024 + (h + 1) * 128],
                                start=True, stop=True)
                            tS.inc(ins)

                    blkA.tensor(a5suA)
                    tb_suA = tS.n

                    def a5sc(te, g=g):
                        for h in range(H):
                            ins = te.matmul(
                                psProd[:, h * 128:(h + 1) * 128],
                                kt_bf[:, h * 512 + g * 128: h * 512 + (g + 1) * 128],
                                qt_bf[:, h * 512 + g * 128: h * 512 + (g + 1) * 128],
                                start=True, stop=True)
                            tS.inc(ins)

                    blkA.tensor(a5sc)
                    tb_sc = tS.n

                    def a5suB(te, g=g):
                        for h in range(H):
                            ins = te.matmul(
                                psS[:, h * 128:(h + 1) * 128],
                                khtm[64:128, h * 512 + g * 128: h * 512 + (g + 1) * 128],
                                v_tm[64:128, g * 1024 + h * 128: g * 1024 + (h + 1) * 128],
                                start=True, stop=True)
                            tS.inc(ins)

                    blkA.tensor(a5suB)
                    tb_suB = tS.n

                    def a5v1(vec, cA=cA, tb_suA=tb_suA):
                        tS.wait(vec, tb_suA)
                        gS.wait(vec)
                        vS.selfw(vec)
                        vS.inc(vec.tensor_tensor(
                            S_sb[:], S_sb[:],
                            bass.AP(eACall, cA, [[64, 128], [8, 8], [0, 128]]),
                            OP.mult))
                        vS.selfw(vec)
                        vS.inc(vec.tensor_add(S_sb[:], S_sb[:], psProd[:, 1024:2048]))

                    blkA.vector(a5v1)
                    vb_stt1 = vS.n

                    def a5vm(vec, tb_sc=tb_sc):
                        tS.wait(vec, tb_sc)
                        vS.selfw(vec)
                        vS.inc(vec.tensor_tensor(
                            psb_buf[:], psProd[:, 0:1024],
                            bass.AP(maskP, 0, [[128, 128], [0, 8], [1, 128]]),
                            OP.mult))

                    blkA.vector(a5vm)
                    vb_mask = vS.n

                    def a5g1(gp, vb_stt1=vb_stt1):
                        gp.wait_ge(vS.sem, vb_stt1)
                        gS.selfw(gp)
                        gS.inc(gp.tensor_copy(S_bf2[:], S_sb[:]))

                    blkA.gpsimd(a5g1)
                    gb_c1 = gS.n

                    def a5o1(te, g=g, vb_mask=vb_mask):
                        te.wait_ge(vS.sem, vb_mask)
                        gS.wait(te)
                        for h in range(H):
                            te.matmul(psProd[:, h * 128:(h + 1) * 128],
                                      psb_buf[:, h * 128:(h + 1) * 128],
                                      v_tm[:, g * 1024 + h * 128: g * 1024 + (h + 1) * 128],
                                      start=(h == 0) or (h == 4), stop=False,
                                      skip_group_check=True)
                            ins = te.matmul(
                                psProd[0:64, h * 128:(h + 1) * 128],
                                qt_bf[:, h * 512 + g * 128: h * 512 + g * 128 + 64],
                                S_bf[:, h * 128:(h + 1) * 128],
                                start=False, stop=False, skip_group_check=True)
                            tS.inc(ins)

                    blkA.tensor(a5o1)
                    tb_o1 = tS.n

                    def a5v2(vec, cB=cB, tb_suB=tb_suB):
                        tS.wait(vec, tb_suB)
                        gS.wait(vec)
                        vS.selfw(vec)
                        vS.inc(vec.tensor_tensor(
                            S_sb[:], S_sb[:],
                            bass.AP(eACall, cB, [[64, 128], [8, 8], [0, 128]]),
                            OP.mult))
                        vS.selfw(vec)
                        vS.inc(vec.tensor_add(S_sb[:], S_sb[:], psS[:, 0:1024]))

                    blkA.vector(a5v2)
                    vb_stt2 = vS.n

                    def a5g2(gp, vb_stt2=vb_stt2, tb_o1=tb_o1):
                        gp.wait_ge(vS.sem, vb_stt2)
                        gp.wait_ge(tS.sem, tb_o1)
                        gS.selfw(gp)
                        gS.inc(gp.tensor_copy(S_bf[:], S_sb[:]))

                    blkA.gpsimd(a5g2)
                    gb_c2 = gS.n

                    def a5o2(te, g=g, gb_c1=gb_c1):
                        te.wait_ge(gS.sem, gb_c1)
                        for h in range(H):
                            ins = te.matmul(
                                psProd[64:128, h * 128:(h + 1) * 128],
                                qt_bf[:, h * 512 + g * 128 + 64: h * 512 + (g + 1) * 128],
                                S_bf2[:, h * 128:(h + 1) * 128],
                                start=False, stop=(h == 3) or (h == 7),
                                skip_group_check=True)
                            tS.inc(ins)

                    blkA.tensor(a5o2)

                    def a5oe(vec, g=g):
                        tS.wait(vec)
                        vS.selfw(vec)
                        vS.inc(vec.tensor_copy(o16[:, g * 1024:(g + 1) * 1024],
                                               psProd[:, 0:1024]))

                    blkA.vector(a5oe)

                if a_stages < 6:
                    continue

                def a6(sc, j=j):
                    vS.wait(sc)
                    sS.selfw(sc)
                    so.selfw(sc)
                    so.inc(sc.dma_start(
                        out=bass.AP(o_sp, j * TILE * E, [[E, 128], [128 * E, 4], [1, E]]),
                        in_=bass.AP(o16, 0, [[4096, 128], [1024, 4], [1, 1024]])), 16)
                    sS.selfw(sc)
                    so.selfw(sc)
                    so.inc(sc.dma_start(
                        out=bass.AP(qh_sp, j * TILE, [[T, 128], [128 * T, 8], [1, 512]]),
                        in_=bass.AP(qhat_bf, 0, [[4096, 128], [512, 8], [1, 512]])), 16)

                blkA.scalar(a6)

            def ae_v(vec):
                gS.wait(vec)
                vS.selfw(vec)
                vS.inc(vec.tensor_scalar(S_sb[:], S_sb[:], evenm[:, 0:1], None,
                                         OP.mult, OP.bypass))

            blkA.vector(ae_v)

            if "S_dbg" in dbg:
                def ae_d(sc):
                    vS.wait(sc)
                    sS.selfw(sc)
                    so.selfw(sc)
                    so.inc(sc.dma_start(out=dbg["S_dbg"][:, :],
                                        in_=bass.AP(S_sb, 0, [[1024, 128], [1, 1024]])), 16)
                blkA.scalar(ae_d)

            def ae_g(gp):
                vS.wait(gp)
                so.wait(gp)
                gld.selfw(gp)
                gld.inc(gp.dma_start(out=cc1_in.ap(),
                                     in_=bass.AP(S_sb, 0, [[1024, 128], [1, 1024]])), 16)

            blkA.gpsimd(ae_g)

        if "AR1" not in phases:
            return nc
        sin_f32 = Eo(nc.sbuf_tensor("sin_f32", [128, 1024], F32))
        blkC1_cm = nc.Block()
        blkC1 = blkC1_cm.__enter__()

        def cc1f(gp):
            gld.wait(gp)
            ccS.inc(gp.collective_compute(
                "AllReduce", OP.add,
                replica_groups=[[0, 1], [2, 3], [4, 5], [6, 7]],
                ins=[cc1_in.ap().opt()], outs=[cc1_out.ap().opt()]))
            ccS.wait(gp)
            gld.selfw(gp)
            gld.inc(gp.dma_start(out=sin_f32[:],
                                 in_=bass.AP(cc1_out, 0, [[1024, 128], [1, 1024]])), 16)

        blkC1.gpsimd(cc1f)

        def cc1v(vec):
            gld.wait(vec)
            vS.selfw(vec)
            vS.inc(vec.tensor_scalar(sin_f32[:], sin_f32[:], oddm[:, 0:1], None,
                                     OP.mult, OP.bypass))
            vS.selfw(vec)
            gS.inc(vec.tensor_copy(S_in_bf[:], sin_f32[:]))

        blkC1.vector(cc1v)
        blkC1_cm.__exit__(None, None, None)

        if "B1" not in phases:
            return nc
        with ExitStack() as bctx:
            Eb = bctx.enter_context
            wg1_sb = Eb(nc.sbuf_tensor("wg1_sb", [128, 8 * 128], BF16))
            wg2_sb = Eb(nc.sbuf_tensor("wg2_sb", [128, E], BF16))
            wout_sb = Eb(nc.sbuf_tensor("wout_sb", [128, 8 * E], BF16))
            lnwb_sb = Eb(nc.sbuf_tensor("lnwb_sb", [128, E], F32))
            x_sb = Eb(nc.sbuf_tensor("xB", [128, 4096], F32))
            xh_tm = Eb(nc.sbuf_tensor("xhB_tm", [128, 1024], BF16))
            xh_fm = Eb(nc.sbuf_tensor("xhB_fm", [128, 4096], BF16))
            y1_bf = Eb(nc.sbuf_tensor("y1_bf", [128, 512], BF16))
            gate_t = Eb(nc.sbuf_tensor("gate_t", [128, 4096], F32))
            o16t = Eb(nc.sbuf_tensor("o16t", [128, 4096], F16))
            qht = Eb(nc.sbuf_tensor("qht", [128, 4096], BF16))
            ofull = Eb(nc.sbuf_tensor("ofull", [128, 1024], F32))
            og_t = Eb(nc.sbuf_tensor("og_t", [128, 4096], F32))
            araw_t = Eb(nc.sbuf_tensor("araw_t", [128, 4096], F32))
            ohat_scr = Eb(nc.sbuf_tensor("ohat_scr", [128, 1024], F32))
            ohat_bf = Eb(nc.sbuf_tensor("ohat_bf", [128, 1024], BF16))
            ohat_fm = Eb(nc.sbuf_tensor("ohat_fm", [128, 4096], BF16))
            x1g = Eb(nc.sbuf_tensor("x1g", [128, 1024], F32))
            t2scr = Eb(nc.sbuf_tensor("t2B", [128, 1024], F32))
            sm = {"st": Eb(nc.sbuf_tensor("smB_st", [128, 48], F32)),
                  "mv": Eb(nc.sbuf_tensor("smB_mv", [128, 8], F32)),
                  "rcol": Eb(nc.sbuf_tensor("smB_rcol", [128, 8], F32))}
            psTP = Eb(nc.psum_tensor("psTPb", [128, 1024], BF16))
            psM = Eb(nc.psum_tensor("psMb", [128, 2048], F32))
            psX = Eb(nc.psum_tensor("psXb", [128, 1024], F32))
            blkB = Eb(nc.Block())

            def b0(sync):
                so.wait(sync)
                sync.wait_ge(wss[0].sem, wst_final[0])
                sync.wait_ge(wss[1].sem, wst_final[1])
                ld.selfw(sync)
                ld.inc(sync.dma_start(
                    out=bass.AP(wg1_sb, 0, [[8 * 128, 128], [1, 8 * 128]]),
                    in_=wb16["wg1T"][:, :]), 16)
                ld.selfw(sync)
                ld.inc(sync.dma_start(
                    out=bass.AP(wg2_sb, 0, [[E, 128], [1, E]]),
                    in_=wb16["wg2T"][:, :]), 16)
                ld.selfw(sync)
                ld.inc(sync.dma_start(
                    out=bass.AP(wout_sb, 0, [[8 * E, 128], [1, 8 * E]]),
                    in_=wb16["woutT"][:, :]), 16)

            blkB.sync(b0)

            def b0s(sync):
                ld.selfw(sync)
                ld.inc(sync.dma_start(out=lnwb_sb[:], in_=din["lnwb"][:, :]), 16)

            blkB.sync(b0s)

            for j in range(NT):
                def b1(sync, j=j):
                    tS.wait(sync)
                    vS.wait(sync)
                    sS.wait(sync)
                    so.wait(sync)
                    ld.selfw(sync)
                    ld.inc(sync.dma_start(
                        out=bass.AP(x_sb, 0, [[4096, 128], [1024, 4], [1, 1024]]),
                        in_=bass.AP(x_d, j * TILE * E, [[E, 128], [128 * E, 4], [1, E]]),
                    ), 16)

                blkB.sync(b1)
                ldw = (ld.sem, ld.n)
                gw = (ld.sem, ld.n)

                emit_ln_bn(blkB, lambda g, h: x_sb[:, g * 1024 + h * 512: g * 1024 + h * 512 + 512],
                           sm, extra_wait=(lambda vec, w=ldw: vec.wait_ge(*w)))
                for g in range(NG):
                    emit_norm_transpose(blkB, x_sb, xh_tm, xh_fm, psTP, sm, g)

                def b3t(te, gw=gw):
                    te.wait_ge(*gw)
                    vS.wait(te)
                    sS.wait(te)
                    for fc in range(8):
                        ins = te.matmul(psM[:, 0:512],
                                        wg1_sb[:, fc * 128:(fc + 1) * 128],
                                        xh_fm[:, fc * 512:(fc + 1) * 512],
                                        start=(fc == 0), stop=(fc == 7))
                    tS.inc(ins)

                blkB.tensor(b3t)

                def b3e(vec):
                    tS.wait(vec)
                    vS.selfw(vec)
                    vS.inc(vec.tensor_copy(y1_bf[:], psM[:, 0:512]))

                blkB.vector(b3e)

                for g in range(NG):
                    def bg_t(te, g=g):
                        vS.wait(te)
                        if g >= 1:
                            sS.wait(te)
                        for half in range(2):
                            ins = te.matmul(psM[:, half * 512:(half + 1) * 512],
                                            y1_bf[:, g * 128:(g + 1) * 128],
                                            wg2_sb[:, half * 512:(half + 1) * 512],
                                            start=True, stop=True)
                        tS.inc(ins)

                    blkB.tensor(bg_t)

                    def bg_s(sc, g=g):
                        tS.wait(sc)
                        sS.selfw(sc)
                        sS.inc(sc.activation(gate_t[:, g * 1024:(g + 1) * 1024],
                                             psM[:, 0:1024], AF.Sigmoid))

                    blkB.scalar(bg_s)

                def b4l(sync, j=j):
                    ld.selfw(sync)
                    ld.inc(sync.dma_start(
                        out=bass.AP(o16t, 0, [[4096, 128], [1024, 4], [1, 1024]]),
                        in_=bass.AP(o_sp, j * TILE * E,
                                    [[E, 128], [128 * E, 4], [1, E]])), 16)
                    ld.selfw(sync)
                    ld.inc(sync.dma_start(
                        out=bass.AP(qht, 0, [[4096, 128], [512, 8], [1, 512]]),
                        in_=bass.AP(qh_sp, j * TILE, [[T, 128], [128 * T, 8], [1, 512]])), 16)

                blkB.sync(b4l)

                for g in range(NG):
                    def b4t(te, g=g):
                        ld.wait(te)
                        gS.wait(te)
                        vS.wait(te)
                        sS.wait(te)
                        for h in range(H):
                            ins = te.matmul(psM[:, 1024 + h * 128: 1024 + (h + 1) * 128],
                                            qht[:, h * 512 + g * 128: h * 512 + (g + 1) * 128],
                                            S_in_bf[:, h * 128:(h + 1) * 128],
                                            start=(h == 0) or (h == 4),
                                            stop=(h == 3) or (h == 7))
                        tS.inc(ins)

                    blkB.tensor(b4t)

                    def b5v(vec, g=g):
                        tS.wait(vec)
                        sS.wait(vec)
                        vS.selfw(vec)
                        vS.inc(vec.tensor_add(ofull[:], psM[:, 1024:2048],
                                              o16t[:, g * 1024:(g + 1) * 1024]))
                        vS.selfw(vec)
                        vS.inc(vec.tensor_mul(og_t[:, g * 1024:(g + 1) * 1024], ofull[:],
                                              gate_t[:, g * 1024:(g + 1) * 1024]))

                    blkB.vector(b5v)

                emit_ln_bn(blkB, lambda g, h: og_t[:, g * 1024 + h * 512: g * 1024 + h * 512 + 512], sm)

                for g in range(NG):
                    def b6v(vec, g=g):
                        sS.wait(vec)
                        tS.wait(vec)
                        vS.selfw(vec)
                        vS.inc(vec.tensor_scalar(ohat_scr[:],
                                                 og_t[:, g * 1024:(g + 1) * 1024],
                                                 sm["mv"][:, 2 * g:2 * g + 1],
                                                 sm["rcol"][:, g:g + 1],
                                                 OP.subtract, OP.mult))
                        vS.selfw(vec)
                        vS.inc(vec.tensor_mul(ohat_bf[:], ohat_scr[:], lnwb_sb[:]))

                    blkB.vector(b6v)

                    def b6t(te, g=g):
                        vS.wait(te)
                        for ci in range(8):
                            ins = te.transpose(psTP[:, ci * 128:(ci + 1) * 128],
                                               ohat_bf[:, ci * 128:(ci + 1) * 128], ident[:])
                        tS.inc(ins)

                    blkB.tensor(b6t)

                    def b6e(vec, g=g):
                        tS.wait(vec)
                        vS.selfw(vec)
                        vS.inc(vec.tensor_copy(
                            bass.AP(ohat_fm, g * 128, [[4096, 128], [512, 8], [1, 128]]),
                            bass.AP(psTP, 0, [[1024, 128], [128, 8], [1, 128]])))

                    blkB.vector(b6e)

                for g in range(NG):
                    def b7t(te, g=g):
                        vS.wait(te)
                        sS.wait(te)
                        for half in range(2):
                            for ec in range(8):
                                ins = te.matmul(
                                    psX[:, half * 512:(half + 1) * 512],
                                    ohat_fm[:, ec * 512 + g * 128: ec * 512 + (g + 1) * 128],
                                    wout_sb[:, ec * E + half * 512: ec * E + (half + 1) * 512],
                                    start=(ec == 0), stop=(ec == 7))
                        tS.inc(ins)

                    blkB.tensor(b7t)

                    def b7e(vec, g=g):
                        tS.wait(vec)
                        vS.selfw(vec)
                        vS.inc(vec.tensor_copy(araw_t[:, g * 1024:(g + 1) * 1024],
                                               psX[:, 0:1024]))

                    blkB.vector(b7e)

                emit_ln_bn(blkB, lambda g, h: araw_t[:, g * 1024 + h * 512: g * 1024 + h * 512 + 512], sm)

                for g in range(NG):
                    def b8v4(vec, g=g):
                        sS.wait(vec)
                        vS.selfw(vec)
                        vS.inc(vec.tensor_scalar(t2scr[:],
                                                 araw_t[:, g * 1024:(g + 1) * 1024],
                                                 sm["mv"][:, 2 * g:2 * g + 1],
                                                 sm["rcol"][:, g:g + 1],
                                                 OP.subtract, OP.mult))
                        so.wait(vec)
                        vS.selfw(vec)
                        vS.inc(vec.tensor_add(x1g[:], t2scr[:],
                                              x_sb[:, g * 1024:(g + 1) * 1024]))

                    blkB.vector(b8v4)

                    def b8d(sc, g=g, j=j):
                        vS.wait(sc)
                        sS.selfw(sc)
                        so.selfw(sc)
                        so.inc(sc.dma_start(
                            out=bass.AP(x1_sp, (j * TILE + g * 128) * E, [[E, 128], [1, E]]),
                            in_=bass.AP(x1g, 0, [[1024, 128], [1, 1024]])), 16)

                    blkB.scalar(b8d)

            def b9(sc):
                so.wait(sc)

            blkB.scalar(b9)

        if "B2" not in phases:
            return nc
        with ExitStack() as b2x:
            Eb = b2x.enter_context
            wQ_sb = Eb(nc.sbuf_tensor("wQ_sb", [128, 8 * E], BF16))
            wK_sb = Eb(nc.sbuf_tensor("wK_sb", [128, 8 * E], BF16))
            winv_sb = Eb(nc.sbuf_tensor("winv_sb", [128, 8 * E], BF16))
            rw1_sb = Eb(nc.sbuf_tensor("rw1_sb", [128, 8 * E], BF16))
            rw2_sb = Eb(nc.sbuf_tensor("rw2_sb", [128, 64], BF16))
            rb1_sb = Eb(nc.sbuf_tensor("rb1_sb", [128, 8], F32))
            vfwd_sb = Eb(nc.sbuf_tensor("vfwd_sb", [128, 8 * 512], BF16))
            wfwd_sb = Eb(nc.sbuf_tensor("wfwd_sb", [128, 8 * 512], BF16))
            winvex_sb = Eb(nc.sbuf_tensor("winvex_sb", [128, 8 * 512], BF16))
            vinv_sb = Eb(nc.sbuf_tensor("vinv_sb", [128, 8 * 512], BF16))
            Ec_sb = Eb(nc.sbuf_tensor("Ec_sb", [8, 512], BF16))
            x1_sb = Eb(nc.sbuf_tensor("x1B2", [128, 4096], F32))
            xh_tm = Eb(nc.sbuf_tensor("xh2_tm", [128, 1024], BF16))
            xh_fm = Eb(nc.sbuf_tensor("xh2_fm", [128, 4096], BF16))
            KA = Eb(nc.sbuf_tensor("KA", [128, 4096], BF16))
            KB = Eb(nc.sbuf_tensor("KB", [128, 4096], BF16))
            h_fm = Eb(nc.sbuf_tensor("h_fm", [128, 4096], BF16))
            yw32 = Eb(nc.sbuf_tensor("yw32", [128, 2048], F32))
            cum32 = Eb(nc.sbuf_tensor("cum32", [128, 2048], F32))
            cum16 = Eb(nc.sbuf_tensor("cum16", [128, 2048], F16))
            xvw16 = Eb(nc.sbuf_tensor("xvw16", [128, 2048], F16))
            wbar = Eb(nc.sbuf_tensor("wbar", [128, 512], BF16))
            wtm = Eb(nc.sbuf_tensor("wtm", [128, 32], F32))
            wtm_bf = Eb(nc.sbuf_tensor("wtm_bf", [128, 32], BF16))
            wsm_fm = Eb(nc.sbuf_tensor("wsm_fm", [8, 512], BF16))
            lgt = Eb(nc.sbuf_tensor("lgt", [128, 32], F32))
            smax = Eb(nc.sbuf_tensor("smax", [128, 8], F32))
            ssum = Eb(nc.sbuf_tensor("ssum", [128, 4], F32))
            esm = Eb(nc.sbuf_tensor("esm", [128, 32], F32))
            carF = Eb(nc.sbuf_tensor("carF", [128, 4], F32))
            carI = Eb(nc.sbuf_tensor("carI", [128, 4], F32))
            sm = {"st": Eb(nc.sbuf_tensor("smC_st", [128, 48], F32)),
                  "mv": Eb(nc.sbuf_tensor("smC_mv", [128, 8], F32)),
                  "rcol": Eb(nc.sbuf_tensor("smC_rcol", [128, 8], F32))}
            psTP = Eb(nc.psum_tensor("psTPc", [128, 1024], BF16))
            psM = Eb(nc.psum_tensor("psMc", [128, 2048], F32))
            psX = Eb(nc.psum_tensor("psXc", [128, 1024], F32))
            blk2 = Eb(nc.Block())

            def c0(sync):
                so.wait(sync)
                sync.wait_ge(wss[0].sem, wst_final[0])
                sync.wait_ge(wss[1].sem, wst_final[1])
                for nm, dst, w in (("wQT", wQ_sb, E), ("wKT", wK_sb, E),
                                   ("winvT", winv_sb, E), ("rw1T", rw1_sb, E),
                                   ("vfwdF", vfwd_sb, 512), ("wfwdF", wfwd_sb, 512),
                                   ("winvexF", winvex_sb, 512), ("vinvF", vinv_sb, 512)):
                    ld.selfw(sync)
                    ld.inc(sync.dma_start(
                        out=bass.AP(dst, 0, [[8 * w, 128], [1, 8 * w]]),
                        in_=wb16[nm][:, :]), 16)
                ld.selfw(sync)
                ld.inc(sync.dma_start(
                    out=bass.AP(rw2_sb, 0, [[64, 128], [1, 64]]),
                    in_=wb16["rw2T"][:, :]), 16)

            blk2.sync(c0)

            def c0g(gp):
                gS.selfw(gp)
                gS.inc(gp.memset(carF[:], 0.0))
                gS.selfw(gp)
                gS.inc(gp.memset(carI[:], 0.0))

            blk2.gpsimd(c0g)

            def c0s(sync):
                ld.selfw(sync)
                ld.inc(sync.dma_start(out=rb1_sb[:], in_=din["rb1c"][:, :]), 16)
                ld.selfw(sync)
                ld.inc(sync.dma_start(out=Ec_sb[:], in_=din["Econst_in"][:, :]), 16)

            blk2.sync(c0s)

            for j in range(NT):
                def d1(sync, j=j):
                    tS.wait(sync)
                    vS.wait(sync)
                    sS.wait(sync)
                    so.wait(sync)
                    ld.selfw(sync)
                    ld.inc(sync.dma_start(
                        out=bass.AP(x1_sb, 0, [[4096, 128], [1024, 4], [1, 1024]]),
                        in_=bass.AP(x1_sp, j * TILE * E, [[E, 128], [128 * E, 4], [1, E]]),
                    ), 16)

                blk2.sync(d1)
                ldw = (ld.sem, ld.n)
                gw = (ld.sem, ld.n)

                emit_ln_bn(blk2, lambda g, h: x1_sb[:, g * 1024 + h * 512: g * 1024 + h * 512 + 512],
                           sm, extra_wait=(lambda vec, w=ldw: vec.wait_ge(*w)))
                for g in range(NG):
                    emit_norm_transpose(blk2, x1_sb, xh_tm, xh_fm, psTP, sm, g)

                def mk_prod_fm(wsb, src, dst):
                    vb0 = vS.n

                    def pf(te):
                        vS.wait(te)
                        sS.wait(te)
                        gw and te.wait_ge(*gw)
                        for oc in range(8):
                            reg = 1024 + (oc % 2) * 512
                            if oc >= 2:
                                te.wait_ge(vS.sem, vb0 + oc - 1)
                            for fc in range(8):
                                ins = te.matmul(psM[:, reg:reg + 512],
                                                wsb[:, fc * E + oc * 128: fc * E + (oc + 1) * 128],
                                                src[:, fc * 512:(fc + 1) * 512],
                                                start=(fc == 0), stop=(fc == 7))
                            tS.inc(ins)

                    tb0 = tS.n
                    blk2.tensor(pf)

                    def pe(vec):
                        for oc in range(8):
                            tS.wait(vec, tb0 + oc + 1)
                            reg = 1024 + (oc % 2) * 512
                            vS.selfw(vec)
                            vS.inc(vec.tensor_copy(dst[:, oc * 512:(oc + 1) * 512],
                                                   psM[:, reg:reg + 512]))

                    blk2.vector(pe)

                mk_prod_fm(wK_sb, xh_fm, KA)
                mk_prod_fm(winv_sb, KA, KB)

                for br, (ksrc, wflat, car) in enumerate(
                        ((KA, wfwd_sb, carF), (KB, vinv_sb, carI))):
                    vb0 = vS.n

                    def dyw(te, ksrc=ksrc, wflat=wflat, vb0=vb0):
                        vS.wait(te)
                        for kc in range(4):
                            reg = 1024 + (kc % 2) * 512
                            if kc >= 2:
                                te.wait_ge(vS.sem, vb0 + kc - 1)
                            for fc in range(8):
                                ins = te.matmul(psM[:, reg:reg + 512],
                                                wflat[:, fc * 512 + kc * 128: fc * 512 + (kc + 1) * 128],
                                                ksrc[:, fc * 512:(fc + 1) * 512],
                                                start=(fc == 0), stop=(fc == 7))
                            tS.inc(ins)

                    tb0 = tS.n
                    blk2.tensor(dyw)

                    def dywe(vec, tb0=tb0):
                        for kc in range(4):
                            tS.wait(vec, tb0 + kc + 1)
                            reg = 1024 + (kc % 2) * 512
                            vS.selfw(vec)
                            vS.inc(vec.tensor_copy(yw32[:, kc * 512:(kc + 1) * 512],
                                                   psM[:, reg:reg + 512]))

                    blk2.vector(dywe)

                    def dsc(vec, car=car):
                        sS.wait(vec)
                        gS.wait(vec)
                        vS.selfw(vec)
                        for kc in range(4):
                            vS.selfw(vec)
                            vS.inc(vec.tensor_tensor_scan(
                                cum32[:, kc * 512:(kc + 1) * 512],
                                yw32[:, kc * 512:(kc + 1) * 512],
                                yw32[:, kc * 512:(kc + 1) * 512],
                                car[:, kc:kc + 1], OP.add, OP.bypass))

                    blk2.vector(dsc)

                    def dcar(sc, car=car, br=br, j=j):
                        vS.wait(sc)
                        so.wait(sc)
                        sS.selfw(sc)
                        sS.inc(sc.copy(car[:], bass.AP(cum32, 511, [[2048, 128], [512, 4]])))
                        sS.selfw(sc)
                        sS.inc(sc.copy(cum16[:], cum32[:]))
                        sS.selfw(sc)
                        so.selfw(sc)
                        so.inc(sc.dma_start(
                            out=bass.AP(cum_sp, br * 4 * 128 * T + j * TILE,
                                        [[T, 128], [128 * T, 4], [1, 512]]),
                            in_=bass.AP(cum16, 0, [[2048, 128], [512, 4], [1, 512]])), 16)

                    blk2.scalar(dcar)

                mk_prod_fm(wQ_sb, xh_fm, KA)
                mk_prod_fm(winv_sb, KA, KB)

                for br, (qsrc, vflat) in enumerate(((KA, vfwd_sb), (KB, winvex_sb))):
                    sb0 = sS.n

                    def dh(te, qsrc=qsrc, sb0=sb0):
                        vS.wait(te)
                        sS.wait(te)
                        for oc in range(8):
                            reg = 1024 + (oc % 2) * 512
                            if oc >= 2:
                                te.wait_ge(sS.sem, sb0 + oc - 1)
                            for fc in range(8):
                                ins = te.matmul(psM[:, reg:reg + 512],
                                                rw1_sb[:, fc * E + oc * 128: fc * E + (oc + 1) * 128],
                                                qsrc[:, fc * 512:(fc + 1) * 512],
                                                start=(fc == 0), stop=(fc == 7))
                            tS.inc(ins)

                    tb0 = tS.n
                    blk2.tensor(dh)

                    def dhe(sc, tb0=tb0):
                        for oc in range(8):
                            tS.wait(sc, tb0 + oc + 1)
                            reg = 1024 + (oc % 2) * 512
                            sS.selfw(sc)
                            sS.inc(sc.activation(h_fm[:, oc * 512:(oc + 1) * 512],
                                                 psM[:, reg:reg + 512], AF.Gelu,
                                                 bias=rb1_sb[:, oc:oc + 1]))

                    blk2.scalar(dhe)

                    def dlg(te):
                        sS.wait(te)
                        vS.wait(te)
                        for g in range(NG):
                            for rc in range(8):
                                ins = te.matmul(
                                    psX[:, g * 8:g * 8 + 8],
                                    h_fm[:, rc * 512 + g * 128: rc * 512 + (g + 1) * 128],
                                    rw2_sb[:, rc * 8:(rc + 1) * 8],
                                    start=(rc == 0), stop=(rc == 7))
                        tS.inc(ins)

                    blk2.tensor(dlg)

                    def dsm(vec):
                        tS.wait(vec)
                        sS.wait(vec)
                        vS.selfw(vec)
                        vS.inc(vec.tensor_tensor(
                            bass.AP(lgt, 0, [[32, 128], [8, 4], [1, 8]]),
                            bass.AP(psX, 0, [[1024, 128], [8, 4], [1, 8]]),
                            bass.AP(b2pe_b, 0, [[8, 128], [0, 4], [1, 8]]),
                            OP.add))
                        for g in range(NG):
                            vS.selfw(vec)
                            vS.inc(vec.tensor_reduce(smax[:, g:g + 1],
                                                     lgt[:, g * 8:(g + 1) * 8],
                                                     AX.X, OP.max))
                        vS.selfw(vec)
                        vS.inc(vec.tensor_scalar(smax[:, 4:8], smax[:, 0:4],
                                                 -1.0, None, OP.mult, OP.bypass))

                    blk2.vector(dsm)

                    def dex(sc):
                        vS.wait(sc)
                        for g in range(NG):
                            sS.selfw(sc)
                            sS.inc(sc.activation(esm[:, g * 8:(g + 1) * 8],
                                                 lgt[:, g * 8:(g + 1) * 8], AF.Exp,
                                                 bias=smax[:, 4 + g:5 + g]))

                    blk2.scalar(dex)

                    def dsm2(vec, br=br, j=j):
                        sS.wait(vec)
                        for g in range(NG):
                            vS.selfw(vec)
                            vS.inc(vec.tensor_reduce(ssum[:, g:g + 1],
                                                     esm[:, g * 8:(g + 1) * 8],
                                                     AX.X, OP.add))
                        vS.selfw(vec)
                        vS.inc(vec.reciprocal(ssum[:, 0:4], ssum[:, 0:4]))
                        if br == 1:
                            vS.selfw(vec)
                            vS.inc(vec.tensor_scalar(ssum[:, 0:4], ssum[:, 0:4],
                                                     alphac[:, 0:1], None,
                                                     OP.mult, OP.bypass))
                        for g in range(NG):
                            vS.selfw(vec)
                            vS.inc(vec.tensor_scalar(wtm[:, g * 8:(g + 1) * 8],
                                                     esm[:, g * 8:(g + 1) * 8],
                                                     ssum[:, g:g + 1],
                                                     rp[:, j * 4 + g: j * 4 + g + 1],
                                                     OP.mult, OP.mult))
                        vS.selfw(vec)
                        vS.inc(vec.tensor_copy(wtm_bf[:], wtm[:]))

                    blk2.vector(dsm2)

                    def dwt(te):
                        vS.wait(te)
                        for g in range(NG):
                            ins = te.transpose(
                                bass.AP(psTP, g * 128, [[1024, 8], [1, 128]]),
                                wtm_bf[:, g * 8:(g + 1) * 8], ident[:])
                        tS.inc(ins)

                    blk2.tensor(dwt)

                    def dwte(vec):
                        tS.wait(vec)
                        so.wait(vec)
                        vS.selfw(vec)
                        vS.inc(vec.tensor_copy(
                            bass.AP(wsm_fm, 0, [[512, 8], [128, 4], [1, 128]]),
                            bass.AP(psTP, 0, [[1024, 8], [128, 4], [1, 128]])))

                    blk2.vector(dwte)

                    def dwsp(sc, br=br, j=j):
                        vS.wait(sc)
                        sS.selfw(sc)
                        so.selfw(sc)
                        so.inc(sc.dma_start(
                            out=bass.AP(w_sp, br * 8 * T + j * TILE, [[T, 8], [1, 512]]),
                            in_=bass.AP(wsm_fm, 0, [[512, 8], [1, 512]])), 16)

                    blk2.scalar(dwsp)

                    vb0 = vS.n

                    def dxv(te, qsrc=qsrc, vflat=vflat, vb0=vb0):
                        vS.wait(te)
                        for kc in range(4):
                            reg = 1024 + (kc % 2) * 512
                            if kc >= 1:
                                te.wait_ge(vS.sem, vb0 + 2 * kc - 1)
                            for fc in range(8):
                                te.matmul(psM[:, reg:reg + 512],
                                          vflat[:, fc * 512 + kc * 128: fc * 512 + (kc + 1) * 128],
                                          qsrc[:, fc * 512:(fc + 1) * 512],
                                          start=(fc == 0), stop=(fc == 7))
                            ins = te.matmul(psX[:, 0:512], Ec_sb[:, kc * 128:(kc + 1) * 128],
                                            wsm_fm[:, :], start=True, stop=True)
                            tS.inc(ins)

                    tb0 = tS.n
                    blk2.tensor(dxv)

                    def dxve(vec, tb0=tb0):
                        so.wait(vec)
                        for kc in range(4):
                            tS.wait(vec, tb0 + kc + 1)
                            reg = 1024 + (kc % 2) * 512
                            vS.selfw(vec)
                            vS.inc(vec.tensor_copy(wbar[:], psX[:, 0:512]))
                            vS.selfw(vec)
                            vS.inc(vec.tensor_mul(xvw16[:, kc * 512:(kc + 1) * 512],
                                                  psM[:, reg:reg + 512], wbar[:]))

                    blk2.vector(dxve)

                    def dxsp(sc, br=br, j=j):
                        vS.wait(sc)
                        sS.selfw(sc)
                        so.selfw(sc)
                        so.inc(sc.dma_start(
                            out=bass.AP(xvw_sp, br * 4 * 128 * T + j * TILE,
                                        [[T, 128], [128 * T, 4], [1, 512]]),
                            in_=bass.AP(xvw16, 0, [[2048, 128], [512, 4], [1, 512]])), 16)

                    blk2.scalar(dxsp)

            def e2v(vec):
                sS.wait(vec)
                vS.selfw(vec)
                vS.inc(vec.tensor_scalar(carF[:], carF[:], evenm[:, 0:1], None,
                                         OP.mult, OP.bypass))
                vS.selfw(vec)
                vS.inc(vec.tensor_scalar(carI[:], carI[:], evenm[:, 0:1], None,
                                         OP.mult, OP.bypass))

            blk2.vector(e2v)

            def e2g(gp):
                vS.wait(gp)
                so.wait(gp)
                gld.selfw(gp)
                gld.inc(gp.dma_start(out=bass.AP(cc2_in, 0, [[4, 128], [1, 4]]),
                                     in_=carF[:]), 16)
                gld.selfw(gp)
                gld.inc(gp.dma_start(out=bass.AP(cc2_in, 512, [[4, 128], [1, 4]]),
                                     in_=carI[:]), 16)

            blk2.gpsimd(e2g)

        if "AR2" not in phases:
            return nc
        blkC2_cm = nc.Block()
        blkC2 = blkC2_cm.__enter__()

        def cc2f(gp):
            gld.wait(gp)
            ccS.inc(gp.collective_compute(
                "AllReduce", OP.add,
                replica_groups=[[0, 1], [2, 3], [4, 5], [6, 7]],
                ins=[cc2_in.ap().opt()], outs=[cc2_out.ap().opt()]))
            ccS.wait(gp)
            gld.selfw(gp)
            gld.inc(gp.dma_start(out=totF[:], in_=bass.AP(cc2_out, 0, [[4, 128], [1, 4]])), 16)
            gld.selfw(gp)
            gld.inc(gp.dma_start(out=totI[:], in_=bass.AP(cc2_out, 512, [[4, 128], [1, 4]])), 16)

        blkC2.gpsimd(cc2f)

        def cc2v(vec):
            gld.wait(vec)
            vS.selfw(vec)
            vS.inc(vec.tensor_scalar(totF[:], totF[:], oddm[:, 0:1], None, OP.mult, OP.bypass))
            vS.selfw(vec)
            gS.inc(vec.tensor_scalar(totI[:], totI[:], oddm[:, 0:1], None, OP.mult, OP.bypass))

        blkC2.vector(cc2v)
        blkC2_cm.__exit__(None, None, None)

        if "C" not in phases:
            return nc
        with ExitStack() as cctx:
            Ec_ = cctx.enter_context
            Uf_sb = Ec_(nc.sbuf_tensor("Uf_sb", [128, 4 * E], BF16))
            Ui_sb = Ec_(nc.sbuf_tensor("Ui_sb", [128, 4 * E], BF16))
            bf_sb = Ec_(nc.sbuf_tensor("bf_sb", [8, E], BF16))
            bi_sb = Ec_(nc.sbuf_tensor("bi_sb", [8, E], BF16))
            wO_sb = Ec_(nc.sbuf_tensor("wO_sb", [128, 8 * E], BF16))
            x1_sb = Ec_(nc.sbuf_tensor("x1C", [128, 4096], F32))
            xvwL = Ec_(nc.sbuf_tensor("xvwL", [128, 4096], F16))
            cumL = Ec_(nc.sbuf_tensor("cumL", [128, 4096], F16))
            wfmL = Ec_(nc.sbuf_tensor("wfmL", [8, 1024], BF16))
            tscr = Ec_(nc.sbuf_tensor("tscr", [128, 2048], F32))
            prodw = Ec_(nc.sbuf_tensor("prodw", [128, 2048], BF16))
            prodw2 = Ec_(nc.sbuf_tensor("prodw2", [128, 2048], BF16))
            comb = Ec_(nc.sbuf_tensor("comb", [128, 4096], BF16))
            yg = Ec_(nc.sbuf_tensor("yg", [128, 1024], F32))
            wraw_t = Ec_(nc.sbuf_tensor("wraw_t", [128, 4096], F32))
            t2scr = Ec_(nc.sbuf_tensor("t2C", [128, 1024], F32))
            sm = {"st": Ec_(nc.sbuf_tensor("smD_st", [128, 48], F32)),
                  "mv": Ec_(nc.sbuf_tensor("smD_mv", [128, 8], F32)),
                  "rcol": Ec_(nc.sbuf_tensor("smD_rcol", [128, 8], F32))}
            psU = Ec_(nc.psum_tensor("psU", [128, 2048], F32))
            psW = Ec_(nc.psum_tensor("psW", [128, 1024], F32))
            psE = Ec_(nc.psum_tensor("psE", [128, 1024], F32))
            blkC = Ec_(nc.Block())

            def f0(sync):
                so.wait(sync)
                sync.wait_ge(wss[0].sem, wst_final[0])
                sync.wait_ge(wss[1].sem, wst_final[1])
                for nm, dst in (("UfF", Uf_sb), ("UiF", Ui_sb)):
                    ld.selfw(sync)
                    ld.inc(sync.dma_start(
                        out=bass.AP(dst, 0, [[4 * E, 128], [1, 4 * E]]),
                        in_=wb16[nm][:, :]), 16)
                ld.selfw(sync)
                ld.inc(sync.dma_start(
                    out=bass.AP(wO_sb, 0, [[8 * E, 128], [1, 8 * E]]),
                    in_=wb16["wOT"][:, :]), 16)
                ld.selfw(sync)
                ld.inc(sync.dma_start(out=bass.AP(bf_sb, 0, [[E, 8], [1, E]]),
                                       in_=wb16["bfw"][:, :]), 16)
                ld.selfw(sync)
                ld.inc(sync.dma_start(out=bass.AP(bi_sb, 0, [[E, 8], [1, E]]),
                                       in_=wb16["biv"][:, :]), 16)

            blkC.sync(f0)

            for j in range(NT):
                def g1(sync, j=j):
                    tS.wait(sync)
                    vS.wait(sync)
                    sS.wait(sync)
                    so.wait(sync)
                    ld.selfw(sync)
                    ld.inc(sync.dma_start(
                        out=bass.AP(x1_sb, 0, [[4096, 128], [1024, 4], [1, 1024]]),
                        in_=bass.AP(x1_sp, j * TILE * E, [[E, 128], [128 * E, 4], [1, E]]),
                    ), 16)

                blkC.sync(g1)

                def g2l(sync, j=j):
                    for br in range(2):
                        ld.inc(sync.dma_start(
                            out=bass.AP(xvwL, br * 2048, [[2 * 2048, 128], [512, 4], [1, 512]]),
                            in_=bass.AP(xvw_sp, br * 4 * 128 * T + j * TILE,
                                        [[T, 128], [128 * T, 4], [1, 512]])), 16)
                        ld.inc(sync.dma_start(
                            out=bass.AP(cumL, br * 2048, [[2 * 2048, 128], [512, 4], [1, 512]]),
                            in_=bass.AP(cum_sp, br * 4 * 128 * T + j * TILE,
                                        [[T, 128], [128 * T, 4], [1, 512]])), 16)
                        ld.inc(sync.dma_start(
                            out=bass.AP(wfmL, br * 512, [[1024, 8], [1, 512]]),
                            in_=bass.AP(w_sp, br * 8 * T + j * TILE, [[T, 8], [1, 512]])), 16)

                blkC.sync(g2l)

                def g2v(vec):
                    ld.wait(vec)
                    gS.wait(vec)
                    tS.wait(vec)
                    for br in range(2):
                        tot = totF if br == 0 else totI
                        dst = prodw if br == 0 else prodw2
                        vS.selfw(vec)
                        vS.inc(vec.tensor_tensor(
                            bass.AP(tscr, 0, [[2048, 128], [512, 4], [1, 512]]),
                            bass.AP(cumL, br * 2048, [[2 * 2048, 128], [512, 4], [1, 512]]),
                            bass.AP(tot, 0, [[4, 128], [1, 4], [0, 512]]),
                            OP.add))
                        vS.selfw(vec)
                        vS.inc(vec.tensor_mul(dst[:, :],
                                              bass.AP(xvwL, br * 2048,
                                                      [[2 * 2048, 128], [1, 2048]]),
                                              tscr[:]))

                blkC.vector(g2v)

                for wave in range(2):
                    def g3t(te, wave=wave):
                        vS.wait(te)
                        te.wait_ge(ld.sem, ld.n)
                        for dq in range(4):
                            dc = wave * 4 + dq
                            for b2 in range(2):
                                usb = Uf_sb if b2 == 0 else Ui_sb
                                psrc = prodw if b2 == 0 else prodw2
                                for kc in range(4):
                                    te.matmul(psU[:, dq * 512:(dq + 1) * 512],
                                              usb[:, kc * E + dc * 128: kc * E + (dc + 1) * 128],
                                              psrc[:, kc * 512:(kc + 1) * 512],
                                              start=(b2 == 0 and kc == 0), stop=False)
                            for b2 in range(2):
                                bsb = bf_sb if b2 == 0 else bi_sb
                                ins = te.matmul(psU[:, dq * 512:(dq + 1) * 512],
                                                bsb[:, dc * 128:(dc + 1) * 128],
                                                wfmL[:, b2 * 512:(b2 + 1) * 512],
                                                start=False, stop=(b2 == 1))
                            tS.inc(ins)

                    tb0 = tS.n
                    blkC.tensor(g3t)

                    def g3e(vec, wave=wave, tb0=tb0):
                        for dq in range(4):
                            dc = wave * 4 + dq
                            tS.wait(vec, tb0 + dq + 1)
                            vS.selfw(vec)
                            vS.inc(vec.tensor_copy(comb[:, dc * 512:(dc + 1) * 512],
                                                   psU[:, dq * 512:(dq + 1) * 512]))

                    blkC.vector(g3e)

                for g in range(NG):
                    def g4t(te, g=g):
                        vS.wait(te)
                        sS.wait(te)
                        for half in range(2):
                            for dc in range(8):
                                ins = te.matmul(
                                    psW[:, half * 512:(half + 1) * 512],
                                    comb[:, dc * 512 + g * 128: dc * 512 + (g + 1) * 128],
                                    wO_sb[:, dc * E + half * 512: dc * E + (half + 1) * 512],
                                    start=(dc == 0), stop=(dc == 7))
                        tS.inc(ins)

                    blkC.tensor(g4t)

                    def g4e(vec, g=g):
                        tS.wait(vec)
                        vS.selfw(vec)
                        vS.inc(vec.tensor_copy(wraw_t[:, g * 1024:(g + 1) * 1024],
                                               psW[:, 0:1024]))

                    blkC.vector(g4e)

                emit_ln_bn(blkC, lambda g, h: wraw_t[:, g * 1024 + h * 512: g * 1024 + h * 512 + 512], sm)

                for g in range(NG):
                    def g5v4(vec, g=g):
                        sS.wait(vec)
                        vS.selfw(vec)
                        vS.inc(vec.tensor_scalar(t2scr[:],
                                                 wraw_t[:, g * 1024:(g + 1) * 1024],
                                                 sm["mv"][:, 2 * g:2 * g + 1],
                                                 sm["rcol"][:, g:g + 1],
                                                 OP.subtract, OP.mult))
                        so.wait(vec)
                        vS.selfw(vec)
                        vS.inc(vec.tensor_add(yg[:], t2scr[:],
                                              x1_sb[:, g * 1024:(g + 1) * 1024]))

                    blkC.vector(g5v4)

                    def g5d(sc, g=g, j=j):
                        vS.wait(sc)
                        sS.selfw(sc)
                        so.selfw(sc)
                        so.inc(sc.dma_start(
                            out=bass.AP(y_d, (j * TILE + g * 128) * E, [[E, 128], [1, E]]),
                            in_=bass.AP(yg, 0, [[1024, 128], [1, 1024]])), 16)

                    blkC.scalar(g5d)

            def gfin(sc):
                so.wait(sc)

            blkC.scalar(gfin)

    return nc


_NC_CACHE = {}


def make_host_inputs(inputs):
    ins = {k: np.asarray(v) for k, v in inputs.items()}
    W_in = ins["L0_attn_W_in"]
    host = {}
    host["wvT"] = np.ascontiguousarray(W_in[0:1024].T)
    host["wqT"] = np.ascontiguousarray(W_in[1024:2048].T)
    host["wlgT"] = np.ascontiguousarray(W_in[2048:3072].T)
    host["woutT"] = np.ascontiguousarray(ins["L0_attn_W_out"].T)
    host["wg1T"] = np.ascontiguousarray(ins["L0_attn_W_g1"].T)
    host["wg2T"] = np.ascontiguousarray(ins["L0_attn_W_g2"].T)
    host["lnwb"] = np.ascontiguousarray(np.tile(ins["L0_attn_ln_w"][None, :], (128, 1)).astype(np.float32))
    host["wQT"] = np.ascontiguousarray(ins["L0_ffn_W_Q"].T)
    host["wKT"] = np.ascontiguousarray(ins["L0_ffn_W_K"].T)
    host["winvT"] = np.ascontiguousarray(ins["L0_ffn_W_inv"].T)
    host["wOT"] = np.ascontiguousarray(ins["L0_ffn_W_O"].T)
    host["rw1T"] = np.ascontiguousarray(ins["L0_ffn_router_w1"].T)
    host["rw2T"] = np.ascontiguousarray(ins["L0_ffn_router_w2"].T)
    host["rb1c"] = np.ascontiguousarray(ins["L0_ffn_router_b1"].reshape(8, 128).T)
    host["vfwdF"] = np.ascontiguousarray(ins["L0_ffn_V_fwd"].transpose(1, 0, 2).reshape(1024, 512))
    host["wfwdF"] = np.ascontiguousarray(ins["L0_ffn_W_fwd"].transpose(1, 0, 2).reshape(1024, 512))
    host["winvexF"] = np.ascontiguousarray(ins["L0_ffn_W_inv_exp"].transpose(1, 0, 2).reshape(1024, 512))
    host["vinvF"] = np.ascontiguousarray(ins["L0_ffn_V_inv"].transpose(1, 0, 2).reshape(1024, 512))
    host["UfF"] = np.ascontiguousarray(ins["L0_ffn_U_fwd"].transpose(0, 2, 1).reshape(512, 1024))
    host["UiF"] = np.ascontiguousarray(ins["L0_ffn_U_inv"].transpose(0, 2, 1).reshape(512, 1024))
    host["bfw"] = np.ascontiguousarray(ins["L0_ffn_b_fwd"])
    host["biv"] = np.ascontiguousarray(ins["L0_ffn_b_inv"])
    host["b2pe_in"] = np.tile(np.concatenate(
        [ins["L0_ffn_router_b2"], ins["L0_ffn_expert_bias"]]).reshape(1, 16), (128, 1)).astype(np.float32)
    host["alpha_in"] = np.tile(np.asarray(ins["L0_ffn_alpha_bi"]).reshape(1, 1), (128, 1)).astype(np.float32)
    host["ident_in"] = np.eye(128, dtype=BF)
    m = np.zeros((128, 128), np.float32)
    for b in range(2):
        i0 = b * 64
        m[i0:i0 + 64, i0:i0 + 64] = np.triu(np.ones((64, 64), np.float32))
    host["maskP_in"] = m
    Ec = np.zeros((8, 512), BF)
    for kk in range(8):
        Ec[kk, kk * 64:(kk + 1) * 64] = 1.0
    host["Econst_in"] = Ec
    return host


def kernel(**inputs):
    x = np.ascontiguousarray(
        np.asarray(inputs["L0_attn_x"]).reshape(4 * 8192, 1024).astype(np.float32))
    host = make_host_inputs(inputs)

    if "nc" not in _NC_CACHE:
        _NC_CACHE["nc"] = build_nc()
    nc = _NC_CACHE["nc"]

    in_maps = []
    for cid in range(NCORES):
        half = cid % 2
        im = dict(host)
        im["x"] = x[cid * T:(cid + 1) * T]
        rpj = np.zeros((128, 32), np.float32)
        for gi in range(32):
            base = half * T + gi * 128
            rpj[:, gi] = 1.0 / (base + 1 + np.arange(128, dtype=np.float32))
        im["rp_in"] = rpj
        ev = 1.0 if half == 0 else 0.0
        im["evenm_in"] = np.full((128, 1), ev, np.float32)
        im["oddm_in"] = np.full((128, 1), 1.0 - ev, np.float32)
        in_maps.append(im)

    res = run_bass_kernel_spmd(nc, in_maps, core_ids=list(range(NCORES)))
    out = np.empty((4 * 8192, 1024), np.float32)
    for cid in range(NCORES):
        out[cid * T:(cid + 1) * T] = np.asarray(res.results[cid]["y"], np.float32)
    return out.reshape(4, 8192, 1024)


# version 11
# speedup vs baseline: 1.3049x; 1.0630x over previous
import numpy as np
import ml_dtypes
from contextlib import ExitStack

import concourse.bass as bass
import concourse.mybir as mybir
from concourse.alu_op_type import AluOpType
from concourse.bass_utils import run_bass_kernel_spmd

F32 = mybir.dt.float32
BF16 = mybir.dt.bfloat16
F16 = mybir.dt.float16
AF = mybir.ActivationFunctionType
OP = AluOpType
AX = mybir.AxisListType

NCORES = 8
T = 4096
TILE = 512
NT = T // TILE
NG = TILE // 128
E = 1024
H = 8
EPS = 1e-5
INV_E = 1.0 / 1024.0
BF = ml_dtypes.bfloat16


class Ctr:
    def __init__(self, sem):
        self.sem = sem
        self.n = 0

    def inc(self, instr, k=1):
        instr.then_inc(self.sem, k)
        self.n += k

    def wait(self, eng, n=None):
        eng.wait_ge(self.sem, self.n if n is None else n)

    def selfw(self, eng):
        eng.wait_ge(self.sem, self.n)


def build_nc(debug_taps=(), phases=("A", "AR1", "B1", "B2", "AR2", "C"), a_tiles=NT, a_stages=9):
    nc = bass.Bass("TRN2", target_bir_lowering=False)
    din = {}

    def dr(name, shape, dt, kind=None):
        t = nc.dram_tensor(name, shape, dt, kind=kind) if kind else nc.dram_tensor(name, shape, dt)
        din[name] = t
        return t

    x_d = dr("x", [T, E], F32, "ExternalInput")
    y_d = dr("y", [T, E], F32, "ExternalOutput")
    for nm in ("wvT", "wqT", "wlgT", "woutT", "wQT", "wKT", "winvT", "rw1T", "wOT"):
        dr(nm, [E, E], F32, "ExternalInput")
    dr("wg1T", [E, 128], F32, "ExternalInput")
    dr("wg2T", [128, E], F32, "ExternalInput")
    dr("lnwb", [128, E], F32, "ExternalInput")
    for nm in ("vfwdF", "wfwdF", "winvexF", "vinvF"):
        dr(nm, [E, 512], F32, "ExternalInput")
    dr("rw2T", [E, 8], F32, "ExternalInput")
    dr("rb1c", [128, 8], F32, "ExternalInput")
    dr("b2pe_in", [128, 16], F32, "ExternalInput")
    dr("alpha_in", [128, 1], F32, "ExternalInput")
    dr("UfF", [512, E], F32, "ExternalInput")
    dr("UiF", [512, E], F32, "ExternalInput")
    dr("bfw", [8, E], F32, "ExternalInput")
    dr("biv", [8, E], F32, "ExternalInput")
    dr("ident_in", [128, 128], BF16, "ExternalInput")
    dr("maskP_in", [128, 128], F32, "ExternalInput")
    dr("Econst_in", [8, 512], BF16, "ExternalInput")
    dr("rp_in", [128, 32], F32, "ExternalInput")
    dr("evenm_in", [128, 1], F32, "ExternalInput")
    dr("oddm_in", [128, 1], F32, "ExternalInput")

    dbg = {}
    for nm in debug_taps:
        if nm == "S_dbg":
            dbg[nm] = dr(nm, [128, 1024], F32, "ExternalOutput")

    o_sp = dr("o_sp", [T, E], F16)
    qh_sp = dr("qh_sp", [H, 128, T], BF16)
    x1_sp = dr("x1_sp", [T, E], F32)
    xvw_sp = dr("xvw_sp", [2, 4, 128, T], F16)
    cum_sp = dr("cum_sp", [2, 4, 128, T], F16)
    w_sp = dr("w_sp", [2, 8, T], BF16)
    wb16 = {}
    for nm, w in (("wvT", E), ("wqT", E), ("wlgT", E), ("woutT", E), ("wQT", E),
                  ("wKT", E), ("winvT", E), ("rw1T", E), ("wOT", E),
                  ("vfwdF", 512), ("wfwdF", 512), ("winvexF", 512), ("vinvF", 512),
                  ("UfF", None), ("UiF", None)):
        if nm in ("UfF", "UiF"):
            wb16[nm] = dr(nm + "_b", [128, 4 * E], BF16)
        else:
            wb16[nm] = dr(nm + "_b", [128, 8 * w], BF16)
    wb16["wg1T"] = dr("wg1T_b", [128, 8 * 128], BF16)
    wb16["wg2T"] = dr("wg2T_b", [128, E], BF16)
    wb16["rw2T"] = dr("rw2T_b", [128, 64], BF16)
    wb16["bfw"] = dr("bfw_b", [8, E], BF16)
    wb16["biv"] = dr("biv_b", [8, E], BF16)
    cc1_in = dr("cc1_in", [128 * E], F32)
    cc1_out = dr("cc1_out", [128 * E], F32)
    cc2_in = dr("cc2_in", [128 * 8], F32)
    cc2_out = dr("cc2_out", [128 * 8], F32)

    with ExitStack() as ctx:
        Eo = ctx.enter_context
        ident = Eo(nc.sbuf_tensor("ident", [128, 128], BF16))
        maskP = Eo(nc.sbuf_tensor("maskP", [128, 128], F32))
        rp = Eo(nc.sbuf_tensor("rp", [128, 32], F32))
        evenm = Eo(nc.sbuf_tensor("evenm", [128, 1], F32))
        oddm = Eo(nc.sbuf_tensor("oddm", [128, 1], F32))
        S_in_bf = Eo(nc.sbuf_tensor("S_in_bf", [128, H * 128], BF16))
        totF = Eo(nc.sbuf_tensor("totF", [128, 4], F32))
        totI = Eo(nc.sbuf_tensor("totI", [128, 4], F32))
        alphac = Eo(nc.sbuf_tensor("alphac", [128, 1], F32))
        epscol = Eo(nc.sbuf_tensor("epscol", [128, 1], F32))
        b2pe_b = Eo(nc.sbuf_tensor("b2pe_b", [128, 8], F32))

        sems = {nm: Ctr(Eo(nc.semaphore(nm)))
                for nm in ("ld", "gld", "t", "v", "s", "g", "so", "cc")}
        ld, gld, tS, vS, sS, gS, so, ccS = (sems[n] for n in
                                            ("ld", "gld", "t", "v", "s", "g", "so", "cc"))

        def emit_ln_bn(blk, src_of, sm, extra_wait=None):
            def l1(vec):
                if extra_wait is not None:
                    extra_wait(vec)
                vS.selfw(vec)
                for g in range(NG):
                    for h in range(2):
                        vS.inc(vec.bn_stats(sm["st"][:, (2 * g + h) * 6:(2 * g + h) * 6 + 6],
                                            src_of(g, h)))
                vS.selfw(vec)
                for g in range(NG):
                    vS.inc(vec.bn_aggr(sm["mv"][:, 2 * g:2 * g + 2],
                                       sm["st"][:, g * 12:g * 12 + 12]))

            blk.vector(l1)

            def l2(sc):
                vS.wait(sc)
                sS.selfw(sc)
                sS.inc(sc.activation(sm["rcol"][:, NG:2 * NG],
                                     bass.AP(sm["mv"], 1, [[2 * NG, 128], [2, NG]]),
                                     AF.Ln, bias=epscol[:, 0:1]))
                sS.selfw(sc)
                sS.inc(sc.activation(sm["rcol"][:, 0:NG], sm["rcol"][:, NG:2 * NG],
                                     AF.Exp, scale=-0.5))

            blk.scalar(l2)

        def emit_norm_transpose(blk, x_src, xh_tm, xh_fm, psTP, sm, g, xoff=0):
            def n1(vec):
                sS.wait(vec)
                tS.wait(vec)
                vS.selfw(vec)
                vS.inc(vec.tensor_scalar(xh_tm[:],
                                         x_src[:, xoff + g * 1024: xoff + (g + 1) * 1024],
                                         sm["mv"][:, 2 * g:2 * g + 1],
                                         sm["rcol"][:, g:g + 1],
                                         OP.subtract, OP.mult))

            blk.vector(n1)

            def n2(te):
                vS.wait(te)
                for ci in range(8):
                    ins = te.transpose(psTP[:, ci * 128:(ci + 1) * 128],
                                       xh_tm[:, ci * 128:(ci + 1) * 128], ident[:])
                tS.inc(ins)

            blk.tensor(n2)

            def n3(vec):
                tS.wait(vec)
                vS.selfw(vec)
                vS.inc(vec.tensor_copy(
                    bass.AP(xh_fm, g * 128, [[4096, 128], [512, 8], [1, 128]]),
                    bass.AP(psTP, 0, [[1024, 128], [128, 8], [1, 128]])))

            blk.vector(n3)

        b2tmp = Eo(nc.sbuf_tensor("b2tmp", [128, 16], F32))
        blk0_cm = nc.Block()
        blk0 = blk0_cm.__enter__()

        def p0_sync(sync):
            for dst, nm in ((ident, "ident_in"), (maskP, "maskP_in"), (rp, "rp_in"),
                            (evenm, "evenm_in"), (oddm, "oddm_in")):
                ld.selfw(sync)
                ld.inc(sync.dma_start(out=dst[:], in_=din[nm][:, :]), 16)
            ld.selfw(sync)
            ld.inc(sync.dma_start(out=b2tmp[:], in_=din["b2pe_in"][:, :]), 16)
            ld.selfw(sync)
            ld.inc(sync.dma_start(out=alphac[:], in_=din["alpha_in"][:, :]), 16)

        blk0.sync(p0_sync)

        def p0_g(gp):
            ld.wait(gp)
            gS.selfw(gp)
            gS.inc(gp.memset(epscol[:], EPS))
            gS.selfw(gp)
            gS.inc(gp.tensor_add(b2pe_b[:], b2tmp[:, 0:8], b2tmp[:, 8:16]))

        blk0.gpsimd(p0_g)

        with nc.sbuf_tensor("wstg32", [128, 2 * 8192], F32) as wstg32, \
             nc.sbuf_tensor("wstg16", [128, 2 * 8192], BF16) as wstg16:
            plist = []
            for nm, w in (("wvT", E), ("wqT", E), ("wlgT", E), ("wg1T", 128),
                          ("woutT", E), ("wQT", E), ("wKT", E), ("winvT", E),
                          ("rw1T", E), ("wOT", E), ("vfwdF", 512), ("wfwdF", 512),
                          ("winvexF", 512), ("vinvF", 512)):
                plist.append((nm, [[w, 128], [128 * w, 8], [1, w]], 8 * w))
            plist.append(("wg2T", [[E, 128], [1, E]], E))
            plist.append(("rw2T", [[8, 128], [128 * 8, 8], [1, 8]], 64))
            plist.append(("UfF", [[E, 128], [128 * E, 4], [1, E]], 4 * E))
            plist.append(("UiF", [[E, 128], [128 * E, 4], [1, E]], 4 * E))

            wls = [Ctr(Eo(nc.semaphore("wl0"))), Ctr(Eo(nc.semaphore("wl1")))]
            wss = [Ctr(Eo(nc.semaphore("ws0"))), Ctr(Eo(nc.semaphore("ws1")))]
            cast_done = []
            for i, (nm, ap, width) in enumerate(plist):
                s = i % 2
                buf = s * 8192

                def w_ld(sync, nm=nm, ap=ap, width=width, buf=buf, i=i, s=s):
                    if i >= 2:
                        vS.wait(sync, cast_done[i - 2])
                    wls[s].inc(sync.dma_start(
                        out=bass.AP(wstg32, buf, [[2 * 8192, 128], [1, width]]),
                        in_=bass.AP(din[nm], 0, ap)), 16)

                blk0.sync(w_ld)
                wl_n = wls[s].n
                ws_n = wss[s].n

                def w_cast(vec, nm=nm, width=width, buf=buf, i=i, s=s,
                           wl_n=wl_n, ws_n=ws_n):
                    vec.wait_ge(wls[s].sem, wl_n)
                    if i >= 2:
                        vec.wait_ge(wss[s].sem, ws_n)
                    vS.selfw(vec)
                    vS.inc(vec.tensor_copy(
                        bass.AP(wstg16, buf, [[2 * 8192, 128], [1, width]]),
                        bass.AP(wstg32, buf, [[2 * 8192, 128], [1, width]])))
                    cast_done.append(vS.n)

                blk0.vector(w_cast)

                def w_st(sc, nm=nm, width=width, buf=buf, i=i, s=s):
                    vS.wait(sc, cast_done[i])
                    wss[s].inc(sc.dma_start(
                        out=bass.AP(wb16[nm], 0, [[width, 128], [1, width]]),
                        in_=bass.AP(wstg16, buf, [[2 * 8192, 128], [1, width]])), 16)

                blk0.scalar(w_st)
            wst_final = (wss[0].n, wss[1].n)

            with nc.sbuf_tensor("bstg32", [8, 2 * E], F32) as bstg32, \
                 nc.sbuf_tensor("bstg16", [8, 2 * E], BF16) as bstg16:
                def b_ld(sync):
                    ld.selfw(sync)
                    ld.inc(sync.dma_start(
                        out=bass.AP(bstg32, 0, [[2 * E, 8], [1, E]]),
                        in_=din["bfw"][:, :]), 16)
                    ld.selfw(sync)
                    ld.inc(sync.dma_start(
                        out=bass.AP(bstg32, E, [[2 * E, 8], [1, E]]),
                        in_=din["biv"][:, :]), 16)

                blk0.sync(b_ld)

                def b_cast(vec):
                    ld.wait(vec)
                    vS.selfw(vec)
                    vS.inc(vec.tensor_copy(bstg16[:, :], bstg32[:, :]))

                blk0.vector(b_cast)

                def b_st(sc):
                    vS.wait(sc)
                    sS.selfw(sc)
                    so.selfw(sc)
                    so.inc(sc.dma_start(out=bass.AP(wb16["bfw"], 0, [[E, 8], [1, E]]),
                                        in_=bass.AP(bstg16, 0, [[2 * E, 8], [1, E]])), 16)
                    so.selfw(sc)
                    so.inc(sc.dma_start(out=bass.AP(wb16["biv"], 0, [[E, 8], [1, E]]),
                                        in_=bass.AP(bstg16, E, [[2 * E, 8], [1, E]])), 16)
                    so.wait(sc)

                blk0.scalar(b_st)
        blk0_cm.__exit__(None, None, None)

        if "A" not in phases:
            return nc
        with ExitStack() as actx:
            Ea = actx.enter_context
            wv_sb = Ea(nc.sbuf_tensor("wv_sb", [128, 8 * E], BF16))
            wq_sb = Ea(nc.sbuf_tensor("wq_sb", [128, 8 * E], BF16))
            wlg_sb = Ea(nc.sbuf_tensor("wlg_sb", [128, 8 * E], BF16))
            x_sb = Ea(nc.sbuf_tensor("xA", [128, 4096], F32))
            xh_tm = Ea(nc.sbuf_tensor("xhA_tm", [128, 1024], BF16))
            xh_fm = Ea(nc.sbuf_tensor("xhA_fm", [128, 4096], BF16))
            v_tm = Ea(nc.sbuf_tensor("v_tm", [128, 4096], BF16))
            qpre = Ea(nc.sbuf_tensor("qpre", [128, 4096], F16))
            f_sb = Ea(nc.sbuf_tensor("f_sb", [128, 4096], F32))
            kf = Ea(nc.sbuf_tensor("kf", [128, 4096], F16))
            Acum = Ea(nc.sbuf_tensor("Acum", [128, 4096], F32))
            carryA = Ea(nc.sbuf_tensor("carryA", [128, 8], F32))
            ends = Ea(nc.sbuf_tensor("ends", [128, 64], F32))
            refs = Ea(nc.sbuf_tensor("refs", [128, 64], F32))
            eACall = Ea(nc.sbuf_tensor("eACall", [128, 64], F32))
            qt_bf = Ea(nc.sbuf_tensor("qt_bf", [128, 4096], BF16))
            kt_bf = Ea(nc.sbuf_tensor("kt_bf", [128, 4096], BF16))
            kh_full = Ea(nc.sbuf_tensor("kh_full", [128, 4096], BF16))
            khtm = Ea(nc.sbuf_tensor("khtm", [128, 4096], BF16))
            qhat_bf = Ea(nc.sbuf_tensor("qhat_bf", [128, 4096], BF16))
            psb_buf = Ea(nc.sbuf_tensor("psb_buf", [128, 1024], BF16))
            S_sb = Ea(nc.sbuf_tensor("S_sb", [128, 1024], F32))
            S_bf = Ea(nc.sbuf_tensor("S_bf", [128, 1024], BF16))
            S_bf2 = Ea(nc.sbuf_tensor("S_bf2", [128, 1024], BF16))
            o16 = Ea(nc.sbuf_tensor("o16", [128, 4096], F16))
            sm = {"st": Ea(nc.sbuf_tensor("smA_st", [128, 48], F32)),
                  "mv": Ea(nc.sbuf_tensor("smA_mv", [128, 8], F32)),
                  "rcol": Ea(nc.sbuf_tensor("smA_rcol", [128, 8], F32))}
            psTP = Ea(nc.psum_tensor("psTPa", [128, 1024], BF16))
            psProd = Ea(nc.psum_tensor("psProdA", [128, 2048], F32))
            psS = Ea(nc.psum_tensor("psSa", [128, 1024], F32))
            blkA = Ea(nc.Block())

            def pa_init(gp):
                gS.selfw(gp)
                gS.inc(gp.memset(S_sb[:], 0.0))
                gS.selfw(gp)
                gS.inc(gp.memset(S_bf[:], 0.0))
                gS.selfw(gp)
                gS.inc(gp.memset(carryA[:], 0.0))

            blkA.gpsimd(pa_init)

            def pa_wld(sync):
                so.wait(sync)
                sync.wait_ge(wss[0].sem, wst_final[0])
                sync.wait_ge(wss[1].sem, wst_final[1])
                for nm, dst in (("wvT", wv_sb), ("wqT", wq_sb), ("wlgT", wlg_sb)):
                    ld.selfw(sync)
                    ld.inc(sync.dma_start(
                        out=bass.AP(dst, 0, [[8 * E, 128], [1, 8 * E]]),
                        in_=wb16[nm][:, :]), 16)

            blkA.sync(pa_wld)

            prevA = {"vb_rel": 0, "sb_e0": None, "vb_a4m_end": 0}
            for j in range(a_tiles):
                def a1(sync, j=j, pv=dict(prevA)):
                    if pv["sb_e0"] is not None:
                        sync.wait_ge(vS.sem, pv["vb_rel"])
                        sync.wait_ge(sS.sem, pv["sb_e0"] + 4)
                    else:
                        vS.wait(sync)
                        sS.wait(sync)
                        tS.wait(sync)
                        so.wait(sync)
                    ld.selfw(sync)
                    ld.inc(sync.dma_start(
                        out=bass.AP(x_sb, 0, [[4096, 128], [1024, 4], [1, 1024]]),
                        in_=bass.AP(x_d, j * TILE * E, [[E, 128], [128 * E, 4], [1, E]]),
                    ), 16)

                blkA.sync(a1)

                ldw = (ld.sem, ld.n)
                gw = (ld.sem, ld.n)

                emit_ln_bn(blkA, lambda g, h: x_sb[:, g * 1024 + h * 512: g * 1024 + h * 512 + 512],
                           sm, extra_wait=(lambda vec, w=ldw: vec.wait_ge(*w)))
                for g in range(NG):
                    emit_norm_transpose(blkA, x_sb, xh_tm, xh_fm, psTP, sm, g)

                if a_stages < 3:
                    continue
                vb_ve = vS.n

                def a3v(te, gw=gw):
                    te.wait_ge(*gw)
                    vS.wait(te)
                    sS.wait(te)
                    for g in range(NG):
                        if g >= 1:
                            te.wait_ge(vS.sem, vb_ve + g)
                        for half in range(2):
                            for fc in range(8):
                                ins = te.matmul(
                                    psProd[:, half * 512:(half + 1) * 512],
                                    xh_fm[:, fc * 512 + g * 128: fc * 512 + (g + 1) * 128],
                                    wv_sb[:, fc * E + half * 512: fc * E + (half + 1) * 512],
                                    start=(fc == 0), stop=(fc == 7))
                        tS.inc(ins)

                tb_v = tS.n
                blkA.tensor(a3v)

                def a3ve(vec):
                    for g in range(NG):
                        tS.wait(vec, tb_v + g + 1)
                        vS.selfw(vec)
                        vS.inc(vec.tensor_copy(v_tm[:, g * 1024:(g + 1) * 1024],
                                               psProd[:, 0:1024]))

                blkA.vector(a3ve)
                assert vS.n == vb_ve + NG

                sb_q0 = sS.n

                def a3q(te, sb_q0=sb_q0):
                    vS.wait(te)
                    sS.wait(te)
                    for cc in range(8):
                        reg = 1024 + (cc % 2) * 512
                        if cc >= 2:
                            te.wait_ge(sS.sem, sb_q0 + cc - 1)
                        for fc in range(8):
                            ins = te.matmul(psProd[:, reg:reg + 512],
                                            wq_sb[:, fc * E + cc * 128: fc * E + (cc + 1) * 128],
                                            xh_fm[:, fc * 512:(fc + 1) * 512],
                                            start=(fc == 0), stop=(fc == 7))
                        tS.inc(ins)

                tb_q = tS.n
                blkA.tensor(a3q)

                def a3qs(sc, tb_q=tb_q, pv=dict(prevA)):
                    sc.wait_ge(vS.sem, pv["vb_a4m_end"])
                    for cc in range(8):
                        tS.wait(sc, tb_q + cc + 1)
                        reg = 1024 + (cc % 2) * 512
                        sS.selfw(sc)
                        sS.inc(sc.activation(qpre[:, cc * 512:(cc + 1) * 512],
                                             psProd[:, reg:reg + 512], AF.Silu))

                blkA.scalar(a3qs)
                assert sS.n == sb_q0 + 8

                sb_l0 = sS.n

                def a3l(te, sb_l0=sb_l0):
                    for lc in range(8):
                        reg = 1024 + (lc % 2) * 512
                        if lc < 2:
                            te.wait_ge(sS.sem, sb_q0 + 7 + lc)
                        else:
                            te.wait_ge(sS.sem, sb_l0 + 2 * (lc - 2) + 2)
                        for fc in range(8):
                            ins = te.matmul(psProd[:, reg:reg + 512],
                                            wlg_sb[:, fc * E + lc * 128: fc * E + (lc + 1) * 128],
                                            xh_fm[:, fc * 512:(fc + 1) * 512],
                                            start=(fc == 0), stop=(fc == 7))
                        tS.inc(ins)

                tb_l = tS.n
                blkA.tensor(a3l)

                def a3ls(sc, tb_l=tb_l, pv=dict(prevA)):
                    sc.wait_ge(vS.sem, pv["vb_a4m_end"])
                    for lc in range(8):
                        tS.wait(sc, tb_l + lc + 1)
                        reg = 1024 + (lc % 2) * 512
                        sS.selfw(sc)
                        sS.inc(sc.activation(f_sb[:, lc * 512:(lc + 1) * 512],
                                             psProd[:, reg:reg + 512], AF.Sigmoid))
                        sS.selfw(sc)
                        sS.inc(sc.activation(kf[:, lc * 512:(lc + 1) * 512],
                                             psProd[:, reg:reg + 512], AF.Sigmoid,
                                             scale=-1.0))

                blkA.scalar(a3ls)

                if a_stages < 4:
                    continue
                def a3g(sc):
                    sS.selfw(sc)
                    sS.inc(sc.activation(f_sb[:], f_sb[:], AF.Ln))

                blkA.scalar(a3g)

                def a4v(vec):
                    gS.wait(vec)
                    sS.wait(vec)
                    for h in range(H):
                        vS.selfw(vec)
                        vS.inc(vec.tensor_tensor_scan(
                            Acum[:, h * 512:(h + 1) * 512],
                            f_sb[:, h * 512:(h + 1) * 512],
                            f_sb[:, h * 512:(h + 1) * 512],
                            carryA[:, h:h + 1], OP.add, OP.bypass))

                blkA.vector(a4v)

                def a4r(vec):
                    vS.selfw(vec)
                    vS.inc(vec.tensor_copy(ends[:],
                                           bass.AP(Acum, 63, [[4096, 128], [64, 64]])))
                    vS.selfw(vec)
                    vS.inc(vec.tensor_copy(bass.AP(refs, 0, [[64, 128], [8, 8]]),
                                           carryA[:, 0:8]))
                    vS.selfw(vec)
                    vS.inc(vec.tensor_copy(bass.AP(refs, 1, [[64, 128], [8, 8], [1, 7]]),
                                           bass.AP(ends, 0, [[64, 128], [8, 8], [1, 7]])))
                    vS.selfw(vec)
                    gS.inc(vec.tensor_copy(carryA[:, 0:8],
                                           bass.AP(ends, 7, [[64, 128], [8, 8]])))
                    vS.selfw(vec)
                    vS.inc(vec.tensor_tensor(
                        bass.AP(f_sb, 0, [[4096, 128], [64, 64], [1, 64]]),
                        bass.AP(Acum, 0, [[4096, 128], [64, 64], [1, 64]]),
                        bass.AP(ends, 0, [[64, 128], [1, 64], [0, 64]]), OP.subtract))
                    vS.selfw(vec)
                    vS.inc(vec.tensor_tensor(
                        bass.AP(x_sb, 0, [[4096, 128], [64, 64], [1, 64]]),
                        bass.AP(Acum, 0, [[4096, 128], [64, 64], [1, 64]]),
                        bass.AP(refs, 0, [[64, 128], [1, 64], [0, 64]]), OP.subtract))

                blkA.vector(a4r)
                vb_rel = vS.n
                vb_scrA2n = vS.n - 1

                sb_e0 = sS.n

                def a4e(sc, vb_rel=vb_rel, vb_scrA2n=vb_scrA2n):
                    so.wait(sc)
                    vS.wait(sc, vb_scrA2n)
                    sS.selfw(sc)
                    sS.inc(sc.activation(kh_full[:],
                                         bass.AP(f_sb, 0, [[4096, 128], [1, 4096]]),
                                         AF.Exp, scale=-1.0))
                    vS.wait(sc, vb_rel)
                    sS.selfw(sc)
                    sS.inc(sc.activation(eACall[:],
                                         bass.AP(x_sb, 63, [[4096, 128], [64, 64]]),
                                         AF.Exp))
                    sS.selfw(sc)
                    sS.inc(sc.activation(qt_bf[:],
                                         bass.AP(x_sb, 0, [[4096, 128], [1, 4096]]),
                                         AF.Exp))
                    sS.selfw(sc)
                    sS.inc(sc.activation(kt_bf[:],
                                         bass.AP(x_sb, 0, [[4096, 128], [1, 4096]]),
                                         AF.Exp, scale=-1.0))
                    sS.selfw(sc)
                    sS.inc(sc.activation(qhat_bf[:], Acum[:], AF.Exp))

                blkA.scalar(a4e)

                def a4m(vec, sb_e0=sb_e0):
                    so.wait(vec)
                    vec.wait_ge(sS.sem, sb_e0 + 1)
                    vS.selfw(vec)
                    vS.inc(vec.tensor_mul(kh_full[:], kh_full[:], kf[:]))
                    vec.wait_ge(sS.sem, sb_e0 + 3)
                    vS.selfw(vec)
                    vS.inc(vec.tensor_mul(qt_bf[:], qt_bf[:], qpre[:]))
                    vec.wait_ge(sS.sem, sb_e0 + 4)
                    vS.selfw(vec)
                    vS.inc(vec.tensor_mul(kt_bf[:], kt_bf[:], kf[:]))
                    vec.wait_ge(sS.sem, sb_e0 + 5)
                    vS.selfw(vec)
                    vS.inc(vec.tensor_mul(qhat_bf[:], qhat_bf[:], qpre[:]))

                blkA.vector(a4m)
                vb_kh = vS.n - 3
                vb_a4m_end = vS.n
                prevA["vb_rel"] = vb_rel
                prevA["sb_e0"] = sb_e0
                prevA["vb_a4m_end"] = vb_a4m_end

                for tb in range(4):
                    def a4tp(te, tb=tb, vb_kh=vb_kh):
                        te.wait_ge(vS.sem, vb_kh)
                        if tb >= 1:
                            vS.wait(te)
                        for bi in range(8):
                            h = tb * 2 + bi // 4
                            g_ = bi % 4
                            ins = te.transpose(
                                psTP[:, bi * 128:(bi + 1) * 128],
                                kh_full[:, h * 512 + g_ * 128: h * 512 + (g_ + 1) * 128],
                                ident[:])
                        tS.inc(ins)

                    blkA.tensor(a4tp)

                    def a4te(vec, tb=tb):
                        tS.wait(vec)
                        vS.selfw(vec)
                        vS.inc(vec.tensor_copy(khtm[:, tb * 1024:(tb + 1) * 1024],
                                               psTP[:, 0:1024]))

                    blkA.vector(a4te)

                if a_stages < 5:
                    continue
                for g in range(NG):
                    cA = 2 * g
                    cB = 2 * g + 1
                    vb_pre = vS.n

                    def a5suA(te, g=g, vb_pre=vb_pre):
                        te.wait_ge(vS.sem, vb_pre)
                        for h in range(H):
                            ins = te.matmul(
                                psProd[:, 1024 + h * 128: 1024 + (h + 1) * 128],
                                khtm[0:64, h * 512 + g * 128: h * 512 + (g + 1) * 128],
                                v_tm[0:64, g * 1024 + h * 128: g * 1024 + (h + 1) * 128],
                                start=True, stop=True)
                            tS.inc(ins)

                    blkA.tensor(a5suA)
                    tb_suA = tS.n

                    def a5sc(te, g=g):
                        for h in range(H):
                            ins = te.matmul(
                                psProd[:, h * 128:(h + 1) * 128],
                                kt_bf[:, h * 512 + g * 128: h * 512 + (g + 1) * 128],
                                qt_bf[:, h * 512 + g * 128: h * 512 + (g + 1) * 128],
                                start=True, stop=True)
                            tS.inc(ins)

                    blkA.tensor(a5sc)
                    tb_sc = tS.n

                    def a5suB(te, g=g):
                        for h in range(H):
                            ins = te.matmul(
                                psS[:, h * 128:(h + 1) * 128],
                                khtm[64:128, h * 512 + g * 128: h * 512 + (g + 1) * 128],
                                v_tm[64:128, g * 1024 + h * 128: g * 1024 + (h + 1) * 128],
                                start=True, stop=True)
                            tS.inc(ins)

                    blkA.tensor(a5suB)
                    tb_suB = tS.n

                    def a5v1(vec, cA=cA, tb_suA=tb_suA):
                        tS.wait(vec, tb_suA)
                        gS.wait(vec)
                        vS.selfw(vec)
                        vS.inc(vec.tensor_tensor(
                            S_sb[:], S_sb[:],
                            bass.AP(eACall, cA, [[64, 128], [8, 8], [0, 128]]),
                            OP.mult))
                        vS.selfw(vec)
                        vS.inc(vec.tensor_add(S_sb[:], S_sb[:], psProd[:, 1024:2048]))

                    blkA.vector(a5v1)
                    vb_stt1 = vS.n

                    def a5vm(vec, tb_sc=tb_sc):
                        tS.wait(vec, tb_sc)
                        vS.selfw(vec)
                        vS.inc(vec.tensor_tensor(
                            psb_buf[:], psProd[:, 0:1024],
                            bass.AP(maskP, 0, [[128, 128], [0, 8], [1, 128]]),
                            OP.mult))

                    blkA.vector(a5vm)
                    vb_mask = vS.n

                    def a5g1(gp, vb_stt1=vb_stt1):
                        gp.wait_ge(vS.sem, vb_stt1)
                        gS.selfw(gp)
                        gS.inc(gp.tensor_copy(S_bf2[:], S_sb[:]))

                    blkA.gpsimd(a5g1)
                    gb_c1 = gS.n

                    def a5o1(te, g=g, vb_mask=vb_mask):
                        te.wait_ge(vS.sem, vb_mask)
                        gS.wait(te)
                        for h in range(H):
                            te.matmul(psProd[:, h * 128:(h + 1) * 128],
                                      psb_buf[:, h * 128:(h + 1) * 128],
                                      v_tm[:, g * 1024 + h * 128: g * 1024 + (h + 1) * 128],
                                      start=(h == 0) or (h == 4), stop=False,
                                      skip_group_check=True)
                            ins = te.matmul(
                                psProd[0:64, h * 128:(h + 1) * 128],
                                qt_bf[:, h * 512 + g * 128: h * 512 + g * 128 + 64],
                                S_bf[:, h * 128:(h + 1) * 128],
                                start=False, stop=False, skip_group_check=True)
                            tS.inc(ins)

                    blkA.tensor(a5o1)
                    tb_o1 = tS.n

                    def a5v2(vec, cB=cB, tb_suB=tb_suB):
                        tS.wait(vec, tb_suB)
                        gS.wait(vec)
                        vS.selfw(vec)
                        vS.inc(vec.tensor_tensor(
                            S_sb[:], S_sb[:],
                            bass.AP(eACall, cB, [[64, 128], [8, 8], [0, 128]]),
                            OP.mult))
                        vS.selfw(vec)
                        vS.inc(vec.tensor_add(S_sb[:], S_sb[:], psS[:, 0:1024]))

                    blkA.vector(a5v2)
                    vb_stt2 = vS.n

                    def a5g2(gp, vb_stt2=vb_stt2, tb_o1=tb_o1):
                        gp.wait_ge(vS.sem, vb_stt2)
                        gp.wait_ge(tS.sem, tb_o1)
                        gS.selfw(gp)
                        gS.inc(gp.tensor_copy(S_bf[:], S_sb[:]))

                    blkA.gpsimd(a5g2)
                    gb_c2 = gS.n

                    def a5o2(te, g=g, gb_c1=gb_c1):
                        te.wait_ge(gS.sem, gb_c1)
                        for h in range(H):
                            ins = te.matmul(
                                psProd[64:128, h * 128:(h + 1) * 128],
                                qt_bf[:, h * 512 + g * 128 + 64: h * 512 + (g + 1) * 128],
                                S_bf2[:, h * 128:(h + 1) * 128],
                                start=False, stop=(h == 3) or (h == 7),
                                skip_group_check=True)
                            tS.inc(ins)

                    blkA.tensor(a5o2)

                    def a5oe(vec, g=g):
                        tS.wait(vec)
                        vS.selfw(vec)
                        vS.inc(vec.tensor_copy(o16[:, g * 1024:(g + 1) * 1024],
                                               psProd[:, 0:1024]))

                    blkA.vector(a5oe)

                if a_stages < 6:
                    continue

                def a6(sc, j=j):
                    vS.wait(sc)
                    sS.selfw(sc)
                    so.selfw(sc)
                    so.inc(sc.dma_start(
                        out=bass.AP(o_sp, j * TILE * E, [[E, 128], [128 * E, 4], [1, E]]),
                        in_=bass.AP(o16, 0, [[4096, 128], [1024, 4], [1, 1024]])), 16)
                    sS.selfw(sc)
                    so.selfw(sc)
                    so.inc(sc.dma_start(
                        out=bass.AP(qh_sp, j * TILE, [[T, 128], [128 * T, 8], [1, 512]]),
                        in_=bass.AP(qhat_bf, 0, [[4096, 128], [512, 8], [1, 512]])), 16)

                blkA.scalar(a6)

            def ae_v(vec):
                gS.wait(vec)
                vS.selfw(vec)
                vS.inc(vec.tensor_scalar(S_sb[:], S_sb[:], evenm[:, 0:1], None,
                                         OP.mult, OP.bypass))

            blkA.vector(ae_v)

            if "S_dbg" in dbg:
                def ae_d(sc):
                    vS.wait(sc)
                    sS.selfw(sc)
                    so.selfw(sc)
                    so.inc(sc.dma_start(out=dbg["S_dbg"][:, :],
                                        in_=bass.AP(S_sb, 0, [[1024, 128], [1, 1024]])), 16)
                blkA.scalar(ae_d)

            def ae_g(gp):
                vS.wait(gp)
                so.wait(gp)
                gld.selfw(gp)
                gld.inc(gp.dma_start(out=cc1_in.ap(),
                                     in_=bass.AP(S_sb, 0, [[1024, 128], [1, 1024]])), 16)

            blkA.gpsimd(ae_g)

        if "AR1" not in phases:
            return nc
        sin_f32 = Eo(nc.sbuf_tensor("sin_f32", [128, 1024], F32))
        blkC1_cm = nc.Block()
        blkC1 = blkC1_cm.__enter__()

        def cc1f(gp):
            gld.wait(gp)
            ccS.inc(gp.collective_compute(
                "AllReduce", OP.add,
                replica_groups=[[0, 1], [2, 3], [4, 5], [6, 7]],
                ins=[cc1_in.ap().opt()], outs=[cc1_out.ap().opt()]))
            ccS.wait(gp)
            gld.selfw(gp)
            gld.inc(gp.dma_start(out=sin_f32[:],
                                 in_=bass.AP(cc1_out, 0, [[1024, 128], [1, 1024]])), 16)

        blkC1.gpsimd(cc1f)

        def cc1v(vec):
            gld.wait(vec)
            vS.selfw(vec)
            vS.inc(vec.tensor_scalar(sin_f32[:], sin_f32[:], oddm[:, 0:1], None,
                                     OP.mult, OP.bypass))
            vS.selfw(vec)
            gS.inc(vec.tensor_copy(S_in_bf[:], sin_f32[:]))

        blkC1.vector(cc1v)
        blkC1_cm.__exit__(None, None, None)

        if "B1" not in phases:
            return nc
        with ExitStack() as bctx:
            Eb = bctx.enter_context
            wg1_sb = Eb(nc.sbuf_tensor("wg1_sb", [128, 8 * 128], BF16))
            wg2_sb = Eb(nc.sbuf_tensor("wg2_sb", [128, E], BF16))
            wout_sb = Eb(nc.sbuf_tensor("wout_sb", [128, 8 * E], BF16))
            lnwb_sb = Eb(nc.sbuf_tensor("lnwb_sb", [128, E], F32))
            x_sb = Eb(nc.sbuf_tensor("xB", [128, 2 * 4096], F32))
            xh_tm = Eb(nc.sbuf_tensor("xhB_tm", [128, 1024], BF16))
            xh_fm = Eb(nc.sbuf_tensor("xhB_fm", [128, 4096], BF16))
            y1_bf = Eb(nc.sbuf_tensor("y1_bf", [128, 512], BF16))
            gate_t = Eb(nc.sbuf_tensor("gate_t", [128, 4096], F32))
            o16t = Eb(nc.sbuf_tensor("o16t", [128, 4096], F16))
            qht = Eb(nc.sbuf_tensor("qht", [128, 4096], BF16))
            ofull = Eb(nc.sbuf_tensor("ofull", [128, 1024], F32))
            og_t = Eb(nc.sbuf_tensor("og_t", [128, 4096], F32))
            araw_t = Eb(nc.sbuf_tensor("araw_t", [128, 4096], F32))
            ohat_scr = Eb(nc.sbuf_tensor("ohat_scr", [128, 1024], F32))
            ohat_bf = Eb(nc.sbuf_tensor("ohat_bf", [128, 1024], BF16))
            ohat_fm = Eb(nc.sbuf_tensor("ohat_fm", [128, 4096], BF16))
            x1g = Eb(nc.sbuf_tensor("x1g", [128, 1024], F32))
            t2scr = Eb(nc.sbuf_tensor("t2B", [128, 1024], F32))
            sm = {"st": Eb(nc.sbuf_tensor("smB_st", [128, 48], F32)),
                  "mv": Eb(nc.sbuf_tensor("smB_mv", [128, 8], F32)),
                  "rcol": Eb(nc.sbuf_tensor("smB_rcol", [128, 8], F32))}
            psTP = Eb(nc.psum_tensor("psTPb", [128, 1024], BF16))
            psM = Eb(nc.psum_tensor("psMb", [128, 2048], F32))
            psX = Eb(nc.psum_tensor("psXb", [128, 1024], F32))
            blkB = Eb(nc.Block())

            def b0(sync):
                so.wait(sync)
                sync.wait_ge(wss[0].sem, wst_final[0])
                sync.wait_ge(wss[1].sem, wst_final[1])
                ld.selfw(sync)
                ld.inc(sync.dma_start(
                    out=bass.AP(wg1_sb, 0, [[8 * 128, 128], [1, 8 * 128]]),
                    in_=wb16["wg1T"][:, :]), 16)
                ld.selfw(sync)
                ld.inc(sync.dma_start(
                    out=bass.AP(wg2_sb, 0, [[E, 128], [1, E]]),
                    in_=wb16["wg2T"][:, :]), 16)
                ld.selfw(sync)
                ld.inc(sync.dma_start(
                    out=bass.AP(wout_sb, 0, [[8 * E, 128], [1, 8 * E]]),
                    in_=wb16["woutT"][:, :]), 16)

            blkB.sync(b0)

            def b0s(sync):
                ld.selfw(sync)
                ld.inc(sync.dma_start(out=lnwb_sb[:], in_=din["lnwb"][:, :]), 16)

            blkB.sync(b0s)

            vb_b8_end = []
            vb_b5_end = []
            for j in range(NT):
                xo = (j % 2) * 4096

                def b1(sync, j=j):
                    if j == 0:
                        tS.wait(sync)
                        vS.wait(sync)
                        sS.wait(sync)
                        so.wait(sync)
                    elif j >= 2:
                        sync.wait_ge(vS.sem, vb_b8_end[j - 2])
                    ld.selfw(sync)
                    ld.inc(sync.dma_start(
                        out=bass.AP(x_sb, xo, [[2 * 4096, 128], [1024, 4], [1, 1024]]),
                        in_=bass.AP(x_d, j * TILE * E, [[E, 128], [128 * E, 4], [1, E]]),
                    ), 16)

                blkB.sync(b1)
                ldw = (ld.sem, ld.n)
                gw = (ld.sem, ld.n)

                emit_ln_bn(blkB, lambda g, h: x_sb[:, xo + g * 1024 + h * 512: xo + g * 1024 + h * 512 + 512],
                           sm, extra_wait=(lambda vec, w=ldw: vec.wait_ge(*w)))
                for g in range(NG):
                    emit_norm_transpose(blkB, x_sb, xh_tm, xh_fm, psTP, sm, g, xoff=xo)

                def b3t(te, gw=gw):
                    te.wait_ge(*gw)
                    vS.wait(te)
                    sS.wait(te)
                    for fc in range(8):
                        ins = te.matmul(psM[:, 0:512],
                                        wg1_sb[:, fc * 128:(fc + 1) * 128],
                                        xh_fm[:, fc * 512:(fc + 1) * 512],
                                        start=(fc == 0), stop=(fc == 7))
                    tS.inc(ins)

                blkB.tensor(b3t)

                def b3e(vec):
                    tS.wait(vec)
                    vS.selfw(vec)
                    vS.inc(vec.tensor_copy(y1_bf[:], psM[:, 0:512]))

                blkB.vector(b3e)

                for g in range(NG):
                    def bg_t(te, g=g):
                        vS.wait(te)
                        if g >= 1:
                            sS.wait(te)
                        for half in range(2):
                            ins = te.matmul(psM[:, half * 512:(half + 1) * 512],
                                            y1_bf[:, g * 128:(g + 1) * 128],
                                            wg2_sb[:, half * 512:(half + 1) * 512],
                                            start=True, stop=True)
                        tS.inc(ins)

                    blkB.tensor(bg_t)

                    def bg_s(sc, g=g, j=j):
                        tS.wait(sc)
                        if j >= 1:
                            sc.wait_ge(vS.sem, vb_b5_end[j - 1])
                        sS.selfw(sc)
                        sS.inc(sc.activation(gate_t[:, g * 1024:(g + 1) * 1024],
                                             psM[:, 0:1024], AF.Sigmoid))

                    blkB.scalar(bg_s)

                def b4l(sync, j=j):
                    if j >= 1:
                        sync.wait_ge(vS.sem, vb_b5_end[j - 1])
                    ld.selfw(sync)
                    ld.inc(sync.dma_start(
                        out=bass.AP(o16t, 0, [[4096, 128], [1024, 4], [1, 1024]]),
                        in_=bass.AP(o_sp, j * TILE * E,
                                    [[E, 128], [128 * E, 4], [1, E]])), 16)
                    ld.selfw(sync)
                    ld.inc(sync.dma_start(
                        out=bass.AP(qht, 0, [[4096, 128], [512, 8], [1, 512]]),
                        in_=bass.AP(qh_sp, j * TILE, [[T, 128], [128 * T, 8], [1, 512]])), 16)

                blkB.sync(b4l)

                for g in range(NG):
                    def b4t(te, g=g):
                        ld.wait(te)
                        gS.wait(te)
                        vS.wait(te)
                        sS.wait(te)
                        for h in range(H):
                            ins = te.matmul(psM[:, 1024 + h * 128: 1024 + (h + 1) * 128],
                                            qht[:, h * 512 + g * 128: h * 512 + (g + 1) * 128],
                                            S_in_bf[:, h * 128:(h + 1) * 128],
                                            start=(h == 0) or (h == 4),
                                            stop=(h == 3) or (h == 7))
                        tS.inc(ins)

                    blkB.tensor(b4t)

                    def b5v(vec, g=g):
                        tS.wait(vec)
                        sS.wait(vec)
                        vS.selfw(vec)
                        vS.inc(vec.tensor_add(ofull[:], psM[:, 1024:2048],
                                              o16t[:, g * 1024:(g + 1) * 1024]))
                        vS.selfw(vec)
                        vS.inc(vec.tensor_mul(og_t[:, g * 1024:(g + 1) * 1024], ofull[:],
                                              gate_t[:, g * 1024:(g + 1) * 1024]))

                    blkB.vector(b5v)

                vb_b5_end.append(vS.n)
                emit_ln_bn(blkB, lambda g, h: og_t[:, g * 1024 + h * 512: g * 1024 + h * 512 + 512], sm)

                for g in range(NG):
                    def b6v(vec, g=g):
                        sS.wait(vec)
                        tS.wait(vec)
                        vS.selfw(vec)
                        vS.inc(vec.tensor_scalar(ohat_scr[:],
                                                 og_t[:, g * 1024:(g + 1) * 1024],
                                                 sm["mv"][:, 2 * g:2 * g + 1],
                                                 sm["rcol"][:, g:g + 1],
                                                 OP.subtract, OP.mult))
                        vS.selfw(vec)
                        vS.inc(vec.tensor_mul(ohat_bf[:], ohat_scr[:], lnwb_sb[:]))

                    blkB.vector(b6v)

                    def b6t(te, g=g):
                        vS.wait(te)
                        for ci in range(8):
                            ins = te.transpose(psTP[:, ci * 128:(ci + 1) * 128],
                                               ohat_bf[:, ci * 128:(ci + 1) * 128], ident[:])
                        tS.inc(ins)

                    blkB.tensor(b6t)

                    def b6e(vec, g=g):
                        tS.wait(vec)
                        vS.selfw(vec)
                        vS.inc(vec.tensor_copy(
                            bass.AP(ohat_fm, g * 128, [[4096, 128], [512, 8], [1, 128]]),
                            bass.AP(psTP, 0, [[1024, 128], [128, 8], [1, 128]])))

                    blkB.vector(b6e)

                for g in range(NG):
                    def b7t(te, g=g):
                        vS.wait(te)
                        sS.wait(te)
                        for half in range(2):
                            for ec in range(8):
                                ins = te.matmul(
                                    psX[:, half * 512:(half + 1) * 512],
                                    ohat_fm[:, ec * 512 + g * 128: ec * 512 + (g + 1) * 128],
                                    wout_sb[:, ec * E + half * 512: ec * E + (half + 1) * 512],
                                    start=(ec == 0), stop=(ec == 7))
                        tS.inc(ins)

                    blkB.tensor(b7t)

                    def b7e(vec, g=g):
                        tS.wait(vec)
                        vS.selfw(vec)
                        vS.inc(vec.tensor_copy(araw_t[:, g * 1024:(g + 1) * 1024],
                                               psX[:, 0:1024]))

                    blkB.vector(b7e)

                emit_ln_bn(blkB, lambda g, h: araw_t[:, g * 1024 + h * 512: g * 1024 + h * 512 + 512], sm)

                for g in range(NG):
                    def b8v4(vec, g=g, xo=xo):
                        sS.wait(vec)
                        vS.selfw(vec)
                        vS.inc(vec.tensor_scalar(t2scr[:],
                                                 araw_t[:, g * 1024:(g + 1) * 1024],
                                                 sm["mv"][:, 2 * g:2 * g + 1],
                                                 sm["rcol"][:, g:g + 1],
                                                 OP.subtract, OP.mult))
                        so.wait(vec)
                        vS.selfw(vec)
                        vS.inc(vec.tensor_add(x1g[:], t2scr[:],
                                              x_sb[:, xo + g * 1024: xo + (g + 1) * 1024]))

                    blkB.vector(b8v4)

                    def b8d(sc, g=g, j=j):
                        vS.wait(sc)
                        sS.selfw(sc)
                        so.selfw(sc)
                        so.inc(sc.dma_start(
                            out=bass.AP(x1_sp, (j * TILE + g * 128) * E, [[E, 128], [1, E]]),
                            in_=bass.AP(x1g, 0, [[1024, 128], [1, 1024]])), 16)

                    blkB.scalar(b8d)
                vb_b8_end.append(vS.n)

            def b9(sc):
                so.wait(sc)

            blkB.scalar(b9)

        if "B2" not in phases:
            return nc
        with ExitStack() as b2x:
            Eb = b2x.enter_context
            wQ_sb = Eb(nc.sbuf_tensor("wQ_sb", [128, 8 * E], BF16))
            wK_sb = Eb(nc.sbuf_tensor("wK_sb", [128, 8 * E], BF16))
            winv_sb = Eb(nc.sbuf_tensor("winv_sb", [128, 8 * E], BF16))
            rw1_sb = Eb(nc.sbuf_tensor("rw1_sb", [128, 8 * E], BF16))
            rw2_sb = Eb(nc.sbuf_tensor("rw2_sb", [128, 64], BF16))
            rb1_sb = Eb(nc.sbuf_tensor("rb1_sb", [128, 8], F32))
            vfwd_sb = Eb(nc.sbuf_tensor("vfwd_sb", [128, 8 * 512], BF16))
            wfwd_sb = Eb(nc.sbuf_tensor("wfwd_sb", [128, 8 * 512], BF16))
            winvex_sb = Eb(nc.sbuf_tensor("winvex_sb", [128, 8 * 512], BF16))
            vinv_sb = Eb(nc.sbuf_tensor("vinv_sb", [128, 8 * 512], BF16))
            Ec_sb = Eb(nc.sbuf_tensor("Ec_sb", [8, 512], BF16))
            x1_sb = Eb(nc.sbuf_tensor("x1B2", [128, 4096], F32))
            xh_tm = Eb(nc.sbuf_tensor("xh2_tm", [128, 1024], BF16))
            xh_fm = Eb(nc.sbuf_tensor("xh2_fm", [128, 4096], BF16))
            KA = Eb(nc.sbuf_tensor("KA", [128, 4096], BF16))
            KB = Eb(nc.sbuf_tensor("KB", [128, 4096], BF16))
            h_fm = Eb(nc.sbuf_tensor("h_fm", [128, 4096], BF16))
            yw32 = Eb(nc.sbuf_tensor("yw32", [128, 2048], F32))
            cum32 = Eb(nc.sbuf_tensor("cum32", [128, 2048], F32))
            cum16 = Eb(nc.sbuf_tensor("cum16", [128, 2048], F16))
            xvw16 = Eb(nc.sbuf_tensor("xvw16", [128, 2048], F16))
            wbar = Eb(nc.sbuf_tensor("wbar", [128, 512], BF16))
            wtm = Eb(nc.sbuf_tensor("wtm", [128, 32], F32))
            wtm_bf = Eb(nc.sbuf_tensor("wtm_bf", [128, 32], BF16))
            wsm_fm = Eb(nc.sbuf_tensor("wsm_fm", [8, 512], BF16))
            lgt = Eb(nc.sbuf_tensor("lgt", [128, 32], F32))
            smax = Eb(nc.sbuf_tensor("smax", [128, 8], F32))
            ssum = Eb(nc.sbuf_tensor("ssum", [128, 4], F32))
            esm = Eb(nc.sbuf_tensor("esm", [128, 32], F32))
            carF = Eb(nc.sbuf_tensor("carF", [128, 4], F32))
            carI = Eb(nc.sbuf_tensor("carI", [128, 4], F32))
            sm = {"st": Eb(nc.sbuf_tensor("smC_st", [128, 48], F32)),
                  "mv": Eb(nc.sbuf_tensor("smC_mv", [128, 8], F32)),
                  "rcol": Eb(nc.sbuf_tensor("smC_rcol", [128, 8], F32))}
            psTP = Eb(nc.psum_tensor("psTPc", [128, 1024], BF16))
            psM = Eb(nc.psum_tensor("psMc", [128, 2048], F32))
            psX = Eb(nc.psum_tensor("psXc", [128, 1024], F32))
            blk2 = Eb(nc.Block())

            def c0(sync):
                so.wait(sync)
                sync.wait_ge(wss[0].sem, wst_final[0])
                sync.wait_ge(wss[1].sem, wst_final[1])
                for nm, dst, w in (("wQT", wQ_sb, E), ("wKT", wK_sb, E),
                                   ("winvT", winv_sb, E), ("rw1T", rw1_sb, E),
                                   ("vfwdF", vfwd_sb, 512), ("wfwdF", wfwd_sb, 512),
                                   ("winvexF", winvex_sb, 512), ("vinvF", vinv_sb, 512)):
                    ld.selfw(sync)
                    ld.inc(sync.dma_start(
                        out=bass.AP(dst, 0, [[8 * w, 128], [1, 8 * w]]),
                        in_=wb16[nm][:, :]), 16)
                ld.selfw(sync)
                ld.inc(sync.dma_start(
                    out=bass.AP(rw2_sb, 0, [[64, 128], [1, 64]]),
                    in_=wb16["rw2T"][:, :]), 16)

            blk2.sync(c0)

            def c0g(gp):
                gS.selfw(gp)
                gS.inc(gp.memset(carF[:], 0.0))
                gS.selfw(gp)
                gS.inc(gp.memset(carI[:], 0.0))

            blk2.gpsimd(c0g)

            def c0s(sync):
                ld.selfw(sync)
                ld.inc(sync.dma_start(out=rb1_sb[:], in_=din["rb1c"][:, :]), 16)
                ld.selfw(sync)
                ld.inc(sync.dma_start(out=Ec_sb[:], in_=din["Econst_in"][:, :]), 16)

            blk2.sync(c0s)

            prevB2 = {"vb_x1free": None}
            for j in range(NT):
                def d1(sync, j=j, pv=dict(prevB2)):
                    if pv["vb_x1free"] is not None:
                        sync.wait_ge(vS.sem, pv["vb_x1free"])
                    else:
                        tS.wait(sync)
                        vS.wait(sync)
                        sS.wait(sync)
                        so.wait(sync)
                    ld.selfw(sync)
                    ld.inc(sync.dma_start(
                        out=bass.AP(x1_sb, 0, [[4096, 128], [1024, 4], [1, 1024]]),
                        in_=bass.AP(x1_sp, j * TILE * E, [[E, 128], [128 * E, 4], [1, E]]),
                    ), 16)

                blk2.sync(d1)
                ldw = (ld.sem, ld.n)
                gw = (ld.sem, ld.n)

                emit_ln_bn(blk2, lambda g, h: x1_sb[:, g * 1024 + h * 512: g * 1024 + h * 512 + 512],
                           sm, extra_wait=(lambda vec, w=ldw: vec.wait_ge(*w)))
                for g in range(NG):
                    emit_norm_transpose(blk2, x1_sb, xh_tm, xh_fm, psTP, sm, g)
                prevB2["vb_x1free"] = vS.n

                def mk_prod_fm(wsb, src, dst):
                    vb0 = vS.n

                    def pf(te):
                        vS.wait(te)
                        sS.wait(te)
                        gw and te.wait_ge(*gw)
                        for oc in range(8):
                            reg = 1024 + (oc % 2) * 512
                            if oc >= 2:
                                te.wait_ge(vS.sem, vb0 + oc - 1)
                            for fc in range(8):
                                ins = te.matmul(psM[:, reg:reg + 512],
                                                wsb[:, fc * E + oc * 128: fc * E + (oc + 1) * 128],
                                                src[:, fc * 512:(fc + 1) * 512],
                                                start=(fc == 0), stop=(fc == 7))
                            tS.inc(ins)

                    tb0 = tS.n
                    blk2.tensor(pf)

                    def pe(vec):
                        for oc in range(8):
                            tS.wait(vec, tb0 + oc + 1)
                            reg = 1024 + (oc % 2) * 512
                            vS.selfw(vec)
                            vS.inc(vec.tensor_copy(dst[:, oc * 512:(oc + 1) * 512],
                                                   psM[:, reg:reg + 512]))

                    blk2.vector(pe)

                mk_prod_fm(wK_sb, xh_fm, KA)
                mk_prod_fm(winv_sb, KA, KB)

                for br, (ksrc, wflat, car) in enumerate(
                        ((KA, wfwd_sb, carF), (KB, vinv_sb, carI))):
                    vb0 = vS.n

                    def dyw(te, ksrc=ksrc, wflat=wflat, vb0=vb0):
                        vS.wait(te)
                        for kc in range(4):
                            reg = 1024 + (kc % 2) * 512
                            if kc >= 2:
                                te.wait_ge(vS.sem, vb0 + kc - 1)
                            for fc in range(8):
                                ins = te.matmul(psM[:, reg:reg + 512],
                                                wflat[:, fc * 512 + kc * 128: fc * 512 + (kc + 1) * 128],
                                                ksrc[:, fc * 512:(fc + 1) * 512],
                                                start=(fc == 0), stop=(fc == 7))
                            tS.inc(ins)

                    tb0 = tS.n
                    blk2.tensor(dyw)

                    def dywe(vec, tb0=tb0):
                        for kc in range(4):
                            tS.wait(vec, tb0 + kc + 1)
                            reg = 1024 + (kc % 2) * 512
                            vS.selfw(vec)
                            vS.inc(vec.tensor_copy(yw32[:, kc * 512:(kc + 1) * 512],
                                                   psM[:, reg:reg + 512]))

                    blk2.vector(dywe)

                    def dsc(vec, car=car):
                        sS.wait(vec)
                        gS.wait(vec)
                        vS.selfw(vec)
                        for kc in range(4):
                            vS.selfw(vec)
                            vS.inc(vec.tensor_tensor_scan(
                                cum32[:, kc * 512:(kc + 1) * 512],
                                yw32[:, kc * 512:(kc + 1) * 512],
                                yw32[:, kc * 512:(kc + 1) * 512],
                                car[:, kc:kc + 1], OP.add, OP.bypass))

                    blk2.vector(dsc)

                    def dcar(sc, car=car, br=br, j=j):
                        vS.wait(sc)
                        so.wait(sc)
                        sS.selfw(sc)
                        sS.inc(sc.copy(car[:], bass.AP(cum32, 511, [[2048, 128], [512, 4]])))
                        sS.selfw(sc)
                        sS.inc(sc.copy(cum16[:], cum32[:]))
                        sS.selfw(sc)
                        so.selfw(sc)
                        so.inc(sc.dma_start(
                            out=bass.AP(cum_sp, br * 4 * 128 * T + j * TILE,
                                        [[T, 128], [128 * T, 4], [1, 512]]),
                            in_=bass.AP(cum16, 0, [[2048, 128], [512, 4], [1, 512]])), 16)

                    blk2.scalar(dcar)

                mk_prod_fm(wQ_sb, xh_fm, KA)
                mk_prod_fm(winv_sb, KA, KB)

                for br, (qsrc, vflat) in enumerate(((KA, vfwd_sb), (KB, winvex_sb))):
                    sb0 = sS.n

                    def dh(te, qsrc=qsrc, sb0=sb0):
                        vS.wait(te)
                        sS.wait(te)
                        for oc in range(8):
                            reg = 1024 + (oc % 2) * 512
                            if oc >= 2:
                                te.wait_ge(sS.sem, sb0 + oc - 1)
                            for fc in range(8):
                                ins = te.matmul(psM[:, reg:reg + 512],
                                                rw1_sb[:, fc * E + oc * 128: fc * E + (oc + 1) * 128],
                                                qsrc[:, fc * 512:(fc + 1) * 512],
                                                start=(fc == 0), stop=(fc == 7))
                            tS.inc(ins)

                    tb0 = tS.n
                    blk2.tensor(dh)

                    def dhe(sc, tb0=tb0):
                        for oc in range(8):
                            tS.wait(sc, tb0 + oc + 1)
                            reg = 1024 + (oc % 2) * 512
                            sS.selfw(sc)
                            sS.inc(sc.activation(h_fm[:, oc * 512:(oc + 1) * 512],
                                                 psM[:, reg:reg + 512], AF.Gelu,
                                                 bias=rb1_sb[:, oc:oc + 1]))

                    blk2.scalar(dhe)

                    def dlg(te):
                        sS.wait(te)
                        vS.wait(te)
                        for g in range(NG):
                            for rc in range(8):
                                ins = te.matmul(
                                    psX[:, g * 8:g * 8 + 8],
                                    h_fm[:, rc * 512 + g * 128: rc * 512 + (g + 1) * 128],
                                    rw2_sb[:, rc * 8:(rc + 1) * 8],
                                    start=(rc == 0), stop=(rc == 7))
                        tS.inc(ins)

                    blk2.tensor(dlg)

                    def dsm(vec):
                        tS.wait(vec)
                        sS.wait(vec)
                        vS.selfw(vec)
                        vS.inc(vec.tensor_tensor(
                            bass.AP(lgt, 0, [[32, 128], [8, 4], [1, 8]]),
                            bass.AP(psX, 0, [[1024, 128], [8, 4], [1, 8]]),
                            bass.AP(b2pe_b, 0, [[8, 128], [0, 4], [1, 8]]),
                            OP.add))
                        for g in range(NG):
                            vS.selfw(vec)
                            vS.inc(vec.tensor_reduce(smax[:, g:g + 1],
                                                     lgt[:, g * 8:(g + 1) * 8],
                                                     AX.X, OP.max))
                        vS.selfw(vec)
                        vS.inc(vec.tensor_scalar(smax[:, 4:8], smax[:, 0:4],
                                                 -1.0, None, OP.mult, OP.bypass))

                    blk2.vector(dsm)

                    def dex(sc):
                        vS.wait(sc)
                        for g in range(NG):
                            sS.selfw(sc)
                            sS.inc(sc.activation(esm[:, g * 8:(g + 1) * 8],
                                                 lgt[:, g * 8:(g + 1) * 8], AF.Exp,
                                                 bias=smax[:, 4 + g:5 + g]))

                    blk2.scalar(dex)

                    def dsm2(vec, br=br, j=j):
                        sS.wait(vec)
                        for g in range(NG):
                            vS.selfw(vec)
                            vS.inc(vec.tensor_reduce(ssum[:, g:g + 1],
                                                     esm[:, g * 8:(g + 1) * 8],
                                                     AX.X, OP.add))
                        vS.selfw(vec)
                        vS.inc(vec.reciprocal(ssum[:, 0:4], ssum[:, 0:4]))
                        if br == 1:
                            vS.selfw(vec)
                            vS.inc(vec.tensor_scalar(ssum[:, 0:4], ssum[:, 0:4],
                                                     alphac[:, 0:1], None,
                                                     OP.mult, OP.bypass))
                        for g in range(NG):
                            vS.selfw(vec)
                            vS.inc(vec.tensor_scalar(wtm[:, g * 8:(g + 1) * 8],
                                                     esm[:, g * 8:(g + 1) * 8],
                                                     ssum[:, g:g + 1],
                                                     rp[:, j * 4 + g: j * 4 + g + 1],
                                                     OP.mult, OP.mult))
                        vS.selfw(vec)
                        vS.inc(vec.tensor_copy(wtm_bf[:], wtm[:]))

                    blk2.vector(dsm2)

                    def dwt(te):
                        vS.wait(te)
                        for g in range(NG):
                            ins = te.transpose(
                                bass.AP(psTP, g * 128, [[1024, 8], [1, 128]]),
                                wtm_bf[:, g * 8:(g + 1) * 8], ident[:])
                        tS.inc(ins)

                    blk2.tensor(dwt)

                    def dwte(vec):
                        tS.wait(vec)
                        so.wait(vec)
                        vS.selfw(vec)
                        vS.inc(vec.tensor_copy(
                            bass.AP(wsm_fm, 0, [[512, 8], [128, 4], [1, 128]]),
                            bass.AP(psTP, 0, [[1024, 8], [128, 4], [1, 128]])))

                    blk2.vector(dwte)

                    def dwsp(sc, br=br, j=j):
                        vS.wait(sc)
                        sS.selfw(sc)
                        so.selfw(sc)
                        so.inc(sc.dma_start(
                            out=bass.AP(w_sp, br * 8 * T + j * TILE, [[T, 8], [1, 512]]),
                            in_=bass.AP(wsm_fm, 0, [[512, 8], [1, 512]])), 16)

                    blk2.scalar(dwsp)

                    vb0 = vS.n

                    def dxv(te, qsrc=qsrc, vflat=vflat, vb0=vb0):
                        vS.wait(te)
                        for kc in range(4):
                            reg = 1024 + (kc % 2) * 512
                            if kc >= 1:
                                te.wait_ge(vS.sem, vb0 + 2 * kc - 1)
                            for fc in range(8):
                                te.matmul(psM[:, reg:reg + 512],
                                          vflat[:, fc * 512 + kc * 128: fc * 512 + (kc + 1) * 128],
                                          qsrc[:, fc * 512:(fc + 1) * 512],
                                          start=(fc == 0), stop=(fc == 7))
                            ins = te.matmul(psX[:, 0:512], Ec_sb[:, kc * 128:(kc + 1) * 128],
                                            wsm_fm[:, :], start=True, stop=True)
                            tS.inc(ins)

                    tb0 = tS.n
                    blk2.tensor(dxv)

                    def dxve(vec, tb0=tb0):
                        so.wait(vec)
                        for kc in range(4):
                            tS.wait(vec, tb0 + kc + 1)
                            reg = 1024 + (kc % 2) * 512
                            vS.selfw(vec)
                            vS.inc(vec.tensor_copy(wbar[:], psX[:, 0:512]))
                            vS.selfw(vec)
                            vS.inc(vec.tensor_mul(xvw16[:, kc * 512:(kc + 1) * 512],
                                                  psM[:, reg:reg + 512], wbar[:]))

                    blk2.vector(dxve)

                    def dxsp(sc, br=br, j=j):
                        vS.wait(sc)
                        sS.selfw(sc)
                        so.selfw(sc)
                        so.inc(sc.dma_start(
                            out=bass.AP(xvw_sp, br * 4 * 128 * T + j * TILE,
                                        [[T, 128], [128 * T, 4], [1, 512]]),
                            in_=bass.AP(xvw16, 0, [[2048, 128], [512, 4], [1, 512]])), 16)

                    blk2.scalar(dxsp)

            def e2v(vec):
                sS.wait(vec)
                vS.selfw(vec)
                vS.inc(vec.tensor_scalar(carF[:], carF[:], evenm[:, 0:1], None,
                                         OP.mult, OP.bypass))
                vS.selfw(vec)
                vS.inc(vec.tensor_scalar(carI[:], carI[:], evenm[:, 0:1], None,
                                         OP.mult, OP.bypass))

            blk2.vector(e2v)

            def e2g(gp):
                vS.wait(gp)
                so.wait(gp)
                gld.selfw(gp)
                gld.inc(gp.dma_start(out=bass.AP(cc2_in, 0, [[4, 128], [1, 4]]),
                                     in_=carF[:]), 16)
                gld.selfw(gp)
                gld.inc(gp.dma_start(out=bass.AP(cc2_in, 512, [[4, 128], [1, 4]]),
                                     in_=carI[:]), 16)

            blk2.gpsimd(e2g)

        if "AR2" not in phases:
            return nc
        blkC2_cm = nc.Block()
        blkC2 = blkC2_cm.__enter__()

        def cc2f(gp):
            gld.wait(gp)
            ccS.inc(gp.collective_compute(
                "AllReduce", OP.add,
                replica_groups=[[0, 1], [2, 3], [4, 5], [6, 7]],
                ins=[cc2_in.ap().opt()], outs=[cc2_out.ap().opt()]))
            ccS.wait(gp)
            gld.selfw(gp)
            gld.inc(gp.dma_start(out=totF[:], in_=bass.AP(cc2_out, 0, [[4, 128], [1, 4]])), 16)
            gld.selfw(gp)
            gld.inc(gp.dma_start(out=totI[:], in_=bass.AP(cc2_out, 512, [[4, 128], [1, 4]])), 16)

        blkC2.gpsimd(cc2f)

        def cc2v(vec):
            gld.wait(vec)
            vS.selfw(vec)
            vS.inc(vec.tensor_scalar(totF[:], totF[:], oddm[:, 0:1], None, OP.mult, OP.bypass))
            vS.selfw(vec)
            gS.inc(vec.tensor_scalar(totI[:], totI[:], oddm[:, 0:1], None, OP.mult, OP.bypass))

        blkC2.vector(cc2v)
        blkC2_cm.__exit__(None, None, None)

        if "C" not in phases:
            return nc
        with ExitStack() as cctx:
            Ec_ = cctx.enter_context
            Uf_sb = Ec_(nc.sbuf_tensor("Uf_sb", [128, 4 * E], BF16))
            Ui_sb = Ec_(nc.sbuf_tensor("Ui_sb", [128, 4 * E], BF16))
            bf_sb = Ec_(nc.sbuf_tensor("bf_sb", [8, E], BF16))
            bi_sb = Ec_(nc.sbuf_tensor("bi_sb", [8, E], BF16))
            wO_sb = Ec_(nc.sbuf_tensor("wO_sb", [128, 8 * E], BF16))
            x1_sb = Ec_(nc.sbuf_tensor("x1C", [128, 2 * 4096], F32))
            xvwL = Ec_(nc.sbuf_tensor("xvwL", [128, 4096], F16))
            cumL = Ec_(nc.sbuf_tensor("cumL", [128, 4096], F16))
            wfmL = Ec_(nc.sbuf_tensor("wfmL", [8, 1024], BF16))
            tscr = Ec_(nc.sbuf_tensor("tscr", [128, 2048], F32))
            prodw = Ec_(nc.sbuf_tensor("prodw", [128, 2048], BF16))
            prodw2 = Ec_(nc.sbuf_tensor("prodw2", [128, 2048], BF16))
            comb = Ec_(nc.sbuf_tensor("comb", [128, 4096], BF16))
            yg = Ec_(nc.sbuf_tensor("yg", [128, 1024], F32))
            wraw_t = Ec_(nc.sbuf_tensor("wraw_t", [128, 4096], F32))
            t2scr = Ec_(nc.sbuf_tensor("t2C", [128, 1024], F32))
            sm = {"st": Ec_(nc.sbuf_tensor("smD_st", [128, 48], F32)),
                  "mv": Ec_(nc.sbuf_tensor("smD_mv", [128, 8], F32)),
                  "rcol": Ec_(nc.sbuf_tensor("smD_rcol", [128, 8], F32))}
            psU = Ec_(nc.psum_tensor("psU", [128, 2048], F32))
            psW = Ec_(nc.psum_tensor("psW", [128, 1024], F32))
            psE = Ec_(nc.psum_tensor("psE", [128, 1024], F32))
            blkC = Ec_(nc.Block())

            def f0(sync):
                so.wait(sync)
                sync.wait_ge(wss[0].sem, wst_final[0])
                sync.wait_ge(wss[1].sem, wst_final[1])
                for nm, dst in (("UfF", Uf_sb), ("UiF", Ui_sb)):
                    ld.selfw(sync)
                    ld.inc(sync.dma_start(
                        out=bass.AP(dst, 0, [[4 * E, 128], [1, 4 * E]]),
                        in_=wb16[nm][:, :]), 16)
                ld.selfw(sync)
                ld.inc(sync.dma_start(
                    out=bass.AP(wO_sb, 0, [[8 * E, 128], [1, 8 * E]]),
                    in_=wb16["wOT"][:, :]), 16)
                ld.selfw(sync)
                ld.inc(sync.dma_start(out=bass.AP(bf_sb, 0, [[E, 8], [1, E]]),
                                       in_=wb16["bfw"][:, :]), 16)
                ld.selfw(sync)
                ld.inc(sync.dma_start(out=bass.AP(bi_sb, 0, [[E, 8], [1, E]]),
                                       in_=wb16["biv"][:, :]), 16)

            blkC.sync(f0)

            vb_g5_end = []
            vb_g3e_end = []
            for j in range(NT):
                xo = (j % 2) * 4096

                def g1(sync, j=j, xo=xo):
                    if j == 0:
                        tS.wait(sync)
                        vS.wait(sync)
                        sS.wait(sync)
                        so.wait(sync)
                    elif j >= 2:
                        sync.wait_ge(vS.sem, vb_g5_end[j - 2])
                    ld.selfw(sync)
                    ld.inc(sync.dma_start(
                        out=bass.AP(x1_sb, xo, [[2 * 4096, 128], [1024, 4], [1, 1024]]),
                        in_=bass.AP(x1_sp, j * TILE * E, [[E, 128], [128 * E, 4], [1, E]]),
                    ), 16)

                blkC.sync(g1)

                def g2l(sync, j=j):
                    if j >= 1:
                        sync.wait_ge(vS.sem, vb_g3e_end[j - 1])
                    for br in range(2):
                        ld.inc(sync.dma_start(
                            out=bass.AP(xvwL, br * 2048, [[2 * 2048, 128], [512, 4], [1, 512]]),
                            in_=bass.AP(xvw_sp, br * 4 * 128 * T + j * TILE,
                                        [[T, 128], [128 * T, 4], [1, 512]])), 16)
                        ld.inc(sync.dma_start(
                            out=bass.AP(cumL, br * 2048, [[2 * 2048, 128], [512, 4], [1, 512]]),
                            in_=bass.AP(cum_sp, br * 4 * 128 * T + j * TILE,
                                        [[T, 128], [128 * T, 4], [1, 512]])), 16)
                        ld.inc(sync.dma_start(
                            out=bass.AP(wfmL, br * 512, [[1024, 8], [1, 512]]),
                            in_=bass.AP(w_sp, br * 8 * T + j * TILE, [[T, 8], [1, 512]])), 16)

                blkC.sync(g2l)

                def g2v(vec):
                    ld.wait(vec)
                    gS.wait(vec)
                    tS.wait(vec)
                    for br in range(2):
                        tot = totF if br == 0 else totI
                        dst = prodw if br == 0 else prodw2
                        vS.selfw(vec)
                        vS.inc(vec.tensor_tensor(
                            bass.AP(tscr, 0, [[2048, 128], [512, 4], [1, 512]]),
                            bass.AP(cumL, br * 2048, [[2 * 2048, 128], [512, 4], [1, 512]]),
                            bass.AP(tot, 0, [[4, 128], [1, 4], [0, 512]]),
                            OP.add))
                        vS.selfw(vec)
                        vS.inc(vec.tensor_mul(dst[:, :],
                                              bass.AP(xvwL, br * 2048,
                                                      [[2 * 2048, 128], [1, 2048]]),
                                              tscr[:]))

                blkC.vector(g2v)

                for wave in range(2):
                    def g3t(te, wave=wave):
                        vS.wait(te)
                        te.wait_ge(ld.sem, ld.n)
                        for dq in range(4):
                            dc = wave * 4 + dq
                            for b2 in range(2):
                                usb = Uf_sb if b2 == 0 else Ui_sb
                                psrc = prodw if b2 == 0 else prodw2
                                for kc in range(4):
                                    te.matmul(psU[:, dq * 512:(dq + 1) * 512],
                                              usb[:, kc * E + dc * 128: kc * E + (dc + 1) * 128],
                                              psrc[:, kc * 512:(kc + 1) * 512],
                                              start=(b2 == 0 and kc == 0), stop=False)
                            for b2 in range(2):
                                bsb = bf_sb if b2 == 0 else bi_sb
                                ins = te.matmul(psU[:, dq * 512:(dq + 1) * 512],
                                                bsb[:, dc * 128:(dc + 1) * 128],
                                                wfmL[:, b2 * 512:(b2 + 1) * 512],
                                                start=False, stop=(b2 == 1))
                            tS.inc(ins)

                    tb0 = tS.n
                    blkC.tensor(g3t)

                    def g3e(vec, wave=wave, tb0=tb0):
                        for dq in range(4):
                            dc = wave * 4 + dq
                            tS.wait(vec, tb0 + dq + 1)
                            vS.selfw(vec)
                            vS.inc(vec.tensor_copy(comb[:, dc * 512:(dc + 1) * 512],
                                                   psU[:, dq * 512:(dq + 1) * 512]))

                    blkC.vector(g3e)
                if len(vb_g3e_end) == j:
                    vb_g3e_end.append(vS.n)

                for g in range(NG):
                    def g4t(te, g=g):
                        vS.wait(te)
                        sS.wait(te)
                        for half in range(2):
                            for dc in range(8):
                                ins = te.matmul(
                                    psW[:, half * 512:(half + 1) * 512],
                                    comb[:, dc * 512 + g * 128: dc * 512 + (g + 1) * 128],
                                    wO_sb[:, dc * E + half * 512: dc * E + (half + 1) * 512],
                                    start=(dc == 0), stop=(dc == 7))
                        tS.inc(ins)

                    blkC.tensor(g4t)

                    def g4e(vec, g=g):
                        tS.wait(vec)
                        vS.selfw(vec)
                        vS.inc(vec.tensor_copy(wraw_t[:, g * 1024:(g + 1) * 1024],
                                               psW[:, 0:1024]))

                    blkC.vector(g4e)

                emit_ln_bn(blkC, lambda g, h: wraw_t[:, g * 1024 + h * 512: g * 1024 + h * 512 + 512], sm)

                for g in range(NG):
                    def g5v4(vec, g=g, xo=xo):
                        sS.wait(vec)
                        vS.selfw(vec)
                        vS.inc(vec.tensor_scalar(t2scr[:],
                                                 wraw_t[:, g * 1024:(g + 1) * 1024],
                                                 sm["mv"][:, 2 * g:2 * g + 1],
                                                 sm["rcol"][:, g:g + 1],
                                                 OP.subtract, OP.mult))
                        so.wait(vec)
                        vS.selfw(vec)
                        vS.inc(vec.tensor_add(yg[:], t2scr[:],
                                              x1_sb[:, xo + g * 1024: xo + (g + 1) * 1024]))

                    blkC.vector(g5v4)

                    def g5d(sc, g=g, j=j):
                        vS.wait(sc)
                        sS.selfw(sc)
                        so.selfw(sc)
                        so.inc(sc.dma_start(
                            out=bass.AP(y_d, (j * TILE + g * 128) * E, [[E, 128], [1, E]]),
                            in_=bass.AP(yg, 0, [[1024, 128], [1, 1024]])), 16)

                    blkC.scalar(g5d)
                vb_g5_end.append(vS.n)

            def gfin(sc):
                so.wait(sc)

            blkC.scalar(gfin)

    return nc


_NC_CACHE = {}


def make_host_inputs(inputs):
    ins = {k: np.asarray(v) for k, v in inputs.items()}
    W_in = ins["L0_attn_W_in"]
    host = {}
    host["wvT"] = np.ascontiguousarray(W_in[0:1024].T)
    host["wqT"] = np.ascontiguousarray(W_in[1024:2048].T)
    host["wlgT"] = np.ascontiguousarray(W_in[2048:3072].T)
    host["woutT"] = np.ascontiguousarray(ins["L0_attn_W_out"].T)
    host["wg1T"] = np.ascontiguousarray(ins["L0_attn_W_g1"].T)
    host["wg2T"] = np.ascontiguousarray(ins["L0_attn_W_g2"].T)
    host["lnwb"] = np.ascontiguousarray(np.tile(ins["L0_attn_ln_w"][None, :], (128, 1)).astype(np.float32))
    host["wQT"] = np.ascontiguousarray(ins["L0_ffn_W_Q"].T)
    host["wKT"] = np.ascontiguousarray(ins["L0_ffn_W_K"].T)
    host["winvT"] = np.ascontiguousarray(ins["L0_ffn_W_inv"].T)
    host["wOT"] = np.ascontiguousarray(ins["L0_ffn_W_O"].T)
    host["rw1T"] = np.ascontiguousarray(ins["L0_ffn_router_w1"].T)
    host["rw2T"] = np.ascontiguousarray(ins["L0_ffn_router_w2"].T)
    host["rb1c"] = np.ascontiguousarray(ins["L0_ffn_router_b1"].reshape(8, 128).T)
    host["vfwdF"] = np.ascontiguousarray(ins["L0_ffn_V_fwd"].transpose(1, 0, 2).reshape(1024, 512))
    host["wfwdF"] = np.ascontiguousarray(ins["L0_ffn_W_fwd"].transpose(1, 0, 2).reshape(1024, 512))
    host["winvexF"] = np.ascontiguousarray(ins["L0_ffn_W_inv_exp"].transpose(1, 0, 2).reshape(1024, 512))
    host["vinvF"] = np.ascontiguousarray(ins["L0_ffn_V_inv"].transpose(1, 0, 2).reshape(1024, 512))
    host["UfF"] = np.ascontiguousarray(ins["L0_ffn_U_fwd"].transpose(0, 2, 1).reshape(512, 1024))
    host["UiF"] = np.ascontiguousarray(ins["L0_ffn_U_inv"].transpose(0, 2, 1).reshape(512, 1024))
    host["bfw"] = np.ascontiguousarray(ins["L0_ffn_b_fwd"])
    host["biv"] = np.ascontiguousarray(ins["L0_ffn_b_inv"])
    host["b2pe_in"] = np.tile(np.concatenate(
        [ins["L0_ffn_router_b2"], ins["L0_ffn_expert_bias"]]).reshape(1, 16), (128, 1)).astype(np.float32)
    host["alpha_in"] = np.tile(np.asarray(ins["L0_ffn_alpha_bi"]).reshape(1, 1), (128, 1)).astype(np.float32)
    host["ident_in"] = np.eye(128, dtype=BF)
    m = np.zeros((128, 128), np.float32)
    for b in range(2):
        i0 = b * 64
        m[i0:i0 + 64, i0:i0 + 64] = np.triu(np.ones((64, 64), np.float32))
    host["maskP_in"] = m
    Ec = np.zeros((8, 512), BF)
    for kk in range(8):
        Ec[kk, kk * 64:(kk + 1) * 64] = 1.0
    host["Econst_in"] = Ec
    return host


def kernel(**inputs):
    x = np.ascontiguousarray(
        np.asarray(inputs["L0_attn_x"]).reshape(4 * 8192, 1024).astype(np.float32))
    host = make_host_inputs(inputs)

    if "nc" not in _NC_CACHE:
        _NC_CACHE["nc"] = build_nc()
    nc = _NC_CACHE["nc"]

    in_maps = []
    for cid in range(NCORES):
        half = cid % 2
        im = dict(host)
        im["x"] = x[cid * T:(cid + 1) * T]
        rpj = np.zeros((128, 32), np.float32)
        for gi in range(32):
            base = half * T + gi * 128
            rpj[:, gi] = 1.0 / (base + 1 + np.arange(128, dtype=np.float32))
        im["rp_in"] = rpj
        ev = 1.0 if half == 0 else 0.0
        im["evenm_in"] = np.full((128, 1), ev, np.float32)
        im["oddm_in"] = np.full((128, 1), 1.0 - ev, np.float32)
        in_maps.append(im)

    res = run_bass_kernel_spmd(nc, in_maps, core_ids=list(range(NCORES)))
    out = np.empty((4 * 8192, 1024), np.float32)
    for cid in range(NCORES):
        out[cid * T:(cid + 1) * T] = np.asarray(res.results[cid]["y"], np.float32)
    return out.reshape(4, 8192, 1024)


# version 12
# speedup vs baseline: 1.3369x; 1.0245x over previous
import numpy as np
import ml_dtypes
from contextlib import ExitStack

import concourse.bass as bass
import concourse.mybir as mybir
from concourse.alu_op_type import AluOpType
from concourse.bass_utils import run_bass_kernel_spmd

F32 = mybir.dt.float32
BF16 = mybir.dt.bfloat16
F16 = mybir.dt.float16
AF = mybir.ActivationFunctionType
OP = AluOpType
AX = mybir.AxisListType

NCORES = 8
T = 4096
TILE = 512
NT = T // TILE
NG = TILE // 128
E = 1024
H = 8
EPS = 1e-5
INV_E = 1.0 / 1024.0
BF = ml_dtypes.bfloat16


class Ctr:
    def __init__(self, sem):
        self.sem = sem
        self.n = 0

    def inc(self, instr, k=1):
        instr.then_inc(self.sem, k)
        self.n += k

    def wait(self, eng, n=None):
        eng.wait_ge(self.sem, self.n if n is None else n)

    def selfw(self, eng):
        eng.wait_ge(self.sem, self.n)


def build_nc(debug_taps=(), phases=("A", "AR1", "B1", "B2", "AR2", "C"), a_tiles=NT, a_stages=9):
    nc = bass.Bass("TRN2", target_bir_lowering=False)
    din = {}

    def dr(name, shape, dt, kind=None):
        t = nc.dram_tensor(name, shape, dt, kind=kind) if kind else nc.dram_tensor(name, shape, dt)
        din[name] = t
        return t

    x_d = dr("x", [T, E], F32, "ExternalInput")
    y_d = dr("y", [T, E], F32, "ExternalOutput")
    for nm in ("wvT", "wqT", "wlgT", "woutT", "wQT", "wKT", "winvT", "rw1T", "wOT"):
        dr(nm, [E, E], F32, "ExternalInput")
    dr("wg1T", [E, 128], F32, "ExternalInput")
    dr("wg2T", [128, E], F32, "ExternalInput")
    dr("lnwb", [128, E], F32, "ExternalInput")
    for nm in ("vfwdF", "wfwdF", "winvexF", "vinvF"):
        dr(nm, [E, 512], F32, "ExternalInput")
    dr("rw2T", [E, 8], F32, "ExternalInput")
    dr("rb1c", [128, 8], F32, "ExternalInput")
    dr("b2pe_in", [128, 16], F32, "ExternalInput")
    dr("alpha_in", [128, 1], F32, "ExternalInput")
    dr("UfF", [512, E], F32, "ExternalInput")
    dr("UiF", [512, E], F32, "ExternalInput")
    dr("bfw", [8, E], F32, "ExternalInput")
    dr("biv", [8, E], F32, "ExternalInput")
    dr("ident_in", [128, 128], BF16, "ExternalInput")
    dr("maskP_in", [128, 128], F32, "ExternalInput")
    dr("Econst_in", [8, 512], BF16, "ExternalInput")
    dr("rp_in", [128, 32], F32, "ExternalInput")
    dr("evenm_in", [128, 1], F32, "ExternalInput")
    dr("oddm_in", [128, 1], F32, "ExternalInput")

    dbg = {}
    for nm in debug_taps:
        if nm == "S_dbg":
            dbg[nm] = dr(nm, [128, 1024], F32, "ExternalOutput")

    o_sp = dr("o_sp", [T, E], F16)
    qh_sp = dr("qh_sp", [H, 128, T], BF16)
    x1_sp = dr("x1_sp", [T, E], F32)
    xvw_sp = dr("xvw_sp", [2, 4, 128, T], F16)
    cum_sp = dr("cum_sp", [2, 4, 128, T], F16)
    w_sp = dr("w_sp", [2, 8, T], BF16)
    wb16 = {}
    for nm, w in (("wvT", E), ("wqT", E), ("wlgT", E), ("woutT", E), ("wQT", E),
                  ("wKT", E), ("winvT", E), ("rw1T", E), ("wOT", E),
                  ("vfwdF", 512), ("wfwdF", 512), ("winvexF", 512), ("vinvF", 512),
                  ("UfF", None), ("UiF", None)):
        if nm in ("UfF", "UiF"):
            wb16[nm] = dr(nm + "_b", [128, 4 * E], BF16)
        else:
            wb16[nm] = dr(nm + "_b", [128, 8 * w], BF16)
    wb16["wg1T"] = dr("wg1T_b", [128, 8 * 128], BF16)
    wb16["wg2T"] = dr("wg2T_b", [128, E], BF16)
    wb16["rw2T"] = dr("rw2T_b", [128, 64], BF16)
    wb16["bfw"] = dr("bfw_b", [8, E], BF16)
    wb16["biv"] = dr("biv_b", [8, E], BF16)
    cc1_in = dr("cc1_in", [128 * E], F32)
    cc1_out = dr("cc1_out", [128 * E], F32)
    cc2_in = dr("cc2_in", [128 * 8], F32)
    cc2_out = dr("cc2_out", [128 * 8], F32)

    with ExitStack() as ctx:
        Eo = ctx.enter_context
        ident = Eo(nc.sbuf_tensor("ident", [128, 128], BF16))
        maskP = Eo(nc.sbuf_tensor("maskP", [128, 128], F32))
        rp = Eo(nc.sbuf_tensor("rp", [128, 32], F32))
        evenm = Eo(nc.sbuf_tensor("evenm", [128, 1], F32))
        oddm = Eo(nc.sbuf_tensor("oddm", [128, 1], F32))
        S_in_bf = Eo(nc.sbuf_tensor("S_in_bf", [128, H * 128], BF16))
        totF = Eo(nc.sbuf_tensor("totF", [128, 4], F32))
        totI = Eo(nc.sbuf_tensor("totI", [128, 4], F32))
        alphac = Eo(nc.sbuf_tensor("alphac", [128, 1], F32))
        epscol = Eo(nc.sbuf_tensor("epscol", [128, 1], F32))
        b2pe_b = Eo(nc.sbuf_tensor("b2pe_b", [128, 8], F32))

        sems = {nm: Ctr(Eo(nc.semaphore(nm)))
                for nm in ("ld", "gld", "t", "v", "s", "g", "so", "cc")}
        ld, gld, tS, vS, sS, gS, so, ccS = (sems[n] for n in
                                            ("ld", "gld", "t", "v", "s", "g", "so", "cc"))

        def emit_ln_bn(blk, src_of, sm, extra_wait=None):
            def l1(vec):
                if extra_wait is not None:
                    extra_wait(vec)
                vS.selfw(vec)
                for g in range(NG):
                    for h in range(2):
                        vS.inc(vec.bn_stats(sm["st"][:, (2 * g + h) * 6:(2 * g + h) * 6 + 6],
                                            src_of(g, h)))
                vS.selfw(vec)
                for g in range(NG):
                    vS.inc(vec.bn_aggr(sm["mv"][:, 2 * g:2 * g + 2],
                                       sm["st"][:, g * 12:g * 12 + 12]))

            blk.vector(l1)

            def l2(sc):
                vS.wait(sc)
                sS.selfw(sc)
                sS.inc(sc.activation(sm["rcol"][:, NG:2 * NG],
                                     bass.AP(sm["mv"], 1, [[2 * NG, 128], [2, NG]]),
                                     AF.Ln, bias=epscol[:, 0:1]))
                sS.selfw(sc)
                sS.inc(sc.activation(sm["rcol"][:, 0:NG], sm["rcol"][:, NG:2 * NG],
                                     AF.Exp, scale=-0.5))

            blk.scalar(l2)

        def emit_norm_transpose(blk, x_src, xh_tm, xh_fm, psTP, sm, g, xoff=0):
            def n1(vec):
                sS.wait(vec)
                tS.wait(vec)
                vS.selfw(vec)
                vS.inc(vec.tensor_scalar(xh_tm[:],
                                         x_src[:, xoff + g * 1024: xoff + (g + 1) * 1024],
                                         sm["mv"][:, 2 * g:2 * g + 1],
                                         sm["rcol"][:, g:g + 1],
                                         OP.subtract, OP.mult))

            blk.vector(n1)

            def n2(te):
                vS.wait(te)
                for ci in range(8):
                    ins = te.transpose(psTP[:, ci * 128:(ci + 1) * 128],
                                       xh_tm[:, ci * 128:(ci + 1) * 128], ident[:])
                tS.inc(ins)

            blk.tensor(n2)

            def n3(vec):
                tS.wait(vec)
                vS.selfw(vec)
                vS.inc(vec.tensor_copy(
                    bass.AP(xh_fm, g * 128, [[4096, 128], [512, 8], [1, 128]]),
                    bass.AP(psTP, 0, [[1024, 128], [128, 8], [1, 128]])))

            blk.vector(n3)

        b2tmp = Eo(nc.sbuf_tensor("b2tmp", [128, 16], F32))
        blk0_cm = nc.Block()
        blk0 = blk0_cm.__enter__()

        def p0_sync(sync):
            for dst, nm in ((ident, "ident_in"), (maskP, "maskP_in"), (rp, "rp_in"),
                            (evenm, "evenm_in"), (oddm, "oddm_in")):
                ld.selfw(sync)
                ld.inc(sync.dma_start(out=dst[:], in_=din[nm][:, :]), 16)
            ld.selfw(sync)
            ld.inc(sync.dma_start(out=b2tmp[:], in_=din["b2pe_in"][:, :]), 16)
            ld.selfw(sync)
            ld.inc(sync.dma_start(out=alphac[:], in_=din["alpha_in"][:, :]), 16)

        blk0.sync(p0_sync)

        def p0_g(gp):
            ld.wait(gp)
            gS.selfw(gp)
            gS.inc(gp.memset(epscol[:], EPS))
            gS.selfw(gp)
            gS.inc(gp.tensor_add(b2pe_b[:], b2tmp[:, 0:8], b2tmp[:, 8:16]))

        blk0.gpsimd(p0_g)

        with nc.sbuf_tensor("wstg32", [128, 2 * 8192], F32) as wstg32, \
             nc.sbuf_tensor("wstg16", [128, 2 * 8192], BF16) as wstg16:
            plist = []
            for nm, w in (("wvT", E), ("wqT", E), ("wlgT", E), ("wg1T", 128),
                          ("woutT", E), ("wQT", E), ("wKT", E), ("winvT", E),
                          ("rw1T", E), ("wOT", E), ("vfwdF", 512), ("wfwdF", 512),
                          ("winvexF", 512), ("vinvF", 512)):
                plist.append((nm, [[w, 128], [128 * w, 8], [1, w]], 8 * w))
            plist.append(("wg2T", [[E, 128], [1, E]], E))
            plist.append(("rw2T", [[8, 128], [128 * 8, 8], [1, 8]], 64))
            plist.append(("UfF", [[E, 128], [128 * E, 4], [1, E]], 4 * E))
            plist.append(("UiF", [[E, 128], [128 * E, 4], [1, E]], 4 * E))

            wls = [Ctr(Eo(nc.semaphore("wl0"))), Ctr(Eo(nc.semaphore("wl1")))]
            wss = [Ctr(Eo(nc.semaphore("ws0"))), Ctr(Eo(nc.semaphore("ws1")))]
            cast_done = []
            for i, (nm, ap, width) in enumerate(plist):
                s = i % 2
                buf = s * 8192

                def w_ld(q, nm=nm, ap=ap, width=width, buf=buf, i=i, s=s):
                    if i >= 2:
                        vS.wait(q, cast_done[i - 2])
                    wls[s].inc(q.dma_start(
                        out=bass.AP(wstg32, buf, [[2 * 8192, 128], [1, width]]),
                        in_=bass.AP(din[nm], 0, ap)), 16)

                if s == 0:
                    blk0.sync(w_ld)
                else:
                    blk0.gpsimd(w_ld)
                wl_n = wls[s].n
                ws_n = wss[s].n

                def w_cast(vec, nm=nm, width=width, buf=buf, i=i, s=s,
                           wl_n=wl_n, ws_n=ws_n):
                    vec.wait_ge(wls[s].sem, wl_n)
                    if i >= 2:
                        vec.wait_ge(wss[s].sem, ws_n)
                    vS.selfw(vec)
                    vS.inc(vec.tensor_copy(
                        bass.AP(wstg16, buf, [[2 * 8192, 128], [1, width]]),
                        bass.AP(wstg32, buf, [[2 * 8192, 128], [1, width]])))
                    cast_done.append(vS.n)

                blk0.vector(w_cast)

                def w_st(sc, nm=nm, width=width, buf=buf, i=i, s=s):
                    vS.wait(sc, cast_done[i])
                    wss[s].inc(sc.dma_start(
                        out=bass.AP(wb16[nm], 0, [[width, 128], [1, width]]),
                        in_=bass.AP(wstg16, buf, [[2 * 8192, 128], [1, width]])), 16)

                blk0.scalar(w_st)
            wst_final = (wss[0].n, wss[1].n)

            with nc.sbuf_tensor("bstg32", [8, 2 * E], F32) as bstg32, \
                 nc.sbuf_tensor("bstg16", [8, 2 * E], BF16) as bstg16:
                def b_ld(sync):
                    ld.selfw(sync)
                    ld.inc(sync.dma_start(
                        out=bass.AP(bstg32, 0, [[2 * E, 8], [1, E]]),
                        in_=din["bfw"][:, :]), 16)
                    ld.selfw(sync)
                    ld.inc(sync.dma_start(
                        out=bass.AP(bstg32, E, [[2 * E, 8], [1, E]]),
                        in_=din["biv"][:, :]), 16)

                blk0.sync(b_ld)

                def b_cast(vec):
                    ld.wait(vec)
                    vS.selfw(vec)
                    vS.inc(vec.tensor_copy(bstg16[:, :], bstg32[:, :]))

                blk0.vector(b_cast)

                def b_st(sc):
                    vS.wait(sc)
                    sS.selfw(sc)
                    so.selfw(sc)
                    so.inc(sc.dma_start(out=bass.AP(wb16["bfw"], 0, [[E, 8], [1, E]]),
                                        in_=bass.AP(bstg16, 0, [[2 * E, 8], [1, E]])), 16)
                    so.selfw(sc)
                    so.inc(sc.dma_start(out=bass.AP(wb16["biv"], 0, [[E, 8], [1, E]]),
                                        in_=bass.AP(bstg16, E, [[2 * E, 8], [1, E]])), 16)
                    so.wait(sc)

                blk0.scalar(b_st)
        blk0_cm.__exit__(None, None, None)

        if "A" not in phases:
            return nc
        with ExitStack() as actx:
            Ea = actx.enter_context
            wv_sb = Ea(nc.sbuf_tensor("wv_sb", [128, 8 * E], BF16))
            wq_sb = Ea(nc.sbuf_tensor("wq_sb", [128, 8 * E], BF16))
            wlg_sb = Ea(nc.sbuf_tensor("wlg_sb", [128, 8 * E], BF16))
            x_sb = Ea(nc.sbuf_tensor("xA", [128, 4096], F32))
            xh_tm = Ea(nc.sbuf_tensor("xhA_tm", [128, 1024], BF16))
            xh_fm = Ea(nc.sbuf_tensor("xhA_fm", [128, 4096], BF16))
            v_tm = Ea(nc.sbuf_tensor("v_tm", [128, 4096], BF16))
            qpre = Ea(nc.sbuf_tensor("qpre", [128, 4096], F16))
            f_sb = Ea(nc.sbuf_tensor("f_sb", [128, 4096], F32))
            kf = Ea(nc.sbuf_tensor("kf", [128, 4096], F16))
            Acum = Ea(nc.sbuf_tensor("Acum", [128, 4096], F32))
            carryA = Ea(nc.sbuf_tensor("carryA", [128, 8], F32))
            ends = Ea(nc.sbuf_tensor("ends", [128, 64], F32))
            refs = Ea(nc.sbuf_tensor("refs", [128, 64], F32))
            eACall = Ea(nc.sbuf_tensor("eACall", [128, 64], F32))
            qt_bf = Ea(nc.sbuf_tensor("qt_bf", [128, 4096], BF16))
            kt_bf = Ea(nc.sbuf_tensor("kt_bf", [128, 4096], BF16))
            kh_full = Ea(nc.sbuf_tensor("kh_full", [128, 4096], BF16))
            khtm = Ea(nc.sbuf_tensor("khtm", [128, 4096], BF16))
            qhat_bf = Ea(nc.sbuf_tensor("qhat_bf", [128, 4096], BF16))
            psb_buf = Ea(nc.sbuf_tensor("psb_buf", [128, 1024], BF16))
            S_sb = Ea(nc.sbuf_tensor("S_sb", [128, 1024], F32))
            S_bf = Ea(nc.sbuf_tensor("S_bf", [128, 1024], BF16))
            S_bf2 = Ea(nc.sbuf_tensor("S_bf2", [128, 1024], BF16))
            o16 = Ea(nc.sbuf_tensor("o16", [128, 4096], F16))
            sm = {"st": Ea(nc.sbuf_tensor("smA_st", [128, 48], F32)),
                  "mv": Ea(nc.sbuf_tensor("smA_mv", [128, 8], F32)),
                  "rcol": Ea(nc.sbuf_tensor("smA_rcol", [128, 8], F32))}
            psTP = Ea(nc.psum_tensor("psTPa", [128, 1024], BF16))
            psProd = Ea(nc.psum_tensor("psProdA", [128, 2048], F32))
            psS = Ea(nc.psum_tensor("psSa", [128, 1024], F32))
            blkA = Ea(nc.Block())

            def pa_init(gp):
                gS.selfw(gp)
                gS.inc(gp.memset(S_sb[:], 0.0))
                gS.selfw(gp)
                gS.inc(gp.memset(S_bf[:], 0.0))
                gS.selfw(gp)
                gS.inc(gp.memset(carryA[:], 0.0))

            blkA.gpsimd(pa_init)

            def pa_wld(sync):
                so.wait(sync)
                sync.wait_ge(wss[0].sem, wst_final[0])
                sync.wait_ge(wss[1].sem, wst_final[1])
                for nm, dst in (("wvT", wv_sb), ("wqT", wq_sb), ("wlgT", wlg_sb)):
                    ld.selfw(sync)
                    ld.inc(sync.dma_start(
                        out=bass.AP(dst, 0, [[8 * E, 128], [1, 8 * E]]),
                        in_=wb16[nm][:, :]), 16)

            blkA.sync(pa_wld)

            prevA = {"vb_rel": 0, "sb_e0": None, "vb_a4m_end": 0}
            for j in range(a_tiles):
                def a1(sync, j=j, pv=dict(prevA)):
                    if pv["sb_e0"] is not None:
                        sync.wait_ge(vS.sem, pv["vb_rel"])
                        sync.wait_ge(sS.sem, pv["sb_e0"] + 4)
                    else:
                        vS.wait(sync)
                        sS.wait(sync)
                        tS.wait(sync)
                        so.wait(sync)
                    ld.selfw(sync)
                    ld.inc(sync.dma_start(
                        out=bass.AP(x_sb, 0, [[4096, 128], [1024, 4], [1, 1024]]),
                        in_=bass.AP(x_d, j * TILE * E, [[E, 128], [128 * E, 4], [1, E]]),
                    ), 16)

                blkA.sync(a1)

                ldw = (ld.sem, ld.n)
                gw = (ld.sem, ld.n)

                emit_ln_bn(blkA, lambda g, h: x_sb[:, g * 1024 + h * 512: g * 1024 + h * 512 + 512],
                           sm, extra_wait=(lambda vec, w=ldw: vec.wait_ge(*w)))
                for g in range(NG):
                    emit_norm_transpose(blkA, x_sb, xh_tm, xh_fm, psTP, sm, g)

                if a_stages < 3:
                    continue
                vb_ve = vS.n

                def a3v(te, gw=gw):
                    te.wait_ge(*gw)
                    vS.wait(te)
                    sS.wait(te)
                    for g in range(NG):
                        if g >= 1:
                            te.wait_ge(vS.sem, vb_ve + g)
                        for half in range(2):
                            for fc in range(8):
                                ins = te.matmul(
                                    psProd[:, half * 512:(half + 1) * 512],
                                    xh_fm[:, fc * 512 + g * 128: fc * 512 + (g + 1) * 128],
                                    wv_sb[:, fc * E + half * 512: fc * E + (half + 1) * 512],
                                    start=(fc == 0), stop=(fc == 7))
                        tS.inc(ins)

                tb_v = tS.n
                blkA.tensor(a3v)

                def a3ve(vec):
                    for g in range(NG):
                        tS.wait(vec, tb_v + g + 1)
                        vS.selfw(vec)
                        vS.inc(vec.tensor_copy(v_tm[:, g * 1024:(g + 1) * 1024],
                                               psProd[:, 0:1024]))

                blkA.vector(a3ve)
                assert vS.n == vb_ve + NG

                sb_q0 = sS.n

                def a3q(te, sb_q0=sb_q0):
                    vS.wait(te)
                    sS.wait(te)
                    for cc in range(8):
                        reg = 1024 + (cc % 2) * 512
                        if cc >= 2:
                            te.wait_ge(sS.sem, sb_q0 + cc - 1)
                        for fc in range(8):
                            ins = te.matmul(psProd[:, reg:reg + 512],
                                            wq_sb[:, fc * E + cc * 128: fc * E + (cc + 1) * 128],
                                            xh_fm[:, fc * 512:(fc + 1) * 512],
                                            start=(fc == 0), stop=(fc == 7))
                        tS.inc(ins)

                tb_q = tS.n
                blkA.tensor(a3q)

                def a3qs(sc, tb_q=tb_q, pv=dict(prevA)):
                    sc.wait_ge(vS.sem, pv["vb_a4m_end"])
                    for cc in range(8):
                        tS.wait(sc, tb_q + cc + 1)
                        reg = 1024 + (cc % 2) * 512
                        sS.selfw(sc)
                        sS.inc(sc.activation(qpre[:, cc * 512:(cc + 1) * 512],
                                             psProd[:, reg:reg + 512], AF.Silu))

                blkA.scalar(a3qs)
                assert sS.n == sb_q0 + 8

                sb_l0 = sS.n

                def a3l(te, sb_l0=sb_l0):
                    for lc in range(8):
                        reg = 1024 + (lc % 2) * 512
                        if lc < 2:
                            te.wait_ge(sS.sem, sb_q0 + 7 + lc)
                        else:
                            te.wait_ge(sS.sem, sb_l0 + 2 * (lc - 2) + 2)
                        for fc in range(8):
                            ins = te.matmul(psProd[:, reg:reg + 512],
                                            wlg_sb[:, fc * E + lc * 128: fc * E + (lc + 1) * 128],
                                            xh_fm[:, fc * 512:(fc + 1) * 512],
                                            start=(fc == 0), stop=(fc == 7))
                        tS.inc(ins)

                tb_l = tS.n
                blkA.tensor(a3l)

                def a3ls(sc, tb_l=tb_l, pv=dict(prevA)):
                    sc.wait_ge(vS.sem, pv["vb_a4m_end"])
                    for lc in range(8):
                        tS.wait(sc, tb_l + lc + 1)
                        reg = 1024 + (lc % 2) * 512
                        sS.selfw(sc)
                        sS.inc(sc.activation(f_sb[:, lc * 512:(lc + 1) * 512],
                                             psProd[:, reg:reg + 512], AF.Sigmoid))
                        sS.selfw(sc)
                        sS.inc(sc.activation(kf[:, lc * 512:(lc + 1) * 512],
                                             psProd[:, reg:reg + 512], AF.Sigmoid,
                                             scale=-1.0))

                blkA.scalar(a3ls)

                if a_stages < 4:
                    continue
                def a3g(sc):
                    sS.selfw(sc)
                    sS.inc(sc.activation(f_sb[:], f_sb[:], AF.Ln))

                blkA.scalar(a3g)

                def a4v(vec):
                    gS.wait(vec)
                    sS.wait(vec)
                    for h in range(H):
                        vS.selfw(vec)
                        vS.inc(vec.tensor_tensor_scan(
                            Acum[:, h * 512:(h + 1) * 512],
                            f_sb[:, h * 512:(h + 1) * 512],
                            f_sb[:, h * 512:(h + 1) * 512],
                            carryA[:, h:h + 1], OP.add, OP.bypass))

                blkA.vector(a4v)

                def a4r(vec):
                    vS.selfw(vec)
                    vS.inc(vec.tensor_copy(ends[:],
                                           bass.AP(Acum, 63, [[4096, 128], [64, 64]])))
                    vS.selfw(vec)
                    vS.inc(vec.tensor_copy(bass.AP(refs, 0, [[64, 128], [8, 8]]),
                                           carryA[:, 0:8]))
                    vS.selfw(vec)
                    vS.inc(vec.tensor_copy(bass.AP(refs, 1, [[64, 128], [8, 8], [1, 7]]),
                                           bass.AP(ends, 0, [[64, 128], [8, 8], [1, 7]])))
                    vS.selfw(vec)
                    gS.inc(vec.tensor_copy(carryA[:, 0:8],
                                           bass.AP(ends, 7, [[64, 128], [8, 8]])))
                    vS.selfw(vec)
                    vS.inc(vec.tensor_tensor(
                        bass.AP(f_sb, 0, [[4096, 128], [64, 64], [1, 64]]),
                        bass.AP(Acum, 0, [[4096, 128], [64, 64], [1, 64]]),
                        bass.AP(ends, 0, [[64, 128], [1, 64], [0, 64]]), OP.subtract))
                    vS.selfw(vec)
                    vS.inc(vec.tensor_tensor(
                        bass.AP(x_sb, 0, [[4096, 128], [64, 64], [1, 64]]),
                        bass.AP(Acum, 0, [[4096, 128], [64, 64], [1, 64]]),
                        bass.AP(refs, 0, [[64, 128], [1, 64], [0, 64]]), OP.subtract))

                blkA.vector(a4r)
                vb_rel = vS.n
                vb_scrA2n = vS.n - 1

                sb_e0 = sS.n

                def a4e(sc, vb_rel=vb_rel, vb_scrA2n=vb_scrA2n):
                    so.wait(sc)
                    vS.wait(sc, vb_scrA2n)
                    sS.selfw(sc)
                    sS.inc(sc.activation(kh_full[:],
                                         bass.AP(f_sb, 0, [[4096, 128], [1, 4096]]),
                                         AF.Exp, scale=-1.0))
                    vS.wait(sc, vb_rel)
                    sS.selfw(sc)
                    sS.inc(sc.activation(eACall[:],
                                         bass.AP(x_sb, 63, [[4096, 128], [64, 64]]),
                                         AF.Exp))
                    sS.selfw(sc)
                    sS.inc(sc.activation(qt_bf[:],
                                         bass.AP(x_sb, 0, [[4096, 128], [1, 4096]]),
                                         AF.Exp))
                    sS.selfw(sc)
                    sS.inc(sc.activation(kt_bf[:],
                                         bass.AP(x_sb, 0, [[4096, 128], [1, 4096]]),
                                         AF.Exp, scale=-1.0))
                    sS.selfw(sc)
                    sS.inc(sc.activation(qhat_bf[:], Acum[:], AF.Exp))

                blkA.scalar(a4e)

                def a4m(vec, sb_e0=sb_e0):
                    so.wait(vec)
                    vec.wait_ge(sS.sem, sb_e0 + 1)
                    vS.selfw(vec)
                    vS.inc(vec.tensor_mul(kh_full[:], kh_full[:], kf[:]))
                    vec.wait_ge(sS.sem, sb_e0 + 3)
                    vS.selfw(vec)
                    vS.inc(vec.tensor_mul(qt_bf[:], qt_bf[:], qpre[:]))
                    vec.wait_ge(sS.sem, sb_e0 + 4)
                    vS.selfw(vec)
                    vS.inc(vec.tensor_mul(kt_bf[:], kt_bf[:], kf[:]))
                    vec.wait_ge(sS.sem, sb_e0 + 5)
                    vS.selfw(vec)
                    vS.inc(vec.tensor_mul(qhat_bf[:], qhat_bf[:], qpre[:]))

                blkA.vector(a4m)
                vb_kh = vS.n - 3
                vb_a4m_end = vS.n
                prevA["vb_rel"] = vb_rel
                prevA["sb_e0"] = sb_e0
                prevA["vb_a4m_end"] = vb_a4m_end

                for tb in range(4):
                    def a4tp(te, tb=tb, vb_kh=vb_kh):
                        te.wait_ge(vS.sem, vb_kh)
                        if tb >= 1:
                            vS.wait(te)
                        for bi in range(8):
                            h = tb * 2 + bi // 4
                            g_ = bi % 4
                            ins = te.transpose(
                                psTP[:, bi * 128:(bi + 1) * 128],
                                kh_full[:, h * 512 + g_ * 128: h * 512 + (g_ + 1) * 128],
                                ident[:])
                        tS.inc(ins)

                    blkA.tensor(a4tp)

                    def a4te(vec, tb=tb):
                        tS.wait(vec)
                        vS.selfw(vec)
                        vS.inc(vec.tensor_copy(khtm[:, tb * 1024:(tb + 1) * 1024],
                                               psTP[:, 0:1024]))

                    blkA.vector(a4te)

                if a_stages < 5:
                    continue
                for g in range(NG):
                    cA = 2 * g
                    cB = 2 * g + 1
                    vb_pre = vS.n

                    def a5suA(te, g=g, vb_pre=vb_pre):
                        te.wait_ge(vS.sem, vb_pre)
                        for h in range(H):
                            ins = te.matmul(
                                psProd[:, 1024 + h * 128: 1024 + (h + 1) * 128],
                                khtm[0:64, h * 512 + g * 128: h * 512 + (g + 1) * 128],
                                v_tm[0:64, g * 1024 + h * 128: g * 1024 + (h + 1) * 128],
                                start=True, stop=True)
                            tS.inc(ins)

                    blkA.tensor(a5suA)
                    tb_suA = tS.n

                    def a5sc(te, g=g):
                        for h in range(H):
                            ins = te.matmul(
                                psProd[:, h * 128:(h + 1) * 128],
                                kt_bf[:, h * 512 + g * 128: h * 512 + (g + 1) * 128],
                                qt_bf[:, h * 512 + g * 128: h * 512 + (g + 1) * 128],
                                start=True, stop=True)
                            tS.inc(ins)

                    blkA.tensor(a5sc)
                    tb_sc = tS.n

                    def a5suB(te, g=g):
                        for h in range(H):
                            ins = te.matmul(
                                psS[:, h * 128:(h + 1) * 128],
                                khtm[64:128, h * 512 + g * 128: h * 512 + (g + 1) * 128],
                                v_tm[64:128, g * 1024 + h * 128: g * 1024 + (h + 1) * 128],
                                start=True, stop=True)
                            tS.inc(ins)

                    blkA.tensor(a5suB)
                    tb_suB = tS.n

                    def a5v1(vec, cA=cA, tb_suA=tb_suA):
                        tS.wait(vec, tb_suA)
                        gS.wait(vec)
                        vS.selfw(vec)
                        vS.inc(vec.tensor_tensor(
                            S_sb[:], S_sb[:],
                            bass.AP(eACall, cA, [[64, 128], [8, 8], [0, 128]]),
                            OP.mult))
                        vS.selfw(vec)
                        vS.inc(vec.tensor_add(S_sb[:], S_sb[:], psProd[:, 1024:2048]))

                    blkA.vector(a5v1)
                    vb_stt1 = vS.n

                    def a5vm(vec, tb_sc=tb_sc):
                        tS.wait(vec, tb_sc)
                        vS.selfw(vec)
                        vS.inc(vec.tensor_tensor(
                            psb_buf[:], psProd[:, 0:1024],
                            bass.AP(maskP, 0, [[128, 128], [0, 8], [1, 128]]),
                            OP.mult))

                    blkA.vector(a5vm)
                    vb_mask = vS.n

                    def a5g1(gp, vb_stt1=vb_stt1):
                        gp.wait_ge(vS.sem, vb_stt1)
                        gS.selfw(gp)
                        gS.inc(gp.tensor_copy(S_bf2[:], S_sb[:]))

                    blkA.gpsimd(a5g1)
                    gb_c1 = gS.n

                    def a5o1(te, g=g, vb_mask=vb_mask):
                        te.wait_ge(vS.sem, vb_mask)
                        gS.wait(te)
                        for h in range(H):
                            te.matmul(psProd[:, h * 128:(h + 1) * 128],
                                      psb_buf[:, h * 128:(h + 1) * 128],
                                      v_tm[:, g * 1024 + h * 128: g * 1024 + (h + 1) * 128],
                                      start=(h == 0) or (h == 4), stop=False,
                                      skip_group_check=True)
                            ins = te.matmul(
                                psProd[0:64, h * 128:(h + 1) * 128],
                                qt_bf[:, h * 512 + g * 128: h * 512 + g * 128 + 64],
                                S_bf[:, h * 128:(h + 1) * 128],
                                start=False, stop=False, skip_group_check=True)
                            tS.inc(ins)

                    blkA.tensor(a5o1)
                    tb_o1 = tS.n

                    def a5v2(vec, cB=cB, tb_suB=tb_suB):
                        tS.wait(vec, tb_suB)
                        gS.wait(vec)
                        vS.selfw(vec)
                        vS.inc(vec.tensor_tensor(
                            S_sb[:], S_sb[:],
                            bass.AP(eACall, cB, [[64, 128], [8, 8], [0, 128]]),
                            OP.mult))
                        vS.selfw(vec)
                        vS.inc(vec.tensor_add(S_sb[:], S_sb[:], psS[:, 0:1024]))

                    blkA.vector(a5v2)
                    vb_stt2 = vS.n

                    def a5g2(gp, vb_stt2=vb_stt2, tb_o1=tb_o1):
                        gp.wait_ge(vS.sem, vb_stt2)
                        gp.wait_ge(tS.sem, tb_o1)
                        gS.selfw(gp)
                        gS.inc(gp.tensor_copy(S_bf[:], S_sb[:]))

                    blkA.gpsimd(a5g2)
                    gb_c2 = gS.n

                    def a5o2(te, g=g, gb_c1=gb_c1):
                        te.wait_ge(gS.sem, gb_c1)
                        for h in range(H):
                            ins = te.matmul(
                                psProd[64:128, h * 128:(h + 1) * 128],
                                qt_bf[:, h * 512 + g * 128 + 64: h * 512 + (g + 1) * 128],
                                S_bf2[:, h * 128:(h + 1) * 128],
                                start=False, stop=(h == 3) or (h == 7),
                                skip_group_check=True)
                            tS.inc(ins)

                    blkA.tensor(a5o2)

                    def a5oe(vec, g=g):
                        tS.wait(vec)
                        vS.selfw(vec)
                        vS.inc(vec.tensor_copy(o16[:, g * 1024:(g + 1) * 1024],
                                               psProd[:, 0:1024]))

                    blkA.vector(a5oe)

                if a_stages < 6:
                    continue

                def a6(sc, j=j):
                    vS.wait(sc)
                    sS.selfw(sc)
                    so.selfw(sc)
                    so.inc(sc.dma_start(
                        out=bass.AP(o_sp, j * TILE * E, [[E, 128], [128 * E, 4], [1, E]]),
                        in_=bass.AP(o16, 0, [[4096, 128], [1024, 4], [1, 1024]])), 16)
                    sS.selfw(sc)
                    so.selfw(sc)
                    so.inc(sc.dma_start(
                        out=bass.AP(qh_sp, j * TILE, [[T, 128], [128 * T, 8], [1, 512]]),
                        in_=bass.AP(qhat_bf, 0, [[4096, 128], [512, 8], [1, 512]])), 16)

                blkA.scalar(a6)

            def ae_v(vec):
                gS.wait(vec)
                vS.selfw(vec)
                vS.inc(vec.tensor_scalar(S_sb[:], S_sb[:], evenm[:, 0:1], None,
                                         OP.mult, OP.bypass))

            blkA.vector(ae_v)

            if "S_dbg" in dbg:
                def ae_d(sc):
                    vS.wait(sc)
                    sS.selfw(sc)
                    so.selfw(sc)
                    so.inc(sc.dma_start(out=dbg["S_dbg"][:, :],
                                        in_=bass.AP(S_sb, 0, [[1024, 128], [1, 1024]])), 16)
                blkA.scalar(ae_d)

            def ae_g(gp):
                vS.wait(gp)
                so.wait(gp)
                gld.selfw(gp)
                gld.inc(gp.dma_start(out=cc1_in.ap(),
                                     in_=bass.AP(S_sb, 0, [[1024, 128], [1, 1024]])), 16)

            blkA.gpsimd(ae_g)

        if "AR1" not in phases:
            return nc
        sin_f32 = Eo(nc.sbuf_tensor("sin_f32", [128, 1024], F32))
        blkC1_cm = nc.Block()
        blkC1 = blkC1_cm.__enter__()

        def cc1f(gp):
            gld.wait(gp)
            ccS.inc(gp.collective_compute(
                "AllReduce", OP.add,
                replica_groups=[[0, 1], [2, 3], [4, 5], [6, 7]],
                ins=[cc1_in.ap().opt()], outs=[cc1_out.ap().opt()]))
            ccS.wait(gp)
            gld.selfw(gp)
            gld.inc(gp.dma_start(out=sin_f32[:],
                                 in_=bass.AP(cc1_out, 0, [[1024, 128], [1, 1024]])), 16)

        blkC1.gpsimd(cc1f)

        def cc1v(vec):
            gld.wait(vec)
            vS.selfw(vec)
            vS.inc(vec.tensor_scalar(sin_f32[:], sin_f32[:], oddm[:, 0:1], None,
                                     OP.mult, OP.bypass))
            vS.selfw(vec)
            gS.inc(vec.tensor_copy(S_in_bf[:], sin_f32[:]))

        blkC1.vector(cc1v)
        blkC1_cm.__exit__(None, None, None)

        if "B1" not in phases:
            return nc
        with ExitStack() as bctx:
            Eb = bctx.enter_context
            wg1_sb = Eb(nc.sbuf_tensor("wg1_sb", [128, 8 * 128], BF16))
            wg2_sb = Eb(nc.sbuf_tensor("wg2_sb", [128, E], BF16))
            wout_sb = Eb(nc.sbuf_tensor("wout_sb", [128, 8 * E], BF16))
            lnwb_sb = Eb(nc.sbuf_tensor("lnwb_sb", [128, E], F32))
            x_sb = Eb(nc.sbuf_tensor("xB", [128, 2 * 4096], F32))
            xh_tm = Eb(nc.sbuf_tensor("xhB_tm", [128, 1024], BF16))
            xh_fm = Eb(nc.sbuf_tensor("xhB_fm", [128, 4096], BF16))
            y1_bf = Eb(nc.sbuf_tensor("y1_bf", [128, 512], BF16))
            gate_t = Eb(nc.sbuf_tensor("gate_t", [128, 4096], F32))
            o16t = Eb(nc.sbuf_tensor("o16t", [128, 4096], F16))
            qht = Eb(nc.sbuf_tensor("qht", [128, 4096], BF16))
            ofull = Eb(nc.sbuf_tensor("ofull", [128, 1024], F32))
            og_t = Eb(nc.sbuf_tensor("og_t", [128, 4096], F32))
            araw_t = Eb(nc.sbuf_tensor("araw_t", [128, 4096], F32))
            ohat_scr = Eb(nc.sbuf_tensor("ohat_scr", [128, 1024], F32))
            ohat_bf = Eb(nc.sbuf_tensor("ohat_bf", [128, 1024], BF16))
            ohat_fm = Eb(nc.sbuf_tensor("ohat_fm", [128, 4096], BF16))
            x1g = Eb(nc.sbuf_tensor("x1g", [128, 1024], F32))
            t2scr = Eb(nc.sbuf_tensor("t2B", [128, 1024], F32))
            sm = {"st": Eb(nc.sbuf_tensor("smB_st", [128, 48], F32)),
                  "mv": Eb(nc.sbuf_tensor("smB_mv", [128, 8], F32)),
                  "rcol": Eb(nc.sbuf_tensor("smB_rcol", [128, 8], F32))}
            psTP = Eb(nc.psum_tensor("psTPb", [128, 1024], BF16))
            psM = Eb(nc.psum_tensor("psMb", [128, 2048], F32))
            psX = Eb(nc.psum_tensor("psXb", [128, 1024], F32))
            blkB = Eb(nc.Block())

            def b0(sync):
                so.wait(sync)
                sync.wait_ge(wss[0].sem, wst_final[0])
                sync.wait_ge(wss[1].sem, wst_final[1])
                ld.selfw(sync)
                ld.inc(sync.dma_start(
                    out=bass.AP(wg1_sb, 0, [[8 * 128, 128], [1, 8 * 128]]),
                    in_=wb16["wg1T"][:, :]), 16)
                ld.selfw(sync)
                ld.inc(sync.dma_start(
                    out=bass.AP(wg2_sb, 0, [[E, 128], [1, E]]),
                    in_=wb16["wg2T"][:, :]), 16)
                ld.selfw(sync)
                ld.inc(sync.dma_start(
                    out=bass.AP(wout_sb, 0, [[8 * E, 128], [1, 8 * E]]),
                    in_=wb16["woutT"][:, :]), 16)

            blkB.sync(b0)

            def b0s(sync):
                ld.selfw(sync)
                ld.inc(sync.dma_start(out=lnwb_sb[:], in_=din["lnwb"][:, :]), 16)

            blkB.sync(b0s)

            vb_b8_end = []
            vb_b5_end = []
            for j in range(NT):
                xo = (j % 2) * 4096

                def b1(sync, j=j):
                    if j == 0:
                        tS.wait(sync)
                        vS.wait(sync)
                        sS.wait(sync)
                        so.wait(sync)
                    elif j >= 2:
                        sync.wait_ge(vS.sem, vb_b8_end[j - 2])
                    ld.selfw(sync)
                    ld.inc(sync.dma_start(
                        out=bass.AP(x_sb, xo, [[2 * 4096, 128], [1024, 4], [1, 1024]]),
                        in_=bass.AP(x_d, j * TILE * E, [[E, 128], [128 * E, 4], [1, E]]),
                    ), 16)

                blkB.sync(b1)
                ldw = (ld.sem, ld.n)
                gw = (ld.sem, ld.n)

                emit_ln_bn(blkB, lambda g, h: x_sb[:, xo + g * 1024 + h * 512: xo + g * 1024 + h * 512 + 512],
                           sm, extra_wait=(lambda vec, w=ldw: vec.wait_ge(*w)))
                for g in range(NG):
                    emit_norm_transpose(blkB, x_sb, xh_tm, xh_fm, psTP, sm, g, xoff=xo)

                def b3t(te, gw=gw):
                    te.wait_ge(*gw)
                    vS.wait(te)
                    sS.wait(te)
                    for fc in range(8):
                        ins = te.matmul(psM[:, 0:512],
                                        wg1_sb[:, fc * 128:(fc + 1) * 128],
                                        xh_fm[:, fc * 512:(fc + 1) * 512],
                                        start=(fc == 0), stop=(fc == 7))
                    tS.inc(ins)

                blkB.tensor(b3t)

                def b3e(vec):
                    tS.wait(vec)
                    vS.selfw(vec)
                    vS.inc(vec.tensor_copy(y1_bf[:], psM[:, 0:512]))

                blkB.vector(b3e)

                for g in range(NG):
                    def bg_t(te, g=g):
                        vS.wait(te)
                        if g >= 1:
                            sS.wait(te)
                        for half in range(2):
                            ins = te.matmul(psM[:, half * 512:(half + 1) * 512],
                                            y1_bf[:, g * 128:(g + 1) * 128],
                                            wg2_sb[:, half * 512:(half + 1) * 512],
                                            start=True, stop=True)
                        tS.inc(ins)

                    blkB.tensor(bg_t)

                    def bg_s(sc, g=g, j=j):
                        tS.wait(sc)
                        if j >= 1:
                            sc.wait_ge(vS.sem, vb_b5_end[j - 1])
                        sS.selfw(sc)
                        sS.inc(sc.activation(gate_t[:, g * 1024:(g + 1) * 1024],
                                             psM[:, 0:1024], AF.Sigmoid))

                    blkB.scalar(bg_s)

                def b4l(sync, j=j):
                    if j >= 1:
                        sync.wait_ge(vS.sem, vb_b5_end[j - 1])
                    ld.selfw(sync)
                    ld.inc(sync.dma_start(
                        out=bass.AP(o16t, 0, [[4096, 128], [1024, 4], [1, 1024]]),
                        in_=bass.AP(o_sp, j * TILE * E,
                                    [[E, 128], [128 * E, 4], [1, E]])), 16)
                    ld.selfw(sync)
                    ld.inc(sync.dma_start(
                        out=bass.AP(qht, 0, [[4096, 128], [512, 8], [1, 512]]),
                        in_=bass.AP(qh_sp, j * TILE, [[T, 128], [128 * T, 8], [1, 512]])), 16)

                blkB.sync(b4l)

                for g in range(NG):
                    def b4t(te, g=g):
                        ld.wait(te)
                        gS.wait(te)
                        vS.wait(te)
                        sS.wait(te)
                        for h in range(H):
                            ins = te.matmul(psM[:, 1024 + h * 128: 1024 + (h + 1) * 128],
                                            qht[:, h * 512 + g * 128: h * 512 + (g + 1) * 128],
                                            S_in_bf[:, h * 128:(h + 1) * 128],
                                            start=(h == 0) or (h == 4),
                                            stop=(h == 3) or (h == 7))
                        tS.inc(ins)

                    blkB.tensor(b4t)

                    def b5v(vec, g=g):
                        tS.wait(vec)
                        sS.wait(vec)
                        vS.selfw(vec)
                        vS.inc(vec.tensor_add(ofull[:], psM[:, 1024:2048],
                                              o16t[:, g * 1024:(g + 1) * 1024]))
                        vS.selfw(vec)
                        vS.inc(vec.tensor_mul(og_t[:, g * 1024:(g + 1) * 1024], ofull[:],
                                              gate_t[:, g * 1024:(g + 1) * 1024]))

                    blkB.vector(b5v)

                vb_b5_end.append(vS.n)
                emit_ln_bn(blkB, lambda g, h: og_t[:, g * 1024 + h * 512: g * 1024 + h * 512 + 512], sm)

                for g in range(NG):
                    def b6v(vec, g=g):
                        sS.wait(vec)
                        tS.wait(vec)
                        vS.selfw(vec)
                        vS.inc(vec.tensor_scalar(ohat_scr[:],
                                                 og_t[:, g * 1024:(g + 1) * 1024],
                                                 sm["mv"][:, 2 * g:2 * g + 1],
                                                 sm["rcol"][:, g:g + 1],
                                                 OP.subtract, OP.mult))
                        vS.selfw(vec)
                        vS.inc(vec.tensor_mul(ohat_bf[:], ohat_scr[:], lnwb_sb[:]))

                    blkB.vector(b6v)

                    def b6t(te, g=g):
                        vS.wait(te)
                        for ci in range(8):
                            ins = te.transpose(psTP[:, ci * 128:(ci + 1) * 128],
                                               ohat_bf[:, ci * 128:(ci + 1) * 128], ident[:])
                        tS.inc(ins)

                    blkB.tensor(b6t)

                    def b6e(vec, g=g):
                        tS.wait(vec)
                        vS.selfw(vec)
                        vS.inc(vec.tensor_copy(
                            bass.AP(ohat_fm, g * 128, [[4096, 128], [512, 8], [1, 128]]),
                            bass.AP(psTP, 0, [[1024, 128], [128, 8], [1, 128]])))

                    blkB.vector(b6e)

                for g in range(NG):
                    def b7t(te, g=g):
                        vS.wait(te)
                        sS.wait(te)
                        for half in range(2):
                            for ec in range(8):
                                ins = te.matmul(
                                    psX[:, half * 512:(half + 1) * 512],
                                    ohat_fm[:, ec * 512 + g * 128: ec * 512 + (g + 1) * 128],
                                    wout_sb[:, ec * E + half * 512: ec * E + (half + 1) * 512],
                                    start=(ec == 0), stop=(ec == 7))
                        tS.inc(ins)

                    blkB.tensor(b7t)

                    def b7e(vec, g=g):
                        tS.wait(vec)
                        vS.selfw(vec)
                        vS.inc(vec.tensor_copy(araw_t[:, g * 1024:(g + 1) * 1024],
                                               psX[:, 0:1024]))

                    blkB.vector(b7e)

                emit_ln_bn(blkB, lambda g, h: araw_t[:, g * 1024 + h * 512: g * 1024 + h * 512 + 512], sm)

                for g in range(NG):
                    def b8v4(vec, g=g, xo=xo):
                        sS.wait(vec)
                        vS.selfw(vec)
                        vS.inc(vec.tensor_scalar(t2scr[:],
                                                 araw_t[:, g * 1024:(g + 1) * 1024],
                                                 sm["mv"][:, 2 * g:2 * g + 1],
                                                 sm["rcol"][:, g:g + 1],
                                                 OP.subtract, OP.mult))
                        so.wait(vec)
                        vS.selfw(vec)
                        vS.inc(vec.tensor_add(x1g[:], t2scr[:],
                                              x_sb[:, xo + g * 1024: xo + (g + 1) * 1024]))

                    blkB.vector(b8v4)

                    def b8d(sc, g=g, j=j):
                        vS.wait(sc)
                        sS.selfw(sc)
                        so.selfw(sc)
                        so.inc(sc.dma_start(
                            out=bass.AP(x1_sp, (j * TILE + g * 128) * E, [[E, 128], [1, E]]),
                            in_=bass.AP(x1g, 0, [[1024, 128], [1, 1024]])), 16)

                    blkB.scalar(b8d)
                vb_b8_end.append(vS.n)

            def b9(sc):
                so.wait(sc)

            blkB.scalar(b9)

        if "B2" not in phases:
            return nc
        with ExitStack() as b2x:
            Eb = b2x.enter_context
            wQ_sb = Eb(nc.sbuf_tensor("wQ_sb", [128, 8 * E], BF16))
            wK_sb = Eb(nc.sbuf_tensor("wK_sb", [128, 8 * E], BF16))
            winv_sb = Eb(nc.sbuf_tensor("winv_sb", [128, 8 * E], BF16))
            rw1_sb = Eb(nc.sbuf_tensor("rw1_sb", [128, 8 * E], BF16))
            rw2_sb = Eb(nc.sbuf_tensor("rw2_sb", [128, 64], BF16))
            rb1_sb = Eb(nc.sbuf_tensor("rb1_sb", [128, 8], F32))
            vfwd_sb = Eb(nc.sbuf_tensor("vfwd_sb", [128, 8 * 512], BF16))
            wfwd_sb = Eb(nc.sbuf_tensor("wfwd_sb", [128, 8 * 512], BF16))
            winvex_sb = Eb(nc.sbuf_tensor("winvex_sb", [128, 8 * 512], BF16))
            vinv_sb = Eb(nc.sbuf_tensor("vinv_sb", [128, 8 * 512], BF16))
            Ec_sb = Eb(nc.sbuf_tensor("Ec_sb", [8, 512], BF16))
            x1_sb = Eb(nc.sbuf_tensor("x1B2", [128, 4096], F32))
            xh_tm = Eb(nc.sbuf_tensor("xh2_tm", [128, 1024], BF16))
            xh_fm = Eb(nc.sbuf_tensor("xh2_fm", [128, 4096], BF16))
            KA = Eb(nc.sbuf_tensor("KA", [128, 4096], BF16))
            KB = Eb(nc.sbuf_tensor("KB", [128, 4096], BF16))
            QA = Eb(nc.sbuf_tensor("QA", [128, 4096], BF16))
            QB = Eb(nc.sbuf_tensor("QB", [128, 4096], BF16))
            h_fm = Eb(nc.sbuf_tensor("h_fm", [128, 4096], BF16))
            yw32 = Eb(nc.sbuf_tensor("yw32", [128, 2 * 2048], F32))
            cum32 = Eb(nc.sbuf_tensor("cum32", [128, 2048], F32))
            cum16 = Eb(nc.sbuf_tensor("cum16", [128, 2048], F16))
            xvw16 = Eb(nc.sbuf_tensor("xvw16", [128, 2048], F16))
            wbar = Eb(nc.sbuf_tensor("wbar", [128, 512], BF16))
            wtm = Eb(nc.sbuf_tensor("wtm", [128, 32], F32))
            wtm_bf = Eb(nc.sbuf_tensor("wtm_bf", [128, 32], BF16))
            wsm_fm = Eb(nc.sbuf_tensor("wsm_fm", [8, 512], BF16))
            lgt = Eb(nc.sbuf_tensor("lgt", [128, 32], F32))
            smax = Eb(nc.sbuf_tensor("smax", [128, 8], F32))
            ssum = Eb(nc.sbuf_tensor("ssum", [128, 4], F32))
            esm = Eb(nc.sbuf_tensor("esm", [128, 32], F32))
            carF = Eb(nc.sbuf_tensor("carF", [128, 4], F32))
            carI = Eb(nc.sbuf_tensor("carI", [128, 4], F32))
            sm = {"st": Eb(nc.sbuf_tensor("smC_st", [128, 48], F32)),
                  "mv": Eb(nc.sbuf_tensor("smC_mv", [128, 8], F32)),
                  "rcol": Eb(nc.sbuf_tensor("smC_rcol", [128, 8], F32))}
            psTP = Eb(nc.psum_tensor("psTPc", [128, 1024], BF16))
            psM = Eb(nc.psum_tensor("psMc", [128, 2048], F32))
            psX = Eb(nc.psum_tensor("psXc", [128, 1024], F32))
            blk2 = Eb(nc.Block())

            def c0(sync):
                so.wait(sync)
                sync.wait_ge(wss[0].sem, wst_final[0])
                sync.wait_ge(wss[1].sem, wst_final[1])
                for nm, dst, w in (("wQT", wQ_sb, E), ("wKT", wK_sb, E),
                                   ("winvT", winv_sb, E), ("rw1T", rw1_sb, E),
                                   ("vfwdF", vfwd_sb, 512), ("wfwdF", wfwd_sb, 512),
                                   ("winvexF", winvex_sb, 512), ("vinvF", vinv_sb, 512)):
                    ld.selfw(sync)
                    ld.inc(sync.dma_start(
                        out=bass.AP(dst, 0, [[8 * w, 128], [1, 8 * w]]),
                        in_=wb16[nm][:, :]), 16)
                ld.selfw(sync)
                ld.inc(sync.dma_start(
                    out=bass.AP(rw2_sb, 0, [[64, 128], [1, 64]]),
                    in_=wb16["rw2T"][:, :]), 16)

            blk2.sync(c0)

            def c0g(gp):
                gS.selfw(gp)
                gS.inc(gp.memset(carF[:], 0.0))
                gS.selfw(gp)
                gS.inc(gp.memset(carI[:], 0.0))

            blk2.gpsimd(c0g)

            def c0s(sync):
                ld.selfw(sync)
                ld.inc(sync.dma_start(out=rb1_sb[:], in_=din["rb1c"][:, :]), 16)
                ld.selfw(sync)
                ld.inc(sync.dma_start(out=Ec_sb[:], in_=din["Econst_in"][:, :]), 16)

            blk2.sync(c0s)

            prevB2 = {"vb_x1free": None}
            for j in range(NT):
                def d1(sync, j=j, pv=dict(prevB2)):
                    if pv["vb_x1free"] is not None:
                        sync.wait_ge(vS.sem, pv["vb_x1free"])
                    else:
                        tS.wait(sync)
                        vS.wait(sync)
                        sS.wait(sync)
                        so.wait(sync)
                    ld.selfw(sync)
                    ld.inc(sync.dma_start(
                        out=bass.AP(x1_sb, 0, [[4096, 128], [1024, 4], [1, 1024]]),
                        in_=bass.AP(x1_sp, j * TILE * E, [[E, 128], [128 * E, 4], [1, E]]),
                    ), 16)

                blk2.sync(d1)
                ldw = (ld.sem, ld.n)
                gw = (ld.sem, ld.n)

                emit_ln_bn(blk2, lambda g, h: x1_sb[:, g * 1024 + h * 512: g * 1024 + h * 512 + 512],
                           sm, extra_wait=(lambda vec, w=ldw: vec.wait_ge(*w)))
                for g in range(NG):
                    emit_norm_transpose(blk2, x1_sb, xh_tm, xh_fm, psTP, sm, g)
                prevB2["vb_x1free"] = vS.n

                def mk_prod_fm(wsb, src, dst, swait=None):
                    vb0 = vS.n

                    def pf(te):
                        vS.wait(te)
                        if swait is None:
                            sS.wait(te)
                        else:
                            te.wait_ge(sS.sem, swait)
                        gw and te.wait_ge(*gw)
                        for oc in range(8):
                            reg = 1024 + (oc % 2) * 512
                            if oc >= 2:
                                te.wait_ge(vS.sem, vb0 + oc - 1)
                            for fc in range(8):
                                ins = te.matmul(psM[:, reg:reg + 512],
                                                wsb[:, fc * E + oc * 128: fc * E + (oc + 1) * 128],
                                                src[:, fc * 512:(fc + 1) * 512],
                                                start=(fc == 0), stop=(fc == 7))
                            tS.inc(ins)

                    tb0 = tS.n
                    blk2.tensor(pf)

                    def pe(vec):
                        for oc in range(8):
                            tS.wait(vec, tb0 + oc + 1)
                            reg = 1024 + (oc % 2) * 512
                            vS.selfw(vec)
                            vS.inc(vec.tensor_copy(dst[:, oc * 512:(oc + 1) * 512],
                                                   psM[:, reg:reg + 512]))

                    blk2.vector(pe)

                mk_prod_fm(wK_sb, xh_fm, KA,
                           swait=prevB2.get("sb_dhe1", 0))
                mk_prod_fm(winv_sb, KA, KB)
                mk_prod_fm(wQ_sb, xh_fm, QA)
                mk_prod_fm(winv_sb, QA, QB)
                vb_qb = vS.n

                for br, (ksrc, wflat, car) in enumerate(
                        ((KA, wfwd_sb, carF), (KB, vinv_sb, carI))):
                    vb0 = vS.n
                    ywo = br * 2048

                    def dyw(te, ksrc=ksrc, wflat=wflat, vb0=vb0, vb_qb=vb_qb):
                        te.wait_ge(vS.sem, vb_qb)
                        for kc in range(4):
                            reg = 1024 + (kc % 2) * 512
                            if kc >= 2:
                                te.wait_ge(vS.sem, vb0 + kc - 1)
                            for fc in range(8):
                                ins = te.matmul(psM[:, reg:reg + 512],
                                                wflat[:, fc * 512 + kc * 128: fc * 512 + (kc + 1) * 128],
                                                ksrc[:, fc * 512:(fc + 1) * 512],
                                                start=(fc == 0), stop=(fc == 7))
                            tS.inc(ins)

                    tb0 = tS.n
                    blk2.tensor(dyw)

                    def dywe(vec, tb0=tb0, ywo=ywo):
                        for kc in range(4):
                            tS.wait(vec, tb0 + kc + 1)
                            reg = 1024 + (kc % 2) * 512
                            vS.selfw(vec)
                            vS.inc(vec.tensor_copy(yw32[:, ywo + kc * 512: ywo + (kc + 1) * 512],
                                                   psM[:, reg:reg + 512]))

                    blk2.vector(dywe)
                    vb_ywe = vS.n

                    def dsc(vec, car=car, ywo=ywo):
                        sS.wait(vec)
                        gS.wait(vec)
                        vS.selfw(vec)
                        for kc in range(4):
                            vS.selfw(vec)
                            vS.inc(vec.tensor_tensor_scan(
                                cum32[:, kc * 512:(kc + 1) * 512],
                                yw32[:, ywo + kc * 512: ywo + (kc + 1) * 512],
                                yw32[:, ywo + kc * 512: ywo + (kc + 1) * 512],
                                car[:, kc:kc + 1], OP.add, OP.bypass))

                    blk2.vector(dsc)

                    def dcar(sc, car=car, br=br, j=j):
                        vS.wait(sc)
                        so.wait(sc)
                        sS.selfw(sc)
                        sS.inc(sc.copy(car[:], bass.AP(cum32, 511, [[2048, 128], [512, 4]])))
                        sS.selfw(sc)
                        sS.inc(sc.copy(cum16[:], cum32[:]))
                        sS.selfw(sc)
                        so.selfw(sc)
                        so.inc(sc.dma_start(
                            out=bass.AP(cum_sp, br * 4 * 128 * T + j * TILE,
                                        [[T, 128], [128 * T, 4], [1, 512]]),
                            in_=bass.AP(cum16, 0, [[2048, 128], [512, 4], [1, 512]])), 16)

                    blk2.scalar(dcar)

                vb_ywe1 = vb_ywe

                for br, (qsrc, vflat) in enumerate(((QA, vfwd_sb), (QB, winvex_sb))):
                    sb0 = sS.n

                    def dh(te, qsrc=qsrc, sb0=sb0, br=br, vb_ywe1=vb_ywe1):
                        if br == 0:
                            te.wait_ge(vS.sem, vb_ywe1)
                        else:
                            vS.wait(te)
                            sS.wait(te)
                        for oc in range(8):
                            reg = 1024 + (oc % 2) * 512
                            if oc >= 2:
                                te.wait_ge(sS.sem, sb0 + oc - 1)
                            for fc in range(8):
                                ins = te.matmul(psM[:, reg:reg + 512],
                                                rw1_sb[:, fc * E + oc * 128: fc * E + (oc + 1) * 128],
                                                qsrc[:, fc * 512:(fc + 1) * 512],
                                                start=(fc == 0), stop=(fc == 7))
                            tS.inc(ins)

                    tb0 = tS.n
                    blk2.tensor(dh)

                    def dhe(sc, tb0=tb0):
                        for oc in range(8):
                            tS.wait(sc, tb0 + oc + 1)
                            reg = 1024 + (oc % 2) * 512
                            sS.selfw(sc)
                            sS.inc(sc.activation(h_fm[:, oc * 512:(oc + 1) * 512],
                                                 psM[:, reg:reg + 512], AF.Gelu,
                                                 bias=rb1_sb[:, oc:oc + 1]))

                    blk2.scalar(dhe)
                    if br == 1:
                        prevB2["sb_dhe1"] = sS.n

                    def dlg(te):
                        sS.wait(te)
                        vS.wait(te)
                        for g in range(NG):
                            for rc in range(8):
                                ins = te.matmul(
                                    psX[:, g * 8:g * 8 + 8],
                                    h_fm[:, rc * 512 + g * 128: rc * 512 + (g + 1) * 128],
                                    rw2_sb[:, rc * 8:(rc + 1) * 8],
                                    start=(rc == 0), stop=(rc == 7))
                        tS.inc(ins)

                    blk2.tensor(dlg)

                    def dsm(vec):
                        tS.wait(vec)
                        sS.wait(vec)
                        vS.selfw(vec)
                        vS.inc(vec.tensor_tensor(
                            bass.AP(lgt, 0, [[32, 128], [8, 4], [1, 8]]),
                            bass.AP(psX, 0, [[1024, 128], [8, 4], [1, 8]]),
                            bass.AP(b2pe_b, 0, [[8, 128], [0, 4], [1, 8]]),
                            OP.add))
                        for g in range(NG):
                            vS.selfw(vec)
                            vS.inc(vec.tensor_reduce(smax[:, g:g + 1],
                                                     lgt[:, g * 8:(g + 1) * 8],
                                                     AX.X, OP.max))
                        vS.selfw(vec)
                        vS.inc(vec.tensor_scalar(smax[:, 4:8], smax[:, 0:4],
                                                 -1.0, None, OP.mult, OP.bypass))

                    blk2.vector(dsm)

                    def dex(sc):
                        vS.wait(sc)
                        for g in range(NG):
                            sS.selfw(sc)
                            sS.inc(sc.activation(esm[:, g * 8:(g + 1) * 8],
                                                 lgt[:, g * 8:(g + 1) * 8], AF.Exp,
                                                 bias=smax[:, 4 + g:5 + g]))

                    blk2.scalar(dex)

                    def dsm2(vec, br=br, j=j):
                        sS.wait(vec)
                        for g in range(NG):
                            vS.selfw(vec)
                            vS.inc(vec.tensor_reduce(ssum[:, g:g + 1],
                                                     esm[:, g * 8:(g + 1) * 8],
                                                     AX.X, OP.add))
                        vS.selfw(vec)
                        vS.inc(vec.reciprocal(ssum[:, 0:4], ssum[:, 0:4]))
                        if br == 1:
                            vS.selfw(vec)
                            vS.inc(vec.tensor_scalar(ssum[:, 0:4], ssum[:, 0:4],
                                                     alphac[:, 0:1], None,
                                                     OP.mult, OP.bypass))
                        for g in range(NG):
                            vS.selfw(vec)
                            vS.inc(vec.tensor_scalar(wtm[:, g * 8:(g + 1) * 8],
                                                     esm[:, g * 8:(g + 1) * 8],
                                                     ssum[:, g:g + 1],
                                                     rp[:, j * 4 + g: j * 4 + g + 1],
                                                     OP.mult, OP.mult))
                        vS.selfw(vec)
                        vS.inc(vec.tensor_copy(wtm_bf[:], wtm[:]))

                    blk2.vector(dsm2)

                    def dwt(te):
                        vS.wait(te)
                        for g in range(NG):
                            ins = te.transpose(
                                bass.AP(psTP, g * 128, [[1024, 8], [1, 128]]),
                                wtm_bf[:, g * 8:(g + 1) * 8], ident[:])
                        tS.inc(ins)

                    blk2.tensor(dwt)

                    def dwte(vec):
                        tS.wait(vec)
                        so.wait(vec)
                        vS.selfw(vec)
                        vS.inc(vec.tensor_copy(
                            bass.AP(wsm_fm, 0, [[512, 8], [128, 4], [1, 128]]),
                            bass.AP(psTP, 0, [[1024, 8], [128, 4], [1, 128]])))

                    blk2.vector(dwte)

                    def dwsp(sc, br=br, j=j):
                        vS.wait(sc)
                        sS.selfw(sc)
                        so.selfw(sc)
                        so.inc(sc.dma_start(
                            out=bass.AP(w_sp, br * 8 * T + j * TILE, [[T, 8], [1, 512]]),
                            in_=bass.AP(wsm_fm, 0, [[512, 8], [1, 512]])), 16)

                    blk2.scalar(dwsp)

                    vb0 = vS.n

                    def dxv(te, qsrc=qsrc, vflat=vflat, vb0=vb0):
                        vS.wait(te)
                        for kc in range(4):
                            reg = 1024 + (kc % 2) * 512
                            if kc >= 1:
                                te.wait_ge(vS.sem, vb0 + 2 * kc - 1)
                            for fc in range(8):
                                te.matmul(psM[:, reg:reg + 512],
                                          vflat[:, fc * 512 + kc * 128: fc * 512 + (kc + 1) * 128],
                                          qsrc[:, fc * 512:(fc + 1) * 512],
                                          start=(fc == 0), stop=(fc == 7))
                            ins = te.matmul(psX[:, 0:512], Ec_sb[:, kc * 128:(kc + 1) * 128],
                                            wsm_fm[:, :], start=True, stop=True)
                            tS.inc(ins)

                    tb0 = tS.n
                    blk2.tensor(dxv)

                    def dxve(vec, tb0=tb0):
                        so.wait(vec)
                        for kc in range(4):
                            tS.wait(vec, tb0 + kc + 1)
                            reg = 1024 + (kc % 2) * 512
                            vS.selfw(vec)
                            vS.inc(vec.tensor_copy(wbar[:], psX[:, 0:512]))
                            vS.selfw(vec)
                            vS.inc(vec.tensor_mul(xvw16[:, kc * 512:(kc + 1) * 512],
                                                  psM[:, reg:reg + 512], wbar[:]))

                    blk2.vector(dxve)

                    def dxsp(sc, br=br, j=j):
                        vS.wait(sc)
                        sS.selfw(sc)
                        so.selfw(sc)
                        so.inc(sc.dma_start(
                            out=bass.AP(xvw_sp, br * 4 * 128 * T + j * TILE,
                                        [[T, 128], [128 * T, 4], [1, 512]]),
                            in_=bass.AP(xvw16, 0, [[2048, 128], [512, 4], [1, 512]])), 16)

                    blk2.scalar(dxsp)

            def e2v(vec):
                sS.wait(vec)
                vS.selfw(vec)
                vS.inc(vec.tensor_scalar(carF[:], carF[:], evenm[:, 0:1], None,
                                         OP.mult, OP.bypass))
                vS.selfw(vec)
                vS.inc(vec.tensor_scalar(carI[:], carI[:], evenm[:, 0:1], None,
                                         OP.mult, OP.bypass))

            blk2.vector(e2v)

            def e2g(gp):
                vS.wait(gp)
                so.wait(gp)
                gld.selfw(gp)
                gld.inc(gp.dma_start(out=bass.AP(cc2_in, 0, [[4, 128], [1, 4]]),
                                     in_=carF[:]), 16)
                gld.selfw(gp)
                gld.inc(gp.dma_start(out=bass.AP(cc2_in, 512, [[4, 128], [1, 4]]),
                                     in_=carI[:]), 16)

            blk2.gpsimd(e2g)

        if "AR2" not in phases:
            return nc
        blkC2_cm = nc.Block()
        blkC2 = blkC2_cm.__enter__()

        def cc2f(gp):
            gld.wait(gp)
            ccS.inc(gp.collective_compute(
                "AllReduce", OP.add,
                replica_groups=[[0, 1], [2, 3], [4, 5], [6, 7]],
                ins=[cc2_in.ap().opt()], outs=[cc2_out.ap().opt()]))
            ccS.wait(gp)
            gld.selfw(gp)
            gld.inc(gp.dma_start(out=totF[:], in_=bass.AP(cc2_out, 0, [[4, 128], [1, 4]])), 16)
            gld.selfw(gp)
            gld.inc(gp.dma_start(out=totI[:], in_=bass.AP(cc2_out, 512, [[4, 128], [1, 4]])), 16)

        blkC2.gpsimd(cc2f)

        def cc2v(vec):
            gld.wait(vec)
            vS.selfw(vec)
            vS.inc(vec.tensor_scalar(totF[:], totF[:], oddm[:, 0:1], None, OP.mult, OP.bypass))
            vS.selfw(vec)
            gS.inc(vec.tensor_scalar(totI[:], totI[:], oddm[:, 0:1], None, OP.mult, OP.bypass))

        blkC2.vector(cc2v)
        blkC2_cm.__exit__(None, None, None)

        if "C" not in phases:
            return nc
        with ExitStack() as cctx:
            Ec_ = cctx.enter_context
            Uf_sb = Ec_(nc.sbuf_tensor("Uf_sb", [128, 4 * E], BF16))
            Ui_sb = Ec_(nc.sbuf_tensor("Ui_sb", [128, 4 * E], BF16))
            bf_sb = Ec_(nc.sbuf_tensor("bf_sb", [8, E], BF16))
            bi_sb = Ec_(nc.sbuf_tensor("bi_sb", [8, E], BF16))
            wO_sb = Ec_(nc.sbuf_tensor("wO_sb", [128, 8 * E], BF16))
            x1_sb = Ec_(nc.sbuf_tensor("x1C", [128, 2 * 4096], F32))
            xvwL = Ec_(nc.sbuf_tensor("xvwL", [128, 4096], F16))
            cumL = Ec_(nc.sbuf_tensor("cumL", [128, 4096], F16))
            wfmL = Ec_(nc.sbuf_tensor("wfmL", [8, 1024], BF16))
            tscr = Ec_(nc.sbuf_tensor("tscr", [128, 2048], F32))
            prodw = Ec_(nc.sbuf_tensor("prodw", [128, 2048], BF16))
            prodw2 = Ec_(nc.sbuf_tensor("prodw2", [128, 2048], BF16))
            comb = Ec_(nc.sbuf_tensor("comb", [128, 4096], BF16))
            yg = Ec_(nc.sbuf_tensor("yg", [128, 1024], F32))
            wraw_t = Ec_(nc.sbuf_tensor("wraw_t", [128, 4096], F32))
            t2scr = Ec_(nc.sbuf_tensor("t2C", [128, 1024], F32))
            sm = {"st": Ec_(nc.sbuf_tensor("smD_st", [128, 48], F32)),
                  "mv": Ec_(nc.sbuf_tensor("smD_mv", [128, 8], F32)),
                  "rcol": Ec_(nc.sbuf_tensor("smD_rcol", [128, 8], F32))}
            psU = Ec_(nc.psum_tensor("psU", [128, 2048], F32))
            psW = Ec_(nc.psum_tensor("psW", [128, 1024], F32))
            psE = Ec_(nc.psum_tensor("psE", [128, 1024], F32))
            blkC = Ec_(nc.Block())

            def f0(sync):
                so.wait(sync)
                sync.wait_ge(wss[0].sem, wst_final[0])
                sync.wait_ge(wss[1].sem, wst_final[1])
                for nm, dst in (("UfF", Uf_sb), ("UiF", Ui_sb)):
                    ld.selfw(sync)
                    ld.inc(sync.dma_start(
                        out=bass.AP(dst, 0, [[4 * E, 128], [1, 4 * E]]),
                        in_=wb16[nm][:, :]), 16)
                ld.selfw(sync)
                ld.inc(sync.dma_start(
                    out=bass.AP(wO_sb, 0, [[8 * E, 128], [1, 8 * E]]),
                    in_=wb16["wOT"][:, :]), 16)
                ld.selfw(sync)
                ld.inc(sync.dma_start(out=bass.AP(bf_sb, 0, [[E, 8], [1, E]]),
                                       in_=wb16["bfw"][:, :]), 16)
                ld.selfw(sync)
                ld.inc(sync.dma_start(out=bass.AP(bi_sb, 0, [[E, 8], [1, E]]),
                                       in_=wb16["biv"][:, :]), 16)

            blkC.sync(f0)

            vb_g5_end = []
            vb_g3e_end = []
            for j in range(NT):
                xo = (j % 2) * 4096

                def g1(sync, j=j, xo=xo):
                    if j == 0:
                        tS.wait(sync)
                        vS.wait(sync)
                        sS.wait(sync)
                        so.wait(sync)
                    elif j >= 2:
                        sync.wait_ge(vS.sem, vb_g5_end[j - 2])
                    ld.selfw(sync)
                    ld.inc(sync.dma_start(
                        out=bass.AP(x1_sb, xo, [[2 * 4096, 128], [1024, 4], [1, 1024]]),
                        in_=bass.AP(x1_sp, j * TILE * E, [[E, 128], [128 * E, 4], [1, E]]),
                    ), 16)

                blkC.sync(g1)

                def g2l(sync, j=j):
                    if j >= 1:
                        sync.wait_ge(vS.sem, vb_g3e_end[j - 1])
                    for br in range(2):
                        ld.inc(sync.dma_start(
                            out=bass.AP(xvwL, br * 2048, [[2 * 2048, 128], [512, 4], [1, 512]]),
                            in_=bass.AP(xvw_sp, br * 4 * 128 * T + j * TILE,
                                        [[T, 128], [128 * T, 4], [1, 512]])), 16)
                        ld.inc(sync.dma_start(
                            out=bass.AP(cumL, br * 2048, [[2 * 2048, 128], [512, 4], [1, 512]]),
                            in_=bass.AP(cum_sp, br * 4 * 128 * T + j * TILE,
                                        [[T, 128], [128 * T, 4], [1, 512]])), 16)
                        ld.inc(sync.dma_start(
                            out=bass.AP(wfmL, br * 512, [[1024, 8], [1, 512]]),
                            in_=bass.AP(w_sp, br * 8 * T + j * TILE, [[T, 8], [1, 512]])), 16)

                blkC.sync(g2l)

                def g2v(vec):
                    ld.wait(vec)
                    gS.wait(vec)
                    tS.wait(vec)
                    for br in range(2):
                        tot = totF if br == 0 else totI
                        dst = prodw if br == 0 else prodw2
                        vS.selfw(vec)
                        vS.inc(vec.tensor_tensor(
                            bass.AP(tscr, 0, [[2048, 128], [512, 4], [1, 512]]),
                            bass.AP(cumL, br * 2048, [[2 * 2048, 128], [512, 4], [1, 512]]),
                            bass.AP(tot, 0, [[4, 128], [1, 4], [0, 512]]),
                            OP.add))
                        vS.selfw(vec)
                        vS.inc(vec.tensor_mul(dst[:, :],
                                              bass.AP(xvwL, br * 2048,
                                                      [[2 * 2048, 128], [1, 2048]]),
                                              tscr[:]))

                blkC.vector(g2v)

                for wave in range(2):
                    def g3t(te, wave=wave):
                        vS.wait(te)
                        te.wait_ge(ld.sem, ld.n)
                        for dq in range(4):
                            dc = wave * 4 + dq
                            for b2 in range(2):
                                usb = Uf_sb if b2 == 0 else Ui_sb
                                psrc = prodw if b2 == 0 else prodw2
                                for kc in range(4):
                                    te.matmul(psU[:, dq * 512:(dq + 1) * 512],
                                              usb[:, kc * E + dc * 128: kc * E + (dc + 1) * 128],
                                              psrc[:, kc * 512:(kc + 1) * 512],
                                              start=(b2 == 0 and kc == 0), stop=False)
                            for b2 in range(2):
                                bsb = bf_sb if b2 == 0 else bi_sb
                                ins = te.matmul(psU[:, dq * 512:(dq + 1) * 512],
                                                bsb[:, dc * 128:(dc + 1) * 128],
                                                wfmL[:, b2 * 512:(b2 + 1) * 512],
                                                start=False, stop=(b2 == 1))
                            tS.inc(ins)

                    tb0 = tS.n
                    blkC.tensor(g3t)

                    def g3e(vec, wave=wave, tb0=tb0):
                        for dq in range(4):
                            dc = wave * 4 + dq
                            tS.wait(vec, tb0 + dq + 1)
                            vS.selfw(vec)
                            vS.inc(vec.tensor_copy(comb[:, dc * 512:(dc + 1) * 512],
                                                   psU[:, dq * 512:(dq + 1) * 512]))

                    blkC.vector(g3e)
                if len(vb_g3e_end) == j:
                    vb_g3e_end.append(vS.n)

                for g in range(NG):
                    def g4t(te, g=g):
                        vS.wait(te)
                        sS.wait(te)
                        for half in range(2):
                            for dc in range(8):
                                ins = te.matmul(
                                    psW[:, half * 512:(half + 1) * 512],
                                    comb[:, dc * 512 + g * 128: dc * 512 + (g + 1) * 128],
                                    wO_sb[:, dc * E + half * 512: dc * E + (half + 1) * 512],
                                    start=(dc == 0), stop=(dc == 7))
                        tS.inc(ins)

                    blkC.tensor(g4t)

                    def g4e(vec, g=g):
                        tS.wait(vec)
                        vS.selfw(vec)
                        vS.inc(vec.tensor_copy(wraw_t[:, g * 1024:(g + 1) * 1024],
                                               psW[:, 0:1024]))

                    blkC.vector(g4e)

                emit_ln_bn(blkC, lambda g, h: wraw_t[:, g * 1024 + h * 512: g * 1024 + h * 512 + 512], sm)

                for g in range(NG):
                    def g5v4(vec, g=g, xo=xo):
                        sS.wait(vec)
                        vS.selfw(vec)
                        vS.inc(vec.tensor_scalar(t2scr[:],
                                                 wraw_t[:, g * 1024:(g + 1) * 1024],
                                                 sm["mv"][:, 2 * g:2 * g + 1],
                                                 sm["rcol"][:, g:g + 1],
                                                 OP.subtract, OP.mult))
                        so.wait(vec)
                        vS.selfw(vec)
                        vS.inc(vec.tensor_add(yg[:], t2scr[:],
                                              x1_sb[:, xo + g * 1024: xo + (g + 1) * 1024]))

                    blkC.vector(g5v4)

                    def g5d(sc, g=g, j=j):
                        vS.wait(sc)
                        sS.selfw(sc)
                        so.selfw(sc)
                        so.inc(sc.dma_start(
                            out=bass.AP(y_d, (j * TILE + g * 128) * E, [[E, 128], [1, E]]),
                            in_=bass.AP(yg, 0, [[1024, 128], [1, 1024]])), 16)

                    blkC.scalar(g5d)
                vb_g5_end.append(vS.n)

            def gfin(sc):
                so.wait(sc)

            blkC.scalar(gfin)

    return nc


_NC_CACHE = {}


def make_host_inputs(inputs):
    ins = {k: np.asarray(v) for k, v in inputs.items()}
    W_in = ins["L0_attn_W_in"]
    host = {}
    host["wvT"] = np.ascontiguousarray(W_in[0:1024].T)
    host["wqT"] = np.ascontiguousarray(W_in[1024:2048].T)
    host["wlgT"] = np.ascontiguousarray(W_in[2048:3072].T)
    host["woutT"] = np.ascontiguousarray(ins["L0_attn_W_out"].T)
    host["wg1T"] = np.ascontiguousarray(ins["L0_attn_W_g1"].T)
    host["wg2T"] = np.ascontiguousarray(ins["L0_attn_W_g2"].T)
    host["lnwb"] = np.ascontiguousarray(np.tile(ins["L0_attn_ln_w"][None, :], (128, 1)).astype(np.float32))
    host["wQT"] = np.ascontiguousarray(ins["L0_ffn_W_Q"].T)
    host["wKT"] = np.ascontiguousarray(ins["L0_ffn_W_K"].T)
    host["winvT"] = np.ascontiguousarray(ins["L0_ffn_W_inv"].T)
    host["wOT"] = np.ascontiguousarray(ins["L0_ffn_W_O"].T)
    host["rw1T"] = np.ascontiguousarray(ins["L0_ffn_router_w1"].T)
    host["rw2T"] = np.ascontiguousarray(ins["L0_ffn_router_w2"].T)
    host["rb1c"] = np.ascontiguousarray(ins["L0_ffn_router_b1"].reshape(8, 128).T)
    host["vfwdF"] = np.ascontiguousarray(ins["L0_ffn_V_fwd"].transpose(1, 0, 2).reshape(1024, 512))
    host["wfwdF"] = np.ascontiguousarray(ins["L0_ffn_W_fwd"].transpose(1, 0, 2).reshape(1024, 512))
    host["winvexF"] = np.ascontiguousarray(ins["L0_ffn_W_inv_exp"].transpose(1, 0, 2).reshape(1024, 512))
    host["vinvF"] = np.ascontiguousarray(ins["L0_ffn_V_inv"].transpose(1, 0, 2).reshape(1024, 512))
    host["UfF"] = np.ascontiguousarray(ins["L0_ffn_U_fwd"].transpose(0, 2, 1).reshape(512, 1024))
    host["UiF"] = np.ascontiguousarray(ins["L0_ffn_U_inv"].transpose(0, 2, 1).reshape(512, 1024))
    host["bfw"] = np.ascontiguousarray(ins["L0_ffn_b_fwd"])
    host["biv"] = np.ascontiguousarray(ins["L0_ffn_b_inv"])
    host["b2pe_in"] = np.tile(np.concatenate(
        [ins["L0_ffn_router_b2"], ins["L0_ffn_expert_bias"]]).reshape(1, 16), (128, 1)).astype(np.float32)
    host["alpha_in"] = np.tile(np.asarray(ins["L0_ffn_alpha_bi"]).reshape(1, 1), (128, 1)).astype(np.float32)
    host["ident_in"] = np.eye(128, dtype=BF)
    m = np.zeros((128, 128), np.float32)
    for b in range(2):
        i0 = b * 64
        m[i0:i0 + 64, i0:i0 + 64] = np.triu(np.ones((64, 64), np.float32))
    host["maskP_in"] = m
    Ec = np.zeros((8, 512), BF)
    for kk in range(8):
        Ec[kk, kk * 64:(kk + 1) * 64] = 1.0
    host["Econst_in"] = Ec
    return host


def kernel(**inputs):
    x = np.ascontiguousarray(
        np.asarray(inputs["L0_attn_x"]).reshape(4 * 8192, 1024).astype(np.float32))
    host = make_host_inputs(inputs)

    if "nc" not in _NC_CACHE:
        _NC_CACHE["nc"] = build_nc()
    nc = _NC_CACHE["nc"]

    in_maps = []
    for cid in range(NCORES):
        half = cid % 2
        im = dict(host)
        im["x"] = x[cid * T:(cid + 1) * T]
        rpj = np.zeros((128, 32), np.float32)
        for gi in range(32):
            base = half * T + gi * 128
            rpj[:, gi] = 1.0 / (base + 1 + np.arange(128, dtype=np.float32))
        im["rp_in"] = rpj
        ev = 1.0 if half == 0 else 0.0
        im["evenm_in"] = np.full((128, 1), ev, np.float32)
        im["oddm_in"] = np.full((128, 1), 1.0 - ev, np.float32)
        in_maps.append(im)

    res = run_bass_kernel_spmd(nc, in_maps, core_ids=list(range(NCORES)))
    out = np.empty((4 * 8192, 1024), np.float32)
    for cid in range(NCORES):
        out[cid * T:(cid + 1) * T] = np.asarray(res.results[cid]["y"], np.float32)
    return out.reshape(4, 8192, 1024)


# version 13 (final)
# speedup vs baseline: 1.3391x; 1.0017x over previous
import numpy as np
import ml_dtypes
from contextlib import ExitStack

import concourse.bass as bass
import concourse.mybir as mybir
from concourse.alu_op_type import AluOpType
from concourse.bass_utils import run_bass_kernel_spmd

F32 = mybir.dt.float32
BF16 = mybir.dt.bfloat16
F16 = mybir.dt.float16
AF = mybir.ActivationFunctionType
OP = AluOpType
AX = mybir.AxisListType

NCORES = 8
T = 4096
TILE = 512
NT = T // TILE
NG = TILE // 128
E = 1024
H = 8
EPS = 1e-5
INV_E = 1.0 / 1024.0
BF = ml_dtypes.bfloat16


class Ctr:
    def __init__(self, sem):
        self.sem = sem
        self.n = 0

    def inc(self, instr, k=1):
        instr.then_inc(self.sem, k)
        self.n += k

    def wait(self, eng, n=None):
        eng.wait_ge(self.sem, self.n if n is None else n)

    def selfw(self, eng):
        eng.wait_ge(self.sem, self.n)


def build_nc(debug_taps=(), phases=("A", "AR1", "B1", "B2", "AR2", "C"), a_tiles=NT, a_stages=9):
    nc = bass.Bass("TRN2", target_bir_lowering=False)
    din = {}

    def dr(name, shape, dt, kind=None):
        t = nc.dram_tensor(name, shape, dt, kind=kind) if kind else nc.dram_tensor(name, shape, dt)
        din[name] = t
        return t

    x_d = dr("x", [T, E], F32, "ExternalInput")
    y_d = dr("y", [T, E], F32, "ExternalOutput")
    for nm in ("wvT", "wqT", "wlgT", "woutT", "wQT", "wKT", "winvT", "rw1T", "wOT"):
        dr(nm, [E, E], F32, "ExternalInput")
    dr("wg1T", [E, 128], F32, "ExternalInput")
    dr("wg2T", [128, E], F32, "ExternalInput")
    dr("lnwb", [128, E], F32, "ExternalInput")
    for nm in ("vfwdF", "wfwdF", "winvexF", "vinvF"):
        dr(nm, [E, 512], F32, "ExternalInput")
    dr("rw2T", [E, 8], F32, "ExternalInput")
    dr("rb1c", [128, 8], F32, "ExternalInput")
    dr("b2pe_in", [128, 16], F32, "ExternalInput")
    dr("alpha_in", [128, 1], F32, "ExternalInput")
    dr("UfF", [512, E], F32, "ExternalInput")
    dr("UiF", [512, E], F32, "ExternalInput")
    dr("bfw", [8, E], F32, "ExternalInput")
    dr("biv", [8, E], F32, "ExternalInput")
    dr("ident_in", [128, 128], BF16, "ExternalInput")
    dr("maskP_in", [128, 128], F32, "ExternalInput")
    dr("Econst_in", [8, 512], BF16, "ExternalInput")
    dr("rp_in", [128, 32], F32, "ExternalInput")
    dr("evenm_in", [128, 1], F32, "ExternalInput")
    dr("oddm_in", [128, 1], F32, "ExternalInput")

    dbg = {}
    for nm in debug_taps:
        if nm == "S_dbg":
            dbg[nm] = dr(nm, [128, 1024], F32, "ExternalOutput")

    o_sp = dr("o_sp", [T, E], F16)
    qh_sp = dr("qh_sp", [H, 128, T], BF16)
    x1_sp = dr("x1_sp", [T, E], F32)
    xvw_sp = dr("xvw_sp", [2, 4, 128, T], F16)
    cum_sp = dr("cum_sp", [2, 4, 128, T], F16)
    w_sp = dr("w_sp", [2, 8, T], BF16)
    wb16 = {}
    for nm, w in (("wvT", E), ("wqT", E), ("wlgT", E), ("woutT", E), ("wQT", E),
                  ("wKT", E), ("winvT", E), ("rw1T", E), ("wOT", E),
                  ("vfwdF", 512), ("wfwdF", 512), ("winvexF", 512), ("vinvF", 512),
                  ("UfF", None), ("UiF", None)):
        if nm in ("UfF", "UiF"):
            wb16[nm] = dr(nm + "_b", [128, 4 * E], BF16)
        else:
            wb16[nm] = dr(nm + "_b", [128, 8 * w], BF16)
    wb16["wg1T"] = dr("wg1T_b", [128, 8 * 128], BF16)
    wb16["wg2T"] = dr("wg2T_b", [128, E], BF16)
    wb16["rw2T"] = dr("rw2T_b", [128, 64], BF16)
    wb16["bfw"] = dr("bfw_b", [8, E], BF16)
    wb16["biv"] = dr("biv_b", [8, E], BF16)
    cc1_in = dr("cc1_in", [128 * E], F32)
    cc1_out = dr("cc1_out", [128 * E], F32)
    cc2_in = dr("cc2_in", [128 * 8], F32)
    cc2_out = dr("cc2_out", [128 * 8], F32)

    with ExitStack() as ctx:
        Eo = ctx.enter_context
        ident = Eo(nc.sbuf_tensor("ident", [128, 128], BF16))
        maskP = Eo(nc.sbuf_tensor("maskP", [128, 128], F32))
        rp = Eo(nc.sbuf_tensor("rp", [128, 32], F32))
        evenm = Eo(nc.sbuf_tensor("evenm", [128, 1], F32))
        oddm = Eo(nc.sbuf_tensor("oddm", [128, 1], F32))
        S_in_bf = Eo(nc.sbuf_tensor("S_in_bf", [128, H * 128], BF16))
        totF = Eo(nc.sbuf_tensor("totF", [128, 4], F32))
        totI = Eo(nc.sbuf_tensor("totI", [128, 4], F32))
        alphac = Eo(nc.sbuf_tensor("alphac", [128, 1], F32))
        epscol = Eo(nc.sbuf_tensor("epscol", [128, 1], F32))
        b2pe_b = Eo(nc.sbuf_tensor("b2pe_b", [128, 8], F32))

        sems = {nm: Ctr(Eo(nc.semaphore(nm)))
                for nm in ("ld", "gld", "t", "v", "s", "g", "so", "cc")}
        ld, gld, tS, vS, sS, gS, so, ccS = (sems[n] for n in
                                            ("ld", "gld", "t", "v", "s", "g", "so", "cc"))

        def emit_ln_bn(blk, src_of, sm, extra_wait=None):
            def l1(vec):
                if extra_wait is not None:
                    extra_wait(vec)
                vS.selfw(vec)
                for g in range(NG):
                    for h in range(2):
                        vS.inc(vec.bn_stats(sm["st"][:, (2 * g + h) * 6:(2 * g + h) * 6 + 6],
                                            src_of(g, h)))
                vS.selfw(vec)
                for g in range(NG):
                    vS.inc(vec.bn_aggr(sm["mv"][:, 2 * g:2 * g + 2],
                                       sm["st"][:, g * 12:g * 12 + 12]))

            blk.vector(l1)

            def l2(sc):
                vS.wait(sc)
                sS.selfw(sc)
                sS.inc(sc.activation(sm["rcol"][:, NG:2 * NG],
                                     bass.AP(sm["mv"], 1, [[2 * NG, 128], [2, NG]]),
                                     AF.Ln, bias=epscol[:, 0:1]))
                sS.selfw(sc)
                sS.inc(sc.activation(sm["rcol"][:, 0:NG], sm["rcol"][:, NG:2 * NG],
                                     AF.Exp, scale=-0.5))

            blk.scalar(l2)

        def emit_norm_transpose(blk, x_src, xh_tm, xh_fm, psTP, sm, g, xoff=0):
            def n1(vec):
                sS.wait(vec)
                tS.wait(vec)
                vS.selfw(vec)
                vS.inc(vec.tensor_scalar(xh_tm[:],
                                         x_src[:, xoff + g * 1024: xoff + (g + 1) * 1024],
                                         sm["mv"][:, 2 * g:2 * g + 1],
                                         sm["rcol"][:, g:g + 1],
                                         OP.subtract, OP.mult))

            blk.vector(n1)

            def n2(te):
                vS.wait(te)
                for ci in range(8):
                    ins = te.transpose(psTP[:, ci * 128:(ci + 1) * 128],
                                       xh_tm[:, ci * 128:(ci + 1) * 128], ident[:])
                tS.inc(ins)

            blk.tensor(n2)

            def n3(vec):
                tS.wait(vec)
                vS.selfw(vec)
                vS.inc(vec.tensor_copy(
                    bass.AP(xh_fm, g * 128, [[4096, 128], [512, 8], [1, 128]]),
                    bass.AP(psTP, 0, [[1024, 128], [128, 8], [1, 128]])))

            blk.vector(n3)

        b2tmp = Eo(nc.sbuf_tensor("b2tmp", [128, 16], F32))
        blk0_cm = nc.Block()
        blk0 = blk0_cm.__enter__()

        def p0_sync(sync):
            for dst, nm in ((ident, "ident_in"), (maskP, "maskP_in"), (rp, "rp_in"),
                            (evenm, "evenm_in"), (oddm, "oddm_in")):
                ld.selfw(sync)
                ld.inc(sync.dma_start(out=dst[:], in_=din[nm][:, :]), 16)
            ld.selfw(sync)
            ld.inc(sync.dma_start(out=b2tmp[:], in_=din["b2pe_in"][:, :]), 16)
            ld.selfw(sync)
            ld.inc(sync.dma_start(out=alphac[:], in_=din["alpha_in"][:, :]), 16)

        blk0.sync(p0_sync)

        def p0_g(gp):
            ld.wait(gp)
            gS.selfw(gp)
            gS.inc(gp.memset(epscol[:], EPS))
            gS.selfw(gp)
            gS.inc(gp.tensor_add(b2pe_b[:], b2tmp[:, 0:8], b2tmp[:, 8:16]))

        blk0.gpsimd(p0_g)

        with nc.sbuf_tensor("wstg32", [128, 2 * 8192], F32) as wstg32, \
             nc.sbuf_tensor("wstg16", [128, 2 * 8192], BF16) as wstg16:
            plist = []
            for nm, w in (("wvT", E), ("wqT", E), ("wlgT", E), ("wg1T", 128),
                          ("woutT", E), ("wQT", E), ("wKT", E), ("winvT", E),
                          ("rw1T", E), ("wOT", E), ("vfwdF", 512), ("wfwdF", 512),
                          ("winvexF", 512), ("vinvF", 512)):
                plist.append((nm, [[w, 128], [128 * w, 8], [1, w]], 8 * w))
            plist.append(("wg2T", [[E, 128], [1, E]], E))
            plist.append(("rw2T", [[8, 128], [128 * 8, 8], [1, 8]], 64))
            plist.append(("UfF", [[E, 128], [128 * E, 4], [1, E]], 4 * E))
            plist.append(("UiF", [[E, 128], [128 * E, 4], [1, E]], 4 * E))

            wls = [Ctr(Eo(nc.semaphore("wl0"))), Ctr(Eo(nc.semaphore("wl1")))]
            wss = [Ctr(Eo(nc.semaphore("ws0"))), Ctr(Eo(nc.semaphore("ws1")))]
            cast_done = []
            for i, (nm, ap, width) in enumerate(plist):
                s = i % 2
                buf = s * 8192

                def w_ld(sync, nm=nm, ap=ap, width=width, buf=buf, i=i, s=s):
                    if i >= 2:
                        vS.wait(sync, cast_done[i - 2])
                    wls[s].inc(sync.dma_start(
                        out=bass.AP(wstg32, buf, [[2 * 8192, 128], [1, width]]),
                        in_=bass.AP(din[nm], 0, ap)), 16)

                blk0.sync(w_ld)
                wl_n = wls[s].n
                ws_n = wss[s].n

                def w_cast(vec, nm=nm, width=width, buf=buf, i=i, s=s,
                           wl_n=wl_n, ws_n=ws_n):
                    vec.wait_ge(wls[s].sem, wl_n)
                    if i >= 2:
                        vec.wait_ge(wss[s].sem, ws_n)
                    vS.selfw(vec)
                    vS.inc(vec.tensor_copy(
                        bass.AP(wstg16, buf, [[2 * 8192, 128], [1, width]]),
                        bass.AP(wstg32, buf, [[2 * 8192, 128], [1, width]])))
                    cast_done.append(vS.n)

                blk0.vector(w_cast)

                def w_st(sc, nm=nm, width=width, buf=buf, i=i, s=s):
                    vS.wait(sc, cast_done[i])
                    wss[s].inc(sc.dma_start(
                        out=bass.AP(wb16[nm], 0, [[width, 128], [1, width]]),
                        in_=bass.AP(wstg16, buf, [[2 * 8192, 128], [1, width]])), 16)

                blk0.scalar(w_st)
            wst_final = (wss[0].n, wss[1].n)

            with nc.sbuf_tensor("bstg32", [8, 2 * E], F32) as bstg32, \
                 nc.sbuf_tensor("bstg16", [8, 2 * E], BF16) as bstg16:
                def b_ld(sync):
                    ld.selfw(sync)
                    ld.inc(sync.dma_start(
                        out=bass.AP(bstg32, 0, [[2 * E, 8], [1, E]]),
                        in_=din["bfw"][:, :]), 16)
                    ld.selfw(sync)
                    ld.inc(sync.dma_start(
                        out=bass.AP(bstg32, E, [[2 * E, 8], [1, E]]),
                        in_=din["biv"][:, :]), 16)

                blk0.sync(b_ld)

                def b_cast(vec):
                    ld.wait(vec)
                    vS.selfw(vec)
                    vS.inc(vec.tensor_copy(bstg16[:, :], bstg32[:, :]))

                blk0.vector(b_cast)

                def b_st(sc):
                    vS.wait(sc)
                    sS.selfw(sc)
                    so.selfw(sc)
                    so.inc(sc.dma_start(out=bass.AP(wb16["bfw"], 0, [[E, 8], [1, E]]),
                                        in_=bass.AP(bstg16, 0, [[2 * E, 8], [1, E]])), 16)
                    so.selfw(sc)
                    so.inc(sc.dma_start(out=bass.AP(wb16["biv"], 0, [[E, 8], [1, E]]),
                                        in_=bass.AP(bstg16, E, [[2 * E, 8], [1, E]])), 16)
                    so.wait(sc)

                blk0.scalar(b_st)
        blk0_cm.__exit__(None, None, None)

        if "A" not in phases:
            return nc
        with ExitStack() as actx:
            Ea = actx.enter_context
            wv_sb = Ea(nc.sbuf_tensor("wv_sb", [128, 8 * E], BF16))
            wq_sb = Ea(nc.sbuf_tensor("wq_sb", [128, 8 * E], BF16))
            wlg_sb = Ea(nc.sbuf_tensor("wlg_sb", [128, 8 * E], BF16))
            x_sb = Ea(nc.sbuf_tensor("xA", [128, 4096], F32))
            xh_tm = Ea(nc.sbuf_tensor("xhA_tm", [128, 1024], BF16))
            xh_fm = Ea(nc.sbuf_tensor("xhA_fm", [128, 4096], BF16))
            v_tm = Ea(nc.sbuf_tensor("v_tm", [128, 4096], BF16))
            qpre = Ea(nc.sbuf_tensor("qpre", [128, 4096], F16))
            f_sb = Ea(nc.sbuf_tensor("f_sb", [128, 4096], F32))
            kf = Ea(nc.sbuf_tensor("kf", [128, 4096], F16))
            Acum = Ea(nc.sbuf_tensor("Acum", [128, 4096], F32))
            carryA = Ea(nc.sbuf_tensor("carryA", [128, 8], F32))
            ends = Ea(nc.sbuf_tensor("ends", [128, 64], F32))
            refs = Ea(nc.sbuf_tensor("refs", [128, 64], F32))
            eACall = Ea(nc.sbuf_tensor("eACall", [128, 64], F32))
            qt_bf = Ea(nc.sbuf_tensor("qt_bf", [128, 4096], BF16))
            kt_bf = Ea(nc.sbuf_tensor("kt_bf", [128, 4096], BF16))
            kh_full = Ea(nc.sbuf_tensor("kh_full", [128, 4096], BF16))
            khtm = Ea(nc.sbuf_tensor("khtm", [128, 4096], BF16))
            qhat_bf = Ea(nc.sbuf_tensor("qhat_bf", [128, 4096], BF16))
            psb_buf = Ea(nc.sbuf_tensor("psb_buf", [128, 1024], BF16))
            S_sb = Ea(nc.sbuf_tensor("S_sb", [128, 1024], F32))
            S_bf = Ea(nc.sbuf_tensor("S_bf", [128, 1024], BF16))
            S_bf2 = Ea(nc.sbuf_tensor("S_bf2", [128, 1024], BF16))
            o16 = Ea(nc.sbuf_tensor("o16", [128, 4096], F16))
            sm = {"st": Ea(nc.sbuf_tensor("smA_st", [128, 48], F32)),
                  "mv": Ea(nc.sbuf_tensor("smA_mv", [128, 8], F32)),
                  "rcol": Ea(nc.sbuf_tensor("smA_rcol", [128, 8], F32))}
            psTP = Ea(nc.psum_tensor("psTPa", [128, 1024], BF16))
            psProd = Ea(nc.psum_tensor("psProdA", [128, 2048], F32))
            psS = Ea(nc.psum_tensor("psSa", [128, 1024], F32))
            blkA = Ea(nc.Block())

            def pa_init(gp):
                gS.selfw(gp)
                gS.inc(gp.memset(S_sb[:], 0.0))
                gS.selfw(gp)
                gS.inc(gp.memset(S_bf[:], 0.0))
                gS.selfw(gp)
                gS.inc(gp.memset(carryA[:], 0.0))

            blkA.gpsimd(pa_init)

            def pa_wld(sync):
                so.wait(sync)
                sync.wait_ge(wss[0].sem, wst_final[0])
                sync.wait_ge(wss[1].sem, wst_final[1])
                for nm, dst in (("wvT", wv_sb), ("wqT", wq_sb), ("wlgT", wlg_sb)):
                    ld.selfw(sync)
                    ld.inc(sync.dma_start(
                        out=bass.AP(dst, 0, [[8 * E, 128], [1, 8 * E]]),
                        in_=wb16[nm][:, :]), 16)

            blkA.sync(pa_wld)

            prevA = {"vb_rel": 0, "sb_e0": None, "vb_a4m_end": 0}
            for j in range(a_tiles):
                def a1(sync, j=j, pv=dict(prevA)):
                    if pv["sb_e0"] is not None:
                        sync.wait_ge(vS.sem, pv["vb_rel"])
                        sync.wait_ge(sS.sem, pv["sb_e0"] + 4)
                    else:
                        vS.wait(sync)
                        sS.wait(sync)
                        tS.wait(sync)
                        so.wait(sync)
                    ld.selfw(sync)
                    ld.inc(sync.dma_start(
                        out=bass.AP(x_sb, 0, [[4096, 128], [1024, 4], [1, 1024]]),
                        in_=bass.AP(x_d, j * TILE * E, [[E, 128], [128 * E, 4], [1, E]]),
                    ), 16)

                blkA.sync(a1)

                ldw = (ld.sem, ld.n)
                gw = (ld.sem, ld.n)

                emit_ln_bn(blkA, lambda g, h: x_sb[:, g * 1024 + h * 512: g * 1024 + h * 512 + 512],
                           sm, extra_wait=(lambda vec, w=ldw: vec.wait_ge(*w)))
                for g in range(NG):
                    emit_norm_transpose(blkA, x_sb, xh_tm, xh_fm, psTP, sm, g)

                if a_stages < 3:
                    continue
                vb_ve = vS.n

                def a3v(te, gw=gw):
                    te.wait_ge(*gw)
                    vS.wait(te)
                    sS.wait(te)
                    for g in range(NG):
                        if g >= 1:
                            te.wait_ge(vS.sem, vb_ve + g)
                        for half in range(2):
                            for fc in range(8):
                                ins = te.matmul(
                                    psProd[:, half * 512:(half + 1) * 512],
                                    xh_fm[:, fc * 512 + g * 128: fc * 512 + (g + 1) * 128],
                                    wv_sb[:, fc * E + half * 512: fc * E + (half + 1) * 512],
                                    start=(fc == 0), stop=(fc == 7))
                        tS.inc(ins)

                tb_v = tS.n
                blkA.tensor(a3v)

                def a3ve(vec):
                    for g in range(NG):
                        tS.wait(vec, tb_v + g + 1)
                        vS.selfw(vec)
                        vS.inc(vec.tensor_copy(v_tm[:, g * 1024:(g + 1) * 1024],
                                               psProd[:, 0:1024]))

                blkA.vector(a3ve)
                assert vS.n == vb_ve + NG

                sb_q0 = sS.n

                def a3q(te, sb_q0=sb_q0):
                    vS.wait(te)
                    sS.wait(te)
                    for cc in range(8):
                        reg = 1024 + (cc % 2) * 512
                        if cc >= 2:
                            te.wait_ge(sS.sem, sb_q0 + cc - 1)
                        for fc in range(8):
                            ins = te.matmul(psProd[:, reg:reg + 512],
                                            wq_sb[:, fc * E + cc * 128: fc * E + (cc + 1) * 128],
                                            xh_fm[:, fc * 512:(fc + 1) * 512],
                                            start=(fc == 0), stop=(fc == 7))
                        tS.inc(ins)

                tb_q = tS.n
                blkA.tensor(a3q)

                def a3qs(sc, tb_q=tb_q, pv=dict(prevA)):
                    sc.wait_ge(vS.sem, pv["vb_a4m_end"])
                    for cc in range(8):
                        tS.wait(sc, tb_q + cc + 1)
                        reg = 1024 + (cc % 2) * 512
                        sS.selfw(sc)
                        sS.inc(sc.activation(qpre[:, cc * 512:(cc + 1) * 512],
                                             psProd[:, reg:reg + 512], AF.Silu))

                blkA.scalar(a3qs)
                assert sS.n == sb_q0 + 8

                sb_l0 = sS.n

                def a3l(te, sb_l0=sb_l0):
                    for lc in range(8):
                        reg = 1024 + (lc % 2) * 512
                        if lc < 2:
                            te.wait_ge(sS.sem, sb_q0 + 7 + lc)
                        else:
                            te.wait_ge(sS.sem, sb_l0 + 2 * (lc - 2) + 2)
                        for fc in range(8):
                            ins = te.matmul(psProd[:, reg:reg + 512],
                                            wlg_sb[:, fc * E + lc * 128: fc * E + (lc + 1) * 128],
                                            xh_fm[:, fc * 512:(fc + 1) * 512],
                                            start=(fc == 0), stop=(fc == 7))
                        tS.inc(ins)

                tb_l = tS.n
                blkA.tensor(a3l)

                def a3ls(sc, tb_l=tb_l, pv=dict(prevA)):
                    sc.wait_ge(vS.sem, pv["vb_a4m_end"])
                    for lc in range(8):
                        tS.wait(sc, tb_l + lc + 1)
                        reg = 1024 + (lc % 2) * 512
                        sS.selfw(sc)
                        sS.inc(sc.activation(f_sb[:, lc * 512:(lc + 1) * 512],
                                             psProd[:, reg:reg + 512], AF.Sigmoid))
                        sS.selfw(sc)
                        sS.inc(sc.activation(kf[:, lc * 512:(lc + 1) * 512],
                                             psProd[:, reg:reg + 512], AF.Sigmoid,
                                             scale=-1.0))

                blkA.scalar(a3ls)

                if a_stages < 4:
                    continue
                def a3g(sc):
                    sS.selfw(sc)
                    sS.inc(sc.activation(f_sb[:], f_sb[:], AF.Ln))

                blkA.scalar(a3g)

                def a4v(vec):
                    gS.wait(vec)
                    sS.wait(vec)
                    for h in range(H):
                        vS.selfw(vec)
                        vS.inc(vec.tensor_tensor_scan(
                            Acum[:, h * 512:(h + 1) * 512],
                            f_sb[:, h * 512:(h + 1) * 512],
                            f_sb[:, h * 512:(h + 1) * 512],
                            carryA[:, h:h + 1], OP.add, OP.bypass))

                blkA.vector(a4v)

                def a4r(vec):
                    vS.selfw(vec)
                    vS.inc(vec.tensor_copy(ends[:],
                                           bass.AP(Acum, 63, [[4096, 128], [64, 64]])))
                    vS.selfw(vec)
                    vS.inc(vec.tensor_copy(bass.AP(refs, 0, [[64, 128], [8, 8]]),
                                           carryA[:, 0:8]))
                    vS.selfw(vec)
                    vS.inc(vec.tensor_copy(bass.AP(refs, 1, [[64, 128], [8, 8], [1, 7]]),
                                           bass.AP(ends, 0, [[64, 128], [8, 8], [1, 7]])))
                    vS.selfw(vec)
                    gS.inc(vec.tensor_copy(carryA[:, 0:8],
                                           bass.AP(ends, 7, [[64, 128], [8, 8]])))
                    vS.selfw(vec)
                    vS.inc(vec.tensor_tensor(
                        bass.AP(f_sb, 0, [[4096, 128], [64, 64], [1, 64]]),
                        bass.AP(Acum, 0, [[4096, 128], [64, 64], [1, 64]]),
                        bass.AP(ends, 0, [[64, 128], [1, 64], [0, 64]]), OP.subtract))
                    vS.selfw(vec)
                    vS.inc(vec.tensor_tensor(
                        bass.AP(x_sb, 0, [[4096, 128], [64, 64], [1, 64]]),
                        bass.AP(Acum, 0, [[4096, 128], [64, 64], [1, 64]]),
                        bass.AP(refs, 0, [[64, 128], [1, 64], [0, 64]]), OP.subtract))

                blkA.vector(a4r)
                vb_rel = vS.n
                vb_scrA2n = vS.n - 1

                sb_e0 = sS.n

                def a4e(sc, vb_rel=vb_rel, vb_scrA2n=vb_scrA2n):
                    so.wait(sc)
                    vS.wait(sc, vb_scrA2n)
                    sS.selfw(sc)
                    sS.inc(sc.activation(kh_full[:],
                                         bass.AP(f_sb, 0, [[4096, 128], [1, 4096]]),
                                         AF.Exp, scale=-1.0))
                    vS.wait(sc, vb_rel)
                    sS.selfw(sc)
                    sS.inc(sc.activation(eACall[:],
                                         bass.AP(x_sb, 63, [[4096, 128], [64, 64]]),
                                         AF.Exp))
                    sS.selfw(sc)
                    sS.inc(sc.activation(qt_bf[:],
                                         bass.AP(x_sb, 0, [[4096, 128], [1, 4096]]),
                                         AF.Exp))
                    sS.selfw(sc)
                    sS.inc(sc.activation(kt_bf[:],
                                         bass.AP(x_sb, 0, [[4096, 128], [1, 4096]]),
                                         AF.Exp, scale=-1.0))
                    sS.selfw(sc)
                    sS.inc(sc.activation(qhat_bf[:], Acum[:], AF.Exp))

                blkA.scalar(a4e)

                def a4m(vec, sb_e0=sb_e0):
                    so.wait(vec)
                    vec.wait_ge(sS.sem, sb_e0 + 1)
                    vS.selfw(vec)
                    vS.inc(vec.tensor_mul(kh_full[:], kh_full[:], kf[:]))
                    vec.wait_ge(sS.sem, sb_e0 + 3)
                    vS.selfw(vec)
                    vS.inc(vec.tensor_mul(qt_bf[:], qt_bf[:], qpre[:]))
                    vec.wait_ge(sS.sem, sb_e0 + 4)
                    vS.selfw(vec)
                    vS.inc(vec.tensor_mul(kt_bf[:], kt_bf[:], kf[:]))
                    vec.wait_ge(sS.sem, sb_e0 + 5)
                    vS.selfw(vec)
                    vS.inc(vec.tensor_mul(qhat_bf[:], qhat_bf[:], qpre[:]))

                blkA.vector(a4m)
                vb_kh = vS.n - 3
                vb_a4m_end = vS.n
                prevA["vb_rel"] = vb_rel
                prevA["sb_e0"] = sb_e0
                prevA["vb_a4m_end"] = vb_a4m_end

                for tb in range(4):
                    def a4tp(te, tb=tb, vb_kh=vb_kh):
                        te.wait_ge(vS.sem, vb_kh)
                        if tb >= 1:
                            vS.wait(te)
                        for bi in range(8):
                            h = tb * 2 + bi // 4
                            g_ = bi % 4
                            ins = te.transpose(
                                psTP[:, bi * 128:(bi + 1) * 128],
                                kh_full[:, h * 512 + g_ * 128: h * 512 + (g_ + 1) * 128],
                                ident[:])
                        tS.inc(ins)

                    blkA.tensor(a4tp)

                    def a4te(vec, tb=tb):
                        tS.wait(vec)
                        vS.selfw(vec)
                        vS.inc(vec.tensor_copy(khtm[:, tb * 1024:(tb + 1) * 1024],
                                               psTP[:, 0:1024]))

                    blkA.vector(a4te)

                if a_stages < 5:
                    continue
                for g in range(NG):
                    cA = 2 * g
                    cB = 2 * g + 1
                    vb_pre = vS.n

                    def a5suA(te, g=g, vb_pre=vb_pre):
                        te.wait_ge(vS.sem, vb_pre)
                        for h in range(H):
                            ins = te.matmul(
                                psProd[:, 1024 + h * 128: 1024 + (h + 1) * 128],
                                khtm[0:64, h * 512 + g * 128: h * 512 + (g + 1) * 128],
                                v_tm[0:64, g * 1024 + h * 128: g * 1024 + (h + 1) * 128],
                                start=True, stop=True)
                            tS.inc(ins)

                    blkA.tensor(a5suA)
                    tb_suA = tS.n

                    def a5sc(te, g=g):
                        for h in range(H):
                            ins = te.matmul(
                                psProd[:, h * 128:(h + 1) * 128],
                                kt_bf[:, h * 512 + g * 128: h * 512 + (g + 1) * 128],
                                qt_bf[:, h * 512 + g * 128: h * 512 + (g + 1) * 128],
                                start=True, stop=True)
                            tS.inc(ins)

                    blkA.tensor(a5sc)
                    tb_sc = tS.n

                    def a5suB(te, g=g):
                        for h in range(H):
                            ins = te.matmul(
                                psS[:, h * 128:(h + 1) * 128],
                                khtm[64:128, h * 512 + g * 128: h * 512 + (g + 1) * 128],
                                v_tm[64:128, g * 1024 + h * 128: g * 1024 + (h + 1) * 128],
                                start=True, stop=True)
                            tS.inc(ins)

                    blkA.tensor(a5suB)
                    tb_suB = tS.n

                    def a5v1(vec, cA=cA, tb_suA=tb_suA):
                        tS.wait(vec, tb_suA)
                        gS.wait(vec)
                        vS.selfw(vec)
                        vS.inc(vec.tensor_tensor(
                            S_sb[:], S_sb[:],
                            bass.AP(eACall, cA, [[64, 128], [8, 8], [0, 128]]),
                            OP.mult))
                        vS.selfw(vec)
                        vS.inc(vec.tensor_add(S_sb[:], S_sb[:], psProd[:, 1024:2048]))

                    blkA.vector(a5v1)
                    vb_stt1 = vS.n

                    def a5vm(vec, tb_sc=tb_sc):
                        tS.wait(vec, tb_sc)
                        vS.selfw(vec)
                        vS.inc(vec.tensor_tensor(
                            psb_buf[:], psProd[:, 0:1024],
                            bass.AP(maskP, 0, [[128, 128], [0, 8], [1, 128]]),
                            OP.mult))

                    blkA.vector(a5vm)
                    vb_mask = vS.n

                    def a5g1(gp, vb_stt1=vb_stt1):
                        gp.wait_ge(vS.sem, vb_stt1)
                        gS.selfw(gp)
                        gS.inc(gp.tensor_copy(S_bf2[:], S_sb[:]))

                    blkA.gpsimd(a5g1)
                    gb_c1 = gS.n

                    def a5o1(te, g=g, vb_mask=vb_mask):
                        te.wait_ge(vS.sem, vb_mask)
                        gS.wait(te)
                        for h in range(H):
                            te.matmul(psProd[:, h * 128:(h + 1) * 128],
                                      psb_buf[:, h * 128:(h + 1) * 128],
                                      v_tm[:, g * 1024 + h * 128: g * 1024 + (h + 1) * 128],
                                      start=(h == 0) or (h == 4), stop=False,
                                      skip_group_check=True)
                            ins = te.matmul(
                                psProd[0:64, h * 128:(h + 1) * 128],
                                qt_bf[:, h * 512 + g * 128: h * 512 + g * 128 + 64],
                                S_bf[:, h * 128:(h + 1) * 128],
                                start=False, stop=False, skip_group_check=True)
                            tS.inc(ins)

                    blkA.tensor(a5o1)
                    tb_o1 = tS.n

                    def a5v2(vec, cB=cB, tb_suB=tb_suB):
                        tS.wait(vec, tb_suB)
                        gS.wait(vec)
                        vS.selfw(vec)
                        vS.inc(vec.tensor_tensor(
                            S_sb[:], S_sb[:],
                            bass.AP(eACall, cB, [[64, 128], [8, 8], [0, 128]]),
                            OP.mult))
                        vS.selfw(vec)
                        vS.inc(vec.tensor_add(S_sb[:], S_sb[:], psS[:, 0:1024]))

                    blkA.vector(a5v2)
                    vb_stt2 = vS.n

                    def a5g2(gp, vb_stt2=vb_stt2, tb_o1=tb_o1):
                        gp.wait_ge(vS.sem, vb_stt2)
                        gp.wait_ge(tS.sem, tb_o1)
                        gS.selfw(gp)
                        gS.inc(gp.tensor_copy(S_bf[:], S_sb[:]))

                    blkA.gpsimd(a5g2)
                    gb_c2 = gS.n

                    def a5o2(te, g=g, gb_c1=gb_c1):
                        te.wait_ge(gS.sem, gb_c1)
                        for h in range(H):
                            ins = te.matmul(
                                psProd[64:128, h * 128:(h + 1) * 128],
                                qt_bf[:, h * 512 + g * 128 + 64: h * 512 + (g + 1) * 128],
                                S_bf2[:, h * 128:(h + 1) * 128],
                                start=False, stop=(h == 3) or (h == 7),
                                skip_group_check=True)
                            tS.inc(ins)

                    blkA.tensor(a5o2)

                    def a5oe(vec, g=g):
                        tS.wait(vec)
                        vS.selfw(vec)
                        vS.inc(vec.tensor_copy(o16[:, g * 1024:(g + 1) * 1024],
                                               psProd[:, 0:1024]))

                    blkA.vector(a5oe)

                if a_stages < 6:
                    continue

                def a6(sc, j=j):
                    vS.wait(sc)
                    sS.selfw(sc)
                    so.selfw(sc)
                    so.inc(sc.dma_start(
                        out=bass.AP(o_sp, j * TILE * E, [[E, 128], [128 * E, 4], [1, E]]),
                        in_=bass.AP(o16, 0, [[4096, 128], [1024, 4], [1, 1024]])), 16)
                    sS.selfw(sc)
                    so.selfw(sc)
                    so.inc(sc.dma_start(
                        out=bass.AP(qh_sp, j * TILE, [[T, 128], [128 * T, 8], [1, 512]]),
                        in_=bass.AP(qhat_bf, 0, [[4096, 128], [512, 8], [1, 512]])), 16)

                blkA.scalar(a6)

            def ae_v(vec):
                gS.wait(vec)
                vS.selfw(vec)
                vS.inc(vec.tensor_scalar(S_sb[:], S_sb[:], evenm[:, 0:1], None,
                                         OP.mult, OP.bypass))

            blkA.vector(ae_v)

            if "S_dbg" in dbg:
                def ae_d(sc):
                    vS.wait(sc)
                    sS.selfw(sc)
                    so.selfw(sc)
                    so.inc(sc.dma_start(out=dbg["S_dbg"][:, :],
                                        in_=bass.AP(S_sb, 0, [[1024, 128], [1, 1024]])), 16)
                blkA.scalar(ae_d)

            def ae_g(gp):
                vS.wait(gp)
                so.wait(gp)
                gld.selfw(gp)
                gld.inc(gp.dma_start(out=cc1_in.ap(),
                                     in_=bass.AP(S_sb, 0, [[1024, 128], [1, 1024]])), 16)

            blkA.gpsimd(ae_g)

        if "AR1" not in phases:
            return nc
        sin_f32 = Eo(nc.sbuf_tensor("sin_f32", [128, 1024], F32))
        blkC1_cm = nc.Block()
        blkC1 = blkC1_cm.__enter__()

        def cc1f(gp):
            gld.wait(gp)
            ccS.inc(gp.collective_compute(
                "AllReduce", OP.add,
                replica_groups=[[0, 1], [2, 3], [4, 5], [6, 7]],
                ins=[cc1_in.ap().opt()], outs=[cc1_out.ap().opt()]))
            ccS.wait(gp)
            gld.selfw(gp)
            gld.inc(gp.dma_start(out=sin_f32[:],
                                 in_=bass.AP(cc1_out, 0, [[1024, 128], [1, 1024]])), 16)

        blkC1.gpsimd(cc1f)

        def cc1v(vec):
            gld.wait(vec)
            vS.selfw(vec)
            vS.inc(vec.tensor_scalar(sin_f32[:], sin_f32[:], oddm[:, 0:1], None,
                                     OP.mult, OP.bypass))
            vS.selfw(vec)
            gS.inc(vec.tensor_copy(S_in_bf[:], sin_f32[:]))

        blkC1.vector(cc1v)
        blkC1_cm.__exit__(None, None, None)

        if "B1" not in phases:
            return nc
        with ExitStack() as bctx:
            Eb = bctx.enter_context
            wg1_sb = Eb(nc.sbuf_tensor("wg1_sb", [128, 8 * 128], BF16))
            wg2_sb = Eb(nc.sbuf_tensor("wg2_sb", [128, E], BF16))
            wout_sb = Eb(nc.sbuf_tensor("wout_sb", [128, 8 * E], BF16))
            lnwb_sb = Eb(nc.sbuf_tensor("lnwb_sb", [128, E], F32))
            x_sb = Eb(nc.sbuf_tensor("xB", [128, 2 * 4096], F32))
            xh_tm = Eb(nc.sbuf_tensor("xhB_tm", [128, 1024], BF16))
            xh_fm = Eb(nc.sbuf_tensor("xhB_fm", [128, 4096], BF16))
            y1_bf = Eb(nc.sbuf_tensor("y1_bf", [128, 512], BF16))
            gate_t = Eb(nc.sbuf_tensor("gate_t", [128, 4096], F32))
            o16t = Eb(nc.sbuf_tensor("o16t", [128, 4096], F16))
            qht = Eb(nc.sbuf_tensor("qht", [128, 4096], BF16))
            ofull = Eb(nc.sbuf_tensor("ofull", [128, 1024], F32))
            og_t = Eb(nc.sbuf_tensor("og_t", [128, 4096], F32))
            araw_t = Eb(nc.sbuf_tensor("araw_t", [128, 4096], F32))
            ohat_scr = Eb(nc.sbuf_tensor("ohat_scr", [128, 1024], F32))
            ohat_bf = Eb(nc.sbuf_tensor("ohat_bf", [128, 1024], BF16))
            ohat_fm = Eb(nc.sbuf_tensor("ohat_fm", [128, 4096], BF16))
            x1g = Eb(nc.sbuf_tensor("x1g", [128, 1024], F32))
            t2scr = Eb(nc.sbuf_tensor("t2B", [128, 1024], F32))
            sm = {"st": Eb(nc.sbuf_tensor("smB_st", [128, 48], F32)),
                  "mv": Eb(nc.sbuf_tensor("smB_mv", [128, 8], F32)),
                  "rcol": Eb(nc.sbuf_tensor("smB_rcol", [128, 8], F32))}
            psTP = Eb(nc.psum_tensor("psTPb", [128, 1024], BF16))
            psM = Eb(nc.psum_tensor("psMb", [128, 2048], F32))
            psX = Eb(nc.psum_tensor("psXb", [128, 1024], F32))
            blkB = Eb(nc.Block())

            def b0(sync):
                so.wait(sync)
                sync.wait_ge(wss[0].sem, wst_final[0])
                sync.wait_ge(wss[1].sem, wst_final[1])
                ld.selfw(sync)
                ld.inc(sync.dma_start(
                    out=bass.AP(wg1_sb, 0, [[8 * 128, 128], [1, 8 * 128]]),
                    in_=wb16["wg1T"][:, :]), 16)
                ld.selfw(sync)
                ld.inc(sync.dma_start(
                    out=bass.AP(wg2_sb, 0, [[E, 128], [1, E]]),
                    in_=wb16["wg2T"][:, :]), 16)
                ld.selfw(sync)
                ld.inc(sync.dma_start(
                    out=bass.AP(wout_sb, 0, [[8 * E, 128], [1, 8 * E]]),
                    in_=wb16["woutT"][:, :]), 16)

            blkB.sync(b0)

            def b0s(sync):
                ld.selfw(sync)
                ld.inc(sync.dma_start(out=lnwb_sb[:], in_=din["lnwb"][:, :]), 16)

            blkB.sync(b0s)

            vb_b8_end = []
            vb_b5_end = []
            for j in range(NT):
                xo = (j % 2) * 4096

                def b1(sync, j=j):
                    if j == 0:
                        tS.wait(sync)
                        vS.wait(sync)
                        sS.wait(sync)
                        so.wait(sync)
                    elif j >= 2:
                        sync.wait_ge(vS.sem, vb_b8_end[j - 2])
                    ld.selfw(sync)
                    ld.inc(sync.dma_start(
                        out=bass.AP(x_sb, xo, [[2 * 4096, 128], [1024, 4], [1, 1024]]),
                        in_=bass.AP(x_d, j * TILE * E, [[E, 128], [128 * E, 4], [1, E]]),
                    ), 16)

                blkB.sync(b1)
                ldw = (ld.sem, ld.n)
                gw = (ld.sem, ld.n)

                emit_ln_bn(blkB, lambda g, h: x_sb[:, xo + g * 1024 + h * 512: xo + g * 1024 + h * 512 + 512],
                           sm, extra_wait=(lambda vec, w=ldw: vec.wait_ge(*w)))
                for g in range(NG):
                    emit_norm_transpose(blkB, x_sb, xh_tm, xh_fm, psTP, sm, g, xoff=xo)

                def b3t(te, gw=gw):
                    te.wait_ge(*gw)
                    vS.wait(te)
                    sS.wait(te)
                    for fc in range(8):
                        ins = te.matmul(psM[:, 0:512],
                                        wg1_sb[:, fc * 128:(fc + 1) * 128],
                                        xh_fm[:, fc * 512:(fc + 1) * 512],
                                        start=(fc == 0), stop=(fc == 7))
                    tS.inc(ins)

                blkB.tensor(b3t)

                def b3e(vec):
                    tS.wait(vec)
                    vS.selfw(vec)
                    vS.inc(vec.tensor_copy(y1_bf[:], psM[:, 0:512]))

                blkB.vector(b3e)

                for g in range(NG):
                    def bg_t(te, g=g):
                        vS.wait(te)
                        if g >= 1:
                            sS.wait(te)
                        for half in range(2):
                            ins = te.matmul(psM[:, half * 512:(half + 1) * 512],
                                            y1_bf[:, g * 128:(g + 1) * 128],
                                            wg2_sb[:, half * 512:(half + 1) * 512],
                                            start=True, stop=True)
                        tS.inc(ins)

                    blkB.tensor(bg_t)

                    def bg_s(sc, g=g, j=j):
                        tS.wait(sc)
                        if j >= 1:
                            sc.wait_ge(vS.sem, vb_b5_end[j - 1])
                        sS.selfw(sc)
                        sS.inc(sc.activation(gate_t[:, g * 1024:(g + 1) * 1024],
                                             psM[:, 0:1024], AF.Sigmoid))

                    blkB.scalar(bg_s)

                def b4l(sync, j=j):
                    if j >= 1:
                        sync.wait_ge(vS.sem, vb_b5_end[j - 1])
                    ld.selfw(sync)
                    ld.inc(sync.dma_start(
                        out=bass.AP(o16t, 0, [[4096, 128], [1024, 4], [1, 1024]]),
                        in_=bass.AP(o_sp, j * TILE * E,
                                    [[E, 128], [128 * E, 4], [1, E]])), 16)
                    ld.selfw(sync)
                    ld.inc(sync.dma_start(
                        out=bass.AP(qht, 0, [[4096, 128], [512, 8], [1, 512]]),
                        in_=bass.AP(qh_sp, j * TILE, [[T, 128], [128 * T, 8], [1, 512]])), 16)

                blkB.sync(b4l)

                for g in range(NG):
                    def b4t(te, g=g):
                        ld.wait(te)
                        gS.wait(te)
                        vS.wait(te)
                        sS.wait(te)
                        for h in range(H):
                            ins = te.matmul(psM[:, 1024 + h * 128: 1024 + (h + 1) * 128],
                                            qht[:, h * 512 + g * 128: h * 512 + (g + 1) * 128],
                                            S_in_bf[:, h * 128:(h + 1) * 128],
                                            start=(h == 0) or (h == 4),
                                            stop=(h == 3) or (h == 7))
                        tS.inc(ins)

                    blkB.tensor(b4t)

                    def b5v(vec, g=g):
                        tS.wait(vec)
                        sS.wait(vec)
                        vS.selfw(vec)
                        vS.inc(vec.tensor_add(ofull[:], psM[:, 1024:2048],
                                              o16t[:, g * 1024:(g + 1) * 1024]))
                        vS.selfw(vec)
                        vS.inc(vec.tensor_mul(og_t[:, g * 1024:(g + 1) * 1024], ofull[:],
                                              gate_t[:, g * 1024:(g + 1) * 1024]))

                    blkB.vector(b5v)

                vb_b5_end.append(vS.n)
                emit_ln_bn(blkB, lambda g, h: og_t[:, g * 1024 + h * 512: g * 1024 + h * 512 + 512], sm)

                for g in range(NG):
                    def b6v(vec, g=g):
                        sS.wait(vec)
                        tS.wait(vec)
                        vS.selfw(vec)
                        vS.inc(vec.tensor_scalar(ohat_scr[:],
                                                 og_t[:, g * 1024:(g + 1) * 1024],
                                                 sm["mv"][:, 2 * g:2 * g + 1],
                                                 sm["rcol"][:, g:g + 1],
                                                 OP.subtract, OP.mult))
                        vS.selfw(vec)
                        vS.inc(vec.tensor_mul(ohat_bf[:], ohat_scr[:], lnwb_sb[:]))

                    blkB.vector(b6v)

                    def b6t(te, g=g):
                        vS.wait(te)
                        for ci in range(8):
                            ins = te.transpose(psTP[:, ci * 128:(ci + 1) * 128],
                                               ohat_bf[:, ci * 128:(ci + 1) * 128], ident[:])
                        tS.inc(ins)

                    blkB.tensor(b6t)

                    def b6e(vec, g=g):
                        tS.wait(vec)
                        vS.selfw(vec)
                        vS.inc(vec.tensor_copy(
                            bass.AP(ohat_fm, g * 128, [[4096, 128], [512, 8], [1, 128]]),
                            bass.AP(psTP, 0, [[1024, 128], [128, 8], [1, 128]])))

                    blkB.vector(b6e)

                for g in range(NG):
                    def b7t(te, g=g):
                        vS.wait(te)
                        sS.wait(te)
                        for half in range(2):
                            for ec in range(8):
                                ins = te.matmul(
                                    psX[:, half * 512:(half + 1) * 512],
                                    ohat_fm[:, ec * 512 + g * 128: ec * 512 + (g + 1) * 128],
                                    wout_sb[:, ec * E + half * 512: ec * E + (half + 1) * 512],
                                    start=(ec == 0), stop=(ec == 7))
                        tS.inc(ins)

                    blkB.tensor(b7t)

                    def b7e(vec, g=g):
                        tS.wait(vec)
                        vS.selfw(vec)
                        vS.inc(vec.tensor_copy(araw_t[:, g * 1024:(g + 1) * 1024],
                                               psX[:, 0:1024]))

                    blkB.vector(b7e)

                emit_ln_bn(blkB, lambda g, h: araw_t[:, g * 1024 + h * 512: g * 1024 + h * 512 + 512], sm)

                for g in range(NG):
                    def b8v4(vec, g=g, xo=xo):
                        sS.wait(vec)
                        vS.selfw(vec)
                        vS.inc(vec.tensor_scalar(t2scr[:],
                                                 araw_t[:, g * 1024:(g + 1) * 1024],
                                                 sm["mv"][:, 2 * g:2 * g + 1],
                                                 sm["rcol"][:, g:g + 1],
                                                 OP.subtract, OP.mult))
                        so.wait(vec)
                        vS.selfw(vec)
                        vS.inc(vec.tensor_add(x1g[:], t2scr[:],
                                              x_sb[:, xo + g * 1024: xo + (g + 1) * 1024]))

                    blkB.vector(b8v4)

                    def b8d(sc, g=g, j=j):
                        vS.wait(sc)
                        sS.selfw(sc)
                        so.selfw(sc)
                        so.inc(sc.dma_start(
                            out=bass.AP(x1_sp, (j * TILE + g * 128) * E, [[E, 128], [1, E]]),
                            in_=bass.AP(x1g, 0, [[1024, 128], [1, 1024]])), 16)

                    blkB.scalar(b8d)
                vb_b8_end.append(vS.n)

            def b9(sc):
                so.wait(sc)

            blkB.scalar(b9)

        if "B2" not in phases:
            return nc
        with ExitStack() as b2x:
            Eb = b2x.enter_context
            wQ_sb = Eb(nc.sbuf_tensor("wQ_sb", [128, 8 * E], BF16))
            wK_sb = Eb(nc.sbuf_tensor("wK_sb", [128, 8 * E], BF16))
            winv_sb = Eb(nc.sbuf_tensor("winv_sb", [128, 8 * E], BF16))
            rw1_sb = Eb(nc.sbuf_tensor("rw1_sb", [128, 8 * E], BF16))
            rw2_sb = Eb(nc.sbuf_tensor("rw2_sb", [128, 64], BF16))
            rb1_sb = Eb(nc.sbuf_tensor("rb1_sb", [128, 8], F32))
            vfwd_sb = Eb(nc.sbuf_tensor("vfwd_sb", [128, 8 * 512], BF16))
            wfwd_sb = Eb(nc.sbuf_tensor("wfwd_sb", [128, 8 * 512], BF16))
            winvex_sb = Eb(nc.sbuf_tensor("winvex_sb", [128, 8 * 512], BF16))
            vinv_sb = Eb(nc.sbuf_tensor("vinv_sb", [128, 8 * 512], BF16))
            Ec_sb = Eb(nc.sbuf_tensor("Ec_sb", [8, 512], BF16))
            x1_sb = Eb(nc.sbuf_tensor("x1B2", [128, 4096], F32))
            xh_tm = Eb(nc.sbuf_tensor("xh2_tm", [128, 1024], BF16))
            xh_fm = Eb(nc.sbuf_tensor("xh2_fm", [128, 4096], BF16))
            KA = Eb(nc.sbuf_tensor("KA", [128, 4096], BF16))
            KB = Eb(nc.sbuf_tensor("KB", [128, 4096], BF16))
            h_fm = Eb(nc.sbuf_tensor("h_fm", [128, 4096], BF16))
            yw32 = Eb(nc.sbuf_tensor("yw32", [128, 2048], F32))
            cum32 = Eb(nc.sbuf_tensor("cum32", [128, 2048], F32))
            cum16 = Eb(nc.sbuf_tensor("cum16", [128, 2048], F16))
            xvw16 = Eb(nc.sbuf_tensor("xvw16", [128, 2048], F16))
            wbar = Eb(nc.sbuf_tensor("wbar", [128, 512], BF16))
            wtm = Eb(nc.sbuf_tensor("wtm", [128, 32], F32))
            wtm_bf = Eb(nc.sbuf_tensor("wtm_bf", [128, 32], BF16))
            wsm_fm = Eb(nc.sbuf_tensor("wsm_fm", [8, 512], BF16))
            lgt = Eb(nc.sbuf_tensor("lgt", [128, 32], F32))
            smax = Eb(nc.sbuf_tensor("smax", [128, 8], F32))
            ssum = Eb(nc.sbuf_tensor("ssum", [128, 4], F32))
            esm = Eb(nc.sbuf_tensor("esm", [128, 32], F32))
            carF = Eb(nc.sbuf_tensor("carF", [128, 4], F32))
            carI = Eb(nc.sbuf_tensor("carI", [128, 4], F32))
            sm = {"st": Eb(nc.sbuf_tensor("smC_st", [128, 48], F32)),
                  "mv": Eb(nc.sbuf_tensor("smC_mv", [128, 8], F32)),
                  "rcol": Eb(nc.sbuf_tensor("smC_rcol", [128, 8], F32))}
            psTP = Eb(nc.psum_tensor("psTPc", [128, 1024], BF16))
            psM = Eb(nc.psum_tensor("psMc", [128, 2048], F32))
            psX = Eb(nc.psum_tensor("psXc", [128, 1024], F32))
            blk2 = Eb(nc.Block())

            def c0(sync):
                so.wait(sync)
                sync.wait_ge(wss[0].sem, wst_final[0])
                sync.wait_ge(wss[1].sem, wst_final[1])
                for nm, dst, w in (("wQT", wQ_sb, E), ("wKT", wK_sb, E),
                                   ("winvT", winv_sb, E), ("rw1T", rw1_sb, E),
                                   ("vfwdF", vfwd_sb, 512), ("wfwdF", wfwd_sb, 512),
                                   ("winvexF", winvex_sb, 512), ("vinvF", vinv_sb, 512)):
                    ld.selfw(sync)
                    ld.inc(sync.dma_start(
                        out=bass.AP(dst, 0, [[8 * w, 128], [1, 8 * w]]),
                        in_=wb16[nm][:, :]), 16)
                ld.selfw(sync)
                ld.inc(sync.dma_start(
                    out=bass.AP(rw2_sb, 0, [[64, 128], [1, 64]]),
                    in_=wb16["rw2T"][:, :]), 16)

            blk2.sync(c0)

            def c0g(gp):
                gS.selfw(gp)
                gS.inc(gp.memset(carF[:], 0.0))
                gS.selfw(gp)
                gS.inc(gp.memset(carI[:], 0.0))

            blk2.gpsimd(c0g)

            def c0s(sync):
                ld.selfw(sync)
                ld.inc(sync.dma_start(out=rb1_sb[:], in_=din["rb1c"][:, :]), 16)
                ld.selfw(sync)
                ld.inc(sync.dma_start(out=Ec_sb[:], in_=din["Econst_in"][:, :]), 16)

            blk2.sync(c0s)

            prevB2 = {"vb_x1free": None}
            for j in range(NT):
                def d1(sync, j=j, pv=dict(prevB2)):
                    if pv["vb_x1free"] is not None:
                        sync.wait_ge(vS.sem, pv["vb_x1free"])
                    else:
                        tS.wait(sync)
                        vS.wait(sync)
                        sS.wait(sync)
                        so.wait(sync)
                    ld.selfw(sync)
                    ld.inc(sync.dma_start(
                        out=bass.AP(x1_sb, 0, [[4096, 128], [1024, 4], [1, 1024]]),
                        in_=bass.AP(x1_sp, j * TILE * E, [[E, 128], [128 * E, 4], [1, E]]),
                    ), 16)

                blk2.sync(d1)
                ldw = (ld.sem, ld.n)
                gw = (ld.sem, ld.n)

                emit_ln_bn(blk2, lambda g, h: x1_sb[:, g * 1024 + h * 512: g * 1024 + h * 512 + 512],
                           sm, extra_wait=(lambda vec, w=ldw: vec.wait_ge(*w)))
                for g in range(NG):
                    emit_norm_transpose(blk2, x1_sb, xh_tm, xh_fm, psTP, sm, g)
                prevB2["vb_x1free"] = vS.n

                def mk_prod_fm(wsb, src, dst):
                    vb0 = vS.n

                    def pf(te):
                        vS.wait(te)
                        sS.wait(te)
                        gw and te.wait_ge(*gw)
                        for oc in range(8):
                            reg = 1024 + (oc % 2) * 512
                            if oc >= 2:
                                te.wait_ge(vS.sem, vb0 + oc - 1)
                            for fc in range(8):
                                ins = te.matmul(psM[:, reg:reg + 512],
                                                wsb[:, fc * E + oc * 128: fc * E + (oc + 1) * 128],
                                                src[:, fc * 512:(fc + 1) * 512],
                                                start=(fc == 0), stop=(fc == 7))
                            tS.inc(ins)

                    tb0 = tS.n
                    blk2.tensor(pf)

                    def pe(vec):
                        for oc in range(8):
                            tS.wait(vec, tb0 + oc + 1)
                            reg = 1024 + (oc % 2) * 512
                            vS.selfw(vec)
                            vS.inc(vec.tensor_copy(dst[:, oc * 512:(oc + 1) * 512],
                                                   psM[:, reg:reg + 512]))

                    blk2.vector(pe)

                mk_prod_fm(wK_sb, xh_fm, KA)
                mk_prod_fm(winv_sb, KA, KB)

                for br, (ksrc, wflat, car) in enumerate(
                        ((KA, wfwd_sb, carF), (KB, vinv_sb, carI))):
                    vb0 = vS.n

                    def dyw(te, ksrc=ksrc, wflat=wflat, vb0=vb0):
                        vS.wait(te)
                        for kc in range(4):
                            reg = 1024 + (kc % 2) * 512
                            if kc >= 2:
                                te.wait_ge(vS.sem, vb0 + kc - 1)
                            for fc in range(8):
                                ins = te.matmul(psM[:, reg:reg + 512],
                                                wflat[:, fc * 512 + kc * 128: fc * 512 + (kc + 1) * 128],
                                                ksrc[:, fc * 512:(fc + 1) * 512],
                                                start=(fc == 0), stop=(fc == 7))
                            tS.inc(ins)

                    tb0 = tS.n
                    blk2.tensor(dyw)

                    def dywe(vec, tb0=tb0):
                        for kc in range(4):
                            tS.wait(vec, tb0 + kc + 1)
                            reg = 1024 + (kc % 2) * 512
                            vS.selfw(vec)
                            vS.inc(vec.tensor_copy(yw32[:, kc * 512:(kc + 1) * 512],
                                                   psM[:, reg:reg + 512]))

                    blk2.vector(dywe)

                    def dsc(vec, car=car):
                        sS.wait(vec)
                        gS.wait(vec)
                        vS.selfw(vec)
                        for kc in range(4):
                            vS.selfw(vec)
                            vS.inc(vec.tensor_tensor_scan(
                                cum32[:, kc * 512:(kc + 1) * 512],
                                yw32[:, kc * 512:(kc + 1) * 512],
                                yw32[:, kc * 512:(kc + 1) * 512],
                                car[:, kc:kc + 1], OP.add, OP.bypass))

                    blk2.vector(dsc)

                    def dcar(sc, car=car, br=br, j=j):
                        vS.wait(sc)
                        so.wait(sc)
                        sS.selfw(sc)
                        sS.inc(sc.copy(car[:], bass.AP(cum32, 511, [[2048, 128], [512, 4]])))
                        sS.selfw(sc)
                        sS.inc(sc.copy(cum16[:], cum32[:]))
                        sS.selfw(sc)
                        so.selfw(sc)
                        so.inc(sc.dma_start(
                            out=bass.AP(cum_sp, br * 4 * 128 * T + j * TILE,
                                        [[T, 128], [128 * T, 4], [1, 512]]),
                            in_=bass.AP(cum16, 0, [[2048, 128], [512, 4], [1, 512]])), 16)

                    blk2.scalar(dcar)

                mk_prod_fm(wQ_sb, xh_fm, KA)
                mk_prod_fm(winv_sb, KA, KB)

                for br, (qsrc, vflat) in enumerate(((KA, vfwd_sb), (KB, winvex_sb))):
                    sb0 = sS.n

                    def dh(te, qsrc=qsrc, sb0=sb0):
                        vS.wait(te)
                        sS.wait(te)
                        for oc in range(8):
                            reg = 1024 + (oc % 2) * 512
                            if oc >= 2:
                                te.wait_ge(sS.sem, sb0 + oc - 1)
                            for fc in range(8):
                                ins = te.matmul(psM[:, reg:reg + 512],
                                                rw1_sb[:, fc * E + oc * 128: fc * E + (oc + 1) * 128],
                                                qsrc[:, fc * 512:(fc + 1) * 512],
                                                start=(fc == 0), stop=(fc == 7))
                            tS.inc(ins)

                    tb0 = tS.n
                    blk2.tensor(dh)

                    def dhe(sc, tb0=tb0):
                        for oc in range(8):
                            tS.wait(sc, tb0 + oc + 1)
                            reg = 1024 + (oc % 2) * 512
                            sS.selfw(sc)
                            sS.inc(sc.activation(h_fm[:, oc * 512:(oc + 1) * 512],
                                                 psM[:, reg:reg + 512], AF.Gelu,
                                                 bias=rb1_sb[:, oc:oc + 1]))

                    blk2.scalar(dhe)

                    def dlg(te):
                        sS.wait(te)
                        vS.wait(te)
                        for g in range(NG):
                            for rc in range(8):
                                ins = te.matmul(
                                    psX[:, g * 8:g * 8 + 8],
                                    h_fm[:, rc * 512 + g * 128: rc * 512 + (g + 1) * 128],
                                    rw2_sb[:, rc * 8:(rc + 1) * 8],
                                    start=(rc == 0), stop=(rc == 7))
                        tS.inc(ins)

                    blk2.tensor(dlg)

                    def dsm(vec):
                        tS.wait(vec)
                        sS.wait(vec)
                        vS.selfw(vec)
                        vS.inc(vec.tensor_tensor(
                            bass.AP(lgt, 0, [[32, 128], [8, 4], [1, 8]]),
                            bass.AP(psX, 0, [[1024, 128], [8, 4], [1, 8]]),
                            bass.AP(b2pe_b, 0, [[8, 128], [0, 4], [1, 8]]),
                            OP.add))
                        for g in range(NG):
                            vS.selfw(vec)
                            vS.inc(vec.tensor_reduce(smax[:, g:g + 1],
                                                     lgt[:, g * 8:(g + 1) * 8],
                                                     AX.X, OP.max))
                        vS.selfw(vec)
                        vS.inc(vec.tensor_scalar(smax[:, 4:8], smax[:, 0:4],
                                                 -1.0, None, OP.mult, OP.bypass))

                    blk2.vector(dsm)

                    def dex(sc):
                        vS.wait(sc)
                        for g in range(NG):
                            sS.selfw(sc)
                            sS.inc(sc.activation(esm[:, g * 8:(g + 1) * 8],
                                                 lgt[:, g * 8:(g + 1) * 8], AF.Exp,
                                                 bias=smax[:, 4 + g:5 + g]))

                    blk2.scalar(dex)

                    def dsm2(vec, br=br, j=j):
                        sS.wait(vec)
                        for g in range(NG):
                            vS.selfw(vec)
                            vS.inc(vec.tensor_reduce(ssum[:, g:g + 1],
                                                     esm[:, g * 8:(g + 1) * 8],
                                                     AX.X, OP.add))
                        vS.selfw(vec)
                        vS.inc(vec.reciprocal(ssum[:, 0:4], ssum[:, 0:4]))
                        if br == 1:
                            vS.selfw(vec)
                            vS.inc(vec.tensor_scalar(ssum[:, 0:4], ssum[:, 0:4],
                                                     alphac[:, 0:1], None,
                                                     OP.mult, OP.bypass))
                        for g in range(NG):
                            vS.selfw(vec)
                            vS.inc(vec.tensor_scalar(wtm[:, g * 8:(g + 1) * 8],
                                                     esm[:, g * 8:(g + 1) * 8],
                                                     ssum[:, g:g + 1],
                                                     rp[:, j * 4 + g: j * 4 + g + 1],
                                                     OP.mult, OP.mult))
                        vS.selfw(vec)
                        vS.inc(vec.tensor_copy(wtm_bf[:], wtm[:]))

                    blk2.vector(dsm2)

                    def dwt(te):
                        vS.wait(te)
                        for g in range(NG):
                            ins = te.transpose(
                                bass.AP(psTP, g * 128, [[1024, 8], [1, 128]]),
                                wtm_bf[:, g * 8:(g + 1) * 8], ident[:])
                        tS.inc(ins)

                    blk2.tensor(dwt)

                    def dwte(vec):
                        tS.wait(vec)
                        so.wait(vec)
                        vS.selfw(vec)
                        vS.inc(vec.tensor_copy(
                            bass.AP(wsm_fm, 0, [[512, 8], [128, 4], [1, 128]]),
                            bass.AP(psTP, 0, [[1024, 8], [128, 4], [1, 128]])))

                    blk2.vector(dwte)

                    def dwsp(sc, br=br, j=j):
                        vS.wait(sc)
                        sS.selfw(sc)
                        so.selfw(sc)
                        so.inc(sc.dma_start(
                            out=bass.AP(w_sp, br * 8 * T + j * TILE, [[T, 8], [1, 512]]),
                            in_=bass.AP(wsm_fm, 0, [[512, 8], [1, 512]])), 16)

                    blk2.scalar(dwsp)

                    vb0 = vS.n

                    def dxv(te, qsrc=qsrc, vflat=vflat, vb0=vb0):
                        vS.wait(te)
                        for kc in range(4):
                            reg = 1024 + (kc % 2) * 512
                            if kc >= 1:
                                te.wait_ge(vS.sem, vb0 + 2 * kc - 1)
                            for fc in range(8):
                                te.matmul(psM[:, reg:reg + 512],
                                          vflat[:, fc * 512 + kc * 128: fc * 512 + (kc + 1) * 128],
                                          qsrc[:, fc * 512:(fc + 1) * 512],
                                          start=(fc == 0), stop=(fc == 7))
                            ins = te.matmul(psX[:, 0:512], Ec_sb[:, kc * 128:(kc + 1) * 128],
                                            wsm_fm[:, :], start=True, stop=True)
                            tS.inc(ins)

                    tb0 = tS.n
                    blk2.tensor(dxv)

                    def dxve(vec, tb0=tb0):
                        so.wait(vec)
                        for kc in range(4):
                            tS.wait(vec, tb0 + kc + 1)
                            reg = 1024 + (kc % 2) * 512
                            vS.selfw(vec)
                            vS.inc(vec.tensor_copy(wbar[:], psX[:, 0:512]))
                            vS.selfw(vec)
                            vS.inc(vec.tensor_mul(xvw16[:, kc * 512:(kc + 1) * 512],
                                                  psM[:, reg:reg + 512], wbar[:]))

                    blk2.vector(dxve)

                    def dxsp(sc, br=br, j=j):
                        vS.wait(sc)
                        sS.selfw(sc)
                        so.selfw(sc)
                        so.inc(sc.dma_start(
                            out=bass.AP(xvw_sp, br * 4 * 128 * T + j * TILE,
                                        [[T, 128], [128 * T, 4], [1, 512]]),
                            in_=bass.AP(xvw16, 0, [[2048, 128], [512, 4], [1, 512]])), 16)

                    blk2.scalar(dxsp)

            def e2v(vec):
                sS.wait(vec)
                vS.selfw(vec)
                vS.inc(vec.tensor_scalar(carF[:], carF[:], evenm[:, 0:1], None,
                                         OP.mult, OP.bypass))
                vS.selfw(vec)
                vS.inc(vec.tensor_scalar(carI[:], carI[:], evenm[:, 0:1], None,
                                         OP.mult, OP.bypass))

            blk2.vector(e2v)

            def e2g(gp):
                vS.wait(gp)
                so.wait(gp)
                gld.selfw(gp)
                gld.inc(gp.dma_start(out=bass.AP(cc2_in, 0, [[4, 128], [1, 4]]),
                                     in_=carF[:]), 16)
                gld.selfw(gp)
                gld.inc(gp.dma_start(out=bass.AP(cc2_in, 512, [[4, 128], [1, 4]]),
                                     in_=carI[:]), 16)

            blk2.gpsimd(e2g)

        if "AR2" not in phases:
            return nc
        blkC2_cm = nc.Block()
        blkC2 = blkC2_cm.__enter__()

        def cc2f(gp):
            gld.wait(gp)
            ccS.inc(gp.collective_compute(
                "AllReduce", OP.add,
                replica_groups=[[0, 1], [2, 3], [4, 5], [6, 7]],
                ins=[cc2_in.ap().opt()], outs=[cc2_out.ap().opt()]))
            ccS.wait(gp)
            gld.selfw(gp)
            gld.inc(gp.dma_start(out=totF[:], in_=bass.AP(cc2_out, 0, [[4, 128], [1, 4]])), 16)
            gld.selfw(gp)
            gld.inc(gp.dma_start(out=totI[:], in_=bass.AP(cc2_out, 512, [[4, 128], [1, 4]])), 16)

        blkC2.gpsimd(cc2f)

        def cc2v(vec):
            gld.wait(vec)
            vS.selfw(vec)
            vS.inc(vec.tensor_scalar(totF[:], totF[:], oddm[:, 0:1], None, OP.mult, OP.bypass))
            vS.selfw(vec)
            gS.inc(vec.tensor_scalar(totI[:], totI[:], oddm[:, 0:1], None, OP.mult, OP.bypass))

        blkC2.vector(cc2v)
        blkC2_cm.__exit__(None, None, None)

        if "C" not in phases:
            return nc
        with ExitStack() as cctx:
            Ec_ = cctx.enter_context
            Uf_sb = Ec_(nc.sbuf_tensor("Uf_sb", [128, 4 * E], BF16))
            Ui_sb = Ec_(nc.sbuf_tensor("Ui_sb", [128, 4 * E], BF16))
            bf_sb = Ec_(nc.sbuf_tensor("bf_sb", [8, E], BF16))
            bi_sb = Ec_(nc.sbuf_tensor("bi_sb", [8, E], BF16))
            wO_sb = Ec_(nc.sbuf_tensor("wO_sb", [128, 8 * E], BF16))
            x1_sb = Ec_(nc.sbuf_tensor("x1C", [128, 2 * 4096], F32))
            xvwL = Ec_(nc.sbuf_tensor("xvwL", [128, 4096], F16))
            cumL = Ec_(nc.sbuf_tensor("cumL", [128, 4096], F16))
            wfmL = Ec_(nc.sbuf_tensor("wfmL", [8, 1024], BF16))
            tscr = Ec_(nc.sbuf_tensor("tscr", [128, 2048], F32))
            prodw = Ec_(nc.sbuf_tensor("prodw", [128, 2048], BF16))
            prodw2 = Ec_(nc.sbuf_tensor("prodw2", [128, 2048], BF16))
            comb = Ec_(nc.sbuf_tensor("comb", [128, 4096], BF16))
            yg = Ec_(nc.sbuf_tensor("yg", [128, 1024], F32))
            wraw_t = Ec_(nc.sbuf_tensor("wraw_t", [128, 4096], F32))
            t2scr = Ec_(nc.sbuf_tensor("t2C", [128, 1024], F32))
            sm = {"st": Ec_(nc.sbuf_tensor("smD_st", [128, 48], F32)),
                  "mv": Ec_(nc.sbuf_tensor("smD_mv", [128, 8], F32)),
                  "rcol": Ec_(nc.sbuf_tensor("smD_rcol", [128, 8], F32))}
            psU = Ec_(nc.psum_tensor("psU", [128, 2048], F32))
            psW = Ec_(nc.psum_tensor("psW", [128, 1024], F32))
            psE = Ec_(nc.psum_tensor("psE", [128, 1024], F32))
            blkC = Ec_(nc.Block())

            def f0(sync):
                so.wait(sync)
                sync.wait_ge(wss[0].sem, wst_final[0])
                sync.wait_ge(wss[1].sem, wst_final[1])
                for nm, dst in (("UfF", Uf_sb), ("UiF", Ui_sb)):
                    ld.selfw(sync)
                    ld.inc(sync.dma_start(
                        out=bass.AP(dst, 0, [[4 * E, 128], [1, 4 * E]]),
                        in_=wb16[nm][:, :]), 16)
                ld.selfw(sync)
                ld.inc(sync.dma_start(
                    out=bass.AP(wO_sb, 0, [[8 * E, 128], [1, 8 * E]]),
                    in_=wb16["wOT"][:, :]), 16)
                ld.selfw(sync)
                ld.inc(sync.dma_start(out=bass.AP(bf_sb, 0, [[E, 8], [1, E]]),
                                       in_=wb16["bfw"][:, :]), 16)
                ld.selfw(sync)
                ld.inc(sync.dma_start(out=bass.AP(bi_sb, 0, [[E, 8], [1, E]]),
                                       in_=wb16["biv"][:, :]), 16)

            blkC.sync(f0)

            vb_g5_end = []
            vb_g3e_end = []
            prevC = {"tb_g3t": 0}
            tb_g3t_last = [0]
            for j in range(NT):
                xo = (j % 2) * 4096

                def g1(sync, j=j, xo=xo):
                    if j == 0:
                        tS.wait(sync)
                        vS.wait(sync)
                        sS.wait(sync)
                        so.wait(sync)
                    elif j >= 2:
                        sync.wait_ge(vS.sem, vb_g5_end[j - 2])
                    ld.selfw(sync)
                    ld.inc(sync.dma_start(
                        out=bass.AP(x1_sb, xo, [[2 * 4096, 128], [1024, 4], [1, 1024]]),
                        in_=bass.AP(x1_sp, j * TILE * E, [[E, 128], [128 * E, 4], [1, E]]),
                    ), 16)

                blkC.sync(g1)

                def g2l(sync, j=j):
                    if j >= 1:
                        sync.wait_ge(vS.sem, vb_g3e_end[j - 1])
                    for br in range(2):
                        ld.inc(sync.dma_start(
                            out=bass.AP(xvwL, br * 2048, [[2 * 2048, 128], [512, 4], [1, 512]]),
                            in_=bass.AP(xvw_sp, br * 4 * 128 * T + j * TILE,
                                        [[T, 128], [128 * T, 4], [1, 512]])), 16)
                        ld.inc(sync.dma_start(
                            out=bass.AP(cumL, br * 2048, [[2 * 2048, 128], [512, 4], [1, 512]]),
                            in_=bass.AP(cum_sp, br * 4 * 128 * T + j * TILE,
                                        [[T, 128], [128 * T, 4], [1, 512]])), 16)
                        ld.inc(sync.dma_start(
                            out=bass.AP(wfmL, br * 512, [[1024, 8], [1, 512]]),
                            in_=bass.AP(w_sp, br * 8 * T + j * TILE, [[T, 8], [1, 512]])), 16)

                blkC.sync(g2l)

                def g2v(vec, pv=dict(prevC)):
                    ld.wait(vec)
                    gS.wait(vec)
                    vec.wait_ge(tS.sem, pv["tb_g3t"])
                    for br in range(2):
                        tot = totF if br == 0 else totI
                        dst = prodw if br == 0 else prodw2
                        vS.selfw(vec)
                        vS.inc(vec.tensor_tensor(
                            bass.AP(tscr, 0, [[2048, 128], [512, 4], [1, 512]]),
                            bass.AP(cumL, br * 2048, [[2 * 2048, 128], [512, 4], [1, 512]]),
                            bass.AP(tot, 0, [[4, 128], [1, 4], [0, 512]]),
                            OP.add))
                        vS.selfw(vec)
                        vS.inc(vec.tensor_mul(dst[:, :],
                                              bass.AP(xvwL, br * 2048,
                                                      [[2 * 2048, 128], [1, 2048]]),
                                              tscr[:]))

                blkC.vector(g2v)
                vb_g2v = vS.n

                for wave in range(2):
                    def g3t(te, wave=wave, vb_g2v=vb_g2v):
                        te.wait_ge(vS.sem, vb_g2v)
                        te.wait_ge(ld.sem, ld.n)
                        for dq in range(4):
                            dc = wave * 4 + dq
                            for b2 in range(2):
                                usb = Uf_sb if b2 == 0 else Ui_sb
                                psrc = prodw if b2 == 0 else prodw2
                                for kc in range(4):
                                    te.matmul(psU[:, dq * 512:(dq + 1) * 512],
                                              usb[:, kc * E + dc * 128: kc * E + (dc + 1) * 128],
                                              psrc[:, kc * 512:(kc + 1) * 512],
                                              start=(b2 == 0 and kc == 0), stop=False)
                            for b2 in range(2):
                                bsb = bf_sb if b2 == 0 else bi_sb
                                ins = te.matmul(psU[:, dq * 512:(dq + 1) * 512],
                                                bsb[:, dc * 128:(dc + 1) * 128],
                                                wfmL[:, b2 * 512:(b2 + 1) * 512],
                                                start=False, stop=(b2 == 1))
                            tS.inc(ins)

                    tb0 = tS.n
                    blkC.tensor(g3t)
                    tb_g3t_last[0] = tS.n

                    def g3e(vec, wave=wave, tb0=tb0):
                        for dq in range(4):
                            dc = wave * 4 + dq
                            tS.wait(vec, tb0 + dq + 1)
                            vS.selfw(vec)
                            vS.inc(vec.tensor_copy(comb[:, dc * 512:(dc + 1) * 512],
                                                   psU[:, dq * 512:(dq + 1) * 512]))

                    blkC.vector(g3e)
                if len(vb_g3e_end) == j:
                    vb_g3e_end.append(vS.n)
                    prevC["tb_g3t"] = tb_g3t_last[0]

                for g in range(NG):
                    def g4t(te, g=g):
                        vS.wait(te)
                        sS.wait(te)
                        for half in range(2):
                            for dc in range(8):
                                ins = te.matmul(
                                    psW[:, half * 512:(half + 1) * 512],
                                    comb[:, dc * 512 + g * 128: dc * 512 + (g + 1) * 128],
                                    wO_sb[:, dc * E + half * 512: dc * E + (half + 1) * 512],
                                    start=(dc == 0), stop=(dc == 7))
                        tS.inc(ins)

                    blkC.tensor(g4t)

                    def g4e(vec, g=g):
                        tS.wait(vec)
                        vS.selfw(vec)
                        vS.inc(vec.tensor_copy(wraw_t[:, g * 1024:(g + 1) * 1024],
                                               psW[:, 0:1024]))

                    blkC.vector(g4e)

                emit_ln_bn(blkC, lambda g, h: wraw_t[:, g * 1024 + h * 512: g * 1024 + h * 512 + 512], sm)

                for g in range(NG):
                    def g5v4(vec, g=g, xo=xo):
                        sS.wait(vec)
                        vS.selfw(vec)
                        vS.inc(vec.tensor_scalar(t2scr[:],
                                                 wraw_t[:, g * 1024:(g + 1) * 1024],
                                                 sm["mv"][:, 2 * g:2 * g + 1],
                                                 sm["rcol"][:, g:g + 1],
                                                 OP.subtract, OP.mult))
                        so.wait(vec)
                        vS.selfw(vec)
                        vS.inc(vec.tensor_add(yg[:], t2scr[:],
                                              x1_sb[:, xo + g * 1024: xo + (g + 1) * 1024]))

                    blkC.vector(g5v4)

                    def g5d(sc, g=g, j=j):
                        vS.wait(sc)
                        sS.selfw(sc)
                        so.selfw(sc)
                        so.inc(sc.dma_start(
                            out=bass.AP(y_d, (j * TILE + g * 128) * E, [[E, 128], [1, E]]),
                            in_=bass.AP(yg, 0, [[1024, 128], [1, 1024]])), 16)

                    blkC.scalar(g5d)
                vb_g5_end.append(vS.n)

            def gfin(sc):
                so.wait(sc)

            blkC.scalar(gfin)

    return nc


_NC_CACHE = {}


def make_host_inputs(inputs):
    ins = {k: np.asarray(v) for k, v in inputs.items()}
    W_in = ins["L0_attn_W_in"]
    host = {}
    host["wvT"] = np.ascontiguousarray(W_in[0:1024].T)
    host["wqT"] = np.ascontiguousarray(W_in[1024:2048].T)
    host["wlgT"] = np.ascontiguousarray(W_in[2048:3072].T)
    host["woutT"] = np.ascontiguousarray(ins["L0_attn_W_out"].T)
    host["wg1T"] = np.ascontiguousarray(ins["L0_attn_W_g1"].T)
    host["wg2T"] = np.ascontiguousarray(ins["L0_attn_W_g2"].T)
    host["lnwb"] = np.ascontiguousarray(np.tile(ins["L0_attn_ln_w"][None, :], (128, 1)).astype(np.float32))
    host["wQT"] = np.ascontiguousarray(ins["L0_ffn_W_Q"].T)
    host["wKT"] = np.ascontiguousarray(ins["L0_ffn_W_K"].T)
    host["winvT"] = np.ascontiguousarray(ins["L0_ffn_W_inv"].T)
    host["wOT"] = np.ascontiguousarray(ins["L0_ffn_W_O"].T)
    host["rw1T"] = np.ascontiguousarray(ins["L0_ffn_router_w1"].T)
    host["rw2T"] = np.ascontiguousarray(ins["L0_ffn_router_w2"].T)
    host["rb1c"] = np.ascontiguousarray(ins["L0_ffn_router_b1"].reshape(8, 128).T)
    host["vfwdF"] = np.ascontiguousarray(ins["L0_ffn_V_fwd"].transpose(1, 0, 2).reshape(1024, 512))
    host["wfwdF"] = np.ascontiguousarray(ins["L0_ffn_W_fwd"].transpose(1, 0, 2).reshape(1024, 512))
    host["winvexF"] = np.ascontiguousarray(ins["L0_ffn_W_inv_exp"].transpose(1, 0, 2).reshape(1024, 512))
    host["vinvF"] = np.ascontiguousarray(ins["L0_ffn_V_inv"].transpose(1, 0, 2).reshape(1024, 512))
    host["UfF"] = np.ascontiguousarray(ins["L0_ffn_U_fwd"].transpose(0, 2, 1).reshape(512, 1024))
    host["UiF"] = np.ascontiguousarray(ins["L0_ffn_U_inv"].transpose(0, 2, 1).reshape(512, 1024))
    host["bfw"] = np.ascontiguousarray(ins["L0_ffn_b_fwd"])
    host["biv"] = np.ascontiguousarray(ins["L0_ffn_b_inv"])
    host["b2pe_in"] = np.tile(np.concatenate(
        [ins["L0_ffn_router_b2"], ins["L0_ffn_expert_bias"]]).reshape(1, 16), (128, 1)).astype(np.float32)
    host["alpha_in"] = np.tile(np.asarray(ins["L0_ffn_alpha_bi"]).reshape(1, 1), (128, 1)).astype(np.float32)
    host["ident_in"] = np.eye(128, dtype=BF)
    m = np.zeros((128, 128), np.float32)
    for b in range(2):
        i0 = b * 64
        m[i0:i0 + 64, i0:i0 + 64] = np.triu(np.ones((64, 64), np.float32))
    host["maskP_in"] = m
    Ec = np.zeros((8, 512), BF)
    for kk in range(8):
        Ec[kk, kk * 64:(kk + 1) * 64] = 1.0
    host["Econst_in"] = Ec
    return host


def kernel(**inputs):
    x = np.ascontiguousarray(
        np.asarray(inputs["L0_attn_x"]).reshape(4 * 8192, 1024).astype(np.float32))
    host = make_host_inputs(inputs)

    if "nc" not in _NC_CACHE:
        _NC_CACHE["nc"] = build_nc()
    nc = _NC_CACHE["nc"]

    in_maps = []
    for cid in range(NCORES):
        half = cid % 2
        im = dict(host)
        im["x"] = x[cid * T:(cid + 1) * T]
        rpj = np.zeros((128, 32), np.float32)
        for gi in range(32):
            base = half * T + gi * 128
            rpj[:, gi] = 1.0 / (base + 1 + np.arange(128, dtype=np.float32))
        im["rp_in"] = rpj
        ev = 1.0 if half == 0 else 0.0
        im["evenm_in"] = np.full((128, 1), ev, np.float32)
        im["oddm_in"] = np.full((128, 1), 1.0 - ev, np.float32)
        in_maps.append(im)

    res = run_bass_kernel_spmd(nc, in_maps, core_ids=list(range(NCORES)))
    out = np.empty((4 * 8192, 1024), np.float32)
    for cid in range(NCORES):
        out[cid * T:(cid + 1) * T] = np.asarray(res.results[cid]["y"], np.float32)
    return out.reshape(4, 8192, 1024)
